# version 01
# baseline (speedup 1.0000x reference)
import contextlib
import math
import numpy as np
import concourse.bass as bass
import concourse.mybir as mybir
from concourse.bass_utils import run_bass_kernel_spmd

F32 = mybir.dt.float32
BF16 = mybir.dt.bfloat16
I32 = mybir.dt.int32
AF = mybir.ActivationFunctionType
ALU = mybir.AluOpType
AX = mybir.AxisListType


_ENG2BLOCK = {"pe": "tensor", "act": "scalar", "dve": "vector", "pool": "gpsimd", "sp": "sync"}
_SEM_CAP = 40000
_DMA_QUEUES = ("sp", "pool", "act")


class _Op:
    __slots__ = ("eng", "fn", "reads", "writes", "kind", "idx", "eidx",
                 "needs_inc", "sem", "val", "amt", "deps", "tag", "wait_list",
                 "_covers", "_dma_idx")

    def __init__(self, eng, fn, reads, writes, kind, tag=""):
        self.eng = eng
        self.fn = fn
        self.reads = reads
        self.writes = writes
        self.kind = kind
        self.tag = tag
        self.needs_inc = False
        self.sem = None
        self.val = None
        self.deps = []


def _rkey(x):
    if isinstance(x, bass.AP):
        t = x.tensor
    else:
        t = x
    return getattr(t, "name", None) or id(t)


class Prog:
    def __init__(self, nc, es):
        self.nc = nc
        self.es = es
        self.ops = []
        self._sem_pools = {}
        self._sem_counts = {}
        self.region_writer = {}
        self.region_readers = {}
        self._final_dma_tokens = []
        self._dma_count = {q: 0 for q in _DMA_QUEUES}
        self._dma_signal = {q: None for q in _DMA_QUEUES}

    def sbuf(self, name, shape, dtype):
        return self.es.enter_context(self.nc.sbuf_tensor(name, list(shape), dtype))

    def psum(self, name, shape, dtype=F32):
        return self.es.enter_context(self.nc.psum_tensor(name, list(shape), dtype))

    def _sem(self, pool, idx):
        lst = self._sem_pools.setdefault(pool, [])
        while len(lst) <= idx:
            lst.append(self.es.enter_context(self.nc.semaphore(f"{pool}_{len(lst)}")))
        return lst[idx]

    def _signal_for(self, queue, upto):
        sig = self._dma_signal.get(queue)
        if sig is not None and sig._covers >= upto:
            return sig
        total = self._dma_count[queue]
        pool = "dma_" + queue
        n = 16 * total
        waits = []
        i = 0
        while i * _SEM_CAP < n:
            waits.append((self._sem(pool, i), min(n - i * _SEM_CAP, _SEM_CAP)))
            i += 1

        def fn(e, waits=tuple(waits)):
            for sem, val in waits:
                e.wait_ge(sem, val)
            return e.nop()

        o = _Op(queue, fn, [], [], "compute", f"dmasig_{queue}_{total}")
        o.idx = len(self.ops)
        o.deps = []
        self.ops.append(o)
        o._covers = total
        self._dma_signal[queue] = o
        return o

    def op(self, eng, fn, reads=(), writes=(), kind="compute", tag=""):
        o = _Op(eng, fn, [_rkey(r) for r in reads], [_rkey(w) for w in writes], kind, tag)
        deps = []
        for r in o.reads:
            w = self.region_writer.get(r)
            if w is not None:
                deps.append(w)
        for wr in o.writes:
            w = self.region_writer.get(wr)
            if w is not None:
                deps.append(w)
            deps.extend(self.region_readers.get(wr, ()))
        uniq = {}
        for d in deps:
            if d is o:
                continue
            if o.eng == "pe" and d.eng == "pe" and o.kind != "dma" and d.kind != "dma":
                continue
            if d.kind == "dma":
                d = self._signal_for(d.eng, d._dma_idx)
            uniq[id(d)] = d
        o.deps = list(uniq.values())
        for d in o.deps:
            d.needs_inc = True
        if kind == "dma":
            self._dma_count[eng] += 1
            o._dma_idx = self._dma_count[eng]
            o.needs_inc = True
        o.idx = len(self.ops)
        self.ops.append(o)
        for wr in o.writes:
            self.region_writer[wr] = o
            self.region_readers[wr] = []
        for r in o.reads:
            self.region_readers.setdefault(r, []).append(o)
        return o

    def dma(self, queue, out, in_, tag=""):
        assert queue in _DMA_QUEUES
        rd = [in_]
        wr = [out]
        return self.op(queue if queue != "act" else "act",
                       lambda e, out=out, in_=in_: e.dma_start(out=out, in_=in_),
                       reads=rd, writes=wr, kind="dma", tag=tag)

    def dma_accum(self, out, in_, tag=""):
        return self.op("pool",
                       lambda e, out=out, in_=in_: e.dma_start(out=out, in_=in_,
                                                               accum_op=ALU.add),
                       reads=[in_, out], writes=[out], kind="dma", tag=tag)

    def mm(self, out, lhsT, rhs, start=True, stop=True, tag=""):
        return self.op("pe",
                       lambda e, out=out, l=lhsT, r=rhs, s1=start, s2=stop:
                           e.matmul(out, l, r, start=s1, stop=s2),
                       reads=[lhsT, rhs], writes=[out], tag=tag)

    def transpose(self, out, in_, ident, tag=""):
        return self.op("pe",
                       lambda e, out=out, i=in_, d=ident: e.transpose(out, i, d),
                       reads=[in_, ident], writes=[out], tag=tag)

    def act(self, out, in_, func, bias=0.0, scale=1.0, accum_out=None, tag=""):
        rd = [in_] + ([bias] if isinstance(bias, bass.AP) else []) \
            + ([scale] if isinstance(scale, bass.AP) else [])
        wr = [out] + ([accum_out] if accum_out is not None else [])
        def fn(e, out=out, in_=in_, func=func, bias=bias, scale=scale, ac=accum_out):
            kw = {}
            if ac is not None:
                kw["accum_out"] = ac
            return e.activation(out, in_, func, bias=bias, scale=scale, **kw)
        return self.op("act", fn, reads=rd, writes=wr, tag=tag)

    def act_copy(self, out, in_, mul=1.0, tag=""):
        rd = [in_] + ([mul] if isinstance(mul, bass.AP) else [])
        return self.op("act", lambda e, o=out, i=in_, m=mul: e.mul(o, i, m),
                       reads=rd, writes=[out], tag=tag)

    def dve(self, method, out, *ins, extra_reads=(), tag="", **kwargs):
        rd = [i for i in ins if isinstance(i, bass.AP)] + list(extra_reads)
        def fn(e, method=method, out=out, ins=ins, kwargs=kwargs):
            return getattr(e, method)(out, *ins, **kwargs)
        return self.op("dve", fn, reads=rd, writes=[out], tag=tag)

    def pool(self, method, out, *ins, tag="", **kwargs):
        rd = [i for i in ins if isinstance(i, bass.AP)]
        def fn(e, method=method, out=out, ins=ins, kwargs=kwargs):
            return getattr(e, method)(out, *ins, **kwargs)
        return self.op("pool", fn, reads=rd, writes=[out], tag=tag)

    def memset(self, eng, ap, val, tag=""):
        return self.op(eng, lambda e, a=ap, v=val: e.memset(a, v),
                       reads=(), writes=[ap], tag=tag)

    def raw(self, eng, fn, reads=(), writes=(), kind="raw", tag=""):
        return self.op(eng, fn, reads=reads, writes=writes, kind=kind, tag=tag)

    def mark_output(self, op):
        pass

    def build(self, block):
        eng_counts = {}
        for o in self.ops:
            if not o.needs_inc:
                continue
            pool = ("dma_" + o.eng) if o.kind == "dma" else ("es_" + o.eng)
            amt = 16 if o.kind == "dma" else 1
            n = self._sem_counts.get(pool, 0) + amt
            self._sem_counts[pool] = n
            idx = (n - 1) // _SEM_CAP
            base = idx * _SEM_CAP
            o.sem = self._sem(pool, idx)
            o.val = n - base
            o.amt = amt

        per_eng = {e: [] for e in _ENG2BLOCK}
        for o in self.ops:
            per_eng[o.eng].append(o)

        floors = {e: {} for e in _ENG2BLOCK}
        for o in self.ops:
            waits = {}
            for d in o.deps:
                assert d.needs_inc and d.sem is not None
                key = d.sem
                waits[key] = max(waits.get(key, 0), d.val)
            fl = floors[o.eng]
            o.wait_list = []
            for sem, val in waits.items():
                if fl.get(sem, 0) >= val:
                    continue
                fl[sem] = val
                o.wait_list.append((sem, val))

        prog = self

        def make_body(eng):
            ops = per_eng[eng]
            def body(e):
                for o in ops:
                    for sem, val in o.wait_list:
                        e.wait_ge(sem, val)
                    inst = o.fn(e)
                    if o.needs_inc:
                        assert inst is not None, f"op {o.tag} needs inc but fn returned None"
                        inst.then_inc(o.sem, o.amt)
            return body

        final_waits = []
        for q in _DMA_QUEUES:
            n = 16 * self._dma_count[q]
            i = 0
            while i * _SEM_CAP < n:
                final_waits.append((self._sem("dma_" + q, i), min(n - i * _SEM_CAP, _SEM_CAP)))
                i += 1

        for eng, battr in _ENG2BLOCK.items():
            ops = per_eng[eng]
            if not ops and not (eng == "pool" and final_waits):
                continue
            body = make_body(eng)
            def wrapped(e, body=body, eng=eng):
                body(e)
                if eng == "pool":
                    for sem, val in final_waits:
                        e.wait_ge(sem, val)
            getattr(block, battr)(wrapped)


BATCH = 4
SEQ = 4096
D = 1024
NCORES = 8
HALF = SEQ // 2
NHEAD = 16
HDIM = 64



def _fm(a):
    return np.ascontiguousarray(np.asarray(a).T)


if __name__ == "__main__":
    print("kernel.py framework module")



def emit_ln_fm(p, x_f32, x_bf, W, out_bf, ps, sc, gamma=None, beta=None,
               nchunks=8, tag="ln"):
    nblk = (W + 511) // 512
    dim = float(nchunks * 128)
    for blk in range(nblk):
        w0 = blk * 512
        bw = min(512, W - w0)
        sums = ps["sums"][0:1, 0:bw]
        sqs = ps["sqs"][0:1, 0:bw]
        for c in range(nchunks):
            xb = x_bf[:, c * W + w0: c * W + w0 + bw]
            p.mm(sums, sc["ones128"][:, 0:1], xb, start=(c == 0), stop=(c == nchunks - 1), tag=f"{tag}_sum")
        for c in range(nchunks):
            xb = x_bf[:, c * W + w0: c * W + w0 + bw]
            sq = sc["sq" + str(c % 2)][:, 0:bw]
            p.act(sq, xb, AF.Square, tag=f"{tag}_sq")
            p.mm(sqs, sc["ones128"][:, 0:1], sq, start=(c == 0), stop=(c == nchunks - 1), tag=f"{tag}_sqs")
        m = sc["row"][0:1, 0:bw]
        msq = sc["row"][0:1, 512:512 + bw]
        var = sc["row"][0:1, 1024:1024 + bw]
        rstd = sc["row"][0:1, 1536:1536 + bw]
        p.dve("tensor_scalar_mul", m, sums, 1.0 / dim, tag=f"{tag}_m")
        p.dve("tensor_scalar_mul", msq, sqs, 1.0 / dim, tag=f"{tag}_msq")
        p.dve("tensor_mul", var, m, m, tag=f"{tag}_m2")
        p.dve("tensor_sub", var, msq, var, tag=f"{tag}_var")
        p.act(rstd, var, AF.Ln, bias=sc["eps"][0:1, 0:1], tag=f"{tag}_ln")
        p.act(rstd, rstd, AF.Exp, scale=-0.5, tag=f"{tag}_rstd")
        p.dve("tensor_scalar_mul", m, m, -1.0, tag=f"{tag}_negm")
        bc_m = ps["bc0"][:, 0:bw]
        bc_r = ps["bc1"][:, 0:bw]
        p.mm(bc_m, sc["ones1_f"][0:1, :], m, tag=f"{tag}_bcm")
        p.mm(bc_r, sc["ones1_f"][0:1, :], rstd, tag=f"{tag}_bcr")
        for c in range(nchunks):
            xf = x_f32[:, c * W + w0: c * W + w0 + bw]
            ob = out_bf[:, c * W + w0: c * W + w0 + bw]
            t = sc["nrm"][:, 0:bw]
            p.dve("tensor_add", t, xf, bc_m, tag=f"{tag}_sub")
            if gamma is None:
                p.dve("tensor_mul", ob, t, bc_r, tag=f"{tag}_mul")
            else:
                t2 = sc["nrm2"][:, 0:bw]
                p.dve("tensor_mul", t2, t, bc_r, tag=f"{tag}_mul")
                p.dve("tensor_scalar", ob, t2, gamma[:, c:c + 1], beta[:, c:c + 1],
                      op0=ALU.mult, op1=ALU.add, tag=f"{tag}_affine")


def emit_linear_fm(p, out_writer, wt, x_bf, W, kchunks, ochunks, pspool,
                   x_col0=0, w_col0=0, w_kc_stride=None, tag="lin", blk_cols=512,
                   extra_rhs=None):
    OUT = ochunks * 128
    if w_kc_stride is None:
        w_kc_stride = OUT
    nblk = (W + blk_cols - 1) // blk_cols
    pi = 0
    for oc in range(ochunks):
        for blk in range(nblk):
            w0 = blk * blk_cols
            bw = min(blk_cols, W - w0)
            ps = pspool[pi % len(pspool)]
            pi += 1
            for kc in range(kchunks):
                c0 = w_col0 + kc * w_kc_stride + oc * 128
                l = wt[:, c0:c0 + 128]
                r = x_bf[:, x_col0 + kc * W + w0: x_col0 + kc * W + w0 + bw]
                p.mm(ps[:, 0:bw], l, r, start=(kc == 0), stop=(kc == kchunks - 1 and extra_rhs is None),
                     tag=f"{tag}_mm")
            if extra_rhs is not None:
                for kc in range(kchunks):
                    c0 = w_col0 + kc * w_kc_stride + oc * 128
                    l = wt[:, c0:c0 + 128]
                    r2 = extra_rhs(kc, w0, bw)
                    if r2 is None:
                        continue
                    p.mm(ps[:, 0:bw], l, r2, start=False, stop=(kc == kchunks - 1),
                         tag=f"{tag}_mm2")
            out_writer(oc, w0, bw, ps[:, 0:bw])


def ln_scratch(p, pref="", blk=512):
    sc = {
        "sq0": p.sbuf(pref + "sq0", [128, blk], BF16),
        "sq1": p.sbuf(pref + "sq1", [128, blk], BF16),
        "row": p.sbuf(pref + "row", [1, 2048], F32),
        "nrm": p.sbuf(pref + "nrm", [128, blk], F32),
        "nrm2": p.sbuf(pref + "nrm2", [128, blk], F32),
        "ones128": p.sbuf(pref + "ones128", [128, 1], BF16),
        "ones1_f": p.sbuf(pref + "ones1f", [1, 128], F32),
        "eps": p.sbuf(pref + "eps", [1, 1], F32),
    }
    p.memset("dve", sc["ones128"][:, :], 1.0)
    p.memset("dve", sc["ones1_f"][:, :], 1.0)
    p.memset("dve", sc["eps"][:, :], 1e-5)
    return sc



def build_launchA():
    nc = bass.Bass("TRN2", target_bir_lowering=False)
    xin = [nc.dram_tensor(f"x{i}", [D, HALF], F32, kind="ExternalInput").ap() for i in range(2)]
    wqk = nc.dram_tensor("wqk", [D, 1024], F32, kind="ExternalInput").ap()
    wv = nc.dram_tensor("wv", [D, 512], F32, kind="ExternalInput").ap()
    qk_out = nc.dram_tensor("qk", [D, SEQ], BF16, kind="ExternalOutput").ap()
    vtm_out = nc.dram_tensor("vtm", [SEQ, 512], BF16, kind="ExternalOutput").ap()

    es = contextlib.ExitStack()
    with es:
        p = Prog(nc, es)
        W = HALF
        xf = p.sbuf("xf", [128, 8 * W], F32)
        xbf = p.sbuf("xbf", [128, 8 * W], BF16)
        xn = p.sbuf("xn", [128, 8 * W], BF16)
        wqk_t = p.sbuf("wqk_t", [128, 8 * 1024], BF16)
        wv_t = p.sbuf("wv_t", [128, 8 * 512], BF16)
        stage = p.sbuf("stage", [128, 512], BF16)
        stage2 = p.sbuf("stage2", [128, 512], BF16)
        sc = ln_scratch(p)
        ps = {
            "sums": p.psum("ps_sums", [1, 512]),
            "sqs": p.psum("ps_sqs", [1, 512]),
            "bc0": p.psum("ps_bc0", [128, 512]),
            "bc1": p.psum("ps_bc1", [128, 512]),
        }
        pspool = [p.psum(f"ps_mm{i}", [128, 512]) for i in range(3)]

        for kc in range(8):
            p.dma("pool", wqk_t[:, kc * 1024:(kc + 1) * 1024], wqk[kc * 128:(kc + 1) * 128, :])
            p.dma("pool", wv_t[:, kc * 512:(kc + 1) * 512], wv[kc * 128:(kc + 1) * 128, :])

        for half in range(2):
            for kc in range(8):
                p.dma("sp", xf[:, kc * W:(kc + 1) * W], xin[half][kc * 128:(kc + 1) * 128, :])
                p.dma("pool", xbf[:, kc * W:(kc + 1) * W], xin[half][kc * 128:(kc + 1) * 128, :])
            emit_ln_fm(p, xf, xbf, W, xn, ps, sc, tag=f"ln0_{half}")

            def qk_writer(oc, w0, bw, ps_ap, half=half):
                st = stage if (oc + w0) % 2 == 0 else stage2
                p.dve("tensor_copy", st[:, 0:bw], ps_ap, tag="qk_evac")
                p.dma("sp", qk_out[oc * 128:(oc + 1) * 128, half * W + w0: half * W + w0 + bw],
                      st[:, 0:bw], tag="qk_st")
            emit_linear_fm(p, qk_writer, wqk_t, xn, W, 8, 8, pspool, tag="qk")

            for tb in range(W // 128):
                psv = pspool[tb % len(pspool)]
                for kc in range(8):
                    l = xn[:, kc * W + tb * 128: kc * W + tb * 128 + 128]
                    r = wv_t[:, kc * 512:(kc + 1) * 512]
                    p.mm(psv[:, 0:512], l, r, start=(kc == 0), stop=(kc == 7), tag="v_mm")
                st = stage if tb % 2 == 0 else stage2
                p.dve("tensor_copy", st[:, 0:512], psv[:, 0:512], tag="v_evac")
                p.dma("sp", vtm_out[half * W + tb * 128: half * W + (tb + 1) * 128, :],
                      st[:, 0:512], tag="v_st")

        block = es.enter_context(nc.Block())
        p.build(block)
    return nc



def build_launchB():
    nc = bass.Bass("TRN2", target_bir_lowering=False)
    qk = nc.dram_tensor("qk", [D, SEQ], BF16, kind="ExternalInput").ap()
    vpack = nc.dram_tensor("vpack", [8 * 128, 2080], BF16, kind="ExternalInput").ap()
    masks = nc.dram_tensor("masks", [4 * 128, 512], BF16, kind="ExternalInput").ap()
    ao = nc.dram_tensor("ao", [512, SEQ], BF16, kind="ExternalOutput").ap()

    es = contextlib.ExitStack()
    with es:
        p = Prog(nc, es)
        qt = [p.sbuf(f"qt{i}", [64, SEQ], BF16) for i in range(2)]
        ktl = [p.sbuf(f"kt{i}", [64, SEQ], BF16) for i in range(2)]
        vt = [p.sbuf(f"vt{i}", [128, 2080], BF16) for i in range(2)]
        mk = p.sbuf("mk", [128, 4 * 512], BF16)
        p2 = [p.sbuf(f"p2_{i}", [128, 1024], BF16) for i in range(2)]
        sq = p.sbuf("sqa", [64, SEQ], BF16)
        ones64 = p.sbuf("ones64", [64, 1], BF16)
        ones1f = p.sbuf("ones1f_b", [1, 128], F32)
        nmax = p.sbuf("nmax", [1, 512], F32)
        cstat = p.sbuf("cstat", [1, 8], F32)
        negc = p.sbuf("negc", [128, 1], F32)
        rrow = p.sbuf("rrow", [1, 512], F32)
        obuf = [p.sbuf(f"obuf{i}", [64, 512], F32) for i in range(2)]
        ostg = [p.sbuf(f"ostg{i}", [64, 512], BF16) for i in range(2)]
        ps_s = [p.psum(f"ps_s{i}", [128, 1024]) for i in range(2)]
        ps_o = [p.psum(f"ps_o{i}", [65, 512]) for i in range(2)]
        ps_bc = p.psum("ps_bcb", [128, 512])
        ps_n = p.psum("ps_n", [1, 512])

        p.memset("dve", ones1f[:, :], 1.0)
        p.memset("dve", ones64[:, :], 1.0)
        for r in range(4):
            p.dma("pool", mk[:, r * 512:(r + 1) * 512], masks[r * 128:(r + 1) * 128, :])

        for h in range(8):
            bi = h % 2
            p.dma("pool", qt[bi][:, :], qk[64 * h:64 * h + 64, :], tag=f"ldq{h}")
            p.dma("pool", ktl[bi][:, :], qk[512 + 64 * h:512 + 64 * h + 64, :], tag=f"ldk{h}")
            p.dma("pool", vt[bi][:, :], vpack[128 * h:128 * h + 128, :], tag=f"ldv{h}")

            for wi, t in ((0, qt[bi]), (1, ktl[bi])):
                p.act(sq[:, :], t[:, :], AF.Square, tag=f"csq{h}")
                for blk in range(8):
                    p.mm(ps_n[0:1, :], ones64[:, 0:1], sq[:, blk * 512:(blk + 1) * 512],
                         tag=f"cn{h}")
                    if blk == 0:
                        p.dve("tensor_copy", nmax[0:1, :], ps_n[0:1, :], tag=f"cmax0{h}")
                    else:
                        p.dve("tensor_max", nmax[0:1, :], nmax[0:1, :], ps_n[0:1, :],
                              tag=f"cmax{h}")
                p.dve("reduce_max", cstat[0:1, wi:wi + 1], nmax[0:1, :], axis=AX.X,
                      tag=f"cred{h}")
            p.dve("tensor_add", cstat[0:1, 2:3], cstat[0:1, 0:1], cstat[0:1, 1:2], tag=f"cc{h}")
            p.dve("tensor_scalar_mul", cstat[0:1, 2:3], cstat[0:1, 2:3], -1.0 / 16.0, tag=f"cs{h}")
            p.mm(ps_bc[:, 0:1], ones1f[0:1, :], cstat[0:1, 2:3], tag=f"cb{h}")
            p.dve("tensor_copy", negc[:, 0:1], ps_bc[:, 0:1], tag=f"cbc{h}")

            for qb in range(8):
                po = ps_o[qb % 2]
                nkt = 4 * qb + 4
                for kp in range(nkt // 2):
                    pss = ps_s[kp % 2]
                    pp = p2[kp % 2]
                    for half in range(2):
                        ktile = 2 * kp + half
                        p.mm(pss[:, half * 512:(half + 1) * 512],
                             ktl[bi][:, 128 * ktile:128 * ktile + 128],
                             qt[bi][:, 512 * qb:512 * qb + 512], tag=f"smm{h}")
                    p.act(pp[:, :], pss[:, :], AF.Exp, bias=negc[:, 0:1], scale=0.125,
                          tag=f"exp{h}")
                    for half in range(2):
                        ktile = 2 * kp + half
                        rel = ktile - 4 * qb
                        if rel >= 0:
                            p.dve("tensor_mul", pp[:, half * 512:(half + 1) * 512],
                                  pp[:, half * 512:(half + 1) * 512],
                                  mk[:, (rel) * 512:(rel + 1) * 512], tag=f"msk{h}")
                    for half in range(2):
                        ktile = 2 * kp + half
                        p.mm(po[:, :], vt[bi][:, 65 * ktile:65 * ktile + 65],
                             pp[:, half * 512:(half + 1) * 512],
                             start=(ktile == 0), stop=(ktile == nkt - 1), tag=f"pv{h}")
                ob = obuf[qb % 2]
                og = ostg[qb % 2]
                p.act(rrow[0:1, :], po[64:65, :], AF.Ln, tag=f"dln{h}")
                p.act(rrow[0:1, :], rrow[0:1, :], AF.Exp, scale=-1.0, tag=f"dexp{h}")
                p.mm(ps_bc[0:64, :], ones1f[0:1, 0:64], rrow[0:1, :], tag=f"dbc{h}")
                p.dve("tensor_copy", ob[:, :], po[0:64, :], tag=f"oevac{h}")
                p.dve("tensor_mul", og[:, :], ob[:, :], ps_bc[0:64, :], tag=f"odiv{h}")
                p.dma("sp", ao[64 * h:64 * h + 64, 512 * qb:512 * (qb + 1)], og[:, :],
                      tag=f"ost{h}")

        block = es.enter_context(nc.Block())
        p.build(block)
    return nc



def build_launchC():
    nc = bass.Bass("TRN2", target_bir_lowering=False)
    xf32 = nc.dram_tensor("xf32", [D, HALF], F32, kind="ExternalInput").ap()
    aof = nc.dram_tensor("aof", [D, HALF], BF16, kind="ExternalInput").ap()
    w_out4 = nc.dram_tensor("w_out4", [D, D], F32, kind="ExternalInput").ap()
    w12 = nc.dram_tensor("w12", [D, D], F32, kind="ExternalInput").ap()
    sw3t = nc.dram_tensor("sw3t", [512, D], F32, kind="ExternalInput").ap()
    pvec = nc.dram_tensor("pvec", [128, 24], F32, kind="ExternalInput").ap()
    sb3v = nc.dram_tensor("sb3v", [128, 8], F32, kind="ExternalInput").ap()
    x1o = nc.dram_tensor("x1o", [D, HALF], F32, kind="ExternalOutput").ap()
    xn1o = nc.dram_tensor("xn1o", [D, HALF], BF16, kind="ExternalOutput").ap()
    sho = nc.dram_tensor("sho", [D, HALF], F32, kind="ExternalOutput").ap()

    es = contextlib.ExitStack()
    with es:
        p = Prog(nc, es)
        W = 512
        wo_t = p.sbuf("wo_t", [128, 8 * 1024], BF16)
        w12_t = p.sbuf("w12_t", [128, 8 * 1024], BF16)
        sw3_t = p.sbuf("sw3_t", [128, 4 * 1024], BF16)
        pv = p.sbuf("pv", [128, 24], F32)
        sb3 = p.sbuf("sb3", [128, 8], F32)
        xs = p.sbuf("xs", [128, 8 * W], F32)
        aos = p.sbuf("aos", [128, 8 * W], BF16)
        x1s = p.sbuf("x1s", [128, 8 * W], F32)
        x1b = p.sbuf("x1b", [128, 8 * W], BF16)
        xn1 = p.sbuf("xn1", [128, 8 * W], BF16)
        h1s = p.sbuf("h1s", [128, 4 * W], BF16)
        h2s = p.sbuf("h2s", [128, 4 * W], BF16)
        prodf = p.sbuf("prodf", [128, 4 * W], F32)
        prodb = p.sbuf("prodb", [128, 4 * W], BF16)
        shn = p.sbuf("shn", [128, 4 * W], BF16)
        shstg = p.sbuf("shstg", [128, W], F32)
        sc = ln_scratch(p)
        ps = {"sums": p.psum("ps_sums", [1, 512]), "sqs": p.psum("ps_sqs", [1, 512]),
              "bc0": p.psum("ps_bc0", [128, 512]), "bc1": p.psum("ps_bc1", [128, 512])}
        pspool = [p.psum(f"ps_mm{i}", [128, 512]) for i in range(3)]

        for kc in range(8):
            p.dma("pool", wo_t[:, kc * 1024:(kc + 1) * 1024], w_out4[kc * 128:(kc + 1) * 128, :])
            p.dma("pool", w12_t[:, kc * 1024:(kc + 1) * 1024], w12[kc * 128:(kc + 1) * 128, :])
        for kc in range(4):
            p.dma("pool", sw3_t[:, kc * 1024:(kc + 1) * 1024], sw3t[kc * 128:(kc + 1) * 128, :])
        p.dma("sp", pv[:, :], pvec)
        p.dma("sp", sb3[:, :], sb3v)

        for s0 in range(0, HALF, W):
            for kc in range(8):
                p.dma("sp", xs[:, kc * W:(kc + 1) * W], xf32[kc * 128:(kc + 1) * 128, s0:s0 + W])
                p.dma("pool", aos[:, kc * W:(kc + 1) * W], aof[kc * 128:(kc + 1) * 128, s0:s0 + W])

            def x1_writer(oc, w0, bw, ps_ap):
                p.dve("scalar_tensor_tensor", x1s[:, oc * W + w0: oc * W + w0 + bw],
                      ps_ap, pv[:, oc:oc + 1], xs[:, oc * W + w0: oc * W + w0 + bw],
                      op0=ALU.add, op1=ALU.add, tag="x1w")
                p.act_copy(x1b[:, oc * W + w0: oc * W + w0 + bw],
                           x1s[:, oc * W + w0: oc * W + w0 + bw], tag="x1cast")
                p.dma("sp", x1o[oc * 128:(oc + 1) * 128, s0 + w0: s0 + w0 + bw],
                      x1s[:, oc * W + w0: oc * W + w0 + bw], tag="x1st")
            emit_linear_fm(p, x1_writer, wo_t, aos, W, 8, 8, pspool, tag="wo")

            emit_ln_fm(p, x1s, x1b, W, xn1, ps, sc, tag="ln1")
            for oc in range(8):
                p.dma("sp", xn1o[oc * 128:(oc + 1) * 128, s0:s0 + W], xn1[:, oc * W:(oc + 1) * W],
                      tag="xn1st")

            def h_writer(oc, w0, bw, ps_ap):
                if oc < 4:
                    p.act(h1s[:, oc * W + w0: oc * W + w0 + bw], ps_ap, AF.Silu,
                          bias=pv[:, 8 + oc:9 + oc], tag="h1w")
                else:
                    p.dve("tensor_scalar", h2s[:, (oc - 4) * W + w0: (oc - 4) * W + w0 + bw],
                          ps_ap, pv[:, 8 + oc:9 + oc], None, op0=ALU.add, tag="h2w")
            emit_linear_fm(p, h_writer, w12_t, xn1, W, 8, 8, pspool, tag="h12")

            for c in range(4):
                p.dve("tensor_mul", prodf[:, c * W:(c + 1) * W], h1s[:, c * W:(c + 1) * W],
                      h2s[:, c * W:(c + 1) * W], tag="prod")
                p.act_copy(prodb[:, c * W:(c + 1) * W], prodf[:, c * W:(c + 1) * W], tag="prodc")
            emit_ln_fm(p, prodf, prodb, W, shn, ps, sc,
                       gamma=pv[:, 16:20], beta=pv[:, 20:24], nchunks=4, tag="sln")

            def sh_writer(oc, w0, bw, ps_ap):
                p.dve("tensor_scalar", shstg[:, w0:w0 + bw], ps_ap, sb3[:, oc:oc + 1], None,
                      op0=ALU.add, tag="shw")
                p.dma("sp", sho[oc * 128:(oc + 1) * 128, s0 + w0: s0 + w0 + bw],
                      shstg[:, w0:w0 + bw], tag="shst")
            emit_linear_fm(p, sh_writer, sw3_t, shn, W, 4, 8, pspool, tag="sw3")

        block = es.enter_context(nc.Block())
        p.build(block)
    return nc



def build_launchD(M=256):
    nc = bass.Bass("TRN2", target_bir_lowering=False)
    xm_in = nc.dram_tensor("xm", [D, M], BF16, kind="ExternalInput").ap()
    shg_in = nc.dram_tensor("shg", [D, M], F32, kind="ExternalInput").ap()
    egw_in = nc.dram_tensor("egw", [D, 9], F32, kind="ExternalInput").ap()
    gb9_in = nc.dram_tensor("gb9", [9, 1], F32, kind="ExternalInput").ap()
    expb_in = nc.dram_tensor("expb", [1, 8], F32, kind="ExternalInput").ap()
    ew12_in = nc.dram_tensor("ew12", [D, 8192], F32, kind="ExternalInput").ap()
    ew3_in = nc.dram_tensor("ew3t", [512, 8192], F32, kind="ExternalInput").ap()
    eb3_in = nc.dram_tensor("eb3t", [8, D], F32, kind="ExternalInput").ap()
    ebv_in = nc.dram_tensor("ebv", [128, 72], F32, kind="ExternalInput").ap()
    id_in = nc.dram_tensor("ident", [128, 128], F32, kind="ExternalInput").ap()
    selm_in = nc.dram_tensor("selm", [9, 9 * 128], F32, kind="ExternalInput").ap()
    comb_o = nc.dram_tensor("comb", [D, M], F32, kind="ExternalOutput").ap()

    nt = M // 128
    es = contextlib.ExitStack()
    with es:
        p = Prog(nc, es)
        xm = p.sbuf("xm_t", [128, 8 * M], BF16)
        shg = p.sbuf("shg_t", [128, 8 * M], F32)
        egw = p.sbuf("egw_t", [128, 8 * 9], BF16)
        gb9 = p.sbuf("gb9_t", [9, 1], F32)
        expb = p.sbuf("expb_t", [1, 8], F32)
        ew12 = [p.sbuf(f"ew12_t{i}", [128, 8 * 1024], BF16) for i in range(2)]
        ew3 = [p.sbuf(f"ew3_t{i}", [128, 4 * 1024], BF16) for i in range(2)]
        eb3 = p.sbuf("eb3_t", [8, D], BF16)
        ebv = p.sbuf("ebv_t", [128, 72], F32)
        ident = p.sbuf("ident_t", [128, 128], F32)
        ones1f = p.sbuf("ones1f_d", [1, 128], F32)
        sgs = p.sbuf("sgs", [9, M], F32)
        stm = p.sbuf("stm", [128, nt * 16], F32)
        gsc = p.sbuf("gsc", [128, nt * 8], F32)
        gml = p.sbuf("gml", [128, nt * 4], F32)
        gpk = p.sbuf("gpk", [128, nt * 9], F32)
        owf = p.sbuf("owf", [9, M], F32)
        owfb = p.sbuf("owfb", [9, M], BF16)
        ebt = p.sbuf("ebt", [128, 8], F32)
        h1s = p.sbuf("h1s_d", [128, 4 * M], BF16)
        h2s = p.sbuf("h2s_d", [128, 4 * M], BF16)
        prodf = p.sbuf("prodf_d", [128, 4 * M], F32)
        prodb = p.sbuf("prodb_d", [128, 4 * M], BF16)
        zn = p.sbuf("zn_d", [128, 4 * M], BF16)
        zs = p.sbuf("zs_d", [128, 4 * M], BF16)
        zbc = p.sbuf("zbc_d", [128, M], F32)
        stg = p.sbuf("stg_d", [128, M], F32)
        sc = ln_scratch(p, "d_")
        stat = p.psum("ps_stat_d", [128, 512])
        ps = {"sums": stat[0:1, :], "sqs": stat[32:33, :],
              "bc0": p.psum("ps_bc0_d", [128, 512]), "bc1": p.psum("ps_bc1_d", [128, 512])}
        pspool = [p.psum("ps_mmd0", [128, 512])]
        ps_out = [p.psum(f"ps_outd{i}", [128, 512]) for i in range(2)]
        acc = p.sbuf("acc_d", [128, 8 * M], F32)
        ps_g = ps["bc1"]

        for kc in range(8):
            p.dma("pool", xm[:, kc * M:(kc + 1) * M], xm_in[kc * 128:(kc + 1) * 128, :])
            p.dma("sp", shg[:, kc * M:(kc + 1) * M], shg_in[kc * 128:(kc + 1) * 128, :])
            p.dma("pool", egw[:, kc * 9:(kc + 1) * 9], egw_in[kc * 128:(kc + 1) * 128, :])
        p.dma("sp", gb9[:, :], gb9_in)
        p.dma("sp", expb[:, :], expb_in)
        p.dma("pool", eb3[:, :], eb3_in)
        p.dma("sp", ebv[:, :], ebv_in)
        p.dma("sp", ident[:, :], id_in)
        selm = p.sbuf("selm_t", [9, 9 * 128], F32)
        p.dma("sp", selm[:, :], selm_in)
        p.memset("dve", ones1f[:, :], 1.0)
        p.memset("dve", acc[:, :], 0.0)

        psg = pspool[0]
        for kc in range(8):
            p.mm(psg[0:9, 0:M], egw[:, kc * 9:(kc + 1) * 9], xm[:, kc * M:(kc + 1) * M],
                 start=(kc == 0), stop=(kc == 7), tag="gmm")
        p.act(sgs[:, :], psg[0:9, 0:M], AF.Sigmoid, bias=gb9[:, 0:1], tag="gsig")
        p.mm(ps_g[:, 0:8], ones1f[0:1, :], expb[0:1, :], tag="ebb")
        p.dve("tensor_copy", ebt[:, :], ps_g[:, 0:8], tag="ebc")
        for t in range(nt):
            p.transpose(ps_g[0:128, 0:9], sgs[:, t * 128:(t + 1) * 128], ident[0:9, 0:9], tag="gt")
            p.dve("tensor_copy", stm[:, t * 16:t * 16 + 9], ps_g[0:128, 0:9], tag="gtc")
            se = stm[:, t * 16:t * 16 + 8]
            ssg = stm[:, t * 16 + 8:t * 16 + 9]
            tp = gsc[:, t * 8:t * 8 + 8]
            m1 = gml[:, t * 4 + 0:t * 4 + 1]
            top1 = gml[:, t * 4 + 1:t * 4 + 2]
            ew = gml[:, t * 4 + 3:t * 4 + 4]
            pk = gpk[:, t * 9:(t + 1) * 9]
            shw = pk[:, 8:9]
            p.dve("tensor_add", tp, se, ebt[:, :], tag="gtp")
            p.dve("reduce_max", m1, tp, axis=AX.X, tag="gm1")
            p.dve("tensor_scalar", tp, tp, m1, None, op0=ALU.is_equal, tag="goh")
            p.dve("tensor_mul", tp, tp, se, tag="gohs")
            p.dve("reduce_max", top1, tp, axis=AX.X, tag="gtop")
            p.dve("tensor_sub", shw, ssg, top1, tag="gd")
            p.act(shw, shw, AF.Sigmoid, tag="gshw")
            p.dve("tensor_scalar", ew, shw, -1.0, 1.0, op0=ALU.mult, op1=ALU.add, tag="gew")
            p.dve("tensor_add", tp, se, ebt[:, :], tag="gtp2")
            p.dve("tensor_scalar", pk[:, 0:8], tp, m1, ew, op0=ALU.is_equal, op1=ALU.mult,
                  tag="gow")
            p.transpose(ps_g[0:9, 0:128], pk, ident[:, :], tag="gt2")
            p.dve("tensor_copy", owf[:, t * 128:(t + 1) * 128], ps_g[0:9, 0:128], tag="gt2c")
            p.dve("tensor_copy", owfb[:, t * 128:(t + 1) * 128], ps_g[0:9, 0:128], tag="gt2b")

        for e in range(8):
            wb = ew12[e % 2]
            w3b = ew3[e % 2]
            for kc in range(8):
                p.dma("pool", wb[:, kc * 1024:(kc + 1) * 1024],
                      ew12_in[kc * 128:(kc + 1) * 128, e * 1024:(e + 1) * 1024], tag=f"ldw12_{e}")
            for kc in range(4):
                p.dma("pool", w3b[:, kc * 1024:(kc + 1) * 1024],
                      ew3_in[kc * 128:(kc + 1) * 128, e * 1024:(e + 1) * 1024], tag=f"ldw3_{e}")
            def h_writer(oc, w0, bw, ps_ap, e=e):
                if oc < 4:
                    p.act(h1s[:, oc * M + w0: oc * M + w0 + bw], ps_ap, AF.Silu,
                          bias=ebv[:, e * 8 + oc:e * 8 + oc + 1], tag="dh1")
                else:
                    p.dve("tensor_scalar", h2s[:, (oc - 4) * M + w0: (oc - 4) * M + w0 + bw],
                          ps_ap, ebv[:, e * 8 + oc:e * 8 + oc + 1], None, op0=ALU.add, tag="dh2")
            emit_linear_fm(p, h_writer, wb, xm, M, 8, 8, pspool, tag=f"dh{e}",
                           blk_cols=min(512, M))
            for c in range(4):
                p.dve("tensor_mul", prodf[:, c * M:(c + 1) * M], h1s[:, c * M:(c + 1) * M],
                      h2s[:, c * M:(c + 1) * M], tag="dprod")
                p.act_copy(prodb[:, c * M:(c + 1) * M], prodf[:, c * M:(c + 1) * M], tag="dprodc")
            emit_ln_fm(p, prodf, prodb, M, zn, ps, sc, gamma=ebv[:, 64:68],
                       beta=ebv[:, 68:72], nchunks=4, tag=f"dsln{e}")
            p.mm(ps["bc0"][:, 0:M], selm[:, e * 128:(e + 1) * 128], owf[:, :], tag="dowb")
            p.dve("tensor_copy", zbc[:, 0:M], ps["bc0"][:, 0:M], tag="dowc")
            for c in range(4):
                p.dve("tensor_mul", zs[:, c * M:(c + 1) * M], zn[:, c * M:(c + 1) * M],
                      zbc[:, 0:M], tag="dzs")
            for oc in range(8):
                po = ps_out[oc % 2][:, 0:M]
                for kc in range(4):
                    c0 = kc * 1024 + oc * 128
                    p.mm(po, w3b[:, c0:c0 + 128], zs[:, kc * M:(kc + 1) * M],
                         start=(kc == 0), stop=(kc == 3), tag=f"dout{e}")
                p.dve("tensor_add", acc[:, oc * M:(oc + 1) * M], acc[:, oc * M:(oc + 1) * M],
                      po, tag="dacc")
        for oc in range(8):
            po = ps_out[oc % 2][:, 0:M]
            p.mm(po, eb3[0:8, oc * 128:(oc + 1) * 128], owfb[0:8, :],
                 start=True, stop=True, tag="deb3")
            p.dve("tensor_add", acc[:, oc * M:(oc + 1) * M], acc[:, oc * M:(oc + 1) * M],
                  po, tag="daccb")
        p.mm(ps["bc0"][:, 0:M], selm[:, 8 * 128:9 * 128], owf[:, :], tag="dshb")
        p.dve("tensor_copy", zbc[:, 0:M], ps["bc0"][:, 0:M], tag="dshc")
        for oc in range(8):
            p.dve("tensor_mul", stg[:, 0:M], shg[:, oc * M:(oc + 1) * M], zbc[:, 0:M],
                  tag="dcmb1")
            p.dve("tensor_add", stg[:, 0:M], stg[:, 0:M], acc[:, oc * M:(oc + 1) * M],
                  tag="dcmb2")
            p.dma("sp", comb_o[oc * 128:(oc + 1) * 128, :], stg[:, 0:M], tag="dcst")
        block = es.enter_context(nc.Block())
        p.build(block)
    return nc



def build_launchE():
    nc = bass.Bass("TRN2", target_bir_lowering=False)
    x1g = nc.dram_tensor("x1g", [D, HALF + 1], F32, kind="ExternalInput").ap()
    f0g = nc.dram_tensor("f0g", [D, HALF + 1], F32, kind="ExternalInput").ap()
    wk3 = nc.dram_tensor("wk3", [D, 3072], F32, kind="ExternalInput").ap()
    lnv_in = nc.dram_tensor("lnv", [128, 16], F32, kind="ExternalInput").ap()
    tmv_in = nc.dram_tensor("tmv", [128, 48], F32, kind="ExternalInput").ap()
    shm_in = nc.dram_tensor("shmask", [128, 1], F32, kind="ExternalInput").ap()
    ko = nc.dram_tensor("ko", [D, HALF], F32, kind="ExternalOutput").ap()
    vo = nc.dram_tensor("vo", [D, HALF], F32, kind="ExternalOutput").ap()
    sro = nc.dram_tensor("sro", [D, HALF], F32, kind="ExternalOutput").ap()
    x2o = nc.dram_tensor("x2o", [D, HALF], F32, kind="ExternalOutput").ap()

    es = contextlib.ExitStack()
    with es:
        p = Prog(nc, es)
        W = 513
        WO = 512
        wkt = p.sbuf("wkt", [128, 8 * 3072], BF16)
        lnv = p.sbuf("lnv_t", [128, 16], F32)
        tmv = p.sbuf("tmv_t", [128, 48], F32)
        shm = p.sbuf("shm_t", [128, 1], F32)
        x1s = p.sbuf("x1s_e", [128, 8 * W], F32)
        f0s = p.sbuf("f0s_e", [128, 8 * W], F32)
        x2s = p.sbuf("x2s_e", [128, 8 * W], F32)
        x2b = p.sbuf("x2b_e", [128, 8 * W], BF16)
        xn0 = p.sbuf("xn0_e", [128, 8 * W], BF16)
        xnl = p.sbuf("xnl_e", [128, 8 * W], BF16)
        dif = p.sbuf("dif_e", [128, 8 * WO], BF16)
        mixk = p.sbuf("mixk_e", [128, 8 * WO], BF16)
        mixv = p.sbuf("mixv_e", [128, 8 * WO], BF16)
        mixr = p.sbuf("mixr_e", [128, 8 * WO], BF16)
        stg = p.sbuf("stg_e", [128, WO], F32)
        sc = ln_scratch(p, "e_")
        stat = p.psum("ps_stat_e", [128, 512])
        ps = {"sums": stat[0:1, :], "sqs": stat[32:33, :],
              "bc0": p.psum("ps_bc0_e", [128, 512]), "bc1": p.psum("ps_bc1_e", [128, 512])}
        pspool = [p.psum(f"ps_mme{i}", [128, 512]) for i in range(3)]

        for kc in range(8):
            p.dma("pool", wkt[:, kc * 3072:(kc + 1) * 3072], wk3[kc * 128:(kc + 1) * 128, :])
        p.dma("sp", lnv[:, :], lnv_in)
        p.dma("sp", tmv[:, :], tmv_in)
        p.dma("sp", shm[:, :], shm_in)

        for si in range(4):
            s0 = si * 512
            for kc in range(8):
                p.dma("sp", x1s[:, kc * W:kc * W + W], x1g[kc * 128:(kc + 1) * 128, s0:s0 + W])
                p.dma("pool", f0s[:, kc * W:kc * W + W], f0g[kc * 128:(kc + 1) * 128, s0:s0 + W])
            for kc in range(8):
                p.dve("tensor_add", x2s[:, kc * W:(kc + 1) * W], x1s[:, kc * W:(kc + 1) * W],
                      f0s[:, kc * W:(kc + 1) * W], tag="x2")
                p.act_copy(x2b[:, kc * W:(kc + 1) * W], x2s[:, kc * W:(kc + 1) * W], tag="x2c")
                p.dma("sp", x2o[kc * 128:(kc + 1) * 128, s0:s0 + WO],
                      x2s[:, kc * W + 1:(kc + 1) * W], tag="x2st")
            emit_ln_fm(p, x2s, x2b, W, xn0, ps, sc, tag=f"eln{si}")
            xn0f = xn0
            emit_ln_fm(p, xn0, xn0, W, xnl, ps, sc, gamma=lnv[:, 0:8], beta=lnv[:, 8:16],
                       tag=f"eln1_{si}")
            if si == 0:
                for kc in range(8):
                    p.dve("tensor_scalar", xnl[:, kc * W:kc * W + 1], xnl[:, kc * W:kc * W + 1],
                          shm[:, 0:1], None, op0=ALU.mult, tag="shmask")
            for kc in range(8):
                a = xnl[:, kc * W + 1:kc * W + W]
                b = xnl[:, kc * W:kc * W + W - 1]
                p.dve("tensor_sub", dif[:, kc * WO:(kc + 1) * WO], a, b, tag="dif")
            for mi, mix in enumerate((mixk, mixv, mixr)):
                for kc in range(8):
                    p.dve("scalar_tensor_tensor", mix[:, kc * WO:(kc + 1) * WO],
                          dif[:, kc * WO:(kc + 1) * WO], tmv[:, mi * 8 + kc:mi * 8 + kc + 1],
                          xnl[:, kc * W:kc * W + W - 1], op0=ALU.mult, op1=ALU.add, tag="mix")
            for wi, (mix, outb) in enumerate(((mixk, ko), (mixv, vo), (mixr, sro))):
                def wr(oc, w0, bw, ps_ap, wi=wi, outb=outb):
                    if wi == 2:
                        p.act(stg[:, 0:bw], ps_ap, AF.Sigmoid, tag="sr")
                    else:
                        p.dve("tensor_copy", stg[:, 0:bw], ps_ap, tag="kv")
                    p.dma("sp", outb[oc * 128:(oc + 1) * 128, s0:s0 + bw], stg[:, 0:bw],
                          tag="kvst")
                emit_linear_fm(p, wr, wkt, mix, WO, 8, 8, pspool,
                               w_col0=wi * 1024, w_kc_stride=3072, tag=f"kv{wi}")

        block = es.enter_context(nc.Block())
        p.build(block)
    return nc



def build_launchF(R=2048):
    nc = bass.Bass("TRN2", target_bir_lowering=False)
    ks_in = nc.dram_tensor("ks", [512, SEQ], F32, kind="ExternalInput").ap()
    vs_in = nc.dram_tensor("vs", [512, SEQ], F32, kind="ExternalInput").ap()
    tf_in = nc.dram_tensor("tf8", [128, 8], F32, kind="ExternalInput").ap()
    yo = nc.dram_tensor("yo", [512, SEQ], F32, kind="ExternalOutput").ap()

    nblk = SEQ // R
    es = contextlib.ExitStack()
    with es:
        p = Prog(nc, es)
        kt = p.sbuf("kt_f", [128, SEQ], F32)
        vt = p.sbuf("vt_f", [128, SEQ], F32)
        tf8 = p.sbuf("tf8_t", [128, 8], F32)
        cn = p.sbuf("cn_f", [128, 8], F32)
        st = p.sbuf("st_f", [128, 8], F32)
        aAb = p.sbuf("aAb_f", [128, 2], F32)
        EWt = p.sbuf("EWt_f", [128, R], F32)
        onesR = p.sbuf("onesR_f", [128, R], F32)
        Et = p.sbuf("Et_f", [128, R], F32)
        Evt = p.sbuf("Evt_f", [128, R], F32)
        At = p.sbuf("At_f", [128, R], F32)
        Bt = p.sbuf("Bt_f", [128, R], F32)
        Eut = p.sbuf("Eut_f", [128, R], F32)
        Euv = p.sbuf("Euv_f", [128, R], F32)
        num = p.sbuf("num_f", [128, R], F32)
        den = p.sbuf("den_f", [128, R], F32)
        yt = p.sbuf("yt_f", [128, R], F32)

        p.dma("sp", tf8[:, :], tf_in)
        p.memset("dve", onesR[:, :], 1.0)

        for c in range(4):
            p.dma("sp", kt[:, :], ks_in[c * 128:(c + 1) * 128, :], tag=f"ldk{c}")
            p.dma("pool", vt[:, :], vs_in[c * 128:(c + 1) * 128, :], tag=f"ldv{c}")
            td = tf8[:, c:c + 1]
            u = tf8[:, 4 + c:5 + c]
            w_ = cn[:, 0:1]
            EW = cn[:, 1:2]
            eu = cn[:, 2:3]
            um = cn[:, 3:4]
            p.act(w_, td, AF.Exp, tag="fw")
            p.dve("tensor_scalar_mul", w_, w_, -1.0, tag="fwn")
            p.act(EW, w_, AF.Exp, tag="few")
            p.act(eu, u, AF.Exp, tag="feu")
            p.dve("tensor_scalar_max", um, u, 0.0, tag="fum")
            p.dve("tensor_scalar", EWt[:, :], onesR[:, :], EW, None, op0=ALU.mult, tag="fewt")
            aa = st[:, 0:1]
            bb = st[:, 1:2]
            pp = st[:, 2:3]
            p.memset("dve", aa, 0.0)
            p.memset("dve", bb, 0.0)
            p.memset("dve", pp, -1e38)
            for b in range(nblk):
                kb = kt[:, b * R:(b + 1) * R]
                vb = vt[:, b * R:(b + 1) * R]
                cmax = st[:, 3:4]
                M = st[:, 4:5]
                negM = st[:, 5:6]
                t1 = st[:, 6:7]
                t2 = st[:, 7:8]
                p.dve("reduce_max", cmax, kb, axis=AX.X, tag="fcm")
                p.dve("tensor_add", t1, pp, w_, tag="ft1")
                p.dve("tensor_add", t2, cmax, um, tag="ft2")
                p.dve("tensor_max", M, t1, t2, tag="fM")
                p.dve("tensor_scalar_mul", negM, M, -1.0, tag="fnM")
                p.act(Et[:, :], kb, AF.Exp, bias=negM, tag="fE")
                p.dve("tensor_mul", Evt[:, :], Et[:, :], vb, tag="fEv")
                cs = t1
                p.act(cs, pp, AF.Exp, bias=negM, tag="fcs")
                p.dve("tensor_mul", aAb[:, 0:1], aa, cs, tag="faA")
                p.dve("tensor_mul", aAb[:, 1:2], bb, cs, tag="fbB")
                p.dve("tensor_tensor_scan", At[:, :], EWt[:, :], Evt[:, :], aAb[:, 0:1],
                      op0=ALU.mult, op1=ALU.add, tag="fsA")
                p.dve("tensor_tensor_scan", Bt[:, :], EWt[:, :], Et[:, :], aAb[:, 1:2],
                      op0=ALU.mult, op1=ALU.add, tag="fsB")
                p.dve("tensor_scalar", Eut[:, :], Et[:, :], eu, None, op0=ALU.mult, tag="fEu")
                p.dve("tensor_mul", Euv[:, :], Eut[:, :], vb, tag="fEuv")
                p.dve("tensor_add", num[:, 1:R], At[:, 0:R - 1], Euv[:, 1:R], tag="fn1")
                p.dve("tensor_add", num[:, 0:1], aAb[:, 0:1], Euv[:, 0:1], tag="fn0")
                p.dve("tensor_add", den[:, 1:R], Bt[:, 0:R - 1], Eut[:, 1:R], tag="fd1")
                p.dve("tensor_add", den[:, 0:1], aAb[:, 1:2], Eut[:, 0:1], tag="fd0")
                p.act(den[:, :], den[:, :], AF.Ln, tag="fln")
                p.act(den[:, :], den[:, :], AF.Exp, scale=-1.0, tag="fexp")
                p.dve("tensor_mul", yt[:, :], num[:, :], den[:, :], tag="fy")
                p.dma("sp", yo[c * 128:(c + 1) * 128, b * R:(b + 1) * R], yt[:, :], tag="fyst")
                p.dve("tensor_copy", aa, At[:, R - 1:R], tag="fua")
                p.dve("tensor_copy", bb, Bt[:, R - 1:R], tag="fub")
                p.dve("tensor_copy", pp, M, tag="fup")

        block = es.enter_context(nc.Block())
        p.build(block)
    return nc



def build_launchG1():
    nc = bass.Bass("TRN2", target_bir_lowering=False)
    x2g = nc.dram_tensor("x2g", [D, HALF + 1], F32, kind="ExternalInput").ap()
    yg = nc.dram_tensor("yg", [D, HALF + 1], F32, kind="ExternalInput").ap()
    srg = nc.dram_tensor("srg", [D, HALF + 1], F32, kind="ExternalInput").ap()
    wog_in = nc.dram_tensor("wog", [D, D], F32, kind="ExternalInput").ap()
    fwk_in = nc.dram_tensor("fwk", [D, 4096], F32, kind="ExternalInput").ap()
    fwv_in = nc.dram_tensor("fwv", [4096, D], F32, kind="ExternalInput").ap()
    fwr_in = nc.dram_tensor("fwr", [D, D], F32, kind="ExternalInput").ap()
    ln2_in = nc.dram_tensor("ln2v", [128, 16], F32, kind="ExternalInput").ap()
    ftm_in = nc.dram_tensor("ftm", [128, 32], F32, kind="ExternalInput").ap()
    shm_in = nc.dram_tensor("shmask", [128, 1], F32, kind="ExternalInput").ap()
    x3m_o = nc.dram_tensor("x3m", [D, HALF], F32, kind="ExternalOutput").ap()

    es = contextlib.ExitStack()
    with es:
        p = Prog(nc, es)
        WO = 256
        W = WO + 1
        wog = p.sbuf("wog_t", [128, 8 * 1024], BF16)
        fwk = p.sbuf("fwk_t", [128, 8 * 4096], BF16)
        fwv = [p.sbuf(f"fwv_t{i}", [128, 8 * 1024], BF16) for i in range(2)]
        fwr = fwv[0]
        ln2v = p.sbuf("ln2v_t", [128, 16], F32)
        ftm = p.sbuf("ftm_t", [128, 32], F32)
        shm = p.sbuf("shm_g1", [128, 1], F32)
        ybf = p.sbuf("ybf", [128, 8 * W], BF16)
        srb = p.sbuf("srb", [128, 8 * W], BF16)
        x2s = p.sbuf("x2s_g", [128, 8 * W], F32)
        x2bf = p.sbuf("x2bf_g", [128, 8 * W], BF16)
        zs = p.sbuf("zs_g", [128, 8 * W], F32)
        xas = p.sbuf("xas_g", [128, 8 * W], F32)
        xab = p.sbuf("xab_g", [128, 8 * W], BF16)
        xn2 = p.sbuf("xn2_g", [128, 8 * W], BF16)
        dif = p.sbuf("dif_g", [128, 8 * WO], BF16)
        xr2 = p.sbuf("xr2_g", [128, 8 * WO], BF16)
        kk = p.sbuf("kk_g", [128, 32 * WO], BF16)
        rr = p.sbuf("rr_g", [128, 8 * WO], BF16)
        rr2 = p.sbuf("rr2_g", [128, 8 * WO], F32)
        stg = p.sbuf("stg_g", [128, W], F32)
        sc = ln_scratch(p, "g_", blk=257)
        stat = p.psum("ps_stat_g", [128, 512])
        ps = {"sums": stat[0:1, :], "sqs": stat[32:33, :],
              "bc0": p.psum("ps_bc0_g", [128, 512]), "bc1": p.psum("ps_bc1_g", [128, 512])}
        pspool = [p.psum(f"ps_mmg{i}", [128, 512]) for i in range(3)]

        for kc in range(8):
            p.dma("pool", wog[:, kc * 1024:(kc + 1) * 1024], wog_in[kc * 128:(kc + 1) * 128, :])
            p.dma("pool", fwk[:, kc * 4096:(kc + 1) * 4096], fwk_in[kc * 128:(kc + 1) * 128, :])
        p.dma("sp", ln2v[:, :], ln2_in)
        p.dma("sp", ftm[:, :], ftm_in)
        p.dma("sp", shm[:, :], shm_in)

        for si in range(HALF // WO):
            s0 = si * WO
            for kc in range(8):
                p.dma("pool", ybf[:, kc * W:(kc + 1) * W], yg[kc * 128:(kc + 1) * 128, s0:s0 + W])
                p.dma("pool", srb[:, kc * W:(kc + 1) * W], srg[kc * 128:(kc + 1) * 128, s0:s0 + W])
                p.dma("sp", x2s[:, kc * W:(kc + 1) * W], x2g[kc * 128:(kc + 1) * 128, s0:s0 + W])
                p.dma("pool", x2bf[:, kc * W:(kc + 1) * W], x2g[kc * 128:(kc + 1) * 128, s0:s0 + W])
            emit_ln_fm(p, x2s, x2bf, W, zs, ps, sc, tag=f"gz{si}")

            def xa_writer(oc, w0, bw, ps_ap):
                sl = slice(oc * W + w0, oc * W + w0 + bw)
                p.dve("tensor_mul", stg[:, 0:bw], srb[:, sl], ps_ap, tag="gxa1")
                p.dve("tensor_add", xas[:, sl], zs[:, sl], stg[:, 0:bw], tag="gxa2")
                p.act_copy(xab[:, sl], xas[:, sl], tag="gxab")
            emit_linear_fm(p, xa_writer, wog, ybf, W, 8, 8, pspool, blk_cols=512, tag="gwo")

            emit_ln_fm(p, xas, xab, W, xn2, ps, sc, gamma=ln2v[:, 0:8], beta=ln2v[:, 8:16],
                       tag=f"gl2{si}")
            if si == 0:
                for kc in range(8):
                    p.dve("tensor_scalar", xn2[:, kc * W:kc * W + 1], xn2[:, kc * W:kc * W + 1],
                          shm[:, 0:1], None, op0=ALU.mult, tag="gshm")
            for kc in range(8):
                a = xn2[:, kc * W + 1:kc * W + W]
                b = xn2[:, kc * W:kc * W + W - 1]
                p.dve("tensor_sub", dif[:, kc * WO:(kc + 1) * WO], a, b, tag="gdif")
            for kc in range(8):
                p.dve("scalar_tensor_tensor", xr2[:, kc * WO:(kc + 1) * WO],
                      dif[:, kc * WO:(kc + 1) * WO], ftm[:, 8 + kc:9 + kc],
                      xn2[:, kc * W:kc * W + W - 1], op0=ALU.mult, op1=ALU.add, tag="gmr")
            for kc in range(8):
                p.dve("scalar_tensor_tensor", dif[:, kc * WO:(kc + 1) * WO],
                      dif[:, kc * WO:(kc + 1) * WO], ftm[:, kc:kc + 1],
                      xn2[:, kc * W:kc * W + W - 1], op0=ALU.mult, op1=ALU.add, tag="gmk")

            def kk_writer(oc, w0, bw, ps_ap):
                p.dve("tensor_relu", stg[:, 0:bw], ps_ap, tag="gkr")
                p.act(kk[:, oc * WO + w0: oc * WO + w0 + bw], stg[:, 0:bw], AF.Square, tag="gks")
            emit_linear_fm(p, kk_writer, fwk, dif, WO, 8, 32, pspool, blk_cols=256,
                           w_kc_stride=4096, tag="gfk")

            for kc in range(8):
                p.dma("pool", fwr[:, kc * 1024:(kc + 1) * 1024],
                      fwr_in[kc * 128:(kc + 1) * 128, :], tag="gldr")
            def rr_writer(oc, w0, bw, ps_ap):
                p.act(rr[:, oc * WO + w0: oc * WO + w0 + bw], ps_ap, AF.Sigmoid, tag="grr")
            emit_linear_fm(p, rr_writer, fwr, xr2, WO, 8, 8, pspool, blk_cols=256, tag="gfr")

            for qt in range(4):
                fb = fwv[qt % 2]
                for kc in range(8):
                    p.dma("pool", fb[:, kc * 1024:(kc + 1) * 1024],
                          fwv_in[(qt * 8 + kc) * 128:(qt * 8 + kc + 1) * 128, :],
                          tag=f"gldv{qt}")
                for oc in range(8):
                    pv_ = pspool[oc % 3]
                    for kc in range(8):
                        c0 = kc * 1024 + oc * 128
                        p.mm(pv_[:, 0:WO], fb[:, c0:c0 + 128],
                             kk[:, (qt * 8 + kc) * WO:(qt * 8 + kc + 1) * WO],
                             start=(kc == 0), stop=(kc == 7), tag="gfv")
                    if qt == 0:
                        p.dve("tensor_copy", rr2[:, oc * WO:(oc + 1) * WO], pv_[:, 0:WO],
                              tag="gvq0")
                    else:
                        p.dve("tensor_add", rr2[:, oc * WO:(oc + 1) * WO],
                              rr2[:, oc * WO:(oc + 1) * WO], pv_[:, 0:WO], tag="gvq")
            for oc in range(8):
                sl = slice(oc * W + 1, oc * W + W)
                slo = slice(oc * WO, (oc + 1) * WO)
                p.dve("tensor_mul", stg[:, 0:WO], rr[:, slo], rr2[:, slo], tag="gv1")
                p.dve("tensor_add", stg[:, 0:WO], stg[:, 0:WO], xas[:, sl], tag="gv2")
                p.dve("tensor_add", stg[:, 0:WO], stg[:, 0:WO], x2s[:, sl], tag="gv3")
                p.dma("sp", x3m_o[oc * 128:(oc + 1) * 128, s0:s0 + WO], stg[:, 0:WO], tag="gvst")

        block = es.enter_context(nc.Block())
        p.build(block)
    return nc



def build_launchG2():
    nc = bass.Bass("TRN2", target_bir_lowering=False)
    x4g = nc.dram_tensor("x4g", [D, HALF], F32, kind="ExternalInput").ap()
    wrt_in = nc.dram_tensor("wrt", [128, 64], F32, kind="ExternalInput").ap()
    brw_in = nc.dram_tensor("brw", [1, 8], F32, kind="ExternalInput").ap()
    w1_in = nc.dram_tensor("w1s", [8192, 4096], F32, kind="ExternalInput").ap()
    w2_in = nc.dram_tensor("w2s", [32768, 1024], F32, kind="ExternalInput").ap()
    b1_in = nc.dram_tensor("b1v", [128, 256], F32, kind="ExternalInput").ap()
    b2_in = nc.dram_tensor("b2v", [128, 64], F32, kind="ExternalInput").ap()
    sel_in = nc.dram_tensor("selm8", [8, 1024], F32, kind="ExternalInput").ap()
    id_in = nc.dram_tensor("ident", [128, 128], F32, kind="ExternalInput").ap()
    fin = nc.dram_tensor("fin", [D, HALF], F32, kind="ExternalOutput").ap()

    es = contextlib.ExitStack()
    with es:
        p = Prog(nc, es)
        W = 256
        w1t = p.sbuf("w1t", [128, 8 * 4096], BF16)
        w2t = p.sbuf("w2t", [128, 32 * 1024], BF16)
        ht = p.sbuf("ht", [128, 32 * W], BF16)
        x4s = ht[:, 0:8 * W * 2].bitcast(F32)
        x4b = ht[:, 8 * W * 2:8 * W * 3]
        xn3 = p.sbuf("xn3", [128, 8 * HALF], BF16)
        wrt = p.sbuf("wrt_t", [128, 8 * 8], BF16)
        brw = p.sbuf("brw_t", [1, 8], F32)
        b1v = p.sbuf("b1v_t", [128, 256], F32)
        b2v = p.sbuf("b2v_t", [128, 64], F32)
        selm = p.sbuf("selm8_t", [8, 1024], F32)
        ident = p.sbuf("ident_g2", [128, 128], F32)
        ones1f = p.sbuf("ones1f_g2", [1, 128], F32)
        brb = p.sbuf("brb", [128, 8], F32)
        ltm = p.sbuf("ltm", [128, 8], F32)
        gx = p.sbuf("gx_g2", [128, 24], F32)
        el = p.sbuf("el_g2", [128, 8], F32)
        l2 = p.sbuf("l2_g2", [128, 8], F32)
        msk = p.sbuf("msk_g2", [128, 8], F32)
        wtm = p.sbuf("wtm_g2", [128, 16 * 8], F32)
        wfm = p.sbuf("wfm_g2", [8, HALF], F32)
        wbc = p.sbuf("wbc_g2", [128, W], F32)
        stg = [p.sbuf(f"stg_g2_{i}", [128, W], F32) for i in range(4)]
        sc = ln_scratch(p, "g2_", blk=256)
        stat = p.psum("ps_stat_g2", [128, 512])
        ps = {"sums": stat[0:1, :], "sqs": stat[32:33, :],
              "bc0": p.psum("ps_bc0_g2", [128, 512]), "bc1": p.psum("ps_bc1_g2", [128, 512])}
        pspool = [p.psum(f"ps_g2_{i}", [128, 512]) for i in range(3)]

        p.dma("pool", wrt[:, :], wrt_in)
        p.dma("sp", brw[:, :], brw_in)
        p.dma("sp", b1v[:, :], b1_in)
        p.dma("sp", b2v[:, :], b2_in)
        p.dma("sp", selm[:, :], sel_in)
        p.dma("sp", ident[:, :], id_in)
        p.memset("dve", ones1f[:, :], 1.0)
        p.dma("sp", fin, x4g, tag="init_fin")
        p.mm(ps["bc0"][:, 0:8], ones1f[0:1, :], brw[0:1, :], tag="brb")
        p.dve("tensor_copy", brb[:, :], ps["bc0"][:, 0:8], tag="brbc")

        for si in range(HALF // W):
            s0 = si * W
            for kc in range(8):
                p.dma("sp", x4s[:, kc * W:(kc + 1) * W], x4g[kc * 128:(kc + 1) * 128, s0:s0 + W])
                p.dma("pool", x4b[:, kc * W:(kc + 1) * W], x4g[kc * 128:(kc + 1) * 128, s0:s0 + W])
            xn3v = xn3[:, 0:0]
            class _shim:
                def __getitem__(self, idx):
                    return None
            emit_ln_fm_strided(p, x4s, x4b, W, xn3, HALF, s0, ps, sc, tag=f"g2ln{si}")
            for tt in range(W // 128):
                t0 = s0 + tt * 128
                psr = pspool[tt % 2]
                for kc in range(8):
                    p.mm(psr[0:128, 0:8], xn3[:, kc * HALF + t0:kc * HALF + t0 + 128],
                         wrt[:, kc * 8:(kc + 1) * 8], start=(kc == 0), stop=(kc == 7),
                         tag="rmm")
                p.dve("tensor_add", ltm[:, :], psr[0:128, 0:8], brb[:, :], tag="rl")
                m1 = gx[:, 0:1]
                negm1 = gx[:, 1:2]
                m2 = gx[:, 2:3]
                den = gx[:, 3:4]
                rec = gx[:, 4:5]
                p.dve("reduce_max", m1, ltm[:, :], axis=AX.X, tag="rm1")
                p.dve("tensor_scalar_mul", negm1, m1, -1.0, tag="rnm")
                p.act(el[:, :], ltm[:, :], AF.Exp, bias=negm1, tag="rel")
                p.dve("tensor_scalar", msk[:, :], ltm[:, :], m1, None, op0=ALU.is_equal, tag="roh")
                p.dve("scalar_tensor_tensor", l2[:, :], msk[:, :], -1e9, ltm[:, :],
                      op0=ALU.mult, op1=ALU.add, tag="rl2")
                p.dve("reduce_max", m2, l2[:, :], axis=AX.X, tag="rm2")
                p.dve("tensor_scalar", l2[:, :], l2[:, :], m2, None, op0=ALU.is_equal, tag="roh2")
                p.dve("tensor_add", msk[:, :], msk[:, :], l2[:, :], tag="rmsk")
                p.dve("tensor_mul", el[:, :], el[:, :], msk[:, :], tag="rnum")
                p.dve("reduce_sum", den, el[:, :], axis=AX.X, tag="rden")
                p.dve("reciprocal", rec, den, tag="rrec")
                p.dve("tensor_scalar", wtm[:, (si * (W // 128) + tt) * 8:(si * (W // 128) + tt) * 8 + 8],
                      el[:, :], rec, None, op0=ALU.mult, tag="rw")
        for t in range(16):
            p.transpose(ps["bc1"][0:8, 0:128], wtm[:, t * 8:(t + 1) * 8], ident[:, :], tag="wt")
            p.dve("tensor_copy", wfm[:, t * 128:(t + 1) * 128], ps["bc1"][0:8, 0:128], tag="wtc")

        for e in range(8):
            for kc in range(8):
                p.dma("pool", w1t[:, kc * 4096:(kc + 1) * 4096],
                      w1_in[e * 1024 + kc * 128:e * 1024 + (kc + 1) * 128, :], tag=f"ldw1_{e}")
            for kc in range(32):
                p.dma("pool", w2t[:, kc * 1024:(kc + 1) * 1024],
                      w2_in[e * 4096 + kc * 128:e * 4096 + (kc + 1) * 128, :], tag=f"ldw2_{e}")
            for blk in range(HALF // W):
                s0 = blk * W
                for hc in range(32):
                    psh = pspool[hc % 2]
                    for kc in range(8):
                        p.mm(psh[:, 0:W], w1t[:, kc * 4096 + hc * 128:kc * 4096 + hc * 128 + 128],
                             xn3[:, kc * HALF + s0:kc * HALF + s0 + W],
                             start=(kc == 0), stop=(kc == 7), tag=f"hmm{e}")
                    p.act(ht[:, hc * W:(hc + 1) * W], psh[:, 0:W], AF.Relu,
                          bias=b1v[:, e * 32 + hc:e * 32 + hc + 1], tag=f"hrelu{e}")
                p.mm(ps["bc0"][:, 0:W], selm[:, e * 128:(e + 1) * 128], wfm[:, s0:s0 + W],
                     tag="wsel")
                p.dve("tensor_copy", wbc[:, :], ps["bc0"][:, 0:W], tag="wselc")
                for oc in range(8):
                    psy = pspool[2]
                    for kc in range(32):
                        p.mm(psy[:, 0:W], w2t[:, kc * 1024 + oc * 128:kc * 1024 + oc * 128 + 128],
                             ht[:, kc * W:(kc + 1) * W], start=(kc == 0), stop=(kc == 31),
                             tag=f"ymm{e}")
                    sg = stg[oc % 4]
                    p.dve("tensor_scalar", sg[:, :], psy[:, 0:W], b2v[:, e * 8 + oc:e * 8 + oc + 1],
                          None, op0=ALU.add, tag="yb")
                    p.dve("tensor_mul", sg[:, :], sg[:, :], wbc[:, :], tag="yw")
                    p.dma_accum(fin[oc * 128:(oc + 1) * 128, s0:s0 + W], sg[:, :], tag="yacc")

        block = es.enter_context(nc.Block())
        p.build(block)
    return nc


def emit_ln_fm_strided(p, x_f32, x_bf, W, out_bf, out_stride, out_off, ps, sc, tag="ln"):
    nblk = (W + 511) // 512
    for blk in range(nblk):
        w0 = blk * 512
        bw = min(512, W - w0)
        sums = ps["sums"][0:1, 0:bw]
        sqs = ps["sqs"][0:1, 0:bw]
        for c in range(8):
            xb = x_bf[:, c * W + w0: c * W + w0 + bw]
            p.mm(sums, sc["ones128"][:, 0:1], xb, start=(c == 0), stop=(c == 7), tag=f"{tag}_sum")
        for c in range(8):
            xb = x_bf[:, c * W + w0: c * W + w0 + bw]
            sq = sc["sq" + str(c % 2)][:, 0:bw]
            p.act(sq, xb, AF.Square, tag=f"{tag}_sq")
            p.mm(sqs, sc["ones128"][:, 0:1], sq, start=(c == 0), stop=(c == 7), tag=f"{tag}_sqs")
        m = sc["row"][0:1, 0:bw]
        msq = sc["row"][0:1, 512:512 + bw]
        var = sc["row"][0:1, 1024:1024 + bw]
        rstd = sc["row"][0:1, 1536:1536 + bw]
        p.dve("tensor_scalar_mul", m, sums, 1.0 / D, tag=f"{tag}_m")
        p.dve("tensor_scalar_mul", msq, sqs, 1.0 / D, tag=f"{tag}_msq")
        p.dve("tensor_mul", var, m, m, tag=f"{tag}_m2")
        p.dve("tensor_sub", var, msq, var, tag=f"{tag}_var")
        p.act(rstd, var, AF.Ln, bias=sc["eps"][0:1, 0:1], tag=f"{tag}_ln")
        p.act(rstd, rstd, AF.Exp, scale=-0.5, tag=f"{tag}_rstd")
        p.dve("tensor_scalar_mul", m, m, -1.0, tag=f"{tag}_negm")
        bc_m = ps["bc0"][:, 0:bw]
        bc_r = ps["bc1"][:, 0:bw]
        p.mm(bc_m, sc["ones1_f"][0:1, :], m, tag=f"{tag}_bcm")
        p.mm(bc_r, sc["ones1_f"][0:1, :], rstd, tag=f"{tag}_bcr")
        for c in range(8):
            xf = x_f32[:, c * W + w0: c * W + w0 + bw]
            ob = out_bf[:, c * out_stride + out_off + w0: c * out_stride + out_off + w0 + bw]
            t = sc["nrm"][:, 0:bw]
            p.dve("tensor_add", t, xf, bc_m, tag=f"{tag}_sub")
            p.dve("tensor_mul", ob, t, bc_r, tag=f"{tag}_mul")



def _pv(v):
    v = np.asarray(v, np.float32).reshape(-1)
    n = v.size // 128
    return np.ascontiguousarray(v.reshape(n, 128).T)


def _make_masks():
    m = np.zeros((4 * 128, 512), np.float32)
    for rel in range(4):
        i = np.arange(128)[:, None] + 128 * rel
        j = np.arange(512)[None, :]
        m[rel * 128:(rel + 1) * 128] = (i <= j).astype(np.float32)
    import ml_dtypes
    return m.astype(ml_dtypes.bfloat16)


def _pack_v(vtm, h2):
    import ml_dtypes
    out = np.zeros((8 * 128, 2080), ml_dtypes.bfloat16)
    one = np.ones((), ml_dtypes.bfloat16)
    for h in range(8):
        blk = out[128 * h:128 * h + 128]
        for c in range(32):
            blk[:, 65 * c:65 * c + 64] = vtm[128 * c:128 * c + 128, 64 * h:64 * h + 64]
            blk[:, 65 * c + 64] = one
    return out


def kernel(**inputs):
    import ml_dtypes
    BF = ml_dtypes.bfloat16
    F = np.float32
    g = lambda k: np.asarray(inputs[k])
    x = g("L0_attn_x").astype(F)
    xT = [np.ascontiguousarray(x[b].T) for b in range(BATCH)]

    w_qkv = g("L0_attn_w_qkv")
    ident = np.eye(128, dtype=F)
    masks = _make_masks()

    ncA = build_launchA()
    imsA = []
    for c in range(NCORES):
        b, h2 = c // 2, c % 2
        q0 = 512 * h2
        wqk = np.ascontiguousarray(np.concatenate(
            [w_qkv[:, q0:q0 + 512], w_qkv[:, 1024 + q0:1024 + q0 + 512]], axis=1))
        wv = np.ascontiguousarray(w_qkv[:, 2048 + q0:2048 + q0 + 512])
        imsA.append({"x0": np.ascontiguousarray(xT[b][:, :HALF]),
                     "x1": np.ascontiguousarray(xT[b][:, HALF:]),
                     "wqk": wqk, "wv": wv})
    resA = run_bass_kernel_spmd(ncA, imsA, core_ids=list(range(NCORES))).results

    ncB = build_launchB()
    imsB = []
    for c in range(NCORES):
        imsB.append({"qk": resA[c]["qk"], "vpack": _pack_v(resA[c]["vtm"], c % 2),
                     "masks": masks})
    resB = run_bass_kernel_spmd(ncB, imsB, core_ids=list(range(NCORES))).results

    sw1, sw2, sw3 = g("L0_ffn_sw1"), g("L0_ffn_sw2"), g("L0_ffn_sw3")
    pvec = np.zeros((128, 24), F)
    pvec[:, 0:8] = _pv(g("L0_attn_b_out"))
    pvec[:, 8:12] = _pv(g("L0_ffn_sb1"))
    pvec[:, 12:16] = _pv(g("L0_ffn_sb2"))
    pvec[:, 16:20] = _pv(g("L0_ffn_s_ln_g"))
    pvec[:, 20:24] = _pv(g("L0_ffn_s_ln_b"))
    w12 = np.ascontiguousarray(np.concatenate([sw1.T, sw2.T], axis=1), dtype=F)
    sw3t = np.ascontiguousarray(sw3.T, dtype=F)
    sb3v = _pv(g("L0_ffn_sb3"))
    ncC = build_launchC()
    imsC = []
    for c in range(NCORES):
        b, h2 = c // 2, c % 2
        ao_full = np.concatenate([resB[2 * b]["ao"], resB[2 * b + 1]["ao"]], axis=0)
        aof = np.ascontiguousarray(ao_full[:, h2 * HALF:(h2 + 1) * HALF])
        imsC.append({"xf32": np.ascontiguousarray(xT[b][:, h2 * HALF:(h2 + 1) * HALF]),
                     "aof": aof, "w_out4": g("L0_attn_w_out").astype(F),
                     "w12": w12, "sw3t": sw3t, "pvec": pvec, "sb3v": sb3v})
    resC = run_bass_kernel_spmd(ncC, imsC, core_ids=list(range(NCORES))).results

    idx_b = g("L0_ffn_index_b").astype(np.int64)
    idx_t = g("L0_ffn_index_t").astype(np.int64)
    Msel = idx_b.size
    Mpad = ((Msel + NCORES * 128 - 1) // (NCORES * 128)) * (NCORES * 128)
    Mper = Mpad // NCORES
    owner = 2 * idx_b + (idx_t >= HALF)
    lcol = idx_t % HALF
    xm_all = np.zeros((D, Mpad), BF)
    shg_all = np.zeros((D, Mpad), F)
    for m in range(Msel):
        xm_all[:, m] = resC[owner[m]]["xn1o"][:, lcol[m]]
        shg_all[:, m] = resC[owner[m]]["sho"][:, lcol[m]]
    egw = np.ascontiguousarray(
        np.concatenate([g("L0_ffn_eg_w"), g("L0_ffn_sg_w")], axis=0).T, dtype=F)
    gb9 = np.concatenate([g("L0_ffn_eg_b"), g("L0_ffn_sg_b")]).reshape(9, 1).astype(F)
    expb = g("L0_ffn_exp_bias").reshape(1, 8).astype(F)
    ew1, ew2, ew3 = g("L0_ffn_ew1"), g("L0_ffn_ew2"), g("L0_ffn_ew3")
    ew12 = np.zeros((D, 8192), F)
    ew3t = np.zeros((512, 8192), F)
    for e in range(8):
        ew12[:, e * 1024:e * 1024 + 512] = ew1[e].T
        ew12[:, e * 1024 + 512:(e + 1) * 1024] = ew2[e].T
        ew3t[:, e * 1024:(e + 1) * 1024] = ew3[e].T
    ebv = np.zeros((128, 72), F)
    eb1, eb2 = g("L0_ffn_eb1"), g("L0_ffn_eb2")
    for e in range(8):
        ebv[:, e * 8:e * 8 + 4] = _pv(eb1[e])
        ebv[:, e * 8 + 4:e * 8 + 8] = _pv(eb2[e])
    ebv[:, 64:68] = _pv(g("L0_ffn_sel_ln_g"))
    ebv[:, 68:72] = _pv(g("L0_ffn_sel_ln_b"))
    selm = np.zeros((9, 9 * 128), F)
    for e in range(9):
        selm[e, e * 128:(e + 1) * 128] = 1.0
    ncD = build_launchD(Mper)
    imsD = []
    for c in range(NCORES):
        imsD.append({"xm": np.ascontiguousarray(xm_all[:, c * Mper:(c + 1) * Mper]),
                     "shg": np.ascontiguousarray(shg_all[:, c * Mper:(c + 1) * Mper]),
                     "egw": egw, "gb9": gb9, "expb": expb, "ew12": ew12,
                     "ew3t": ew3t, "eb3t": g("L0_ffn_eb3").astype(F), "ebv": ebv,
                     "ident": ident, "selm": selm})
    resD = run_bass_kernel_spmd(ncD, imsD, core_ids=list(range(NCORES))).results
    final0 = [np.array(resC[c]["sho"]) for c in range(NCORES)]
    for m in range(Msel):
        comb = resD[m // Mper]["comb"][:, m % Mper]
        final0[owner[m]][:, lcol[m]] = comb

    tmk = g("L1_attn_tmk").reshape(-1)
    tmv_ = g("L1_attn_tmv").reshape(-1)
    tmr = g("L1_attn_tmr").reshape(-1)
    lnvE = np.concatenate([_pv(g("L1_attn_ln1_g")), _pv(g("L1_attn_ln1_b"))], axis=1)
    tmvE = np.concatenate([_pv(tmk), _pv(tmv_), _pv(tmr),
                           _pv(1 - tmk), _pv(1 - tmv_), _pv(1 - tmr)], axis=1)
    wk3 = np.ascontiguousarray(np.concatenate(
        [g("L1_attn_Wk").T, g("L1_attn_Wv").T, g("L1_attn_Wr").T], axis=1), dtype=F)
    ncE = build_launchE()
    imsE = []
    for c in range(NCORES):
        b, h2 = c // 2, c % 2
        if h2 == 0:
            x1p = np.zeros((D, 1), F)
            f0p = np.zeros((D, 1), F)
            shm = np.zeros((128, 1), F)
        else:
            x1p = resC[2 * b]["x1o"][:, -1:]
            f0p = final0[2 * b][:, -1:]
            shm = np.ones((128, 1), F)
        imsE.append({"x1g": np.ascontiguousarray(np.concatenate([x1p, resC[c]["x1o"]], axis=1)),
                     "f0g": np.ascontiguousarray(np.concatenate([f0p, final0[c]], axis=1)),
                     "wk3": wk3, "lnv": lnvE, "tmv": tmvE, "shmask": shm})
    resE = run_bass_kernel_spmd(ncE, imsE, core_ids=list(range(NCORES))).results

    td = g("L1_attn_time_decay").reshape(-1)
    tf = g("L1_attn_time_first").reshape(-1)
    ncF = build_launchF()
    imsF = []
    for c in range(NCORES):
        b, h2 = c // 2, c % 2
        kfull = np.concatenate([resE[2 * b]["ko"], resE[2 * b + 1]["ko"]], axis=1)
        vfull = np.concatenate([resE[2 * b]["vo"], resE[2 * b + 1]["vo"]], axis=1)
        r0 = 512 * h2
        tf8 = np.zeros((128, 8), F)
        for cc in range(4):
            tf8[:, cc] = td[r0 + cc * 128:r0 + (cc + 1) * 128]
            tf8[:, 4 + cc] = tf[r0 + cc * 128:r0 + (cc + 1) * 128]
        imsF.append({"ks": np.ascontiguousarray(kfull[r0:r0 + 512]),
                     "vs": np.ascontiguousarray(vfull[r0:r0 + 512]), "tf8": tf8})
    resF = run_bass_kernel_spmd(ncF, imsF, core_ids=list(range(NCORES))).results

    ftmk = g("L1_attn_f_tmk").reshape(-1)
    ftmr = g("L1_attn_f_tmr").reshape(-1)
    ln2v = np.concatenate([_pv(g("L1_attn_ln2_g")), _pv(g("L1_attn_ln2_b"))], axis=1)
    ftm = np.concatenate([_pv(ftmk), _pv(ftmr), _pv(1 - ftmk), _pv(1 - ftmr)], axis=1)
    ncG1 = build_launchG1()
    imsG1 = []
    yfull = {}
    for b in range(BATCH):
        yfull[b] = np.concatenate([resF[2 * b]["yo"], resF[2 * b + 1]["yo"]], axis=0)
    for c in range(NCORES):
        b, h2 = c // 2, c % 2
        t0 = h2 * HALF
        def win(arr2049src_full, prev_zero):
            if h2 == 0:
                pc = np.zeros((D, 1), F)
            else:
                pc = arr2049src_full[:, t0 - 1:t0]
            return np.ascontiguousarray(
                np.concatenate([pc, arr2049src_full[:, t0:t0 + HALF]], axis=1))
        x2full = np.concatenate([resE[2 * b]["x2o"], resE[2 * b + 1]["x2o"]], axis=1)
        srfull = np.concatenate([resE[2 * b]["sro"], resE[2 * b + 1]["sro"]], axis=1)
        imsG1.append({"x2g": win(x2full, True), "yg": win(yfull[b], True),
                      "srg": win(srfull, True),
                      "wog": np.ascontiguousarray(g("L1_attn_Wo").T, dtype=F),
                      "fwk": np.ascontiguousarray(g("L1_attn_fWk").T, dtype=F),
                      "fwv": np.ascontiguousarray(g("L1_attn_fWv").T, dtype=F),
                      "fwr": np.ascontiguousarray(g("L1_attn_fWr").T, dtype=F),
                      "ln2v": ln2v, "ftm": ftm,
                      "shmask": np.full((128, 1), 0.0 if h2 == 0 else 1.0, F)})
    resG1 = run_bass_kernel_spmd(ncG1, imsG1, core_ids=list(range(NCORES))).results

    Wrr = g("L1_ffn_W_router").astype(F)
    b1, b2 = g("L1_ffn_b1"), g("L1_ffn_b2")
    b1v = np.zeros((128, 256), F)
    b2v = np.zeros((128, 64), F)
    for e in range(8):
        b1v[:, e * 32:(e + 1) * 32] = _pv(b1[e])
        b2v[:, e * 8:(e + 1) * 8] = _pv(b2[e])
    selm8 = np.zeros((8, 1024), F)
    for e in range(8):
        selm8[e, e * 128:(e + 1) * 128] = 1.0
    ncG2 = build_launchG2()
    imsG2 = []
    for c in range(NCORES):
        imsG2.append({"x4g": resG1[c]["x3m"],
                      "wrt": np.ascontiguousarray(
                          Wrr.reshape(8, 128, 8).transpose(1, 0, 2).reshape(128, 64)),
                      "brw": g("L1_ffn_b_router").reshape(1, 8).astype(F),
                      "w1s": g("L1_ffn_W1").reshape(8 * D, 4096).astype(F),
                      "w2s": g("L1_ffn_W2").reshape(8 * 4096, D).astype(F),
                      "b1v": b1v, "b2v": b2v, "selm8": selm8, "ident": ident})
    resG2 = run_bass_kernel_spmd(ncG2, imsG2, core_ids=list(range(NCORES))).results

    out = np.empty((BATCH, SEQ, D), np.float32)
    for c in range(NCORES):
        b, h2 = c // 2, c % 2
        out[b, h2 * HALF:(h2 + 1) * HALF, :] = np.asarray(resG2[c]["fin"]).T
    return out


# version 02
# speedup vs baseline: 1.0408x; 1.0408x over previous
import contextlib
import math
import numpy as np
import concourse.bass as bass
import concourse.mybir as mybir
from concourse.bass_utils import run_bass_kernel_spmd

F32 = mybir.dt.float32
BF16 = mybir.dt.bfloat16
I32 = mybir.dt.int32
U8 = mybir.dt.uint8
AF = mybir.ActivationFunctionType
ALU = mybir.AluOpType
AX = mybir.AxisListType


_ENG2BLOCK = {"pe": "tensor", "act": "scalar", "dve": "vector", "pool": "gpsimd", "sp": "sync"}
_SEM_CAP = 40000
_DMA_QUEUES = ("sp", "pool", "act")


class _Op:
    __slots__ = ("eng", "fn", "reads", "writes", "kind", "idx", "eidx",
                 "needs_inc", "sem", "val", "amt", "deps", "tag", "wait_list",
                 "_covers", "_dma_idx")

    def __init__(self, eng, fn, reads, writes, kind, tag=""):
        self.eng = eng
        self.fn = fn
        self.reads = reads
        self.writes = writes
        self.kind = kind
        self.tag = tag
        self.needs_inc = False
        self.sem = None
        self.val = None
        self.deps = []


def _rkey(x):
    if isinstance(x, bass.AP):
        t = x.tensor
        name = getattr(t, "name", None) or id(t)
        esz = mybir.dt.size(x.dtype) if hasattr(x, "dtype") else mybir.dt.size(t.dtype)
        dims = x.ap
        is_sbuf = not hasattr(t, "kind") or "DRam" not in type(t).__name__
        try:
            if is_sbuf and len(dims) >= 2:
                span = sum(abs(s) * (c - 1) for s, c in dims[1:]) + 1
            else:
                span = sum(abs(s) * (c - 1) for s, c in dims) + 1
        except Exception:
            span = 1 << 40
        lo = x.offset * esz
        return (name, lo, lo + span * esz)
    t = x
    name = getattr(t, "name", None) or id(t)
    return (name, 0, 1 << 60)


def _overlap(a, b):
    return a[0] == b[0] and a[1] < b[4 - 2] and b[1] < a[2]


class Prog:
    def __init__(self, nc, es):
        self.nc = nc
        self.es = es
        self.ops = []
        self._sem_pools = {}
        self._sem_counts = {}
        self.region_writer = {}
        self.region_readers = {}
        self._final_dma_tokens = []
        self._dma_count = {q: 0 for q in _DMA_QUEUES}
        self._dma_signal = {q: None for q in _DMA_QUEUES}

    def sbuf(self, name, shape, dtype):
        return self.es.enter_context(self.nc.sbuf_tensor(name, list(shape), dtype))

    def psum(self, name, shape, dtype=F32):
        return self.es.enter_context(self.nc.psum_tensor(name, list(shape), dtype))

    def _sem(self, pool, idx):
        lst = self._sem_pools.setdefault(pool, [])
        while len(lst) <= idx:
            lst.append(self.es.enter_context(self.nc.semaphore(f"{pool}_{len(lst)}")))
        return lst[idx]

    def _signal_for(self, queue, upto):
        sig = self._dma_signal.get(queue)
        if sig is not None and sig._covers >= upto:
            return sig
        total = self._dma_count[queue]
        pool = "dma_" + queue
        n = 16 * total
        waits = []
        i = 0
        while i * _SEM_CAP < n:
            waits.append((self._sem(pool, i), min(n - i * _SEM_CAP, _SEM_CAP)))
            i += 1

        def fn(e, waits=tuple(waits)):
            for sem, val in waits:
                e.wait_ge(sem, val)
            return e.nop()

        o = _Op(queue, fn, [], [], "compute", f"dmasig_{queue}_{total}")
        o.idx = len(self.ops)
        o.deps = []
        self.ops.append(o)
        o._covers = total
        self._dma_signal[queue] = o
        return o

    def op(self, eng, fn, reads=(), writes=(), kind="compute", tag=""):
        o = _Op(eng, fn, [_rkey(r) for r in reads], [_rkey(w) for w in writes], kind, tag)
        deps = []
        for (name, lo, hi) in o.reads:
            for (l2, h2, w) in self.region_writer.get(name, ()):
                if lo < h2 and l2 < hi:
                    deps.append(w)
        for (name, lo, hi) in o.writes:
            for (l2, h2, w) in self.region_writer.get(name, ()):
                if lo < h2 and l2 < hi:
                    deps.append(w)
            for (l2, h2, r) in self.region_readers.get(name, ()):
                if lo < h2 and l2 < hi:
                    deps.append(r)
        uniq = {}
        for d in deps:
            if d is o:
                continue
            if o.eng == "pe" and d.eng == "pe" and o.kind != "dma" and d.kind != "dma":
                continue
            if d.kind == "dma":
                d = self._signal_for(d.eng, d._dma_idx)
            uniq[id(d)] = d
        o.deps = list(uniq.values())
        for d in o.deps:
            d.needs_inc = True
        if kind == "dma":
            self._dma_count[eng] += 1
            o._dma_idx = self._dma_count[eng]
            o.needs_inc = True
        o.idx = len(self.ops)
        self.ops.append(o)
        for (name, lo, hi) in o.writes:
            ws = self.region_writer.setdefault(name, [])
            ws[:] = [(l2, h2, w) for (l2, h2, w) in ws if not (lo <= l2 and h2 <= hi)]
            ws.append((lo, hi, o))
            rs = self.region_readers.get(name)
            if rs:
                rs[:] = [(l2, h2, r) for (l2, h2, r) in rs if not (lo < h2 and l2 < hi)]
        for (name, lo, hi) in o.reads:
            self.region_readers.setdefault(name, []).append((lo, hi, o))
        return o

    def dma(self, queue, out, in_, tag=""):
        assert queue in _DMA_QUEUES
        rd = [in_]
        wr = [out]
        return self.op(queue if queue != "act" else "act",
                       lambda e, out=out, in_=in_: e.dma_start(out=out, in_=in_),
                       reads=rd, writes=wr, kind="dma", tag=tag)

    def dma_accum(self, out, in_, tag=""):
        return self.op("pool",
                       lambda e, out=out, in_=in_: e.dma_start(out=out, in_=in_,
                                                               accum_op=ALU.add),
                       reads=[in_, out], writes=[out], kind="dma", tag=tag)

    def mm(self, out, lhsT, rhs, start=True, stop=True, tag=""):
        return self.op("pe",
                       lambda e, out=out, l=lhsT, r=rhs, s1=start, s2=stop:
                           e.matmul(out, l, r, start=s1, stop=s2),
                       reads=[lhsT, rhs], writes=[out], tag=tag)

    def transpose(self, out, in_, ident, tag=""):
        return self.op("pe",
                       lambda e, out=out, i=in_, d=ident: e.transpose(out, i, d),
                       reads=[in_, ident], writes=[out], tag=tag)

    def act(self, out, in_, func, bias=0.0, scale=1.0, accum_out=None, tag=""):
        rd = [in_] + ([bias] if isinstance(bias, bass.AP) else []) \
            + ([scale] if isinstance(scale, bass.AP) else [])
        wr = [out] + ([accum_out] if accum_out is not None else [])
        def fn(e, out=out, in_=in_, func=func, bias=bias, scale=scale, ac=accum_out):
            kw = {}
            if ac is not None:
                kw["accum_out"] = ac
            return e.activation(out, in_, func, bias=bias, scale=scale, **kw)
        return self.op("act", fn, reads=rd, writes=wr, tag=tag)

    def act_copy(self, out, in_, mul=1.0, tag=""):
        rd = [in_] + ([mul] if isinstance(mul, bass.AP) else [])
        return self.op("act", lambda e, o=out, i=in_, m=mul: e.mul(o, i, m),
                       reads=rd, writes=[out], tag=tag)

    def dve(self, method, out, *ins, extra_reads=(), tag="", **kwargs):
        rd = [i for i in ins if isinstance(i, bass.AP)] + list(extra_reads)
        def fn(e, method=method, out=out, ins=ins, kwargs=kwargs):
            return getattr(e, method)(out, *ins, **kwargs)
        return self.op("dve", fn, reads=rd, writes=[out], tag=tag)

    def pool(self, method, out, *ins, tag="", **kwargs):
        rd = [i for i in ins if isinstance(i, bass.AP)]
        def fn(e, method=method, out=out, ins=ins, kwargs=kwargs):
            return getattr(e, method)(out, *ins, **kwargs)
        return self.op("pool", fn, reads=rd, writes=[out], tag=tag)

    def memset(self, eng, ap, val, tag=""):
        return self.op(eng, lambda e, a=ap, v=val: e.memset(a, v),
                       reads=(), writes=[ap], tag=tag)

    def raw(self, eng, fn, reads=(), writes=(), kind="raw", tag=""):
        return self.op(eng, fn, reads=reads, writes=writes, kind=kind, tag=tag)

    def mark_output(self, op):
        pass

    def build(self, block):
        eng_counts = {}
        for o in self.ops:
            if not o.needs_inc:
                continue
            pool = ("dma_" + o.eng) if o.kind == "dma" else ("es_" + o.eng)
            amt = 16 if o.kind == "dma" else 1
            n = self._sem_counts.get(pool, 0) + amt
            self._sem_counts[pool] = n
            idx = (n - 1) // _SEM_CAP
            base = idx * _SEM_CAP
            o.sem = self._sem(pool, idx)
            o.val = n - base
            o.amt = amt

        per_eng = {e: [] for e in _ENG2BLOCK}
        for o in self.ops:
            per_eng[o.eng].append(o)

        floors = {e: {} for e in _ENG2BLOCK}
        for o in self.ops:
            waits = {}
            for d in o.deps:
                assert d.needs_inc and d.sem is not None
                key = d.sem
                waits[key] = max(waits.get(key, 0), d.val)
            fl = floors[o.eng]
            o.wait_list = []
            for sem, val in waits.items():
                if fl.get(sem, 0) >= val:
                    continue
                fl[sem] = val
                o.wait_list.append((sem, val))

        prog = self

        def make_body(eng):
            ops = per_eng[eng]
            def body(e):
                for o in ops:
                    for sem, val in o.wait_list:
                        e.wait_ge(sem, val)
                    inst = o.fn(e)
                    if o.needs_inc:
                        assert inst is not None, f"op {o.tag} needs inc but fn returned None"
                        inst.then_inc(o.sem, o.amt)
            return body

        final_waits = []
        for q in _DMA_QUEUES:
            n = 16 * self._dma_count[q]
            i = 0
            while i * _SEM_CAP < n:
                final_waits.append((self._sem("dma_" + q, i), min(n - i * _SEM_CAP, _SEM_CAP)))
                i += 1

        for eng, battr in _ENG2BLOCK.items():
            ops = per_eng[eng]
            if not ops and not (eng == "pool" and final_waits):
                continue
            body = make_body(eng)
            def wrapped(e, body=body, eng=eng):
                body(e)
                if eng == "pool":
                    for sem, val in final_waits:
                        e.wait_ge(sem, val)
            getattr(block, battr)(wrapped)


BATCH = 4
SEQ = 4096
D = 1024
NCORES = 8
HALF = SEQ // 2
NHEAD = 16
HDIM = 64



def _fm(a):
    return np.ascontiguousarray(np.asarray(a).T)


if __name__ == "__main__":
    print("kernel.py framework module")



def emit_ln_fm(p, x_f32, x_bf, W, out_bf, ps, sc, gamma=None, beta=None,
               nchunks=8, tag="ln"):
    nblk = (W + 511) // 512
    dim = float(nchunks * 128)
    for blk in range(nblk):
        w0 = blk * 512
        bw = min(512, W - w0)
        sums = ps["sums"][0:1, 0:bw]
        sqs = ps["sqs"][0:1, 0:bw]
        for c in range(nchunks):
            xb = x_bf[:, c * W + w0: c * W + w0 + bw]
            p.mm(sums, sc["ones128"][:, 0:1], xb, start=(c == 0), stop=(c == nchunks - 1), tag=f"{tag}_sum")
        for c in range(nchunks):
            xb = x_bf[:, c * W + w0: c * W + w0 + bw]
            sq = sc["sq" + str(c % 2)][:, 0:bw]
            p.act(sq, xb, AF.Square, tag=f"{tag}_sq")
            p.mm(sqs, sc["ones128"][:, 0:1], sq, start=(c == 0), stop=(c == nchunks - 1), tag=f"{tag}_sqs")
        m = sc["row"][0:1, 0:bw]
        msq = sc["row"][0:1, 512:512 + bw]
        var = sc["row"][0:1, 1024:1024 + bw]
        rstd = sc["row"][0:1, 1536:1536 + bw]
        p.dve("tensor_scalar_mul", m, sums, 1.0 / dim, tag=f"{tag}_m")
        p.dve("tensor_scalar_mul", msq, sqs, 1.0 / dim, tag=f"{tag}_msq")
        p.dve("tensor_mul", var, m, m, tag=f"{tag}_m2")
        p.dve("tensor_sub", var, msq, var, tag=f"{tag}_var")
        p.act(rstd, var, AF.Ln, bias=sc["eps"][0:1, 0:1], tag=f"{tag}_ln")
        p.act(rstd, rstd, AF.Exp, scale=-0.5, tag=f"{tag}_rstd")
        p.dve("tensor_scalar_mul", m, m, -1.0, tag=f"{tag}_negm")
        bc_m = ps["bc0"][:, 0:bw]
        bc_r = ps["bc1"][:, 0:bw]
        p.mm(bc_m, sc["ones1_f"][0:1, :], m, tag=f"{tag}_bcm")
        p.mm(bc_r, sc["ones1_f"][0:1, :], rstd, tag=f"{tag}_bcr")
        for c in range(nchunks):
            xf = x_f32[:, c * W + w0: c * W + w0 + bw]
            ob = out_bf[:, c * W + w0: c * W + w0 + bw]
            t = sc["nrm"][:, 0:bw]
            p.dve("tensor_add", t, xf, bc_m, tag=f"{tag}_sub")
            if gamma is None:
                p.dve("tensor_mul", ob, t, bc_r, tag=f"{tag}_mul")
            else:
                t2 = sc["nrm2"][:, 0:bw]
                p.dve("tensor_mul", t2, t, bc_r, tag=f"{tag}_mul")
                p.dve("tensor_scalar", ob, t2, gamma[:, c:c + 1], beta[:, c:c + 1],
                      op0=ALU.mult, op1=ALU.add, tag=f"{tag}_affine")


def emit_linear_fm(p, out_writer, wt, x_bf, W, kchunks, ochunks, pspool,
                   x_col0=0, w_col0=0, w_kc_stride=None, tag="lin", blk_cols=512,
                   extra_rhs=None):
    OUT = ochunks * 128
    if w_kc_stride is None:
        w_kc_stride = OUT
    nblk = (W + blk_cols - 1) // blk_cols
    pi = 0
    for oc in range(ochunks):
        for blk in range(nblk):
            w0 = blk * blk_cols
            bw = min(blk_cols, W - w0)
            ps = pspool[pi % len(pspool)]
            pi += 1
            for kc in range(kchunks):
                c0 = w_col0 + kc * w_kc_stride + oc * 128
                l = wt[:, c0:c0 + 128]
                r = x_bf[:, x_col0 + kc * W + w0: x_col0 + kc * W + w0 + bw]
                p.mm(ps[:, 0:bw], l, r, start=(kc == 0), stop=(kc == kchunks - 1 and extra_rhs is None),
                     tag=f"{tag}_mm")
            if extra_rhs is not None:
                for kc in range(kchunks):
                    c0 = w_col0 + kc * w_kc_stride + oc * 128
                    l = wt[:, c0:c0 + 128]
                    r2 = extra_rhs(kc, w0, bw)
                    if r2 is None:
                        continue
                    p.mm(ps[:, 0:bw], l, r2, start=False, stop=(kc == kchunks - 1),
                         tag=f"{tag}_mm2")
            out_writer(oc, w0, bw, ps[:, 0:bw])


def ln_scratch_pool(p, pool, blk=512):
    sc = {
        "sq0": pool.get(blk, BF16),
        "sq1": pool.get(blk, BF16),
        "row": pool.get(2048, F32, rows=1),
        "nrm": pool.get(blk, F32),
        "nrm2": pool.get(blk, F32),
        "ones128": pool.get(1, BF16),
        "ones1_f": pool.get(128, F32, rows=1),
        "eps": pool.get(1, F32, rows=1),
    }
    p.memset("dve", sc["ones128"][:, :], 1.0)
    p.memset("dve", sc["ones1_f"][:, :], 1.0)
    p.memset("dve", sc["eps"][:, :], 1e-5)
    return sc


def ln_scratch(p, pref="", blk=512):
    sc = {
        "sq0": p.sbuf(pref + "sq0", [128, blk], BF16),
        "sq1": p.sbuf(pref + "sq1", [128, blk], BF16),
        "row": p.sbuf(pref + "row", [1, 2048], F32),
        "nrm": p.sbuf(pref + "nrm", [128, blk], F32),
        "nrm2": p.sbuf(pref + "nrm2", [128, blk], F32),
        "ones128": p.sbuf(pref + "ones128", [128, 1], BF16),
        "ones1_f": p.sbuf(pref + "ones1f", [1, 128], F32),
        "eps": p.sbuf(pref + "eps", [1, 1], F32),
    }
    p.memset("dve", sc["ones128"][:, :], 1.0)
    p.memset("dve", sc["ones1_f"][:, :], 1.0)
    p.memset("dve", sc["eps"][:, :], 1e-5)
    return sc



def build_launchA():
    nc = bass.Bass("TRN2", target_bir_lowering=False)
    xin = [nc.dram_tensor(f"x{i}", [D, HALF], F32, kind="ExternalInput").ap() for i in range(2)]
    wqk = nc.dram_tensor("wqk", [D, 1024], F32, kind="ExternalInput").ap()
    wv = nc.dram_tensor("wv", [D, 512], F32, kind="ExternalInput").ap()
    qk_out = nc.dram_tensor("qk", [D, SEQ], BF16, kind="ExternalOutput").ap()
    vtm_out = nc.dram_tensor("vtm", [SEQ, 512], BF16, kind="ExternalOutput").ap()

    es = contextlib.ExitStack()
    with es:
        p = Prog(nc, es)
        W = HALF
        xf = p.sbuf("xf", [128, 8 * W], F32)
        xbf = p.sbuf("xbf", [128, 8 * W], BF16)
        xn = p.sbuf("xn", [128, 8 * W], BF16)
        wqk_t = p.sbuf("wqk_t", [128, 8 * 1024], BF16)
        wv_t = p.sbuf("wv_t", [128, 8 * 512], BF16)
        stage = p.sbuf("stage", [128, 512], BF16)
        stage2 = p.sbuf("stage2", [128, 512], BF16)
        sc = ln_scratch(p)
        ps = {
            "sums": p.psum("ps_sums", [1, 512]),
            "sqs": p.psum("ps_sqs", [1, 512]),
            "bc0": p.psum("ps_bc0", [128, 512]),
            "bc1": p.psum("ps_bc1", [128, 512]),
        }
        pspool = [p.psum(f"ps_mm{i}", [128, 512]) for i in range(3)]

        for kc in range(8):
            p.dma("pool", wqk_t[:, kc * 1024:(kc + 1) * 1024], wqk[kc * 128:(kc + 1) * 128, :])
            p.dma("pool", wv_t[:, kc * 512:(kc + 1) * 512], wv[kc * 128:(kc + 1) * 128, :])

        for half in range(2):
            for kc in range(8):
                p.dma("sp", xf[:, kc * W:(kc + 1) * W], xin[half][kc * 128:(kc + 1) * 128, :])
                p.dma("pool", xbf[:, kc * W:(kc + 1) * W], xin[half][kc * 128:(kc + 1) * 128, :])
            emit_ln_fm(p, xf, xbf, W, xn, ps, sc, tag=f"ln0_{half}")

            def qk_writer(oc, w0, bw, ps_ap, half=half):
                st = stage if (oc + w0) % 2 == 0 else stage2
                p.dve("tensor_copy", st[:, 0:bw], ps_ap, tag="qk_evac")
                p.dma("sp", qk_out[oc * 128:(oc + 1) * 128, half * W + w0: half * W + w0 + bw],
                      st[:, 0:bw], tag="qk_st")
            emit_linear_fm(p, qk_writer, wqk_t, xn, W, 8, 8, pspool, tag="qk")

            for tb in range(W // 128):
                psv = pspool[tb % len(pspool)]
                for kc in range(8):
                    l = xn[:, kc * W + tb * 128: kc * W + tb * 128 + 128]
                    r = wv_t[:, kc * 512:(kc + 1) * 512]
                    p.mm(psv[:, 0:512], l, r, start=(kc == 0), stop=(kc == 7), tag="v_mm")
                st = stage if tb % 2 == 0 else stage2
                p.dve("tensor_copy", st[:, 0:512], psv[:, 0:512], tag="v_evac")
                p.dma("sp", vtm_out[half * W + tb * 128: half * W + (tb + 1) * 128, :],
                      st[:, 0:512], tag="v_st")

        block = es.enter_context(nc.Block())
        p.build(block)
    return nc



def build_launchB():
    nc = bass.Bass("TRN2", target_bir_lowering=False)
    qk = nc.dram_tensor("qk", [D, SEQ], BF16, kind="ExternalInput").ap()
    vpack = nc.dram_tensor("vpack", [8 * 128, 2080], BF16, kind="ExternalInput").ap()
    masks = nc.dram_tensor("masks", [4 * 128, 512], BF16, kind="ExternalInput").ap()
    ao = nc.dram_tensor("ao", [512, SEQ], BF16, kind="ExternalOutput").ap()

    es = contextlib.ExitStack()
    with es:
        p = Prog(nc, es)
        qt = [p.sbuf(f"qt{i}", [64, SEQ], BF16) for i in range(2)]
        ktl = [p.sbuf(f"kt{i}", [64, SEQ], BF16) for i in range(2)]
        vt = [p.sbuf(f"vt{i}", [128, 2080], BF16) for i in range(2)]
        mk = p.sbuf("mk", [128, 4 * 512], BF16)
        p2 = [p.sbuf(f"p2_{i}", [128, 1024], BF16) for i in range(2)]
        sq = p.sbuf("sqa", [64, SEQ], BF16)
        ones64 = p.sbuf("ones64", [64, 1], BF16)
        ones1f = p.sbuf("ones1f_b", [1, 128], F32)
        nmax = p.sbuf("nmax", [1, 512], F32)
        cstat = p.sbuf("cstat", [1, 8], F32)
        negc = p.sbuf("negc", [128, 1], F32)
        rrow = p.sbuf("rrow", [1, 512], F32)
        obuf = [p.sbuf(f"obuf{i}", [64, 512], F32) for i in range(2)]
        ostg = [p.sbuf(f"ostg{i}", [64, 512], BF16) for i in range(2)]
        ps_s = [p.psum(f"ps_s{i}", [128, 1024]) for i in range(2)]
        ps_o = [p.psum(f"ps_o{i}", [65, 512]) for i in range(2)]
        ps_bc = p.psum("ps_bcb", [128, 512])
        ps_n = p.psum("ps_n", [1, 512])

        p.memset("dve", ones1f[:, :], 1.0)
        p.memset("dve", ones64[:, :], 1.0)
        for r in range(4):
            p.dma("pool", mk[:, r * 512:(r + 1) * 512], masks[r * 128:(r + 1) * 128, :])

        for h in range(8):
            bi = h % 2
            p.dma("pool", qt[bi][:, :], qk[64 * h:64 * h + 64, :], tag=f"ldq{h}")
            p.dma("pool", ktl[bi][:, :], qk[512 + 64 * h:512 + 64 * h + 64, :], tag=f"ldk{h}")
            p.dma("pool", vt[bi][:, :], vpack[128 * h:128 * h + 128, :], tag=f"ldv{h}")

            for wi, t in ((0, qt[bi]), (1, ktl[bi])):
                p.act(sq[:, :], t[:, :], AF.Square, tag=f"csq{h}")
                for blk in range(8):
                    p.mm(ps_n[0:1, :], ones64[:, 0:1], sq[:, blk * 512:(blk + 1) * 512],
                         tag=f"cn{h}")
                    if blk == 0:
                        p.dve("tensor_copy", nmax[0:1, :], ps_n[0:1, :], tag=f"cmax0{h}")
                    else:
                        p.dve("tensor_max", nmax[0:1, :], nmax[0:1, :], ps_n[0:1, :],
                              tag=f"cmax{h}")
                p.dve("reduce_max", cstat[0:1, wi:wi + 1], nmax[0:1, :], axis=AX.X,
                      tag=f"cred{h}")
            p.dve("tensor_add", cstat[0:1, 2:3], cstat[0:1, 0:1], cstat[0:1, 1:2], tag=f"cc{h}")
            p.dve("tensor_scalar_mul", cstat[0:1, 2:3], cstat[0:1, 2:3], -1.0 / 16.0, tag=f"cs{h}")
            p.mm(ps_bc[:, 0:1], ones1f[0:1, :], cstat[0:1, 2:3], tag=f"cb{h}")
            p.dve("tensor_copy", negc[:, 0:1], ps_bc[:, 0:1], tag=f"cbc{h}")

            for qb in range(8):
                po = ps_o[qb % 2]
                nkt = 4 * qb + 4
                for kp in range(nkt // 2):
                    pss = ps_s[kp % 2]
                    pp = p2[kp % 2]
                    for half in range(2):
                        ktile = 2 * kp + half
                        p.mm(pss[:, half * 512:(half + 1) * 512],
                             ktl[bi][:, 128 * ktile:128 * ktile + 128],
                             qt[bi][:, 512 * qb:512 * qb + 512], tag=f"smm{h}")
                    p.act(pp[:, :], pss[:, :], AF.Exp, bias=negc[:, 0:1], scale=0.125,
                          tag=f"exp{h}")
                    for half in range(2):
                        ktile = 2 * kp + half
                        rel = ktile - 4 * qb
                        if rel >= 0:
                            p.dve("tensor_mul", pp[:, half * 512:(half + 1) * 512],
                                  pp[:, half * 512:(half + 1) * 512],
                                  mk[:, (rel) * 512:(rel + 1) * 512], tag=f"msk{h}")
                    for half in range(2):
                        ktile = 2 * kp + half
                        p.mm(po[:, :], vt[bi][:, 65 * ktile:65 * ktile + 65],
                             pp[:, half * 512:(half + 1) * 512],
                             start=(ktile == 0), stop=(ktile == nkt - 1), tag=f"pv{h}")
                ob = obuf[qb % 2]
                og = ostg[qb % 2]
                p.act(rrow[0:1, :], po[64:65, :], AF.Ln, tag=f"dln{h}")
                p.act(rrow[0:1, :], rrow[0:1, :], AF.Exp, scale=-1.0, tag=f"dexp{h}")
                p.mm(ps_bc[0:64, :], ones1f[0:1, 0:64], rrow[0:1, :], tag=f"dbc{h}")
                p.dve("tensor_copy", ob[:, :], po[0:64, :], tag=f"oevac{h}")
                p.dve("tensor_mul", og[:, :], ob[:, :], ps_bc[0:64, :], tag=f"odiv{h}")
                p.dma("sp", ao[64 * h:64 * h + 64, 512 * qb:512 * (qb + 1)], og[:, :],
                      tag=f"ost{h}")

        block = es.enter_context(nc.Block())
        p.build(block)
    return nc



def build_launchC():
    nc = bass.Bass("TRN2", target_bir_lowering=False)
    xf32 = nc.dram_tensor("xf32", [D, HALF], F32, kind="ExternalInput").ap()
    aof = nc.dram_tensor("aof", [D, HALF], BF16, kind="ExternalInput").ap()
    w_out4 = nc.dram_tensor("w_out4", [D, D], F32, kind="ExternalInput").ap()
    w12 = nc.dram_tensor("w12", [D, D], F32, kind="ExternalInput").ap()
    sw3t = nc.dram_tensor("sw3t", [512, D], F32, kind="ExternalInput").ap()
    pvec = nc.dram_tensor("pvec", [128, 24], F32, kind="ExternalInput").ap()
    sb3v = nc.dram_tensor("sb3v", [128, 8], F32, kind="ExternalInput").ap()
    x1o = nc.dram_tensor("x1o", [D, HALF], F32, kind="ExternalOutput").ap()
    xn1o = nc.dram_tensor("xn1o", [D, HALF], BF16, kind="ExternalOutput").ap()
    sho = nc.dram_tensor("sho", [D, HALF], F32, kind="ExternalOutput").ap()

    es = contextlib.ExitStack()
    with es:
        p = Prog(nc, es)
        W = 512
        wo_t = p.sbuf("wo_t", [128, 8 * 1024], BF16)
        w12_t = p.sbuf("w12_t", [128, 8 * 1024], BF16)
        sw3_t = p.sbuf("sw3_t", [128, 4 * 1024], BF16)
        pv = p.sbuf("pv", [128, 24], F32)
        sb3 = p.sbuf("sb3", [128, 8], F32)
        xs = p.sbuf("xs", [128, 8 * W], F32)
        aos = p.sbuf("aos", [128, 8 * W], BF16)
        x1s = p.sbuf("x1s", [128, 8 * W], F32)
        x1b = p.sbuf("x1b", [128, 8 * W], BF16)
        xn1 = p.sbuf("xn1", [128, 8 * W], BF16)
        h1s = p.sbuf("h1s", [128, 4 * W], BF16)
        h2s = p.sbuf("h2s", [128, 4 * W], BF16)
        prodf = p.sbuf("prodf", [128, 4 * W], F32)
        prodb = p.sbuf("prodb", [128, 4 * W], BF16)
        shn = p.sbuf("shn", [128, 4 * W], BF16)
        shstg = p.sbuf("shstg", [128, W], F32)
        sc = ln_scratch(p)
        ps = {"sums": p.psum("ps_sums", [1, 512]), "sqs": p.psum("ps_sqs", [1, 512]),
              "bc0": p.psum("ps_bc0", [128, 512]), "bc1": p.psum("ps_bc1", [128, 512])}
        pspool = [p.psum(f"ps_mm{i}", [128, 512]) for i in range(3)]

        for kc in range(8):
            p.dma("pool", wo_t[:, kc * 1024:(kc + 1) * 1024], w_out4[kc * 128:(kc + 1) * 128, :])
            p.dma("pool", w12_t[:, kc * 1024:(kc + 1) * 1024], w12[kc * 128:(kc + 1) * 128, :])
        for kc in range(4):
            p.dma("pool", sw3_t[:, kc * 1024:(kc + 1) * 1024], sw3t[kc * 128:(kc + 1) * 128, :])
        p.dma("sp", pv[:, :], pvec)
        p.dma("sp", sb3[:, :], sb3v)

        for s0 in range(0, HALF, W):
            for kc in range(8):
                p.dma("sp", xs[:, kc * W:(kc + 1) * W], xf32[kc * 128:(kc + 1) * 128, s0:s0 + W])
                p.dma("pool", aos[:, kc * W:(kc + 1) * W], aof[kc * 128:(kc + 1) * 128, s0:s0 + W])

            def x1_writer(oc, w0, bw, ps_ap):
                p.dve("scalar_tensor_tensor", x1s[:, oc * W + w0: oc * W + w0 + bw],
                      ps_ap, pv[:, oc:oc + 1], xs[:, oc * W + w0: oc * W + w0 + bw],
                      op0=ALU.add, op1=ALU.add, tag="x1w")
                p.act_copy(x1b[:, oc * W + w0: oc * W + w0 + bw],
                           x1s[:, oc * W + w0: oc * W + w0 + bw], tag="x1cast")
                p.dma("sp", x1o[oc * 128:(oc + 1) * 128, s0 + w0: s0 + w0 + bw],
                      x1s[:, oc * W + w0: oc * W + w0 + bw], tag="x1st")
            emit_linear_fm(p, x1_writer, wo_t, aos, W, 8, 8, pspool, tag="wo")

            emit_ln_fm(p, x1s, x1b, W, xn1, ps, sc, tag="ln1")
            for oc in range(8):
                p.dma("sp", xn1o[oc * 128:(oc + 1) * 128, s0:s0 + W], xn1[:, oc * W:(oc + 1) * W],
                      tag="xn1st")

            def h_writer(oc, w0, bw, ps_ap):
                if oc < 4:
                    p.act(h1s[:, oc * W + w0: oc * W + w0 + bw], ps_ap, AF.Silu,
                          bias=pv[:, 8 + oc:9 + oc], tag="h1w")
                else:
                    p.dve("tensor_scalar", h2s[:, (oc - 4) * W + w0: (oc - 4) * W + w0 + bw],
                          ps_ap, pv[:, 8 + oc:9 + oc], None, op0=ALU.add, tag="h2w")
            emit_linear_fm(p, h_writer, w12_t, xn1, W, 8, 8, pspool, tag="h12")

            for c in range(4):
                p.dve("tensor_mul", prodf[:, c * W:(c + 1) * W], h1s[:, c * W:(c + 1) * W],
                      h2s[:, c * W:(c + 1) * W], tag="prod")
                p.act_copy(prodb[:, c * W:(c + 1) * W], prodf[:, c * W:(c + 1) * W], tag="prodc")
            emit_ln_fm(p, prodf, prodb, W, shn, ps, sc,
                       gamma=pv[:, 16:20], beta=pv[:, 20:24], nchunks=4, tag="sln")

            def sh_writer(oc, w0, bw, ps_ap):
                p.dve("tensor_scalar", shstg[:, w0:w0 + bw], ps_ap, sb3[:, oc:oc + 1], None,
                      op0=ALU.add, tag="shw")
                p.dma("sp", sho[oc * 128:(oc + 1) * 128, s0 + w0: s0 + w0 + bw],
                      shstg[:, w0:w0 + bw], tag="shst")
            emit_linear_fm(p, sh_writer, sw3_t, shn, W, 4, 8, pspool, tag="sw3")

        block = es.enter_context(nc.Block())
        p.build(block)
    return nc



def build_launchD(M=256):
    nc = bass.Bass("TRN2", target_bir_lowering=False)
    xm_in = nc.dram_tensor("xm", [D, M], BF16, kind="ExternalInput").ap()
    shg_in = nc.dram_tensor("shg", [D, M], F32, kind="ExternalInput").ap()
    egw_in = nc.dram_tensor("egw", [D, 9], F32, kind="ExternalInput").ap()
    gb9_in = nc.dram_tensor("gb9", [9, 1], F32, kind="ExternalInput").ap()
    expb_in = nc.dram_tensor("expb", [1, 8], F32, kind="ExternalInput").ap()
    ew12_in = nc.dram_tensor("ew12", [D, 8192], F32, kind="ExternalInput").ap()
    ew3_in = nc.dram_tensor("ew3t", [512, 8192], F32, kind="ExternalInput").ap()
    eb3_in = nc.dram_tensor("eb3t", [8, D], F32, kind="ExternalInput").ap()
    ebv_in = nc.dram_tensor("ebv", [128, 72], F32, kind="ExternalInput").ap()
    id_in = nc.dram_tensor("ident", [128, 128], F32, kind="ExternalInput").ap()
    selm_in = nc.dram_tensor("selm", [9, 9 * 128], F32, kind="ExternalInput").ap()
    comb_o = nc.dram_tensor("comb", [D, M], F32, kind="ExternalOutput").ap()

    nt = M // 128
    es = contextlib.ExitStack()
    with es:
        p = Prog(nc, es)
        xm = p.sbuf("xm_t", [128, 8 * M], BF16)
        shg = p.sbuf("shg_t", [128, 8 * M], F32)
        egw = p.sbuf("egw_t", [128, 8 * 9], BF16)
        gb9 = p.sbuf("gb9_t", [9, 1], F32)
        expb = p.sbuf("expb_t", [1, 8], F32)
        ew12 = [p.sbuf(f"ew12_t{i}", [128, 8 * 1024], BF16) for i in range(2)]
        ew3 = [p.sbuf(f"ew3_t{i}", [128, 4 * 1024], BF16) for i in range(2)]
        eb3 = p.sbuf("eb3_t", [8, D], BF16)
        ebv = p.sbuf("ebv_t", [128, 72], F32)
        ident = p.sbuf("ident_t", [128, 128], F32)
        ones1f = p.sbuf("ones1f_d", [1, 128], F32)
        sgs = p.sbuf("sgs", [9, M], F32)
        stm = p.sbuf("stm", [128, nt * 16], F32)
        gsc = p.sbuf("gsc", [128, nt * 8], F32)
        gml = p.sbuf("gml", [128, nt * 4], F32)
        gpk = p.sbuf("gpk", [128, nt * 9], F32)
        owf = p.sbuf("owf", [9, M], F32)
        owfb = p.sbuf("owfb", [9, M], BF16)
        ebt = p.sbuf("ebt", [128, 8], F32)
        h1s = p.sbuf("h1s_d", [128, 4 * M], BF16)
        h2s = p.sbuf("h2s_d", [128, 4 * M], BF16)
        prodf = p.sbuf("prodf_d", [128, 4 * M], F32)
        prodb = p.sbuf("prodb_d", [128, 4 * M], BF16)
        zn = p.sbuf("zn_d", [128, 4 * M], BF16)
        zs = p.sbuf("zs_d", [128, 4 * M], BF16)
        zbc = p.sbuf("zbc_d", [128, M], F32)
        stg = p.sbuf("stg_d", [128, M], F32)
        sc = ln_scratch(p, "d_")
        stat = p.psum("ps_stat_d", [128, 512])
        ps = {"sums": stat[0:1, :], "sqs": stat[32:33, :],
              "bc0": p.psum("ps_bc0_d", [128, 512]), "bc1": p.psum("ps_bc1_d", [128, 512])}
        pspool = [p.psum("ps_mmd0", [128, 512])]
        ps_out = [p.psum(f"ps_outd{i}", [128, 512]) for i in range(2)]
        acc = p.sbuf("acc_d", [128, 8 * M], F32)
        ps_g = ps["bc1"]

        for kc in range(8):
            p.dma("pool", xm[:, kc * M:(kc + 1) * M], xm_in[kc * 128:(kc + 1) * 128, :])
            p.dma("sp", shg[:, kc * M:(kc + 1) * M], shg_in[kc * 128:(kc + 1) * 128, :])
            p.dma("pool", egw[:, kc * 9:(kc + 1) * 9], egw_in[kc * 128:(kc + 1) * 128, :])
        p.dma("sp", gb9[:, :], gb9_in)
        p.dma("sp", expb[:, :], expb_in)
        p.dma("pool", eb3[:, :], eb3_in)
        p.dma("sp", ebv[:, :], ebv_in)
        p.dma("sp", ident[:, :], id_in)
        selm = p.sbuf("selm_t", [9, 9 * 128], F32)
        p.dma("sp", selm[:, :], selm_in)
        p.memset("dve", ones1f[:, :], 1.0)
        p.memset("dve", acc[:, :], 0.0)

        psg = pspool[0]
        for kc in range(8):
            p.mm(psg[0:9, 0:M], egw[:, kc * 9:(kc + 1) * 9], xm[:, kc * M:(kc + 1) * M],
                 start=(kc == 0), stop=(kc == 7), tag="gmm")
        p.act(sgs[:, :], psg[0:9, 0:M], AF.Sigmoid, bias=gb9[:, 0:1], tag="gsig")
        p.mm(ps_g[:, 0:8], ones1f[0:1, :], expb[0:1, :], tag="ebb")
        p.dve("tensor_copy", ebt[:, :], ps_g[:, 0:8], tag="ebc")
        for t in range(nt):
            p.transpose(ps_g[0:128, 0:9], sgs[:, t * 128:(t + 1) * 128], ident[0:9, 0:9], tag="gt")
            p.dve("tensor_copy", stm[:, t * 16:t * 16 + 9], ps_g[0:128, 0:9], tag="gtc")
            se = stm[:, t * 16:t * 16 + 8]
            ssg = stm[:, t * 16 + 8:t * 16 + 9]
            tp = gsc[:, t * 8:t * 8 + 8]
            m1 = gml[:, t * 4 + 0:t * 4 + 1]
            top1 = gml[:, t * 4 + 1:t * 4 + 2]
            ew = gml[:, t * 4 + 3:t * 4 + 4]
            pk = gpk[:, t * 9:(t + 1) * 9]
            shw = pk[:, 8:9]
            p.dve("tensor_add", tp, se, ebt[:, :], tag="gtp")
            p.dve("reduce_max", m1, tp, axis=AX.X, tag="gm1")
            p.dve("tensor_scalar", tp, tp, m1, None, op0=ALU.is_equal, tag="goh")
            p.dve("tensor_mul", tp, tp, se, tag="gohs")
            p.dve("reduce_max", top1, tp, axis=AX.X, tag="gtop")
            p.dve("tensor_sub", shw, ssg, top1, tag="gd")
            p.act(shw, shw, AF.Sigmoid, tag="gshw")
            p.dve("tensor_scalar", ew, shw, -1.0, 1.0, op0=ALU.mult, op1=ALU.add, tag="gew")
            p.dve("tensor_add", tp, se, ebt[:, :], tag="gtp2")
            p.dve("tensor_scalar", pk[:, 0:8], tp, m1, ew, op0=ALU.is_equal, op1=ALU.mult,
                  tag="gow")
            p.transpose(ps_g[0:9, 0:128], pk, ident[:, :], tag="gt2")
            p.dve("tensor_copy", owf[:, t * 128:(t + 1) * 128], ps_g[0:9, 0:128], tag="gt2c")
            p.dve("tensor_copy", owfb[:, t * 128:(t + 1) * 128], ps_g[0:9, 0:128], tag="gt2b")

        for e in range(8):
            wb = ew12[e % 2]
            w3b = ew3[e % 2]
            for kc in range(8):
                p.dma("pool", wb[:, kc * 1024:(kc + 1) * 1024],
                      ew12_in[kc * 128:(kc + 1) * 128, e * 1024:(e + 1) * 1024], tag=f"ldw12_{e}")
            for kc in range(4):
                p.dma("pool", w3b[:, kc * 1024:(kc + 1) * 1024],
                      ew3_in[kc * 128:(kc + 1) * 128, e * 1024:(e + 1) * 1024], tag=f"ldw3_{e}")
            def h_writer(oc, w0, bw, ps_ap, e=e):
                if oc < 4:
                    p.act(h1s[:, oc * M + w0: oc * M + w0 + bw], ps_ap, AF.Silu,
                          bias=ebv[:, e * 8 + oc:e * 8 + oc + 1], tag="dh1")
                else:
                    p.dve("tensor_scalar", h2s[:, (oc - 4) * M + w0: (oc - 4) * M + w0 + bw],
                          ps_ap, ebv[:, e * 8 + oc:e * 8 + oc + 1], None, op0=ALU.add, tag="dh2")
            emit_linear_fm(p, h_writer, wb, xm, M, 8, 8, pspool, tag=f"dh{e}",
                           blk_cols=min(512, M))
            for c in range(4):
                p.dve("tensor_mul", prodf[:, c * M:(c + 1) * M], h1s[:, c * M:(c + 1) * M],
                      h2s[:, c * M:(c + 1) * M], tag="dprod")
                p.act_copy(prodb[:, c * M:(c + 1) * M], prodf[:, c * M:(c + 1) * M], tag="dprodc")
            emit_ln_fm(p, prodf, prodb, M, zn, ps, sc, gamma=ebv[:, 64:68],
                       beta=ebv[:, 68:72], nchunks=4, tag=f"dsln{e}")
            p.mm(ps["bc0"][:, 0:M], selm[:, e * 128:(e + 1) * 128], owf[:, :], tag="dowb")
            p.dve("tensor_copy", zbc[:, 0:M], ps["bc0"][:, 0:M], tag="dowc")
            for c in range(4):
                p.dve("tensor_mul", zs[:, c * M:(c + 1) * M], zn[:, c * M:(c + 1) * M],
                      zbc[:, 0:M], tag="dzs")
            for oc in range(8):
                po = ps_out[oc % 2][:, 0:M]
                for kc in range(4):
                    c0 = kc * 1024 + oc * 128
                    p.mm(po, w3b[:, c0:c0 + 128], zs[:, kc * M:(kc + 1) * M],
                         start=(kc == 0), stop=(kc == 3), tag=f"dout{e}")
                p.dve("tensor_add", acc[:, oc * M:(oc + 1) * M], acc[:, oc * M:(oc + 1) * M],
                      po, tag="dacc")
        for oc in range(8):
            po = ps_out[oc % 2][:, 0:M]
            p.mm(po, eb3[0:8, oc * 128:(oc + 1) * 128], owfb[0:8, :],
                 start=True, stop=True, tag="deb3")
            p.dve("tensor_add", acc[:, oc * M:(oc + 1) * M], acc[:, oc * M:(oc + 1) * M],
                  po, tag="daccb")
        p.mm(ps["bc0"][:, 0:M], selm[:, 8 * 128:9 * 128], owf[:, :], tag="dshb")
        p.dve("tensor_copy", zbc[:, 0:M], ps["bc0"][:, 0:M], tag="dshc")
        for oc in range(8):
            p.dve("tensor_mul", stg[:, 0:M], shg[:, oc * M:(oc + 1) * M], zbc[:, 0:M],
                  tag="dcmb1")
            p.dve("tensor_add", stg[:, 0:M], stg[:, 0:M], acc[:, oc * M:(oc + 1) * M],
                  tag="dcmb2")
            p.dma("sp", comb_o[oc * 128:(oc + 1) * 128, :], stg[:, 0:M], tag="dcst")
        block = es.enter_context(nc.Block())
        p.build(block)
    return nc



def build_launchE():
    nc = bass.Bass("TRN2", target_bir_lowering=False)
    x1g = nc.dram_tensor("x1g", [D, HALF + 1], F32, kind="ExternalInput").ap()
    f0g = nc.dram_tensor("f0g", [D, HALF + 1], F32, kind="ExternalInput").ap()
    wk3 = nc.dram_tensor("wk3", [D, 3072], F32, kind="ExternalInput").ap()
    lnv_in = nc.dram_tensor("lnv", [128, 16], F32, kind="ExternalInput").ap()
    tmv_in = nc.dram_tensor("tmv", [128, 48], F32, kind="ExternalInput").ap()
    shm_in = nc.dram_tensor("shmask", [128, 1], F32, kind="ExternalInput").ap()
    ko = nc.dram_tensor("ko", [D, HALF], F32, kind="ExternalOutput").ap()
    vo = nc.dram_tensor("vo", [D, HALF], F32, kind="ExternalOutput").ap()
    sro = nc.dram_tensor("sro", [D, HALF], F32, kind="ExternalOutput").ap()
    x2o = nc.dram_tensor("x2o", [D, HALF], F32, kind="ExternalOutput").ap()

    es = contextlib.ExitStack()
    with es:
        p = Prog(nc, es)
        W = 513
        WO = 512
        wkt = p.sbuf("wkt", [128, 8 * 3072], BF16)
        lnv = p.sbuf("lnv_t", [128, 16], F32)
        tmv = p.sbuf("tmv_t", [128, 48], F32)
        shm = p.sbuf("shm_t", [128, 1], F32)
        x1s = p.sbuf("x1s_e", [128, 8 * W], F32)
        f0s = p.sbuf("f0s_e", [128, 8 * W], F32)
        x2s = p.sbuf("x2s_e", [128, 8 * W], F32)
        x2b = p.sbuf("x2b_e", [128, 8 * W], BF16)
        xn0 = p.sbuf("xn0_e", [128, 8 * W], BF16)
        xnl = p.sbuf("xnl_e", [128, 8 * W], BF16)
        dif = p.sbuf("dif_e", [128, 8 * WO], BF16)
        mixk = p.sbuf("mixk_e", [128, 8 * WO], BF16)
        mixv = p.sbuf("mixv_e", [128, 8 * WO], BF16)
        mixr = p.sbuf("mixr_e", [128, 8 * WO], BF16)
        stg = p.sbuf("stg_e", [128, WO], F32)
        sc = ln_scratch(p, "e_")
        stat = p.psum("ps_stat_e", [128, 512])
        ps = {"sums": stat[0:1, :], "sqs": stat[32:33, :],
              "bc0": p.psum("ps_bc0_e", [128, 512]), "bc1": p.psum("ps_bc1_e", [128, 512])}
        pspool = [p.psum(f"ps_mme{i}", [128, 512]) for i in range(3)]

        for kc in range(8):
            p.dma("pool", wkt[:, kc * 3072:(kc + 1) * 3072], wk3[kc * 128:(kc + 1) * 128, :])
        p.dma("sp", lnv[:, :], lnv_in)
        p.dma("sp", tmv[:, :], tmv_in)
        p.dma("sp", shm[:, :], shm_in)

        for si in range(4):
            s0 = si * 512
            for kc in range(8):
                p.dma("sp", x1s[:, kc * W:kc * W + W], x1g[kc * 128:(kc + 1) * 128, s0:s0 + W])
                p.dma("pool", f0s[:, kc * W:kc * W + W], f0g[kc * 128:(kc + 1) * 128, s0:s0 + W])
            for kc in range(8):
                p.dve("tensor_add", x2s[:, kc * W:(kc + 1) * W], x1s[:, kc * W:(kc + 1) * W],
                      f0s[:, kc * W:(kc + 1) * W], tag="x2")
                p.act_copy(x2b[:, kc * W:(kc + 1) * W], x2s[:, kc * W:(kc + 1) * W], tag="x2c")
                p.dma("sp", x2o[kc * 128:(kc + 1) * 128, s0:s0 + WO],
                      x2s[:, kc * W + 1:(kc + 1) * W], tag="x2st")
            emit_ln_fm(p, x2s, x2b, W, xn0, ps, sc, tag=f"eln{si}")
            xn0f = xn0
            emit_ln_fm(p, xn0, xn0, W, xnl, ps, sc, gamma=lnv[:, 0:8], beta=lnv[:, 8:16],
                       tag=f"eln1_{si}")
            if si == 0:
                for kc in range(8):
                    p.dve("tensor_scalar", xnl[:, kc * W:kc * W + 1], xnl[:, kc * W:kc * W + 1],
                          shm[:, 0:1], None, op0=ALU.mult, tag="shmask")
            for kc in range(8):
                a = xnl[:, kc * W + 1:kc * W + W]
                b = xnl[:, kc * W:kc * W + W - 1]
                p.dve("tensor_sub", dif[:, kc * WO:(kc + 1) * WO], a, b, tag="dif")
            for mi, mix in enumerate((mixk, mixv, mixr)):
                for kc in range(8):
                    p.dve("scalar_tensor_tensor", mix[:, kc * WO:(kc + 1) * WO],
                          dif[:, kc * WO:(kc + 1) * WO], tmv[:, mi * 8 + kc:mi * 8 + kc + 1],
                          xnl[:, kc * W:kc * W + W - 1], op0=ALU.mult, op1=ALU.add, tag="mix")
            for wi, (mix, outb) in enumerate(((mixk, ko), (mixv, vo), (mixr, sro))):
                def wr(oc, w0, bw, ps_ap, wi=wi, outb=outb):
                    if wi == 2:
                        p.act(stg[:, 0:bw], ps_ap, AF.Sigmoid, tag="sr")
                    else:
                        p.dve("tensor_copy", stg[:, 0:bw], ps_ap, tag="kv")
                    p.dma("sp", outb[oc * 128:(oc + 1) * 128, s0:s0 + bw], stg[:, 0:bw],
                          tag="kvst")
                emit_linear_fm(p, wr, wkt, mix, WO, 8, 8, pspool,
                               w_col0=wi * 1024, w_kc_stride=3072, tag=f"kv{wi}")

        block = es.enter_context(nc.Block())
        p.build(block)
    return nc



def build_launchF(R=2048):
    nc = bass.Bass("TRN2", target_bir_lowering=False)
    ks_in = nc.dram_tensor("ks", [512, SEQ], F32, kind="ExternalInput").ap()
    vs_in = nc.dram_tensor("vs", [512, SEQ], F32, kind="ExternalInput").ap()
    tf_in = nc.dram_tensor("tf8", [128, 8], F32, kind="ExternalInput").ap()
    yo = nc.dram_tensor("yo", [512, SEQ], F32, kind="ExternalOutput").ap()

    nblk = SEQ // R
    es = contextlib.ExitStack()
    with es:
        p = Prog(nc, es)
        kt = p.sbuf("kt_f", [128, SEQ], F32)
        vt = p.sbuf("vt_f", [128, SEQ], F32)
        tf8 = p.sbuf("tf8_t", [128, 8], F32)
        cn = p.sbuf("cn_f", [128, 8], F32)
        st = p.sbuf("st_f", [128, 8], F32)
        aAb = p.sbuf("aAb_f", [128, 2], F32)
        EWt = p.sbuf("EWt_f", [128, R], F32)
        onesR = p.sbuf("onesR_f", [128, R], F32)
        Et = p.sbuf("Et_f", [128, R], F32)
        Evt = p.sbuf("Evt_f", [128, R], F32)
        At = p.sbuf("At_f", [128, R], F32)
        Bt = p.sbuf("Bt_f", [128, R], F32)
        Eut = p.sbuf("Eut_f", [128, R], F32)
        Euv = p.sbuf("Euv_f", [128, R], F32)
        num = p.sbuf("num_f", [128, R], F32)
        den = p.sbuf("den_f", [128, R], F32)
        yt = p.sbuf("yt_f", [128, R], F32)

        p.dma("sp", tf8[:, :], tf_in)
        p.memset("dve", onesR[:, :], 1.0)

        for c in range(4):
            p.dma("sp", kt[:, :], ks_in[c * 128:(c + 1) * 128, :], tag=f"ldk{c}")
            p.dma("pool", vt[:, :], vs_in[c * 128:(c + 1) * 128, :], tag=f"ldv{c}")
            td = tf8[:, c:c + 1]
            u = tf8[:, 4 + c:5 + c]
            w_ = cn[:, 0:1]
            EW = cn[:, 1:2]
            eu = cn[:, 2:3]
            um = cn[:, 3:4]
            p.act(w_, td, AF.Exp, tag="fw")
            p.dve("tensor_scalar_mul", w_, w_, -1.0, tag="fwn")
            p.act(EW, w_, AF.Exp, tag="few")
            p.act(eu, u, AF.Exp, tag="feu")
            p.dve("tensor_scalar_max", um, u, 0.0, tag="fum")
            p.dve("tensor_scalar", EWt[:, :], onesR[:, :], EW, None, op0=ALU.mult, tag="fewt")
            aa = st[:, 0:1]
            bb = st[:, 1:2]
            pp = st[:, 2:3]
            p.memset("dve", aa, 0.0)
            p.memset("dve", bb, 0.0)
            p.memset("dve", pp, -1e38)
            for b in range(nblk):
                kb = kt[:, b * R:(b + 1) * R]
                vb = vt[:, b * R:(b + 1) * R]
                cmax = st[:, 3:4]
                M = st[:, 4:5]
                negM = st[:, 5:6]
                t1 = st[:, 6:7]
                t2 = st[:, 7:8]
                p.dve("reduce_max", cmax, kb, axis=AX.X, tag="fcm")
                p.dve("tensor_add", t1, pp, w_, tag="ft1")
                p.dve("tensor_add", t2, cmax, um, tag="ft2")
                p.dve("tensor_max", M, t1, t2, tag="fM")
                p.dve("tensor_scalar_mul", negM, M, -1.0, tag="fnM")
                p.act(Et[:, :], kb, AF.Exp, bias=negM, tag="fE")
                p.dve("tensor_mul", Evt[:, :], Et[:, :], vb, tag="fEv")
                cs = t1
                p.act(cs, pp, AF.Exp, bias=negM, tag="fcs")
                p.dve("tensor_mul", aAb[:, 0:1], aa, cs, tag="faA")
                p.dve("tensor_mul", aAb[:, 1:2], bb, cs, tag="fbB")
                p.dve("tensor_tensor_scan", At[:, :], EWt[:, :], Evt[:, :], aAb[:, 0:1],
                      op0=ALU.mult, op1=ALU.add, tag="fsA")
                p.dve("tensor_tensor_scan", Bt[:, :], EWt[:, :], Et[:, :], aAb[:, 1:2],
                      op0=ALU.mult, op1=ALU.add, tag="fsB")
                p.dve("tensor_scalar", Eut[:, :], Et[:, :], eu, None, op0=ALU.mult, tag="fEu")
                p.dve("tensor_mul", Euv[:, :], Eut[:, :], vb, tag="fEuv")
                p.dve("tensor_add", num[:, 1:R], At[:, 0:R - 1], Euv[:, 1:R], tag="fn1")
                p.dve("tensor_add", num[:, 0:1], aAb[:, 0:1], Euv[:, 0:1], tag="fn0")
                p.dve("tensor_add", den[:, 1:R], Bt[:, 0:R - 1], Eut[:, 1:R], tag="fd1")
                p.dve("tensor_add", den[:, 0:1], aAb[:, 1:2], Eut[:, 0:1], tag="fd0")
                p.act(den[:, :], den[:, :], AF.Ln, tag="fln")
                p.act(den[:, :], den[:, :], AF.Exp, scale=-1.0, tag="fexp")
                p.dve("tensor_mul", yt[:, :], num[:, :], den[:, :], tag="fy")
                p.dma("sp", yo[c * 128:(c + 1) * 128, b * R:(b + 1) * R], yt[:, :], tag="fyst")
                p.dve("tensor_copy", aa, At[:, R - 1:R], tag="fua")
                p.dve("tensor_copy", bb, Bt[:, R - 1:R], tag="fub")
                p.dve("tensor_copy", pp, M, tag="fup")

        block = es.enter_context(nc.Block())
        p.build(block)
    return nc



def build_launchG1():
    nc = bass.Bass("TRN2", target_bir_lowering=False)
    x2g = nc.dram_tensor("x2g", [D, HALF + 1], F32, kind="ExternalInput").ap()
    yg = nc.dram_tensor("yg", [D, HALF + 1], F32, kind="ExternalInput").ap()
    srg = nc.dram_tensor("srg", [D, HALF + 1], F32, kind="ExternalInput").ap()
    wog_in = nc.dram_tensor("wog", [D, D], F32, kind="ExternalInput").ap()
    fwk_in = nc.dram_tensor("fwk", [D, 4096], F32, kind="ExternalInput").ap()
    fwv_in = nc.dram_tensor("fwv", [4096, D], F32, kind="ExternalInput").ap()
    fwr_in = nc.dram_tensor("fwr", [D, D], F32, kind="ExternalInput").ap()
    ln2_in = nc.dram_tensor("ln2v", [128, 16], F32, kind="ExternalInput").ap()
    ftm_in = nc.dram_tensor("ftm", [128, 32], F32, kind="ExternalInput").ap()
    shm_in = nc.dram_tensor("shmask", [128, 1], F32, kind="ExternalInput").ap()
    x3m_o = nc.dram_tensor("x3m", [D, HALF], F32, kind="ExternalOutput").ap()

    es = contextlib.ExitStack()
    with es:
        p = Prog(nc, es)
        WO = 256
        W = WO + 1
        wog = p.sbuf("wog_t", [128, 8 * 1024], BF16)
        fwk = p.sbuf("fwk_t", [128, 8 * 4096], BF16)
        fwv = [p.sbuf(f"fwv_t{i}", [128, 8 * 1024], BF16) for i in range(2)]
        fwr = fwv[0]
        ln2v = p.sbuf("ln2v_t", [128, 16], F32)
        ftm = p.sbuf("ftm_t", [128, 32], F32)
        shm = p.sbuf("shm_g1", [128, 1], F32)
        ybf = p.sbuf("ybf", [128, 8 * W], BF16)
        srb = p.sbuf("srb", [128, 8 * W], BF16)
        x2s = p.sbuf("x2s_g", [128, 8 * W], F32)
        x2bf = p.sbuf("x2bf_g", [128, 8 * W], BF16)
        zs = p.sbuf("zs_g", [128, 8 * W], F32)
        xas = p.sbuf("xas_g", [128, 8 * W], F32)
        xab = p.sbuf("xab_g", [128, 8 * W], BF16)
        xn2 = p.sbuf("xn2_g", [128, 8 * W], BF16)
        dif = p.sbuf("dif_g", [128, 8 * WO], BF16)
        xr2 = p.sbuf("xr2_g", [128, 8 * WO], BF16)
        kk = p.sbuf("kk_g", [128, 32 * WO], BF16)
        rr = p.sbuf("rr_g", [128, 8 * WO], BF16)
        rr2 = p.sbuf("rr2_g", [128, 8 * WO], F32)
        stg = p.sbuf("stg_g", [128, W], F32)
        sc = ln_scratch(p, "g_", blk=257)
        stat = p.psum("ps_stat_g", [128, 512])
        ps = {"sums": stat[0:1, :], "sqs": stat[32:33, :],
              "bc0": p.psum("ps_bc0_g", [128, 512]), "bc1": p.psum("ps_bc1_g", [128, 512])}
        pspool = [p.psum(f"ps_mmg{i}", [128, 512]) for i in range(3)]

        for kc in range(8):
            p.dma("pool", wog[:, kc * 1024:(kc + 1) * 1024], wog_in[kc * 128:(kc + 1) * 128, :])
            p.dma("pool", fwk[:, kc * 4096:(kc + 1) * 4096], fwk_in[kc * 128:(kc + 1) * 128, :])
        p.dma("sp", ln2v[:, :], ln2_in)
        p.dma("sp", ftm[:, :], ftm_in)
        p.dma("sp", shm[:, :], shm_in)

        for si in range(HALF // WO):
            s0 = si * WO
            for kc in range(8):
                p.dma("pool", ybf[:, kc * W:(kc + 1) * W], yg[kc * 128:(kc + 1) * 128, s0:s0 + W])
                p.dma("pool", srb[:, kc * W:(kc + 1) * W], srg[kc * 128:(kc + 1) * 128, s0:s0 + W])
                p.dma("sp", x2s[:, kc * W:(kc + 1) * W], x2g[kc * 128:(kc + 1) * 128, s0:s0 + W])
                p.dma("pool", x2bf[:, kc * W:(kc + 1) * W], x2g[kc * 128:(kc + 1) * 128, s0:s0 + W])
            emit_ln_fm(p, x2s, x2bf, W, zs, ps, sc, tag=f"gz{si}")

            def xa_writer(oc, w0, bw, ps_ap):
                sl = slice(oc * W + w0, oc * W + w0 + bw)
                p.dve("tensor_mul", stg[:, 0:bw], srb[:, sl], ps_ap, tag="gxa1")
                p.dve("tensor_add", xas[:, sl], zs[:, sl], stg[:, 0:bw], tag="gxa2")
                p.act_copy(xab[:, sl], xas[:, sl], tag="gxab")
            emit_linear_fm(p, xa_writer, wog, ybf, W, 8, 8, pspool, blk_cols=512, tag="gwo")

            emit_ln_fm(p, xas, xab, W, xn2, ps, sc, gamma=ln2v[:, 0:8], beta=ln2v[:, 8:16],
                       tag=f"gl2{si}")
            if si == 0:
                for kc in range(8):
                    p.dve("tensor_scalar", xn2[:, kc * W:kc * W + 1], xn2[:, kc * W:kc * W + 1],
                          shm[:, 0:1], None, op0=ALU.mult, tag="gshm")
            for kc in range(8):
                a = xn2[:, kc * W + 1:kc * W + W]
                b = xn2[:, kc * W:kc * W + W - 1]
                p.dve("tensor_sub", dif[:, kc * WO:(kc + 1) * WO], a, b, tag="gdif")
            for kc in range(8):
                p.dve("scalar_tensor_tensor", xr2[:, kc * WO:(kc + 1) * WO],
                      dif[:, kc * WO:(kc + 1) * WO], ftm[:, 8 + kc:9 + kc],
                      xn2[:, kc * W:kc * W + W - 1], op0=ALU.mult, op1=ALU.add, tag="gmr")
            for kc in range(8):
                p.dve("scalar_tensor_tensor", dif[:, kc * WO:(kc + 1) * WO],
                      dif[:, kc * WO:(kc + 1) * WO], ftm[:, kc:kc + 1],
                      xn2[:, kc * W:kc * W + W - 1], op0=ALU.mult, op1=ALU.add, tag="gmk")

            def kk_writer(oc, w0, bw, ps_ap):
                p.dve("tensor_relu", stg[:, 0:bw], ps_ap, tag="gkr")
                p.act(kk[:, oc * WO + w0: oc * WO + w0 + bw], stg[:, 0:bw], AF.Square, tag="gks")
            emit_linear_fm(p, kk_writer, fwk, dif, WO, 8, 32, pspool, blk_cols=256,
                           w_kc_stride=4096, tag="gfk")

            for kc in range(8):
                p.dma("pool", fwr[:, kc * 1024:(kc + 1) * 1024],
                      fwr_in[kc * 128:(kc + 1) * 128, :], tag="gldr")
            def rr_writer(oc, w0, bw, ps_ap):
                p.act(rr[:, oc * WO + w0: oc * WO + w0 + bw], ps_ap, AF.Sigmoid, tag="grr")
            emit_linear_fm(p, rr_writer, fwr, xr2, WO, 8, 8, pspool, blk_cols=256, tag="gfr")

            for qt in range(4):
                fb = fwv[qt % 2]
                for kc in range(8):
                    p.dma("pool", fb[:, kc * 1024:(kc + 1) * 1024],
                          fwv_in[(qt * 8 + kc) * 128:(qt * 8 + kc + 1) * 128, :],
                          tag=f"gldv{qt}")
                for oc in range(8):
                    pv_ = pspool[oc % 3]
                    for kc in range(8):
                        c0 = kc * 1024 + oc * 128
                        p.mm(pv_[:, 0:WO], fb[:, c0:c0 + 128],
                             kk[:, (qt * 8 + kc) * WO:(qt * 8 + kc + 1) * WO],
                             start=(kc == 0), stop=(kc == 7), tag="gfv")
                    if qt == 0:
                        p.dve("tensor_copy", rr2[:, oc * WO:(oc + 1) * WO], pv_[:, 0:WO],
                              tag="gvq0")
                    else:
                        p.dve("tensor_add", rr2[:, oc * WO:(oc + 1) * WO],
                              rr2[:, oc * WO:(oc + 1) * WO], pv_[:, 0:WO], tag="gvq")
            for oc in range(8):
                sl = slice(oc * W + 1, oc * W + W)
                slo = slice(oc * WO, (oc + 1) * WO)
                p.dve("tensor_mul", stg[:, 0:WO], rr[:, slo], rr2[:, slo], tag="gv1")
                p.dve("tensor_add", stg[:, 0:WO], stg[:, 0:WO], xas[:, sl], tag="gv2")
                p.dve("tensor_add", stg[:, 0:WO], stg[:, 0:WO], x2s[:, sl], tag="gv3")
                p.dma("sp", x3m_o[oc * 128:(oc + 1) * 128, s0:s0 + WO], stg[:, 0:WO], tag="gvst")

        block = es.enter_context(nc.Block())
        p.build(block)
    return nc



def build_launchG2():
    nc = bass.Bass("TRN2", target_bir_lowering=False)
    x4g = nc.dram_tensor("x4g", [D, HALF], F32, kind="ExternalInput").ap()
    wrt_in = nc.dram_tensor("wrt", [128, 64], F32, kind="ExternalInput").ap()
    brw_in = nc.dram_tensor("brw", [1, 8], F32, kind="ExternalInput").ap()
    w1_in = nc.dram_tensor("w1s", [8192, 4096], F32, kind="ExternalInput").ap()
    w2_in = nc.dram_tensor("w2s", [32768, 1024], F32, kind="ExternalInput").ap()
    b1_in = nc.dram_tensor("b1v", [128, 256], F32, kind="ExternalInput").ap()
    b2_in = nc.dram_tensor("b2v", [128, 64], F32, kind="ExternalInput").ap()
    sel_in = nc.dram_tensor("selm8", [8, 1024], F32, kind="ExternalInput").ap()
    id_in = nc.dram_tensor("ident", [128, 128], F32, kind="ExternalInput").ap()
    fin = nc.dram_tensor("fin", [D, HALF], F32, kind="ExternalOutput").ap()

    es = contextlib.ExitStack()
    with es:
        p = Prog(nc, es)
        W = 256
        w1t = p.sbuf("w1t", [128, 8 * 4096], BF16)
        w2t = p.sbuf("w2t", [128, 32 * 1024], BF16)
        ht = p.sbuf("ht", [128, 32 * W], BF16)
        x4s = ht[:, 0:8 * W * 2].bitcast(F32)
        x4b = ht[:, 8 * W * 2:8 * W * 3]
        xn3 = p.sbuf("xn3", [128, 8 * HALF], BF16)
        wrt = p.sbuf("wrt_t", [128, 8 * 8], BF16)
        brw = p.sbuf("brw_t", [1, 8], F32)
        b1v = p.sbuf("b1v_t", [128, 256], F32)
        b2v = p.sbuf("b2v_t", [128, 64], F32)
        selm = p.sbuf("selm8_t", [8, 1024], F32)
        ident = p.sbuf("ident_g2", [128, 128], F32)
        ones1f = p.sbuf("ones1f_g2", [1, 128], F32)
        brb = p.sbuf("brb", [128, 8], F32)
        ltm = p.sbuf("ltm", [128, 8], F32)
        gx = p.sbuf("gx_g2", [128, 24], F32)
        el = p.sbuf("el_g2", [128, 8], F32)
        l2 = p.sbuf("l2_g2", [128, 8], F32)
        msk = p.sbuf("msk_g2", [128, 8], F32)
        wtm = p.sbuf("wtm_g2", [128, 16 * 8], F32)
        wfm = p.sbuf("wfm_g2", [8, HALF], F32)
        wbc = p.sbuf("wbc_g2", [128, W], F32)
        stg = [p.sbuf(f"stg_g2_{i}", [128, W], F32) for i in range(4)]
        sc = ln_scratch(p, "g2_", blk=256)
        stat = p.psum("ps_stat_g2", [128, 512])
        ps = {"sums": stat[0:1, :], "sqs": stat[32:33, :],
              "bc0": p.psum("ps_bc0_g2", [128, 512]), "bc1": p.psum("ps_bc1_g2", [128, 512])}
        pspool = [p.psum(f"ps_g2_{i}", [128, 512]) for i in range(3)]

        p.dma("pool", wrt[:, :], wrt_in)
        p.dma("sp", brw[:, :], brw_in)
        p.dma("sp", b1v[:, :], b1_in)
        p.dma("sp", b2v[:, :], b2_in)
        p.dma("sp", selm[:, :], sel_in)
        p.dma("sp", ident[:, :], id_in)
        p.memset("dve", ones1f[:, :], 1.0)
        p.dma("sp", fin, x4g, tag="init_fin")
        p.mm(ps["bc0"][:, 0:8], ones1f[0:1, :], brw[0:1, :], tag="brb")
        p.dve("tensor_copy", brb[:, :], ps["bc0"][:, 0:8], tag="brbc")

        for si in range(HALF // W):
            s0 = si * W
            for kc in range(8):
                p.dma("sp", x4s[:, kc * W:(kc + 1) * W], x4g[kc * 128:(kc + 1) * 128, s0:s0 + W])
                p.dma("pool", x4b[:, kc * W:(kc + 1) * W], x4g[kc * 128:(kc + 1) * 128, s0:s0 + W])
            xn3v = xn3[:, 0:0]
            class _shim:
                def __getitem__(self, idx):
                    return None
            emit_ln_fm_strided(p, x4s, x4b, W, xn3, HALF, s0, ps, sc, tag=f"g2ln{si}")
            for tt in range(W // 128):
                t0 = s0 + tt * 128
                psr = pspool[tt % 2]
                for kc in range(8):
                    p.mm(psr[0:128, 0:8], xn3[:, kc * HALF + t0:kc * HALF + t0 + 128],
                         wrt[:, kc * 8:(kc + 1) * 8], start=(kc == 0), stop=(kc == 7),
                         tag="rmm")
                p.dve("tensor_add", ltm[:, :], psr[0:128, 0:8], brb[:, :], tag="rl")
                m1 = gx[:, 0:1]
                negm1 = gx[:, 1:2]
                m2 = gx[:, 2:3]
                den = gx[:, 3:4]
                rec = gx[:, 4:5]
                p.dve("reduce_max", m1, ltm[:, :], axis=AX.X, tag="rm1")
                p.dve("tensor_scalar_mul", negm1, m1, -1.0, tag="rnm")
                p.act(el[:, :], ltm[:, :], AF.Exp, bias=negm1, tag="rel")
                p.dve("tensor_scalar", msk[:, :], ltm[:, :], m1, None, op0=ALU.is_equal, tag="roh")
                p.dve("scalar_tensor_tensor", l2[:, :], msk[:, :], -1e9, ltm[:, :],
                      op0=ALU.mult, op1=ALU.add, tag="rl2")
                p.dve("reduce_max", m2, l2[:, :], axis=AX.X, tag="rm2")
                p.dve("tensor_scalar", l2[:, :], l2[:, :], m2, None, op0=ALU.is_equal, tag="roh2")
                p.dve("tensor_add", msk[:, :], msk[:, :], l2[:, :], tag="rmsk")
                p.dve("tensor_mul", el[:, :], el[:, :], msk[:, :], tag="rnum")
                p.dve("reduce_sum", den, el[:, :], axis=AX.X, tag="rden")
                p.dve("reciprocal", rec, den, tag="rrec")
                p.dve("tensor_scalar", wtm[:, (si * (W // 128) + tt) * 8:(si * (W // 128) + tt) * 8 + 8],
                      el[:, :], rec, None, op0=ALU.mult, tag="rw")
        for t in range(16):
            p.transpose(ps["bc1"][0:8, 0:128], wtm[:, t * 8:(t + 1) * 8], ident[:, :], tag="wt")
            p.dve("tensor_copy", wfm[:, t * 128:(t + 1) * 128], ps["bc1"][0:8, 0:128], tag="wtc")

        for e in range(8):
            for kc in range(8):
                p.dma("pool", w1t[:, kc * 4096:(kc + 1) * 4096],
                      w1_in[e * 1024 + kc * 128:e * 1024 + (kc + 1) * 128, :], tag=f"ldw1_{e}")
            for kc in range(32):
                p.dma("pool", w2t[:, kc * 1024:(kc + 1) * 1024],
                      w2_in[e * 4096 + kc * 128:e * 4096 + (kc + 1) * 128, :], tag=f"ldw2_{e}")
            for blk in range(HALF // W):
                s0 = blk * W
                for hc in range(32):
                    psh = pspool[hc % 2]
                    for kc in range(8):
                        p.mm(psh[:, 0:W], w1t[:, kc * 4096 + hc * 128:kc * 4096 + hc * 128 + 128],
                             xn3[:, kc * HALF + s0:kc * HALF + s0 + W],
                             start=(kc == 0), stop=(kc == 7), tag=f"hmm{e}")
                    p.act(ht[:, hc * W:(hc + 1) * W], psh[:, 0:W], AF.Relu,
                          bias=b1v[:, e * 32 + hc:e * 32 + hc + 1], tag=f"hrelu{e}")
                p.mm(ps["bc0"][:, 0:W], selm[:, e * 128:(e + 1) * 128], wfm[:, s0:s0 + W],
                     tag="wsel")
                p.dve("tensor_copy", wbc[:, :], ps["bc0"][:, 0:W], tag="wselc")
                for oc in range(8):
                    psy = pspool[2]
                    for kc in range(32):
                        p.mm(psy[:, 0:W], w2t[:, kc * 1024 + oc * 128:kc * 1024 + oc * 128 + 128],
                             ht[:, kc * W:(kc + 1) * W], start=(kc == 0), stop=(kc == 31),
                             tag=f"ymm{e}")
                    sg = stg[oc % 4]
                    p.dve("tensor_scalar", sg[:, :], psy[:, 0:W], b2v[:, e * 8 + oc:e * 8 + oc + 1],
                          None, op0=ALU.add, tag="yb")
                    p.dve("tensor_mul", sg[:, :], sg[:, :], wbc[:, :], tag="yw")
                    p.dma_accum(fin[oc * 128:(oc + 1) * 128, s0:s0 + W], sg[:, :], tag="yacc")

        block = es.enter_context(nc.Block())
        p.build(block)
    return nc


def emit_ln_fm_strided(p, x_f32, x_bf, W, out_bf, out_stride, out_off, ps, sc, tag="ln"):
    nblk = (W + 511) // 512
    for blk in range(nblk):
        w0 = blk * 512
        bw = min(512, W - w0)
        sums = ps["sums"][0:1, 0:bw]
        sqs = ps["sqs"][0:1, 0:bw]
        for c in range(8):
            xb = x_bf[:, c * W + w0: c * W + w0 + bw]
            p.mm(sums, sc["ones128"][:, 0:1], xb, start=(c == 0), stop=(c == 7), tag=f"{tag}_sum")
        for c in range(8):
            xb = x_bf[:, c * W + w0: c * W + w0 + bw]
            sq = sc["sq" + str(c % 2)][:, 0:bw]
            p.act(sq, xb, AF.Square, tag=f"{tag}_sq")
            p.mm(sqs, sc["ones128"][:, 0:1], sq, start=(c == 0), stop=(c == 7), tag=f"{tag}_sqs")
        m = sc["row"][0:1, 0:bw]
        msq = sc["row"][0:1, 512:512 + bw]
        var = sc["row"][0:1, 1024:1024 + bw]
        rstd = sc["row"][0:1, 1536:1536 + bw]
        p.dve("tensor_scalar_mul", m, sums, 1.0 / D, tag=f"{tag}_m")
        p.dve("tensor_scalar_mul", msq, sqs, 1.0 / D, tag=f"{tag}_msq")
        p.dve("tensor_mul", var, m, m, tag=f"{tag}_m2")
        p.dve("tensor_sub", var, msq, var, tag=f"{tag}_var")
        p.act(rstd, var, AF.Ln, bias=sc["eps"][0:1, 0:1], tag=f"{tag}_ln")
        p.act(rstd, rstd, AF.Exp, scale=-0.5, tag=f"{tag}_rstd")
        p.dve("tensor_scalar_mul", m, m, -1.0, tag=f"{tag}_negm")
        bc_m = ps["bc0"][:, 0:bw]
        bc_r = ps["bc1"][:, 0:bw]
        p.mm(bc_m, sc["ones1_f"][0:1, :], m, tag=f"{tag}_bcm")
        p.mm(bc_r, sc["ones1_f"][0:1, :], rstd, tag=f"{tag}_bcr")
        for c in range(8):
            xf = x_f32[:, c * W + w0: c * W + w0 + bw]
            ob = out_bf[:, c * out_stride + out_off + w0: c * out_stride + out_off + w0 + bw]
            t = sc["nrm"][:, 0:bw]
            p.dve("tensor_add", t, xf, bc_m, tag=f"{tag}_sub")
            p.dve("tensor_mul", ob, t, bc_r, tag=f"{tag}_mul")



def _pv(v):
    v = np.asarray(v, np.float32).reshape(-1)
    n = v.size // 128
    return np.ascontiguousarray(v.reshape(n, 128).T)


def _make_masks():
    m = np.zeros((4 * 128, 512), np.float32)
    for rel in range(4):
        i = np.arange(128)[:, None] + 128 * rel
        j = np.arange(512)[None, :]
        m[rel * 128:(rel + 1) * 128] = (i <= j).astype(np.float32)
    import ml_dtypes
    return m.astype(ml_dtypes.bfloat16)


def _pack_v(vtm, h2):
    import ml_dtypes
    out = np.zeros((8 * 128, 2080), ml_dtypes.bfloat16)
    one = np.ones((), ml_dtypes.bfloat16)
    for h in range(8):
        blk = out[128 * h:128 * h + 128]
        for c in range(32):
            blk[:, 65 * c:65 * c + 64] = vtm[128 * c:128 * c + 128, 64 * h:64 * h + 64]
            blk[:, 65 * c + 64] = one
    return out


def kernel(**inputs):
    import ml_dtypes
    BF = ml_dtypes.bfloat16
    F = np.float32
    g = lambda k: np.asarray(inputs[k])
    x = g("L0_attn_x").astype(F)
    xT = [np.ascontiguousarray(x[b].T) for b in range(BATCH)]

    w_qkv = g("L0_attn_w_qkv")
    ident = np.eye(128, dtype=F)
    masks = _make_masks()

    ncA = build_launchA()
    imsA = []
    for c in range(NCORES):
        b, h2 = c // 2, c % 2
        q0 = 512 * h2
        wqk = np.ascontiguousarray(np.concatenate(
            [w_qkv[:, q0:q0 + 512], w_qkv[:, 1024 + q0:1024 + q0 + 512]], axis=1))
        wv = np.ascontiguousarray(w_qkv[:, 2048 + q0:2048 + q0 + 512])
        imsA.append({"x0": np.ascontiguousarray(xT[b][:, :HALF]),
                     "x1": np.ascontiguousarray(xT[b][:, HALF:]),
                     "wqk": wqk, "wv": wv})
    resA = run_bass_kernel_spmd(ncA, imsA, core_ids=list(range(NCORES))).results

    ncB = build_launchB()
    imsB = []
    for c in range(NCORES):
        imsB.append({"qk": resA[c]["qk"], "vpack": _pack_v(resA[c]["vtm"], c % 2),
                     "masks": masks})
    resB = run_bass_kernel_spmd(ncB, imsB, core_ids=list(range(NCORES))).results

    sw1, sw2, sw3 = g("L0_ffn_sw1"), g("L0_ffn_sw2"), g("L0_ffn_sw3")
    pvec = np.zeros((128, 24), F)
    pvec[:, 0:8] = _pv(g("L0_attn_b_out"))
    pvec[:, 8:12] = _pv(g("L0_ffn_sb1"))
    pvec[:, 12:16] = _pv(g("L0_ffn_sb2"))
    pvec[:, 16:20] = _pv(g("L0_ffn_s_ln_g"))
    pvec[:, 20:24] = _pv(g("L0_ffn_s_ln_b"))
    w12 = np.ascontiguousarray(np.concatenate([sw1.T, sw2.T], axis=1), dtype=F)
    sw3t = np.ascontiguousarray(sw3.T, dtype=F)
    sb3v = _pv(g("L0_ffn_sb3"))
    ncC = build_launchC()
    imsC = []
    for c in range(NCORES):
        b, h2 = c // 2, c % 2
        ao_full = np.concatenate([resB[2 * b]["ao"], resB[2 * b + 1]["ao"]], axis=0)
        aof = np.ascontiguousarray(ao_full[:, h2 * HALF:(h2 + 1) * HALF])
        imsC.append({"xf32": np.ascontiguousarray(xT[b][:, h2 * HALF:(h2 + 1) * HALF]),
                     "aof": aof, "w_out4": g("L0_attn_w_out").astype(F),
                     "w12": w12, "sw3t": sw3t, "pvec": pvec, "sb3v": sb3v})
    resC = run_bass_kernel_spmd(ncC, imsC, core_ids=list(range(NCORES))).results

    idx_b = g("L0_ffn_index_b").astype(np.int64)
    idx_t = g("L0_ffn_index_t").astype(np.int64)
    Msel = idx_b.size
    Mpad = ((Msel + NCORES * 128 - 1) // (NCORES * 128)) * (NCORES * 128)
    Mper = Mpad // NCORES
    owner = 2 * idx_b + (idx_t >= HALF)
    lcol = idx_t % HALF
    xm_all = np.zeros((D, Mpad), BF)
    shg_all = np.zeros((D, Mpad), F)
    for m in range(Msel):
        xm_all[:, m] = resC[owner[m]]["xn1o"][:, lcol[m]]
        shg_all[:, m] = resC[owner[m]]["sho"][:, lcol[m]]
    egw = np.ascontiguousarray(
        np.concatenate([g("L0_ffn_eg_w"), g("L0_ffn_sg_w")], axis=0).T, dtype=F)
    gb9 = np.concatenate([g("L0_ffn_eg_b"), g("L0_ffn_sg_b")]).reshape(9, 1).astype(F)
    expb = g("L0_ffn_exp_bias").reshape(1, 8).astype(F)
    ew1, ew2, ew3 = g("L0_ffn_ew1"), g("L0_ffn_ew2"), g("L0_ffn_ew3")
    ew12 = np.zeros((D, 8192), F)
    ew3t = np.zeros((512, 8192), F)
    for e in range(8):
        ew12[:, e * 1024:e * 1024 + 512] = ew1[e].T
        ew12[:, e * 1024 + 512:(e + 1) * 1024] = ew2[e].T
        ew3t[:, e * 1024:(e + 1) * 1024] = ew3[e].T
    ebv = np.zeros((128, 72), F)
    eb1, eb2 = g("L0_ffn_eb1"), g("L0_ffn_eb2")
    for e in range(8):
        ebv[:, e * 8:e * 8 + 4] = _pv(eb1[e])
        ebv[:, e * 8 + 4:e * 8 + 8] = _pv(eb2[e])
    ebv[:, 64:68] = _pv(g("L0_ffn_sel_ln_g"))
    ebv[:, 68:72] = _pv(g("L0_ffn_sel_ln_b"))
    selm = np.zeros((9, 9 * 128), F)
    for e in range(9):
        selm[e, e * 128:(e + 1) * 128] = 1.0
    ncD = build_launchD(Mper)
    imsD = []
    for c in range(NCORES):
        imsD.append({"xm": np.ascontiguousarray(xm_all[:, c * Mper:(c + 1) * Mper]),
                     "shg": np.ascontiguousarray(shg_all[:, c * Mper:(c + 1) * Mper]),
                     "egw": egw, "gb9": gb9, "expb": expb, "ew12": ew12,
                     "ew3t": ew3t, "eb3t": g("L0_ffn_eb3").astype(F), "ebv": ebv,
                     "ident": ident, "selm": selm})
    resD = run_bass_kernel_spmd(ncD, imsD, core_ids=list(range(NCORES))).results
    final0 = [np.array(resC[c]["sho"]) for c in range(NCORES)]
    for m in range(Msel):
        comb = resD[m // Mper]["comb"][:, m % Mper]
        final0[owner[m]][:, lcol[m]] = comb

    tmk = g("L1_attn_tmk").reshape(-1)
    tmv_ = g("L1_attn_tmv").reshape(-1)
    tmr = g("L1_attn_tmr").reshape(-1)
    lnvE = np.concatenate([_pv(g("L1_attn_ln1_g")), _pv(g("L1_attn_ln1_b"))], axis=1)
    tmvE = np.concatenate([_pv(tmk), _pv(tmv_), _pv(tmr),
                           _pv(1 - tmk), _pv(1 - tmv_), _pv(1 - tmr)], axis=1)
    wk3 = np.ascontiguousarray(np.concatenate(
        [g("L1_attn_Wk").T, g("L1_attn_Wv").T, g("L1_attn_Wr").T], axis=1), dtype=F)
    ncE = build_launchE()
    imsE = []
    for c in range(NCORES):
        b, h2 = c // 2, c % 2
        if h2 == 0:
            x1p = np.zeros((D, 1), F)
            f0p = np.zeros((D, 1), F)
            shm = np.zeros((128, 1), F)
        else:
            x1p = resC[2 * b]["x1o"][:, -1:]
            f0p = final0[2 * b][:, -1:]
            shm = np.ones((128, 1), F)
        imsE.append({"x1g": np.ascontiguousarray(np.concatenate([x1p, resC[c]["x1o"]], axis=1)),
                     "f0g": np.ascontiguousarray(np.concatenate([f0p, final0[c]], axis=1)),
                     "wk3": wk3, "lnv": lnvE, "tmv": tmvE, "shmask": shm})
    resE = run_bass_kernel_spmd(ncE, imsE, core_ids=list(range(NCORES))).results

    td = g("L1_attn_time_decay").reshape(-1)
    tf = g("L1_attn_time_first").reshape(-1)
    ncF = build_launchF()
    imsF = []
    for c in range(NCORES):
        b, h2 = c // 2, c % 2
        kfull = np.concatenate([resE[2 * b]["ko"], resE[2 * b + 1]["ko"]], axis=1)
        vfull = np.concatenate([resE[2 * b]["vo"], resE[2 * b + 1]["vo"]], axis=1)
        r0 = 512 * h2
        tf8 = np.zeros((128, 8), F)
        for cc in range(4):
            tf8[:, cc] = td[r0 + cc * 128:r0 + (cc + 1) * 128]
            tf8[:, 4 + cc] = tf[r0 + cc * 128:r0 + (cc + 1) * 128]
        imsF.append({"ks": np.ascontiguousarray(kfull[r0:r0 + 512]),
                     "vs": np.ascontiguousarray(vfull[r0:r0 + 512]), "tf8": tf8})
    resF = run_bass_kernel_spmd(ncF, imsF, core_ids=list(range(NCORES))).results

    ftmk = g("L1_attn_f_tmk").reshape(-1)
    ftmr = g("L1_attn_f_tmr").reshape(-1)
    ln2v = np.concatenate([_pv(g("L1_attn_ln2_g")), _pv(g("L1_attn_ln2_b"))], axis=1)
    ftm = np.concatenate([_pv(ftmk), _pv(ftmr), _pv(1 - ftmk), _pv(1 - ftmr)], axis=1)
    ncG1 = build_launchG1()
    imsG1 = []
    yfull = {}
    for b in range(BATCH):
        yfull[b] = np.concatenate([resF[2 * b]["yo"], resF[2 * b + 1]["yo"]], axis=0)
    for c in range(NCORES):
        b, h2 = c // 2, c % 2
        t0 = h2 * HALF
        def win(arr2049src_full, prev_zero):
            if h2 == 0:
                pc = np.zeros((D, 1), F)
            else:
                pc = arr2049src_full[:, t0 - 1:t0]
            return np.ascontiguousarray(
                np.concatenate([pc, arr2049src_full[:, t0:t0 + HALF]], axis=1))
        x2full = np.concatenate([resE[2 * b]["x2o"], resE[2 * b + 1]["x2o"]], axis=1)
        srfull = np.concatenate([resE[2 * b]["sro"], resE[2 * b + 1]["sro"]], axis=1)
        imsG1.append({"x2g": win(x2full, True), "yg": win(yfull[b], True),
                      "srg": win(srfull, True),
                      "wog": np.ascontiguousarray(g("L1_attn_Wo").T, dtype=F),
                      "fwk": np.ascontiguousarray(g("L1_attn_fWk").T, dtype=F),
                      "fwv": np.ascontiguousarray(g("L1_attn_fWv").T, dtype=F),
                      "fwr": np.ascontiguousarray(g("L1_attn_fWr").T, dtype=F),
                      "ln2v": ln2v, "ftm": ftm,
                      "shmask": np.full((128, 1), 0.0 if h2 == 0 else 1.0, F)})
    resG1 = run_bass_kernel_spmd(ncG1, imsG1, core_ids=list(range(NCORES))).results

    Wrr = g("L1_ffn_W_router").astype(F)
    b1, b2 = g("L1_ffn_b1"), g("L1_ffn_b2")
    b1v = np.zeros((128, 256), F)
    b2v = np.zeros((128, 64), F)
    for e in range(8):
        b1v[:, e * 32:(e + 1) * 32] = _pv(b1[e])
        b2v[:, e * 8:(e + 1) * 8] = _pv(b2[e])
    selm8 = np.zeros((8, 1024), F)
    for e in range(8):
        selm8[e, e * 128:(e + 1) * 128] = 1.0
    ncG2 = build_launchG2()
    imsG2 = []
    for c in range(NCORES):
        imsG2.append({"x4g": resG1[c]["x3m"],
                      "wrt": np.ascontiguousarray(
                          Wrr.reshape(8, 128, 8).transpose(1, 0, 2).reshape(128, 64)),
                      "brw": g("L1_ffn_b_router").reshape(1, 8).astype(F),
                      "w1s": g("L1_ffn_W1").reshape(8 * D, 4096).astype(F),
                      "w2s": g("L1_ffn_W2").reshape(8 * 4096, D).astype(F),
                      "b1v": b1v, "b2v": b2v, "selm8": selm8, "ident": ident})
    resG2 = run_bass_kernel_spmd(ncG2, imsG2, core_ids=list(range(NCORES))).results

    out = np.empty((BATCH, SEQ, D), np.float32)
    for c in range(NCORES):
        b, h2 = c // 2, c % 2
        out[b, h2 * HALF:(h2 + 1) * HALF, :] = np.asarray(resG2[c]["fin"]).T
    return out



class TilePool:

    def __init__(self, p, arena_bytes=2048 * 111):
        self.p = p
        self.arenas = []
        self.live = {}
        self.arena_bytes = arena_bytes

    def _new_arena(self, need):
        size = max(self.arena_bytes, need)
        att = 0
        while size >= need:
            try:
                t = self.p.sbuf(f"arena{len(self.arenas)}_{att}", [128, size // 2], BF16)
                break
            except AssertionError:
                size -= 2048
                att += 1
        else:
            raise AssertionError(f"TilePool: cannot allocate arena of {need}B")
        self.arenas.append([t, [[0, size]]])
        return len(self.arenas) - 1

    def get(self, cols, dtype, rows=128, want=0):
        esz = 2 if dtype == BF16 else (1 if dtype == U8 else 4)
        need = (cols * esz + 63) // 64 * 64
        small = need <= 8448
        best = None
        for ai, (t, frees) in enumerate(self.arenas):
            for fi, (lo, hi) in enumerate(frees):
                if hi - lo < need:
                    continue
                if small:
                    if best is None or hi > best[3]:
                        best = (ai, fi, lo, hi)
                else:
                    if best is None or (hi - lo) < (best[3] - best[2]):
                        best = (ai, fi, lo, hi)
        if best is None:
            ai = self._new_arena(need)
            fi = 0
        else:
            ai, fi = best[0], best[1]
        t, frees = self.arenas[ai]
        iv = frees[fi]
        if small:
            lo = iv[1] - need
            if iv[1] - iv[0] == need:
                frees.pop(fi)
            else:
                iv[1] -= need
        else:
            lo = iv[0]
            if iv[1] - iv[0] == need:
                frees.pop(fi)
            else:
                iv[0] += need
        if dtype == BF16:
            view = t[0:rows, lo // 2: lo // 2 + cols]
        elif dtype == U8:
            view = t[:, lo // 2: lo // 2 + (cols + 1) // 2].bitcast(U8)[0:rows, 0:cols]
        else:
            view = t[:, lo // 2: lo // 2 + 2 * cols].bitcast(dtype)[0:rows, 0:cols]
        self.live[id(view)] = (ai, lo, lo + need)
        return view

    def put(self, *views):
        for v in views:
            ai, lo, hi = self.live.pop(id(v))
            frees = self.arenas[ai][1]
            frees.append([lo, hi])
            frees.sort()
            merged = []
            for iv in frees:
                if merged and merged[-1][1] == iv[0]:
                    merged[-1][1] = iv[1]
                else:
                    merged.append(iv)
            self.arenas[ai][1] = merged


def pair_signal(p, flag_sh, epv, parity_reads, seq, deps):
    def fn(e):
        with e.register(f"sg{seq}") as r:
            e.load(r, parity_reads[0][0:1, 0:1])
            with e.If_eq(r, 0):
                e.dma_start(out=flag_sh.ap()[0:1, 0:1], in_=epv[0:1, seq - 1:seq]
                            ).then_inc(p._par_dma_sem, 16)
            with e.Else():
                e.dma_start(out=flag_sh.ap()[1:2, 0:1], in_=epv[0:1, seq - 1:seq]
                            ).then_inc(p._par_dma_sem, 16)
        p._par_dma_count += 16
        return e.nop()
    return p.raw("pool", fn, reads=list(deps) + [epv, parity_reads[0]],
                 writes=[flag_sh], kind="compute", tag=f"sig{seq}")


def pair_wait(p, flag_sh, flagv, trash, parity_reads, epv, seq, written):
    def fn(e):
        with e.register(f"pw{seq}") as r, e.register(f"pf{seq}") as f, \
                e.register(f"pg{seq}") as go, e.register(f"pt{seq}") as tgt, \
                e.register(f"ph{seq}") as hi, e.register(f"pc{seq}") as c1:
            e.load(r, parity_reads[0][0:1, 0:1])
            e.load(tgt, epv[0:1, seq - 1:seq])
            e.load(hi, epv[0:1, 7:8])
            e.reg_mov(go, 1)
            with e.While(go):
                e.nop(cycle_cnt=1000)
                with e.If_eq(r, 0):
                    e.dma_start(out=flagv[0:1, :], in_=flag_sh.ap()[1:2, :]).then_inc(trash, 16)
                with e.Else():
                    e.dma_start(out=flagv[0:1, :], in_=flag_sh.ap()[0:1, :]).then_inc(trash, 16)
                e.nop(cycle_cnt=2000)
                e.load(f, flagv[0:1, 0:1])
                e.reg_alu(go, f, tgt, ALU.is_lt)
                e.reg_alu(c1, f, hi, ALU.is_lt)
                e.reg_alu(go, go, 1, ALU.add)
                e.reg_alu(go, go, c1, ALU.subtract)
        return e.nop()
    return p.raw("pool", fn, reads=[flag_sh, parity_reads[0], epv],
                 writes=list(written) + [flagv], kind="compute", tag=f"wait{seq}")


def par_dma(p, parity_reads, even_list, odd_list, reads, writes, tag=""):
    def fn(e):
        with e.register("pd" + tag) as r:
            e.load(r, parity_reads[0][0:1, 0:1])
            with e.If_eq(r, 0):
                for out, in_ in even_list:
                    e.dma_start(out=out, in_=in_, allow_slow_non_contiguous=True
                                ).then_inc(p._par_dma_sem, 16)
            with e.Else():
                for out, in_ in odd_list:
                    e.dma_start(out=out, in_=in_, allow_slow_non_contiguous=True
                                ).then_inc(p._par_dma_sem, 16)
        n = max(len(even_list), len(odd_list))
        assert len(even_list) == len(odd_list), "parity branches must issue equally"
        e.wait_ge(p._par_dma_sem, p._par_dma_count + 16 * n)
        p._par_dma_count += 16 * n
        return e.nop()
    return p.raw("pool", fn, reads=list(reads) + [parity_reads[0]], writes=list(writes),
                 kind="compute", tag="pdma" + tag)


def strided_cols(tile, c0, step, n):
    t = tile.tensor if isinstance(tile, bass.AP) else tile
    base_off = tile.offset if isinstance(tile, bass.AP) else 0
    row = t.shape[1]
    return bass.AP(t, base_off + c0, [[row, 128], [step, n]])


def build_fused(gather_info, dbg=None):
    nc = bass.Bass("TRN2", target_bir_lowering=False)
    A = lambda name, shape, dt=F32: nc.dram_tensor(name, list(shape), dt, kind="ExternalInput").ap()
    x0 = A("x0", [D, HALF]); x1 = A("x1", [D, HALF]); xh = A("xh", [D, HALF])
    wqk_i = A("wqk", [D, 1024]); wv_i = A("wv", [D, 512])
    masks_i = A("masks", [4 * 128, 512], BF16)
    wout_i = A("w_out4", [D, D]); w12_i = A("w12", [D, D]); sw3_i = A("sw3t", [512, D])
    pvec_i = A("pvec", [128, 24]); sb3_i = A("sb3v", [128, 8])
    egw_i = A("egw", [D, 9]); gb9_i = A("gb9", [9, 1]); expb_i = A("expb", [1, 8])
    ew12_i = A("ew12", [D, 8192]); ew3_i = A("ew3t", [512, 8192]); eb3_i = A("eb3t", [8, D])
    ebv_i = A("ebv", [128, 72]); id_i = A("ident", [128, 128]); selm_i = A("selm", [9, 9 * 128])
    mmask_i = A("mmask", [128, HALF], U8)
    wk3_i = A("wk3", [D, 3072]); lnvE_i = A("lnv", [128, 16]); tmvE_i = A("tmv", [128, 48])
    shm_i = A("shmask", [128, 1]); tf8_i = A("tf8", [128, 8])
    wog_i = A("wog", [D, D]); fwk_i = A("fwk", [D, 4096]); fwv_i = A("fwv", [4096, D])
    fwr_i = A("fwr", [D, D]); ln2_i = A("ln2v", [128, 16]); ftm_i = A("ftm", [128, 32])
    wrt_i = A("wrt", [128, 64]); brw_i = A("brw", [1, 8])
    w1_i = A("w1s", [8192, 4096]); w2_i = A("w2s", [32768, 1024])
    b1_i = A("b1v", [128, 256]); b2_i = A("b2v", [128, 64]); sel8_i = A("selm8", [8, 1024])
    par_i = A("parity", [1, 1], I32)
    epv_i = A("epv", [1, 8], I32)
    fin = nc.dram_tensor("fin", [D, HALF], F32, kind="ExternalOutput").ap()

    L = lambda name, shape, dt=F32: nc.dram_tensor(name, list(shape), dt)
    qk_d = L("qk_d", [D, SEQ], BF16)
    vpl_d = L("vpl_d", [D, 2048], BF16)
    x1o_d = L("x1o_d", [D, HALF])
    xn1_d = L("xn1_d", [D, HALF], BF16)
    sho_d = L("sho_d", [D, HALF])
    f0_d = L("f0_d", [D, HALF])
    k_d = L("k_d", [D, HALF]); v_d = L("v_d", [D, HALF])
    sr_d = L("sr_d", [D, HALF]); x2_d = L("x2_d", [D, HALF])
    x3m_d = L("x3m_d", [D, HALF])
    S = lambda name, shape, dt=F32: nc.dram_tensor(name, list(shape), dt, addr_space="Shared")
    ao_sh = S("ao_sh", [1024, SEQ], BF16)
    mg_sh = S("mg_sh", [2 * D, 2048], BF16)
    ms_sh = S("ms_sh", [2 * D, 2048])
    comb_sh = S("comb_sh", [2 * D, 512])
    shw_sh = S("shw_sh", [2, 512])
    x1b_sh = S("x1b_sh", [2 * D, 1]); f0b_sh = S("f0b_sh", [2 * D, 1])
    x2b_sh = S("x2b_sh", [2 * D, 1]); srb_sh = S("srb_sh", [2 * D, 1])
    k_sh = S("k_sh", [1024, SEQ]); v_sh = S("v_sh", [1024, SEQ])
    y_sh = S("y_sh", [1024, SEQ + 1])
    dbg_d = L("dbg_d", [D, HALF])
    flags_sh = nc.dram_tensor("flags_sh", [2, 16], I32, addr_space="Shared")

    es = contextlib.ExitStack()
    with es:
        p = Prog(nc, es)
        p._par_dma_sem = p._sem("pardma", 0)
        p._par_dma_count = 0
        pool = TilePool(p)
        part = pool.get(1, I32, rows=1)
        flagv = pool.get(16, I32, rows=1)
        flagc = pool.get(16, I32, rows=1)
        trash = p._sem("trash", 0)
        p.dma("sp", part[0:1, :], par_i)
        epv = p.sbuf("epv_t", [1, 8], I32)
        p.dma("sp", epv[0:1, :], epv_i)
        ident = pool.get(128, F32, rows=128)
        p.dma("sp", ident[:, :], id_i)
        seq = [0]

        def SIG(deps):
            seq[0] += 1
            pair_signal(p, flags_sh, epv, [part], seq[0], deps)
            return seq[0]

        def WAIT(n, written):
            pair_wait(p, flags_sh, flagv, trash, [part], epv, n, written)

        wide0 = p.psum("ps_wide0", [128, 1024])
        wide1 = p.psum("ps_wide1", [128, 1024])
        mm0 = p.psum("ps_m0", [128, 512])
        mm1 = p.psum("ps_m1", [128, 512])
        bc0 = p.psum("ps_b0", [128, 512])
        bc1 = p.psum("ps_b1", [128, 512])
        ln_ps = {"sums": wide0[0:1, 0:512], "sqs": wide0[32:33, 0:512], "bc0": bc0, "bc1": bc1}
        pool3 = [mm0, mm1, wide1[:, 0:512]]

        W = 1024
        xf = pool.get(8 * W, F32, want=65536)
        xbf = pool.get(8 * W, BF16, want=65536)
        xn = pool.get(8 * W, BF16)
        wqk_t = pool.get(8 * 1024, BF16)
        wv_t = pool.get(8 * 512, BF16)
        stage = pool.get(512, BF16)
        stage2 = pool.get(512, BF16)
        sc = ln_scratch_pool(p, pool)
        for kc in range(8):
            p.dma("pool", wqk_t[:, kc * 1024:(kc + 1) * 1024], wqk_i[kc * 128:(kc + 1) * 128, :])
            p.dma("pool", wv_t[:, kc * 512:(kc + 1) * 512], wv_i[kc * 128:(kc + 1) * 128, :])
        for qtr in range(4):
            xin = (x0, x1)[qtr // 2]
            q0 = (qtr % 2) * W
            g0 = qtr * W
            for kc in range(8):
                p.dma("sp", xf[:, kc * W:(kc + 1) * W], xin[kc * 128:(kc + 1) * 128, q0:q0 + W])
                p.dma("pool", xbf[:, kc * W:(kc + 1) * W], xin[kc * 128:(kc + 1) * 128, q0:q0 + W])
            emit_ln_fm(p, xf, xbf, W, xn, ln_ps, sc, tag=f"fa_ln{qtr}")
            def qk_writer(oc, w0, bw, ps_ap, g0=g0):
                st = stage if oc % 2 == 0 else stage2
                p.dve("tensor_copy", st[:, 0:bw], ps_ap, tag="fa_qke")
                p.dma("sp", qk_d.ap()[oc * 128:(oc + 1) * 128, g0 + w0: g0 + w0 + bw],
                      st[:, 0:bw], tag="fa_qks")
            emit_linear_fm(p, qk_writer, wqk_t, xn, W, 8, 8, pool3, tag="fa_qk")
            for tb in range(W // 128):
                psv = pool3[tb % 3][:, 0:512]
                for kc in range(8):
                    p.mm(psv, xn[:, kc * W + tb * 128: kc * W + tb * 128 + 128],
                         wv_t[:, kc * 512:(kc + 1) * 512], start=(kc == 0), stop=(kc == 7),
                         tag="fa_vmm")
                st = stage if tb % 2 == 0 else stage2
                p.dve("tensor_copy", st[:, 0:512], psv, tag="fa_ve")
                gc = qtr * 8 + tb
                for h in range(8):
                    p.dma("sp", vpl_d.ap()[128 * h:128 * h + 128, 64 * gc:64 * gc + 64],
                          st[:, 64 * h:64 * h + 64], tag="fa_vs")
        pool.put(xf, xbf, xn, wqk_t, wv_t, stage, stage2)

        qt = [pool.get(SEQ, BF16, rows=64) for _ in range(2)]
        ktl = [pool.get(SEQ, BF16, rows=64) for _ in range(2)]
        vt = [pool.get(2080, BF16) for _ in range(2)]
        mk = pool.get(4 * 512, BF16)
        p2 = [pool.get(1024, BF16) for _ in range(2)]
        sqB = pool.get(SEQ, BF16, rows=64)
        ones64 = pool.get(1, BF16, rows=64)
        ones1f = pool.get(128, F32, rows=1)
        nmax = pool.get(512, F32, rows=1)
        cstat = pool.get(8, F32, rows=1)
        negc = pool.get(1, F32, rows=128)
        rrow = pool.get(512, F32, rows=1)
        obuf = [pool.get(512, F32, rows=64) for _ in range(2)]
        ostg = [pool.get(512, BF16, rows=64) for _ in range(2)]
        p.memset("dve", ones1f[:, :], 1.0)
        p.memset("dve", ones64[:, :], 1.0)
        for i in range(2):
            p.memset("dve", vt[i][:, :], 1.0)
        for r in range(4):
            p.dma("pool", mk[:, r * 512:(r + 1) * 512], masks_i[r * 128:(r + 1) * 128, :])
        ps_s = [wide0, wide1]
        ps_o = [mm0[0:65, :], mm1[0:65, :]]
        ao_stores = []
        for h in range(8):
            bi = h % 2
            p.dma("pool", qt[bi][:, :], qk_d.ap()[64 * h:64 * h + 64, :], tag=f"fb_q{h}")
            p.dma("pool", ktl[bi][:, :], qk_d.ap()[512 + 64 * h:512 + 64 * h + 64, :], tag=f"fb_k{h}")
            vdst = bass.AP(vt[bi].tensor, vt[bi].offset, [[vt[bi].tensor.shape[1], 128], [65, 32], [1, 64]])
            p.dma("pool", vdst, vpl_d.ap()[128 * h:128 * h + 128, :].rearrange("p (c j) -> p c j", j=64),
                  tag=f"fb_v{h}")
            for wi, t in ((0, qt[bi]), (1, ktl[bi])):
                p.act(sqB[:, :], t[:, :], AF.Square, tag=f"fb_sq{h}")
                for blk in range(8):
                    p.mm(bc1[0:1, :], ones64[:, 0:1], sqB[:, blk * 512:(blk + 1) * 512], tag=f"fb_n{h}")
                    if blk == 0:
                        p.dve("tensor_copy", nmax[0:1, :], bc1[0:1, :], tag=f"fb_c0{h}")
                    else:
                        p.dve("tensor_max", nmax[0:1, :], nmax[0:1, :], bc1[0:1, :], tag=f"fb_c{h}")
                p.dve("reduce_max", cstat[0:1, wi:wi + 1], nmax[0:1, :], axis=AX.X, tag=f"fb_r{h}")
            p.dve("tensor_add", cstat[0:1, 2:3], cstat[0:1, 0:1], cstat[0:1, 1:2], tag=f"fb_a{h}")
            p.dve("tensor_scalar_mul", cstat[0:1, 2:3], cstat[0:1, 2:3], -1.0 / 16.0, tag=f"fb_s{h}")
            p.mm(bc0[:, 0:1], ones1f[0:1, :], cstat[0:1, 2:3], tag=f"fb_b{h}")
            p.dve("tensor_copy", negc[:, 0:1], bc0[:, 0:1], tag=f"fb_bc{h}")
            for qb in range(8):
                po = ps_o[qb % 2]
                nkt = 4 * qb + 4
                for kp in range(nkt // 2):
                    pss = ps_s[kp % 2]
                    pp = p2[kp % 2]
                    for hf in range(2):
                        ktile = 2 * kp + hf
                        p.mm(pss[:, hf * 512:(hf + 1) * 512],
                             ktl[bi][:, 128 * ktile:128 * ktile + 128],
                             qt[bi][:, 512 * qb:512 * qb + 512], tag=f"fb_smm{h}")
                    p.act(pp[:, :], pss[:, :], AF.Exp, bias=negc[:, 0:1], scale=0.125, tag=f"fb_e{h}")
                    for hf in range(2):
                        ktile = 2 * kp + hf
                        rel = ktile - 4 * qb
                        if rel >= 0:
                            p.dve("tensor_mul", pp[:, hf * 512:(hf + 1) * 512],
                                  pp[:, hf * 512:(hf + 1) * 512], mk[:, rel * 512:(rel + 1) * 512],
                                  tag=f"fb_m{h}")
                    for hf in range(2):
                        ktile = 2 * kp + hf
                        p.mm(po, vt[bi][:, 65 * ktile:65 * ktile + 65],
                             pp[:, hf * 512:(hf + 1) * 512],
                             start=(ktile == 0), stop=(ktile == nkt - 1), tag=f"fb_pv{h}")
                ob = obuf[qb % 2]
                og = ostg[qb % 2]
                p.act(rrow[0:1, :], po[64:65, :], AF.Ln, tag=f"fb_l{h}")
                p.act(rrow[0:1, :], rrow[0:1, :], AF.Exp, scale=-1.0, tag=f"fb_x{h}")
                p.mm(bc0[0:64, :], ones1f[0:1, 0:64], rrow[0:1, :], tag=f"fb_d{h}")
                p.dve("tensor_copy", ob[:, :], po[0:64, :], tag=f"fb_oe{h}")
                p.dve("tensor_mul", og[:, :], ob[:, :], bc0[0:64, :], tag=f"fb_od{h}")
                par_dma(p, [part],
                        [(ao_sh.ap()[64 * h:64 * h + 64, 512 * qb:512 * (qb + 1)], og[:, :])],
                        [(ao_sh.ap()[512 + 64 * h:512 + 64 * h + 64, 512 * qb:512 * (qb + 1)], og[:, :])],
                        reads=[og], writes=[ao_sh], tag=f"ao{h}_{qb}")
        s_ao = SIG([ao_sh])
        pool.put(*qt, *ktl, *vt, mk, *p2, sqB, nmax, rrow, *obuf, *ostg)

        WAIT(s_ao, [ao_sh])
        Wc = 512
        wo_t = pool.get(8 * 1024, BF16)
        w12_t = pool.get(8 * 1024, BF16)
        sw3_t = pool.get(4 * 1024, BF16)
        pv = pool.get(24, F32, rows=128)
        sb3 = pool.get(8, F32, rows=128)
        xs = pool.get(8 * Wc, F32)
        aos = pool.get(8 * Wc, BF16)
        x1s = pool.get(8 * Wc, F32)
        x1b = pool.get(8 * Wc, BF16)
        xn1 = pool.get(8 * Wc, BF16)
        h1s = pool.get(4 * Wc, BF16)
        h2s = pool.get(4 * Wc, BF16)
        prodf = pool.get(4 * Wc, F32)
        prodb = pool.get(4 * Wc, BF16)
        shn = pool.get(4 * Wc, BF16)
        shstg = pool.get(Wc, F32)
        for kc in range(8):
            p.dma("pool", wo_t[:, kc * 1024:(kc + 1) * 1024], wout_i[kc * 128:(kc + 1) * 128, :])
            p.dma("pool", w12_t[:, kc * 1024:(kc + 1) * 1024], w12_i[kc * 128:(kc + 1) * 128, :])
        for kc in range(4):
            p.dma("pool", sw3_t[:, kc * 1024:(kc + 1) * 1024], sw3_i[kc * 128:(kc + 1) * 128, :])
        p.dma("sp", pv[:, :], pvec_i)
        p.dma("sp", sb3[:, :], sb3_i)
        for s0 in range(0, HALF, Wc):
            for kc in range(8):
                p.dma("sp", xs[:, kc * Wc:(kc + 1) * Wc], xh[kc * 128:(kc + 1) * 128, s0:s0 + Wc])
            par_dma(p, [part],
                    [(aos[:, kc * Wc:(kc + 1) * Wc],
                      ao_sh.ap()[kc * 128:(kc + 1) * 128, s0:s0 + Wc]) for kc in range(8)],
                    [(aos[:, kc * Wc:(kc + 1) * Wc],
                      ao_sh.ap()[kc * 128:(kc + 1) * 128, HALF + s0:HALF + s0 + Wc]) for kc in range(8)],
                    reads=[ao_sh], writes=[aos], tag=f"aold{s0}")
            def x1_writer(oc, w0, bw, ps_ap, s0=s0):
                p.dve("scalar_tensor_tensor", x1s[:, oc * Wc + w0: oc * Wc + w0 + bw],
                      ps_ap, pv[:, oc:oc + 1], xs[:, oc * Wc + w0: oc * Wc + w0 + bw],
                      op0=ALU.add, op1=ALU.add, tag="fc_x1")
                p.act_copy(x1b[:, oc * Wc + w0: oc * Wc + w0 + bw],
                           x1s[:, oc * Wc + w0: oc * Wc + w0 + bw], tag="fc_x1c")
                p.dma("sp", x1o_d.ap()[oc * 128:(oc + 1) * 128, s0 + w0: s0 + w0 + bw],
                      x1s[:, oc * Wc + w0: oc * Wc + w0 + bw], tag="fc_x1s")
            emit_linear_fm(p, x1_writer, wo_t, aos, Wc, 8, 8, pool3, tag="fc_wo")
            emit_ln_fm(p, x1s, x1b, Wc, xn1, ln_ps, sc, tag=f"fc_ln{s0}")
            for oc in range(8):
                p.dma("sp", xn1_d.ap()[oc * 128:(oc + 1) * 128, s0:s0 + Wc],
                      xn1[:, oc * Wc:(oc + 1) * Wc], tag="fc_xn1s")
            def h_writer(oc, w0, bw, ps_ap):
                if oc < 4:
                    p.act(h1s[:, oc * Wc + w0: oc * Wc + w0 + bw], ps_ap, AF.Silu,
                          bias=pv[:, 8 + oc:9 + oc], tag="fc_h1")
                else:
                    p.dve("tensor_scalar", h2s[:, (oc - 4) * Wc + w0: (oc - 4) * Wc + w0 + bw],
                          ps_ap, pv[:, 8 + oc:9 + oc], None, op0=ALU.add, tag="fc_h2")
            emit_linear_fm(p, h_writer, w12_t, xn1, Wc, 8, 8, pool3, tag="fc_h12")
            for c in range(4):
                p.dve("tensor_mul", prodf[:, c * Wc:(c + 1) * Wc], h1s[:, c * Wc:(c + 1) * Wc],
                      h2s[:, c * Wc:(c + 1) * Wc], tag="fc_pr")
                p.act_copy(prodb[:, c * Wc:(c + 1) * Wc], prodf[:, c * Wc:(c + 1) * Wc], tag="fc_prc")
            emit_ln_fm(p, prodf, prodb, Wc, shn, ln_ps, sc, gamma=pv[:, 16:20], beta=pv[:, 20:24],
                       nchunks=4, tag=f"fc_sln{s0}")
            def sh_writer(oc, w0, bw, ps_ap, s0=s0):
                p.dve("tensor_scalar", shstg[:, w0:w0 + bw], ps_ap, sb3[:, oc:oc + 1], None,
                      op0=ALU.add, tag="fc_sh")
                p.dma("sp", sho_d.ap()[oc * 128:(oc + 1) * 128, s0 + w0: s0 + w0 + bw],
                      shstg[:, w0:w0 + bw], tag="fc_shs")
            emit_linear_fm(p, sh_writer, sw3_t, shn, Wc, 4, 8, pool3, tag="fc_sw3")
        par_dma(p, [part],
                [(mg_sh.ap()[0:D, :], xn1_d.ap()), (ms_sh.ap()[0:D, :], sho_d.ap()),
                 (x1b_sh.ap()[0:D, :], x1o_d.ap()[:, HALF - 1:HALF])],
                [(mg_sh.ap()[D:2 * D, :], xn1_d.ap()), (ms_sh.ap()[D:2 * D, :], sho_d.ap()),
                 (x1b_sh.ap()[D:2 * D, :], x1o_d.ap()[:, HALF - 1:HALF])],
                reads=[xn1_d, sho_d, x1o_d], writes=[mg_sh, ms_sh, x1b_sh], tag="moepush")
        s_moe = SIG([mg_sh, ms_sh, x1b_sh])
        pool.put(wo_t, w12_t, sw3_t, xs, aos, x1s, x1b, xn1, h1s, h2s, prodf, prodb, shn, shstg)

        env0 = locals()
        env0["dbg"] = dbg
        env0["dbg_d"] = dbg_d
        _fused_DEFG(p, pool, env0)

        if dbg and dbg != "g1":
            srcs = {"dbg": (dbg_d, slice(0, D), slice(0, HALF)),
                    "qk": (qk_d, slice(0, D), slice(0, HALF)),
                    "x1o": (x1o_d, slice(0, D), slice(0, HALF)),
                    "xn1": (xn1_d, slice(0, D), slice(0, HALF)),
                    "sho": (sho_d, slice(0, D), slice(0, HALF)),
                    "f0": (f0_d, slice(0, D), slice(0, HALF)),
                    "x2": (x2_d, slice(0, D), slice(0, HALF)),
                    "k": (k_d, slice(0, D), slice(0, HALF)),
                    "v": (v_d, slice(0, D), slice(0, HALF)),
                    "sr": (sr_d, slice(0, D), slice(0, HALF)),
                    "y": (y_sh, slice(0, D), slice(1, HALF + 1))}
            key = "dbg" if (dbg.startswith("D_") or dbg.startswith("G2_")) else dbg
            t, rs, cs = srcs[key]
            p.dma("pool", fin, t.ap()[rs, cs], tag="dbgdump")

        block = es.enter_context(nc.Block())
        p.build(block)
    return nc


def _fused_DEFG(p, pool, env):
    nc = p.nc
    part = env["part"]
    SIG, WAIT = env["SIG"], env["WAIT"]
    ln_ps, pool3, sc = env["ln_ps"], env["pool3"], env["sc"]
    wide0, wide1, mm0, mm1, bc0, bc1 = (env[k] for k in ("wide0", "wide1", "mm0", "mm1", "bc0", "bc1"))
    ident, ones1f = env["ident"], env["ones1f"]
    (mg_sh, ms_sh, comb_sh, x1b_sh, f0b_sh, x2b_sh, srb_sh, k_sh, v_sh, y_sh) = (
        env[k] for k in ("mg_sh", "ms_sh", "comb_sh", "x1b_sh", "f0b_sh", "x2b_sh",
                         "srb_sh", "k_sh", "v_sh", "y_sh"))
    (x1o_d, xn1_d, sho_d, f0_d, k_d, v_d, sr_d, x2_d, x3m_d) = (
        env[k] for k in ("x1o_d", "xn1_d", "sho_d", "f0_d", "k_d", "v_d", "sr_d", "x2_d", "x3m_d"))
    (egw_i, gb9_i, expb_i, ew12_i, ew3_i, eb3_i, ebv_i, selm_i, mmask_i) = (
        env[k] for k in ("egw_i", "gb9_i", "expb_i", "ew12_i", "ew3_i", "eb3_i", "ebv_i",
                         "selm_i", "mmask_i"))
    (wk3_i, lnvE_i, tmvE_i, shm_i, tf8_i) = (env[k] for k in ("wk3_i", "lnvE_i", "tmvE_i", "shm_i", "tf8_i"))
    (wog_i, fwk_i, fwv_i, fwr_i, ln2_i, ftm_i) = (env[k] for k in ("wog_i", "fwk_i", "fwv_i", "fwr_i", "ln2_i", "ftm_i"))
    (wrt_i, brw_i, w1_i, w2_i, b1_i, b2_i, sel8_i) = (env[k] for k in ("wrt_i", "brw_i", "w1_i", "w2_i", "b1_i", "b2_i", "sel8_i"))
    fin = env["fin"]
    s_moe = env["s_moe"]

    M = 512
    WAIT(s_moe, [mg_sh, ms_sh, x1b_sh])
    xg = pool.get(8 * 1024, BF16)
    egw = pool.get(8 * 9, BF16)
    gb9 = pool.get(1, F32, rows=9)
    expb = pool.get(8, F32, rows=1)
    ew12 = [pool.get(8 * 1024, BF16) for _ in range(2)]
    ew3 = [pool.get(4 * 1024, BF16) for _ in range(2)]
    eb3 = pool.get(D, BF16, rows=8)
    ebv = pool.get(72, F32, rows=128)
    selm = pool.get(9 * 128, F32, rows=9)
    env["selm_t"] = selm
    sgs = pool.get(M, F32, rows=9)
    nt = M // 128
    stm = pool.get(nt * 16, F32)
    gsc = pool.get(nt * 8, F32)
    gml = pool.get(nt * 4, F32)
    gpk = pool.get(nt * 9, F32)
    owf = pool.get(M, F32, rows=9)
    owfb = pool.get(M, BF16, rows=9)
    ebt = pool.get(8, F32, rows=128)
    h1s = pool.get(4 * M, BF16)
    h2s = pool.get(4 * M, BF16)
    prodf = pool.get(4 * M, F32)
    prodb = pool.get(4 * M, BF16)
    zn = pool.get(4 * M, BF16)
    zs = pool.get(4 * M, BF16)
    zbc = pool.get(M, F32)
    acc = pool.get(8 * M, F32)
    par_dma(p, [part],
            [(xg[:, kc * 1024:(kc + 1) * 1024], xn1_d.ap()[kc * 128:(kc + 1) * 128, 0:1024])
             for kc in range(8)],
            [(xg[:, kc * 1024:(kc + 1) * 1024], mg_sh.ap()[kc * 128:(kc + 1) * 128, 1024:2048])
             for kc in range(8)],
            reads=[xn1_d, mg_sh], writes=[xg], tag="dgather")
    def xm_ap(kc):
        base_off = xg.offset + kc * 1024
        return bass.AP(xg.tensor, base_off, [[xg.tensor.shape[1], 128], [2, M]])
    for kc in range(8):
        p.dma("pool", egw[:, kc * 9:(kc + 1) * 9], egw_i[kc * 128:(kc + 1) * 128, :])
    p.dma("sp", gb9[:, :], gb9_i)
    p.dma("sp", expb[:, :], expb_i)
    p.dma("pool", eb3[:, :], eb3_i)
    p.dma("sp", ebv[:, :], ebv_i)
    p.dma("sp", selm[:, :], selm_i)
    ps_out = [mm0, mm1]
    psg = wide1[:, 0:512]
    p.memset("dve", acc[:, :], 0.0)
    for kc in range(8):
        p.mm(psg[0:9, 0:M], egw[:, kc * 9:(kc + 1) * 9], xm_ap(kc),
             start=(kc == 0), stop=(kc == 7), tag="fd_gmm")
    p.act(sgs[:, :], psg[0:9, 0:M], AF.Sigmoid, bias=gb9[:, 0:1], tag="fd_gsig")
    p.mm(bc1[:, 0:8], ones1f[0:1, :], expb[0:1, :], tag="fd_ebb")
    p.dve("tensor_copy", ebt[:, :], bc1[:, 0:8], tag="fd_ebc")
    for t in range(nt):
        p.transpose(bc1[0:128, 0:9], sgs[:, t * 128:(t + 1) * 128], ident[0:9, 0:9], tag="fd_gt")
        p.dve("tensor_copy", stm[:, t * 16:t * 16 + 9], bc1[0:128, 0:9], tag="fd_gtc")
        se = stm[:, t * 16:t * 16 + 8]
        ssg = stm[:, t * 16 + 8:t * 16 + 9]
        tp = gsc[:, t * 8:t * 8 + 8]
        m1 = gml[:, t * 4 + 0:t * 4 + 1]
        top1 = gml[:, t * 4 + 1:t * 4 + 2]
        ew = gml[:, t * 4 + 3:t * 4 + 4]
        pk = gpk[:, t * 9:(t + 1) * 9]
        shw = pk[:, 8:9]
        p.dve("tensor_add", tp, se, ebt[:, :], tag="fd_tp")
        p.dve("reduce_max", m1, tp, axis=AX.X, tag="fd_m1")
        p.dve("tensor_scalar", tp, tp, m1, None, op0=ALU.is_equal, tag="fd_oh")
        p.dve("tensor_mul", tp, tp, se, tag="fd_ohs")
        p.dve("reduce_max", top1, tp, axis=AX.X, tag="fd_top")
        p.dve("tensor_sub", shw, ssg, top1, tag="fd_d")
        p.act(shw, shw, AF.Sigmoid, tag="fd_shw")
        p.dve("tensor_scalar", ew, shw, -1.0, 1.0, op0=ALU.mult, op1=ALU.add, tag="fd_ew")
        p.dve("tensor_add", tp, se, ebt[:, :], tag="fd_tp2")
        p.dve("tensor_scalar", pk[:, 0:8], tp, m1, ew, op0=ALU.is_equal, op1=ALU.mult, tag="fd_ow")
        p.transpose(bc1[0:9, 0:128], pk, ident[:, :], tag="fd_gt2")
        p.dve("tensor_copy", owf[:, t * 128:(t + 1) * 128], bc1[0:9, 0:128], tag="fd_g2c")
        p.dve("tensor_copy", owfb[:, t * 128:(t + 1) * 128], bc1[0:9, 0:128], tag="fd_g2b")
    for e in range(8):
        wb = ew12[e % 2]
        w3b = ew3[e % 2]
        for kc in range(8):
            p.dma("pool", wb[:, kc * 1024:(kc + 1) * 1024],
                  ew12_i[kc * 128:(kc + 1) * 128, e * 1024:(e + 1) * 1024], tag=f"fd_w12_{e}")
        for kc in range(4):
            p.dma("pool", w3b[:, kc * 1024:(kc + 1) * 1024],
                  ew3_i[kc * 128:(kc + 1) * 128, e * 1024:(e + 1) * 1024], tag=f"fd_w3_{e}")
        for oc in range(8):
            ps_ = pool3[oc % 2][:, 0:M]
            for kc in range(8):
                l = wb[:, kc * 1024 + oc * 128:kc * 1024 + oc * 128 + 128]
                p.mm(ps_, l, xm_ap(kc), start=(kc == 0), stop=(kc == 7), tag=f"fd_h{e}")
            if oc < 4:
                p.act(h1s[:, oc * M:(oc + 1) * M], ps_, AF.Silu,
                      bias=ebv[:, e * 8 + oc:e * 8 + oc + 1], tag="fd_h1")
            else:
                p.dve("tensor_scalar", h2s[:, (oc - 4) * M:(oc - 3) * M],
                      ps_, ebv[:, e * 8 + oc:e * 8 + oc + 1], None, op0=ALU.add, tag="fd_h2")
        for c in range(4):
            p.dve("tensor_mul", prodf[:, c * M:(c + 1) * M], h1s[:, c * M:(c + 1) * M],
                  h2s[:, c * M:(c + 1) * M], tag="fd_pr")
            p.act_copy(prodb[:, c * M:(c + 1) * M], prodf[:, c * M:(c + 1) * M], tag="fd_prc")
        emit_ln_fm(p, prodf, prodb, M, zn, ln_ps, sc, gamma=ebv[:, 64:68], beta=ebv[:, 68:72],
                   nchunks=4, tag=f"fd_sln{e}")
        p.mm(bc0[:, 0:M], selm[:, e * 128:(e + 1) * 128], owf[:, :], tag="fd_owb")
        p.dve("tensor_copy", zbc[:, 0:M], bc0[:, 0:M], tag="fd_owc")
        for c in range(4):
            p.dve("tensor_mul", zs[:, c * M:(c + 1) * M], zn[:, c * M:(c + 1) * M],
                  zbc[:, 0:M], tag="fd_zs")
        for oc in range(8):
            po = ps_out[oc % 2][:, 0:M]
            for kc in range(4):
                c0 = kc * 1024 + oc * 128
                p.mm(po, w3b[:, c0:c0 + 128], zs[:, kc * M:(kc + 1) * M],
                     start=(kc == 0), stop=(kc == 3), tag=f"fd_o{e}")
            p.dve("tensor_add", acc[:, oc * M:(oc + 1) * M], acc[:, oc * M:(oc + 1) * M],
                  po, tag="fd_acc")
    for oc in range(8):
        po = ps_out[oc % 2][:, 0:M]
        p.mm(po, eb3[0:8, oc * 128:(oc + 1) * 128], owfb[0:8, :], start=True, stop=True,
             tag="fd_eb3")
        p.dve("tensor_add", acc[:, oc * M:(oc + 1) * M], acc[:, oc * M:(oc + 1) * M], po,
              tag="fd_accb")
    dbgk = env.get("dbg")
    dbg_d = env.get("dbg_d")
    if dbgk == "D_xg":
        for kc in range(8):
            p.dma("pool", dbg_d.ap()[kc * 128:(kc + 1) * 128, 0:1024],
                  xg[:, kc * 1024:(kc + 1) * 1024], tag="dxg")
    if dbgk == "D_owf":
        p.dma("sp", dbg_d.ap()[0:9, 0:M], owf[:, :], tag="dowf")
    if dbgk == "D_acc":
        for oc in range(8):
            p.dma("sp", dbg_d.ap()[oc * 128:(oc + 1) * 128, 0:M], acc[:, oc * M:(oc + 1) * M],
                  tag="dacc")
    pool.put(xg, egw, *ew12, *ew3, sgs, stm, gsc, gml, gpk, h1s, h2s,
             prodf, prodb, zn, zs, zbc)
    shw_sh = env["shw_sh"]
    comb_pushes_e = [(shw_sh.ap()[0:1, :], owf[8:9, :])]
    comb_pushes_o = [(shw_sh.ap()[1:2, :], owf[8:9, :])]
    for oc in range(8):
        comb_pushes_e.append((comb_sh.ap()[oc * 128:(oc + 1) * 128, :], acc[:, oc * M:(oc + 1) * M]))
        comb_pushes_o.append((comb_sh.ap()[D + oc * 128:D + (oc + 1) * 128, :], acc[:, oc * M:(oc + 1) * M]))
    par_dma(p, [part], comb_pushes_e, comb_pushes_o, reads=[acc, owf], writes=[comb_sh, shw_sh],
            tag="combpush")
    s_comb = SIG([comb_sh, shw_sh])
    WAIT(s_comb, [comb_sh, shw_sh])
    combp = pool.get(8 * M, F32)
    shwp = pool.get(M, F32, rows=1)
    par_dma(p, [part],
            [(combp[:, oc * M:(oc + 1) * M], comb_sh.ap()[D + oc * 128:D + (oc + 1) * 128, :])
             for oc in range(8)] + [(shwp[0:1, :], shw_sh.ap()[1:2, :])],
            [(combp[:, oc * M:(oc + 1) * M], comb_sh.ap()[oc * 128:(oc + 1) * 128, :])
             for oc in range(8)] + [(shwp[0:1, :], shw_sh.ap()[0:1, :])],
            reads=[comb_sh, shw_sh], writes=[combp, shwp], tag="combpull")
    shs = pool.get(8 * 512, F32)
    scat = pool.get(8 * 512, F32)
    mms = pool.get(512, U8)
    f0stg = pool.get(512, F32)
    f0stg2 = pool.get(512, F32)
    shwrow = pool.get(512, F32, rows=1)
    exshw = pool.get(M, F32, rows=1)
    p.memset("dve", shwrow[0:1, :], 0.0)
    p.mm(bc1[0:1, 0:M], selm[0:9, 1024:1025], owf[0:9, :], tag="fd_exsh")
    p.dve("tensor_copy", exshw[0:1, :], bc1[0:1, 0:M], tag="fd_exshc")
    for si in range(4):
        s0 = si * 512
        src = acc if si < 2 else combp
        shw_src = exshw[0:1, :] if si < 2 else shwp[0:1, :]
        half = si % 2
        p.dma("pool", mms[0:128, :], mmask_i[:, s0:s0 + 512], tag="fd_mm")
        for kc in range(8):
            p.dma("sp", shs[:, kc * 512:(kc + 1) * 512], sho_d.ap()[kc * 128:(kc + 1) * 128, s0:s0 + 512])
        p.dve("tensor_copy",
              bass.AP(shwrow.tensor, shwrow.offset, [[shwrow.tensor.shape[1], 1], [2, 256]]),
              shw_src[0:1, half * 256:half * 256 + 256], tag="fd_shwp")
        p.mm(bc0[:, 0:512], ones1f[0:1, :], shwrow[0:1, :], tag="fd_shwb")
        for kc in range(8):
            p.dve("tensor_copy",
                  bass.AP(scat.tensor, scat.offset + kc * 512, [[scat.tensor.shape[1], 128], [2, 256]]),
                  src[:, kc * M + half * 256: kc * M + half * 256 + 256], tag="fd_sc")
        for kc in range(8):
            p.dve("tensor_mul", f0stg[:, :], shs[:, kc * 512:(kc + 1) * 512], bc0[:, 0:512],
                  tag="fd_sw")
            p.dve("tensor_add", f0stg[:, :], f0stg[:, :], scat[:, kc * 512:(kc + 1) * 512],
                  tag="fd_sa")
            p.dve("select", f0stg2[:, :], mms[0:128, :], f0stg[:, :],
                  shs[:, kc * 512:(kc + 1) * 512], tag="fd_sel")
            p.dma("sp", f0_d.ap()[kc * 128:(kc + 1) * 128, s0:s0 + 512], f0stg2[:, :], tag="fd_f0s")
    par_dma(p, [part],
            [(f0b_sh.ap()[0:D, :], f0_d.ap()[:, HALF - 1:HALF])],
            [(f0b_sh.ap()[D:2 * D, :], f0_d.ap()[:, HALF - 1:HALF])],
            reads=[f0_d], writes=[f0b_sh], tag="f0push")
    s_f0 = SIG([f0b_sh])
    pool.put(owf, owfb, acc, combp, shwp, shs, scat, mms, f0stg, f0stg2, shwrow, exshw)

    WAIT(s_f0, [f0b_sh])
    WE = 513
    WO = 512
    wkt = pool.get(8 * 3072, BF16)
    lnv = pool.get(16, F32, rows=128)
    tmv = pool.get(48, F32, rows=128)
    shm = pool.get(1, F32, rows=128)
    x1s = pool.get(8 * WE, F32)
    f0s = pool.get(8 * WE, F32)
    x2s = pool.get(8 * WE, F32)
    x2b = pool.get(8 * WE, BF16)
    xn0 = pool.get(8 * WE, BF16)
    xnl = pool.get(8 * WE, BF16)
    dif = pool.get(8 * WO, BF16)
    mixk = pool.get(8 * WO, BF16)
    mixv = pool.get(8 * WO, BF16)
    mixr = pool.get(8 * WO, BF16)
    stgE = pool.get(WO, F32)
    for kc in range(8):
        p.dma("pool", wkt[:, kc * 3072:(kc + 1) * 3072], wk3_i[kc * 128:(kc + 1) * 128, :])
    p.dma("sp", lnv[:, :], lnvE_i)
    p.dma("sp", tmv[:, :], tmvE_i)
    p.dma("sp", shm[:, :], shm_i)
    for si in range(4):
        s0 = si * 512
        for kc in range(8):
            if si == 0:
                p.dma("sp", x1s[:, kc * WE:kc * WE + 1], x1b_sh.ap()[kc * 128:(kc + 1) * 128, :])
                p.dma("sp", f0s[:, kc * WE:kc * WE + 1], f0b_sh.ap()[kc * 128:(kc + 1) * 128, :])
                p.dma("sp", x1s[:, kc * WE + 1:kc * WE + WE], x1o_d.ap()[kc * 128:(kc + 1) * 128, 0:WO])
                p.dma("pool", f0s[:, kc * WE + 1:kc * WE + WE], f0_d.ap()[kc * 128:(kc + 1) * 128, 0:WO])
            else:
                p.dma("sp", x1s[:, kc * WE:kc * WE + WE], x1o_d.ap()[kc * 128:(kc + 1) * 128, s0 - 1:s0 + WO])
                p.dma("pool", f0s[:, kc * WE:kc * WE + WE], f0_d.ap()[kc * 128:(kc + 1) * 128, s0 - 1:s0 + WO])
        for kc in range(8):
            p.dve("tensor_add", x2s[:, kc * WE:(kc + 1) * WE], x1s[:, kc * WE:(kc + 1) * WE],
                  f0s[:, kc * WE:(kc + 1) * WE], tag="fe_x2")
            p.act_copy(x2b[:, kc * WE:(kc + 1) * WE], x2s[:, kc * WE:(kc + 1) * WE], tag="fe_x2c")
            p.dma("sp", x2_d.ap()[kc * 128:(kc + 1) * 128, s0:s0 + WO],
                  x2s[:, kc * WE + 1:(kc + 1) * WE], tag="fe_x2s")
        emit_ln_fm(p, x2s, x2b, WE, xn0, ln_ps, sc, tag=f"fe_ln{si}")
        emit_ln_fm(p, xn0, xn0, WE, xnl, ln_ps, sc, gamma=lnv[:, 0:8], beta=lnv[:, 8:16],
                   tag=f"fe_l1{si}")
        if si == 0:
            for kc in range(8):
                p.dve("tensor_scalar", xnl[:, kc * WE:kc * WE + 1], xnl[:, kc * WE:kc * WE + 1],
                      shm[:, 0:1], None, op0=ALU.mult, tag="fe_shm")
        for kc in range(8):
            a = xnl[:, kc * WE + 1:kc * WE + WE]
            b = xnl[:, kc * WE:kc * WE + WE - 1]
            p.dve("tensor_sub", dif[:, kc * WO:(kc + 1) * WO], a, b, tag="fe_dif")
        for mi, mix in enumerate((mixk, mixv, mixr)):
            for kc in range(8):
                p.dve("scalar_tensor_tensor", mix[:, kc * WO:(kc + 1) * WO],
                      dif[:, kc * WO:(kc + 1) * WO], tmv[:, mi * 8 + kc:mi * 8 + kc + 1],
                      xnl[:, kc * WE:kc * WE + WE - 1], op0=ALU.mult, op1=ALU.add, tag="fe_mix")
        for wi, (mix, outb) in enumerate(((mixk, k_d), (mixv, v_d), (mixr, sr_d))):
            def wr(oc, w0, bw, ps_ap, wi=wi, outb=outb, s0=s0):
                if wi == 2:
                    p.act(stgE[:, 0:bw], ps_ap, AF.Sigmoid, tag="fe_sr")
                else:
                    p.dve("tensor_copy", stgE[:, 0:bw], ps_ap, tag="fe_kv")
                p.dma("sp", outb.ap()[oc * 128:(oc + 1) * 128, s0 + w0:s0 + w0 + bw],
                      stgE[:, 0:bw], tag="fe_kvs")
            emit_linear_fm(p, wr, wkt, mix, WO, 8, 8, pool3,
                           w_col0=wi * 1024, w_kc_stride=3072, tag=f"fe_m{wi}")
    par_dma(p, [part],
            [(k_sh.ap()[:, 0:HALF], k_d.ap()), (v_sh.ap()[:, 0:HALF], v_d.ap()),
             (x2b_sh.ap()[0:D, :], x2_d.ap()[:, HALF - 1:HALF]),
             (srb_sh.ap()[0:D, :], sr_d.ap()[:, HALF - 1:HALF])],
            [(k_sh.ap()[:, HALF:SEQ], k_d.ap()), (v_sh.ap()[:, HALF:SEQ], v_d.ap()),
             (x2b_sh.ap()[D:2 * D, :], x2_d.ap()[:, HALF - 1:HALF]),
             (srb_sh.ap()[D:2 * D, :], sr_d.ap()[:, HALF - 1:HALF])],
            reads=[k_d, v_d, x2_d, sr_d], writes=[k_sh, v_sh, x2b_sh, srb_sh], tag="kvpush")
    s_kv = SIG([k_sh, v_sh, x2b_sh, srb_sh])
    pool.put(wkt, x1s, f0s, x2s, x2b, xn0, xnl, dif, mixk, mixv, mixr, stgE)

    WAIT(s_kv, [k_sh, v_sh])
    R = 2048
    kt = pool.get(SEQ, F32)
    vt = pool.get(SEQ, F32)
    tf8 = pool.get(8, F32, rows=128)
    cn = pool.get(8, F32, rows=128)
    st = pool.get(8, F32, rows=128)
    aAb = pool.get(2, F32, rows=128)
    EWt = pool.get(R, F32)
    onesR = pool.get(R, F32)
    Et = pool.get(R, F32)
    Evt = pool.get(R, F32)
    At = pool.get(R, F32)
    Bt = pool.get(R, F32)
    Eut = pool.get(R, F32)
    Euv = pool.get(R, F32)
    num = pool.get(R, F32)
    den = pool.get(R, F32)
    yt = pool.get(R, F32)
    p.dma("sp", tf8[:, :], tf8_i)
    p.memset("dve", onesR[:, :], 1.0)
    for c in range(4):
        par_dma(p, [part],
                [(kt[:, :], k_sh.ap()[c * 128:(c + 1) * 128, :]),
                 (vt[:, :], v_sh.ap()[c * 128:(c + 1) * 128, :])],
                [(kt[:, :], k_sh.ap()[512 + c * 128:512 + (c + 1) * 128, :]),
                 (vt[:, :], v_sh.ap()[512 + c * 128:512 + (c + 1) * 128, :])],
                reads=[k_sh, v_sh], writes=[kt, vt], tag=f"fkv{c}")
        td = tf8[:, c:c + 1]
        u = tf8[:, 4 + c:5 + c]
        w_ = cn[:, 0:1]; EW = cn[:, 1:2]; eu = cn[:, 2:3]; um = cn[:, 3:4]
        p.act(w_, td, AF.Exp, tag="ff_w")
        p.dve("tensor_scalar_mul", w_, w_, -1.0, tag="ff_wn")
        p.act(EW, w_, AF.Exp, tag="ff_ew")
        p.act(eu, u, AF.Exp, tag="ff_eu")
        p.dve("tensor_scalar_max", um, u, 0.0, tag="ff_um")
        p.dve("tensor_scalar", EWt[:, :], onesR[:, :], EW, None, op0=ALU.mult, tag="ff_ewt")
        aa = st[:, 0:1]; bb = st[:, 1:2]; pp = st[:, 2:3]
        p.memset("dve", aa, 0.0)
        p.memset("dve", bb, 0.0)
        p.memset("dve", pp, -1e38)
        for b in range(SEQ // R):
            kb = kt[:, b * R:(b + 1) * R]
            vb = vt[:, b * R:(b + 1) * R]
            cmax = st[:, 3:4]; Mx = st[:, 4:5]; negM = st[:, 5:6]
            t1 = st[:, 6:7]; t2 = st[:, 7:8]
            p.dve("reduce_max", cmax, kb, axis=AX.X, tag="ff_cm")
            p.dve("tensor_add", t1, pp, w_, tag="ff_t1")
            p.dve("tensor_add", t2, cmax, um, tag="ff_t2")
            p.dve("tensor_max", Mx, t1, t2, tag="ff_M")
            p.dve("tensor_scalar_mul", negM, Mx, -1.0, tag="ff_nM")
            p.act(Et[:, :], kb, AF.Exp, bias=negM, tag="ff_E")
            p.dve("tensor_mul", Evt[:, :], Et[:, :], vb, tag="ff_Ev")
            cs = t1
            p.act(cs, pp, AF.Exp, bias=negM, tag="ff_cs")
            p.dve("tensor_mul", aAb[:, 0:1], aa, cs, tag="ff_aA")
            p.dve("tensor_mul", aAb[:, 1:2], bb, cs, tag="ff_bB")
            p.dve("tensor_tensor_scan", At[:, :], EWt[:, :], Evt[:, :], aAb[:, 0:1],
                  op0=ALU.mult, op1=ALU.add, tag="ff_sA")
            p.dve("tensor_tensor_scan", Bt[:, :], EWt[:, :], Et[:, :], aAb[:, 1:2],
                  op0=ALU.mult, op1=ALU.add, tag="ff_sB")
            p.dve("tensor_scalar", Eut[:, :], Et[:, :], eu, None, op0=ALU.mult, tag="ff_Eu")
            p.dve("tensor_mul", Euv[:, :], Eut[:, :], vb, tag="ff_Euv")
            p.dve("tensor_add", num[:, 1:R], At[:, 0:R - 1], Euv[:, 1:R], tag="ff_n1")
            p.dve("tensor_add", num[:, 0:1], aAb[:, 0:1], Euv[:, 0:1], tag="ff_n0")
            p.dve("tensor_add", den[:, 1:R], Bt[:, 0:R - 1], Eut[:, 1:R], tag="ff_d1")
            p.dve("tensor_add", den[:, 0:1], aAb[:, 1:2], Eut[:, 0:1], tag="ff_d0")
            p.act(den[:, :], den[:, :], AF.Ln, tag="ff_ln")
            p.act(den[:, :], den[:, :], AF.Exp, scale=-1.0, tag="ff_exp")
            p.dve("tensor_mul", yt[:, :], num[:, :], den[:, :], tag="ff_y")
            ylist_e = [(y_sh.ap()[c * 128:(c + 1) * 128, 1 + b * R:1 + (b + 1) * R], yt[:, :])]
            ylist_o = [(y_sh.ap()[512 + c * 128:512 + (c + 1) * 128, 1 + b * R:1 + (b + 1) * R], yt[:, :])]
            if b == 0:
                ylist_e.append((y_sh.ap()[c * 128:(c + 1) * 128, 0:1], yt[:, 0:1]))
                ylist_o.append((y_sh.ap()[512 + c * 128:512 + (c + 1) * 128, 0:1], yt[:, 0:1]))
            par_dma(p, [part], ylist_e, ylist_o, reads=[yt], writes=[y_sh], tag=f"fy{c}_{b}")
            p.dve("tensor_copy", aa, At[:, R - 1:R], tag="ff_ua")
            p.dve("tensor_copy", bb, Bt[:, R - 1:R], tag="ff_ub")
            p.dve("tensor_copy", pp, Mx, tag="ff_up")
    s_y = SIG([y_sh])
    pool.put(kt, vt, EWt, onesR, Et, Evt, At, Bt, Eut, Euv, num, den, yt)

    env["shm"] = shm
    _fused_G(p, pool, env, s_y)


def _fused_G(p, pool, env, s_y):
    nc = p.nc
    part = env["part"]
    SIG, WAIT = env["SIG"], env["WAIT"]
    ln_ps, pool3, sc = env["ln_ps"], env["pool3"], env["sc"]
    wide0, wide1, mm0, mm1, bc0, bc1 = (env[k] for k in ("wide0", "wide1", "mm0", "mm1", "bc0", "bc1"))
    ident, ones1f = env["ident"], env["ones1f"]
    (x2b_sh, srb_sh, y_sh) = (env[k] for k in ("x2b_sh", "srb_sh", "y_sh"))
    (sr_d, x2_d) = (env[k] for k in ("sr_d", "x2_d"))
    (wog_i, fwk_i, fwv_i, fwr_i, ln2_i, ftm_i, shm_i) = (
        env[k] for k in ("wog_i", "fwk_i", "fwv_i", "fwr_i", "ln2_i", "ftm_i", "shm_i"))
    (wrt_i, brw_i, w1_i, w2_i, b1_i, b2_i, sel8_i) = (
        env[k] for k in ("wrt_i", "brw_i", "w1_i", "w2_i", "b1_i", "b2_i", "sel8_i"))
    fin = env["fin"]

    WAIT(s_y, [y_sh])
    WO = 256
    W = WO + 1
    wog = pool.get(8 * 1024, BF16)
    fwk = pool.get(8 * 4096, BF16)
    fwv = [pool.get(8 * 1024, BF16) for _ in range(2)]
    fwr = fwv[0]
    ln2v = pool.get(16, F32, rows=128)
    ftm = pool.get(32, F32, rows=128)
    shmG = env["shm"]
    for kc in range(8):
        p.dma("pool", wog[:, kc * 1024:(kc + 1) * 1024], wog_i[kc * 128:(kc + 1) * 128, :])
        p.dma("pool", fwk[:, kc * 4096:(kc + 1) * 4096], fwk_i[kc * 128:(kc + 1) * 128, :])
    p.dma("sp", ln2v[:, :], ln2_i)
    p.dma("sp", ftm[:, :], ftm_i)
    for si in range(HALF // WO):
        s0 = si * WO
        ybf = pool.get(8 * W, BF16)
        srb = pool.get(8 * W, BF16)
        x2s = pool.get(8 * W, F32)
        x2bf = pool.get(8 * W, BF16)
        zs = pool.get(8 * W, F32)
        xas = pool.get(8 * W, F32)
        xab = pool.get(8 * W, BF16)
        stg = pool.get(W, F32)
        par_dma(p, [part],
                [(ybf[:, kc * W:(kc + 1) * W], y_sh.ap()[kc * 128:(kc + 1) * 128, s0:s0 + W])
                 for kc in range(8)],
                [(ybf[:, kc * W:(kc + 1) * W], y_sh.ap()[kc * 128:(kc + 1) * 128, HALF + s0:HALF + s0 + W])
                 for kc in range(8)],
                reads=[y_sh], writes=[ybf], tag=f"fgy{si}")
        for kc in range(8):
            if si == 0:
                p.dma("sp", x2s[:, kc * W:kc * W + 1], x2b_sh.ap()[kc * 128:(kc + 1) * 128, :])
                p.dma("pool", x2bf[:, kc * W:kc * W + 1], x2b_sh.ap()[kc * 128:(kc + 1) * 128, :])
                p.dma("pool", srb[:, kc * W:kc * W + 1], srb_sh.ap()[kc * 128:(kc + 1) * 128, :])
                p.dma("sp", x2s[:, kc * W + 1:(kc + 1) * W], x2_d.ap()[kc * 128:(kc + 1) * 128, 0:WO])
                p.dma("pool", x2bf[:, kc * W + 1:(kc + 1) * W], x2_d.ap()[kc * 128:(kc + 1) * 128, 0:WO])
                p.dma("pool", srb[:, kc * W + 1:(kc + 1) * W], sr_d.ap()[kc * 128:(kc + 1) * 128, 0:WO])
            else:
                p.dma("sp", x2s[:, kc * W:(kc + 1) * W], x2_d.ap()[kc * 128:(kc + 1) * 128, s0 - 1:s0 + WO])
                p.dma("pool", x2bf[:, kc * W:(kc + 1) * W], x2_d.ap()[kc * 128:(kc + 1) * 128, s0 - 1:s0 + WO])
                p.dma("pool", srb[:, kc * W:(kc + 1) * W], sr_d.ap()[kc * 128:(kc + 1) * 128, s0 - 1:s0 + WO])
        emit_ln_fm(p, x2s, x2bf, W, zs, ln_ps, sc, tag=f"fg_z{si}")
        def xa_writer(oc, w0, bw, ps_ap):
            sl = slice(oc * W + w0, oc * W + w0 + bw)
            p.dve("tensor_mul", stg[:, 0:bw], srb[:, sl], ps_ap, tag="fg_xa1")
            p.dve("tensor_add", xas[:, sl], zs[:, sl], stg[:, 0:bw], tag="fg_xa2")
            p.act_copy(xab[:, sl], xas[:, sl], tag="fg_xab")
        emit_linear_fm(p, xa_writer, wog, ybf, W, 8, 8, pool3, blk_cols=512, tag="fg_wo")
        pool.put(ybf, srb, zs, x2bf)
        xn2 = pool.get(8 * W, BF16)
        emit_ln_fm(p, xas, xab, W, xn2, ln_ps, sc, gamma=ln2v[:, 0:8], beta=ln2v[:, 8:16],
                   tag=f"fg_l2{si}")
        pool.put(xab)
        dif = pool.get(8 * WO, BF16)
        xr2 = pool.get(8 * WO, BF16)
        if si == 0:
            for kc in range(8):
                p.dve("tensor_scalar", xn2[:, kc * W:kc * W + 1], xn2[:, kc * W:kc * W + 1],
                      shmG[:, 0:1], None, op0=ALU.mult, tag="fg_shm")
        for kc in range(8):
            a = xn2[:, kc * W + 1:kc * W + W]
            b = xn2[:, kc * W:kc * W + W - 1]
            p.dve("tensor_sub", dif[:, kc * WO:(kc + 1) * WO], a, b, tag="fg_dif")
        for kc in range(8):
            p.dve("scalar_tensor_tensor", xr2[:, kc * WO:(kc + 1) * WO],
                  dif[:, kc * WO:(kc + 1) * WO], ftm[:, 8 + kc:9 + kc],
                  xn2[:, kc * W:kc * W + W - 1], op0=ALU.mult, op1=ALU.add, tag="fg_mr")
        for kc in range(8):
            p.dve("scalar_tensor_tensor", dif[:, kc * WO:(kc + 1) * WO],
                  dif[:, kc * WO:(kc + 1) * WO], ftm[:, kc:kc + 1],
                  xn2[:, kc * W:kc * W + W - 1], op0=ALU.mult, op1=ALU.add, tag="fg_mk")
        pool.put(xn2)
        kk = pool.get(32 * WO, BF16)
        def kk_writer(oc, w0, bw, ps_ap):
            p.dve("tensor_relu", stg[:, 0:bw], ps_ap, tag="fg_kr")
            p.act(kk[:, oc * WO + w0: oc * WO + w0 + bw], stg[:, 0:bw], AF.Square, tag="fg_ks")
        emit_linear_fm(p, kk_writer, fwk, dif, WO, 8, 32, pool3, blk_cols=256,
                       w_kc_stride=4096, tag="fg_fk")
        pool.put(dif)
        rr = pool.get(8 * WO, BF16)
        for kc in range(8):
            p.dma("pool", fwr[:, kc * 1024:(kc + 1) * 1024], fwr_i[kc * 128:(kc + 1) * 128, :],
                  tag="fg_ldr")
        def rr_writer(oc, w0, bw, ps_ap):
            p.act(rr[:, oc * WO + w0: oc * WO + w0 + bw], ps_ap, AF.Sigmoid, tag="fg_rr")
        emit_linear_fm(p, rr_writer, fwr, xr2, WO, 8, 8, pool3, blk_cols=256, tag="fg_fr")
        pool.put(xr2)
        rr2 = pool.get(8 * WO, F32)
        for qt in range(4):
            fb = fwv[qt % 2]
            for kc in range(8):
                p.dma("pool", fb[:, kc * 1024:(kc + 1) * 1024],
                      fwv_i[(qt * 8 + kc) * 128:(qt * 8 + kc + 1) * 128, :], tag=f"fg_ldv{qt}")
            for oc in range(8):
                pv_ = pool3[oc % 3][:, 0:WO]
                for kc in range(8):
                    c0 = kc * 1024 + oc * 128
                    p.mm(pv_, fb[:, c0:c0 + 128],
                         kk[:, (qt * 8 + kc) * WO:(qt * 8 + kc + 1) * WO],
                         start=(kc == 0), stop=(kc == 7), tag="fg_fv")
                if qt == 0:
                    p.dve("tensor_copy", rr2[:, oc * WO:(oc + 1) * WO], pv_, tag="fg_vq0")
                else:
                    p.dve("tensor_add", rr2[:, oc * WO:(oc + 1) * WO],
                          rr2[:, oc * WO:(oc + 1) * WO], pv_, tag="fg_vq")
        for oc in range(8):
            sl = slice(oc * W + 1, oc * W + W)
            slo = slice(oc * WO, (oc + 1) * WO)
            p.dve("tensor_mul", stg[:, 0:WO], rr[:, slo], rr2[:, slo], tag="fg_v1")
            p.dve("tensor_add", stg[:, 0:WO], stg[:, 0:WO], xas[:, sl], tag="fg_v2")
            p.dve("tensor_add", stg[:, 0:WO], stg[:, 0:WO], x2s[:, sl], tag="fg_v3")
            p.dma("sp", fin[oc * 128:(oc + 1) * 128, s0:s0 + WO], stg[:, 0:WO], tag="fg_st")
        pool.put(x2s, xas, kk, rr, rr2, stg)
    pool.put(wog, fwk, *fwv)

    if env.get("dbg") == "g1":
        return
    WG = 512
    xn3 = pool.get(8 * HALF, BF16)
    w1q = [pool.get(8 * 1024, BF16) for _ in range(2)]
    w2t = pool.get(32 * 1024, BF16)
    ht = pool.get(32 * WG, BF16)
    x4s = ht[0:128, 0:2 * 8 * WG].bitcast(F32)
    x4b = ht[0:128, 2 * 8 * WG:3 * 8 * WG]
    wrt = pool.get(64, BF16)
    brw = pool.get(8, F32, rows=1)
    b1v = pool.get(256, F32, rows=128)
    b2v = pool.get(64, F32, rows=128)
    selm8 = env["selm_t"][0:9, 0:1152]
    brb = pool.get(8, F32, rows=128)
    ltm = pool.get(8, F32, rows=128)
    gx = pool.get(24, F32, rows=128)
    el = pool.get(8, F32, rows=128)
    l2t = pool.get(8, F32, rows=128)
    msk = pool.get(8, F32, rows=128)
    wtm = pool.get(16 * 8, F32, rows=128)
    wfm = pool.get(HALF, F32, rows=8)
    wbc = pool.get(WG, F32)
    stg2 = [pool.get(WG, F32) for _ in range(2)]
    p.dma("pool", wrt[:, :], wrt_i)
    p.dma("sp", brw[:, :], brw_i)
    p.dma("sp", b1v[:, :], b1_i)
    p.dma("sp", b2v[:, :], b2_i)

    p.mm(bc0[:, 0:8], ones1f[0:1, :], brw[0:1, :], tag="fg2_brb")
    p.dve("tensor_copy", brb[:, :], bc0[:, 0:8], tag="fg2_brbc")
    for si in range(HALF // WG):
        s0 = si * WG
        for kc in range(8):
            p.dma("sp", x4s[:, kc * WG:(kc + 1) * WG], fin[kc * 128:(kc + 1) * 128, s0:s0 + WG])
            p.dma("pool", x4b[:, kc * WG:(kc + 1) * WG], fin[kc * 128:(kc + 1) * 128, s0:s0 + WG])
        emit_ln_fm_strided(p, x4s, x4b, WG, xn3, HALF, s0, ln_ps, sc, tag=f"fg2_ln{si}")
        for tt in range(WG // 128):
            t0 = s0 + tt * 128
            psr = pool3[tt % 2]
            for kc in range(8):
                p.mm(psr[0:128, 0:8], xn3[:, kc * HALF + t0:kc * HALF + t0 + 128],
                     wrt[:, kc * 8:(kc + 1) * 8], start=(kc == 0), stop=(kc == 7), tag="fg2_rmm")
            p.dve("tensor_add", ltm[:, :], psr[0:128, 0:8], brb[:, :], tag="fg2_rl")
            m1 = gx[:, 0:1]; negm1 = gx[:, 1:2]; m2 = gx[:, 2:3]
            den = gx[:, 3:4]; rec = gx[:, 4:5]
            p.dve("reduce_max", m1, ltm[:, :], axis=AX.X, tag="fg2_m1")
            p.dve("tensor_scalar_mul", negm1, m1, -1.0, tag="fg2_nm")
            p.act(el[:, :], ltm[:, :], AF.Exp, bias=negm1, tag="fg2_el")
            p.dve("tensor_scalar", msk[:, :], ltm[:, :], m1, None, op0=ALU.is_equal, tag="fg2_oh")
            p.dve("scalar_tensor_tensor", l2t[:, :], msk[:, :], -1e9, ltm[:, :],
                  op0=ALU.mult, op1=ALU.add, tag="fg2_l2")
            p.dve("reduce_max", m2, l2t[:, :], axis=AX.X, tag="fg2_m2")
            p.dve("tensor_scalar", l2t[:, :], l2t[:, :], m2, None, op0=ALU.is_equal, tag="fg2_oh2")
            p.dve("tensor_add", msk[:, :], msk[:, :], l2t[:, :], tag="fg2_msk")
            p.dve("tensor_mul", el[:, :], el[:, :], msk[:, :], tag="fg2_num")
            p.dve("reduce_sum", den, el[:, :], axis=AX.X, tag="fg2_den")
            p.dve("reciprocal", rec, den, tag="fg2_rec")
            p.dve("tensor_scalar", wtm[:, (si * (WG // 128) + tt) * 8:(si * (WG // 128) + tt) * 8 + 8],
                  el[:, :], rec, None, op0=ALU.mult, tag="fg2_w")
    for t in range(16):
        p.transpose(bc1[0:8, 0:128], wtm[:, t * 8:(t + 1) * 8], ident[:, :], tag="fg2_wt")
        p.dve("tensor_copy", wfm[:, t * 128:(t + 1) * 128], bc1[0:8, 0:128], tag="fg2_wtc")
    dbgk = env.get("dbg")
    dbg_d = env.get("dbg_d")
    if dbgk == "G2_xn3":
        for kc in range(8):
            p.dma("pool", dbg_d.ap()[kc * 128:(kc + 1) * 128, :], xn3[:, kc * HALF:(kc + 1) * HALF],
                  tag="dxn3")
    if dbgk == "G2_wfm":
        p.dma("sp", dbg_d.ap()[0:8, 0:HALF], wfm[:, :], tag="dwfm")
    for e in range(8):
        for kc in range(32):
            p.dma("pool", w2t[:, kc * 1024:(kc + 1) * 1024],
                  w2_i[e * 4096 + kc * 128:e * 4096 + (kc + 1) * 128, :], tag=f"fg2_w2_{e}")
        for blk in range(HALF // WG):
            s0 = blk * WG
            for q in range(4):
                wq = w1q[q % 2]
                for kc in range(8):
                    p.dma("pool", wq[:, kc * 1024:(kc + 1) * 1024],
                          w1_i[e * 1024 + kc * 128:e * 1024 + (kc + 1) * 128,
                               q * 1024:(q + 1) * 1024], tag=f"fg2_w1_{e}_{q}")
                for hq in range(8):
                    hc = q * 8 + hq
                    psh = pool3[hc % 2]
                    for kc in range(8):
                        p.mm(psh[:, 0:WG], wq[:, kc * 1024 + hq * 128:kc * 1024 + hq * 128 + 128],
                             xn3[:, kc * HALF + s0:kc * HALF + s0 + WG],
                             start=(kc == 0), stop=(kc == 7), tag=f"fg2_h{e}")
                    p.act(ht[:, hc * WG:(hc + 1) * WG], psh[:, 0:WG], AF.Relu,
                          bias=b1v[:, e * 32 + hc:e * 32 + hc + 1], tag=f"fg2_hr{e}")
            p.mm(bc0[:, 0:WG], selm8[0:8, e * 128:(e + 1) * 128], wfm[:, s0:s0 + WG], tag="fg2_ws")
            p.dve("tensor_copy", wbc[:, :], bc0[:, 0:WG], tag="fg2_wsc")
            for oc in range(8):
                psy = pool3[2][:, 0:WG]
                for kc in range(32):
                    p.mm(psy, w2t[:, kc * 1024 + oc * 128:kc * 1024 + oc * 128 + 128],
                         ht[:, kc * WG:(kc + 1) * WG], start=(kc == 0), stop=(kc == 31),
                         tag=f"fg2_y{e}")
                sg = stg2[oc % 2]
                p.dve("tensor_scalar", sg[:, :], psy, b2v[:, e * 8 + oc:e * 8 + oc + 1],
                      None, op0=ALU.add, tag="fg2_yb")
                p.dve("tensor_mul", sg[:, :], sg[:, :], wbc[:, :], tag="fg2_yw")
                p.dma_accum(fin[oc * 128:(oc + 1) * 128, s0:s0 + WG], sg[:, :], tag="fg2_acc")


def _check_uniform_moe(idx_b, idx_t):
    try:
        per = {b: [] for b in range(BATCH)}
        for b, t in zip(idx_b.tolist(), idx_t.tolist()):
            per[b].append(t)
        sel_batches = [b for b in range(BATCH) if per[b]]
        for b in sel_batches:
            if per[b] != list(range(0, HALF, 2)):
                return False
        return True
    except Exception:
        return False


def kernel_fused(inputs, dbg=None):
    import ml_dtypes
    BF = ml_dtypes.bfloat16
    F = np.float32
    g = lambda k: np.asarray(inputs[k])
    x = g("L0_attn_x").astype(F)
    xT = [np.ascontiguousarray(x[b].T) for b in range(BATCH)]
    w_qkv = g("L0_attn_w_qkv")
    ident = np.eye(128, dtype=F)
    masks = _make_masks()

    idx_b = g("L0_ffn_index_b").astype(np.int64)
    idx_t = g("L0_ffn_index_t").astype(np.int64)
    sw1, sw2, sw3 = g("L0_ffn_sw1"), g("L0_ffn_sw2"), g("L0_ffn_sw3")
    pvec = np.zeros((128, 24), F)
    pvec[:, 0:8] = _pv(g("L0_attn_b_out"))
    pvec[:, 8:12] = _pv(g("L0_ffn_sb1"))
    pvec[:, 12:16] = _pv(g("L0_ffn_sb2"))
    pvec[:, 16:20] = _pv(g("L0_ffn_s_ln_g"))
    pvec[:, 20:24] = _pv(g("L0_ffn_s_ln_b"))
    w12 = np.ascontiguousarray(np.concatenate([sw1.T, sw2.T], axis=1), dtype=F)
    sw3t = np.ascontiguousarray(sw3.T, dtype=F)
    sb3v = _pv(g("L0_ffn_sb3"))
    egw = np.ascontiguousarray(
        np.concatenate([g("L0_ffn_eg_w"), g("L0_ffn_sg_w")], axis=0).T, dtype=F)
    gb9 = np.concatenate([g("L0_ffn_eg_b"), g("L0_ffn_sg_b")]).reshape(9, 1).astype(F)
    expb = g("L0_ffn_exp_bias").reshape(1, 8).astype(F)
    ew1, ew2, ew3 = g("L0_ffn_ew1"), g("L0_ffn_ew2"), g("L0_ffn_ew3")
    ew12 = np.zeros((D, 8192), F)
    ew3t = np.zeros((512, 8192), F)
    for e in range(8):
        ew12[:, e * 1024:e * 1024 + 512] = ew1[e].T
        ew12[:, e * 1024 + 512:(e + 1) * 1024] = ew2[e].T
        ew3t[:, e * 1024:(e + 1) * 1024] = ew3[e].T
    ebv = np.zeros((128, 72), F)
    eb1, eb2 = g("L0_ffn_eb1"), g("L0_ffn_eb2")
    for e in range(8):
        ebv[:, e * 8:e * 8 + 4] = _pv(eb1[e])
        ebv[:, e * 8 + 4:e * 8 + 8] = _pv(eb2[e])
    ebv[:, 64:68] = _pv(g("L0_ffn_sel_ln_g"))
    ebv[:, 68:72] = _pv(g("L0_ffn_sel_ln_b"))
    selm = np.zeros((9, 9 * 128), F)
    for e in range(9):
        selm[e, e * 128:(e + 1) * 128] = 1.0
    selm8 = np.zeros((8, 1024), F)
    for e in range(8):
        selm8[e, e * 128:(e + 1) * 128] = 1.0
    tmk = g("L1_attn_tmk").reshape(-1)
    tmv_ = g("L1_attn_tmv").reshape(-1)
    tmr = g("L1_attn_tmr").reshape(-1)
    lnvE = np.concatenate([_pv(g("L1_attn_ln1_g")), _pv(g("L1_attn_ln1_b"))], axis=1)
    tmvE = np.concatenate([_pv(tmk), _pv(tmv_), _pv(tmr),
                           _pv(1 - tmk), _pv(1 - tmv_), _pv(1 - tmr)], axis=1)
    wk3 = np.ascontiguousarray(np.concatenate(
        [g("L1_attn_Wk").T, g("L1_attn_Wv").T, g("L1_attn_Wr").T], axis=1), dtype=F)
    td = g("L1_attn_time_decay").reshape(-1)
    tf = g("L1_attn_time_first").reshape(-1)
    ftmk = g("L1_attn_f_tmk").reshape(-1)
    ftmr = g("L1_attn_f_tmr").reshape(-1)
    ln2v = np.concatenate([_pv(g("L1_attn_ln2_g")), _pv(g("L1_attn_ln2_b"))], axis=1)
    ftm = np.concatenate([_pv(ftmk), _pv(ftmr), _pv(1 - ftmk), _pv(1 - ftmr)], axis=1)
    Wrr = g("L1_ffn_W_router").astype(F)
    b1, b2 = g("L1_ffn_b1"), g("L1_ffn_b2")
    b1v = np.zeros((128, 256), F)
    b2v = np.zeros((128, 64), F)
    for e in range(8):
        b1v[:, e * 32:(e + 1) * 32] = _pv(b1[e])
        b2v[:, e * 8:(e + 1) * 8] = _pv(b2[e])

    sel_set = {(int(b), int(t)) for b, t in zip(idx_b, idx_t)}
    import time as _time
    ep = int(_time.time_ns() // 1_000_000) % 1_800_000_000
    epv_arr = np.array([[ep + 1, ep + 2, ep + 3, ep + 4, ep + 5, ep + 6, 0, ep + 1000]],
                       np.int32)
    nc = build_fused({}, dbg=dbg)
    ims = []
    for c in range(NCORES):
        b, h2 = c // 2, c % 2
        q0 = 512 * h2
        wqk = np.ascontiguousarray(np.concatenate(
            [w_qkv[:, q0:q0 + 512], w_qkv[:, 1024 + q0:1024 + q0 + 512]], axis=1))
        wv = np.ascontiguousarray(w_qkv[:, 2048 + q0:2048 + q0 + 512])
        mmask = np.zeros((128, HALF), F)
        for t in range(HALF):
            if (b, h2 * HALF + t) in sel_set:
                mmask[:, t] = 1.0
        r0 = 512 * h2
        tf8 = np.zeros((128, 8), F)
        for cc in range(4):
            tf8[:, cc] = td[r0 + cc * 128:r0 + (cc + 1) * 128]
            tf8[:, 4 + cc] = tf[r0 + cc * 128:r0 + (cc + 1) * 128]
        ims.append({
            "x0": np.ascontiguousarray(xT[b][:, :HALF]),
            "x1": np.ascontiguousarray(xT[b][:, HALF:]),
            "xh": np.ascontiguousarray(xT[b][:, h2 * HALF:(h2 + 1) * HALF]),
            "wqk": wqk, "wv": wv, "masks": masks,
            "w_out4": g("L0_attn_w_out").astype(F), "w12": w12, "sw3t": sw3t,
            "pvec": pvec, "sb3v": sb3v,
            "egw": egw, "gb9": gb9, "expb": expb, "ew12": ew12, "ew3t": ew3t,
            "eb3t": g("L0_ffn_eb3").astype(F), "ebv": ebv, "ident": ident,
            "selm": selm, "mmask": mmask.astype(np.uint8),
            "wk3": wk3, "lnv": lnvE, "tmv": tmvE,
            "shmask": np.full((128, 1), 0.0 if h2 == 0 else 1.0, F),
            "tf8": tf8,
            "wog": np.ascontiguousarray(g("L1_attn_Wo").T, dtype=F),
            "fwk": np.ascontiguousarray(g("L1_attn_fWk").T, dtype=F),
            "fwv": np.ascontiguousarray(g("L1_attn_fWv").T, dtype=F),
            "fwr": np.ascontiguousarray(g("L1_attn_fWr").T, dtype=F),
            "ln2v": ln2v, "ftm": ftm,
            "wrt": np.ascontiguousarray(
                Wrr.reshape(8, 128, 8).transpose(1, 0, 2).reshape(128, 64)),
            "brw": g("L1_ffn_b_router").reshape(1, 8).astype(F),
            "w1s": g("L1_ffn_W1").reshape(8 * D, 4096).astype(F),
            "w2s": g("L1_ffn_W2").reshape(8 * 4096, D).astype(F),
            "b1v": b1v, "b2v": b2v, "selm8": selm8,
            "parity": np.array([[h2]], np.int32),
            "epv": epv_arr,
        })
    res = run_bass_kernel_spmd(nc, ims, core_ids=list(range(NCORES))).results
    out = np.empty((BATCH, SEQ, D), np.float32)
    for c in range(NCORES):
        b, h2 = c // 2, c % 2
        out[b, h2 * HALF:(h2 + 1) * HALF, :] = np.asarray(res[c]["fin"]).T
    return out


_kernel_multilaunch = kernel


def kernel(**inputs):
    idx_b = np.asarray(inputs["L0_ffn_index_b"]).astype(np.int64)
    idx_t = np.asarray(inputs["L0_ffn_index_t"]).astype(np.int64)
    if _check_uniform_moe(idx_b, idx_t):
        return kernel_fused(inputs)
    return _kernel_multilaunch(**inputs)


# version 03
# speedup vs baseline: 1.1857x; 1.1393x over previous
import contextlib
import math
import numpy as np
import concourse.bass as bass
import concourse.mybir as mybir
from concourse.bass_utils import run_bass_kernel_spmd

F32 = mybir.dt.float32
BF16 = mybir.dt.bfloat16
I32 = mybir.dt.int32
U8 = mybir.dt.uint8
AF = mybir.ActivationFunctionType
ALU = mybir.AluOpType
AX = mybir.AxisListType


_ENG2BLOCK = {"pe": "tensor", "act": "scalar", "dve": "vector", "pool": "gpsimd", "sp": "sync"}
_SEM_CAP = 40000
_DMA_QUEUES = ("sp", "pool", "act")


class _Op:
    __slots__ = ("eng", "fn", "reads", "writes", "kind", "idx", "eidx",
                 "needs_inc", "sem", "val", "amt", "deps", "tag", "wait_list",
                 "_covers", "_dma_idx")

    def __init__(self, eng, fn, reads, writes, kind, tag=""):
        self.eng = eng
        self.fn = fn
        self.reads = reads
        self.writes = writes
        self.kind = kind
        self.tag = tag
        self.needs_inc = False
        self.sem = None
        self.val = None
        self.deps = []


def _rkey(x):
    if isinstance(x, bass.AP):
        t = x.tensor
        name = getattr(t, "name", None) or id(t)
        esz = mybir.dt.size(x.dtype) if hasattr(x, "dtype") else mybir.dt.size(t.dtype)
        dims = x.ap
        is_sbuf = not hasattr(t, "kind") or "DRam" not in type(t).__name__
        try:
            if is_sbuf and len(dims) >= 2:
                span = sum(abs(s) * (c - 1) for s, c in dims[1:]) + 1
            else:
                span = sum(abs(s) * (c - 1) for s, c in dims) + 1
        except Exception:
            span = 1 << 40
        lo = x.offset * esz
        return (name, lo, lo + span * esz)
    t = x
    name = getattr(t, "name", None) or id(t)
    return (name, 0, 1 << 60)


def _overlap(a, b):
    return a[0] == b[0] and a[1] < b[4 - 2] and b[1] < a[2]


class Prog:
    def __init__(self, nc, es):
        self.nc = nc
        self.es = es
        self.ops = []
        self._sem_pools = {}
        self._sem_counts = {}
        self.region_writer = {}
        self.region_readers = {}
        self._final_dma_tokens = []
        self._dma_count = {q: 0 for q in _DMA_QUEUES}
        self._dma_signal = {q: None for q in _DMA_QUEUES}

    def sbuf(self, name, shape, dtype):
        return self.es.enter_context(self.nc.sbuf_tensor(name, list(shape), dtype))

    def psum(self, name, shape, dtype=F32):
        return self.es.enter_context(self.nc.psum_tensor(name, list(shape), dtype))

    def _sem(self, pool, idx):
        lst = self._sem_pools.setdefault(pool, [])
        while len(lst) <= idx:
            lst.append(self.es.enter_context(self.nc.semaphore(f"{pool}_{len(lst)}")))
        return lst[idx]

    def _signal_for(self, queue, upto):
        sig = self._dma_signal.get(queue)
        if sig is not None and sig._covers >= upto:
            return sig
        total = self._dma_count[queue]
        pool = "dma_" + queue
        n = 16 * total
        waits = []
        i = 0
        while i * _SEM_CAP < n:
            waits.append((self._sem(pool, i), min(n - i * _SEM_CAP, _SEM_CAP)))
            i += 1

        def fn(e, waits=tuple(waits)):
            for sem, val in waits:
                e.wait_ge(sem, val)
            return e.nop()

        o = _Op(queue, fn, [], [], "compute", f"dmasig_{queue}_{total}")
        o.idx = len(self.ops)
        o.deps = []
        self.ops.append(o)
        o._covers = total
        self._dma_signal[queue] = o
        return o

    def op(self, eng, fn, reads=(), writes=(), kind="compute", tag=""):
        o = _Op(eng, fn, [_rkey(r) for r in reads], [_rkey(w) for w in writes], kind, tag)
        deps = []
        for (name, lo, hi) in o.reads:
            for (l2, h2, w) in self.region_writer.get(name, ()):
                if lo < h2 and l2 < hi:
                    deps.append(w)
        for (name, lo, hi) in o.writes:
            for (l2, h2, w) in self.region_writer.get(name, ()):
                if lo < h2 and l2 < hi:
                    deps.append(w)
            for (l2, h2, r) in self.region_readers.get(name, ()):
                if lo < h2 and l2 < hi:
                    deps.append(r)
        uniq = {}
        for d in deps:
            if d is o:
                continue
            if o.eng == "pe" and d.eng == "pe" and o.kind != "dma" and d.kind != "dma":
                continue
            if d.kind == "dma":
                d = self._signal_for(d.eng, d._dma_idx)
            uniq[id(d)] = d
        o.deps = list(uniq.values())
        for d in o.deps:
            d.needs_inc = True
        if kind == "dma":
            self._dma_count[eng] += 1
            o._dma_idx = self._dma_count[eng]
            o.needs_inc = True
        o.idx = len(self.ops)
        self.ops.append(o)
        for (name, lo, hi) in o.writes:
            ws = self.region_writer.setdefault(name, [])
            ws[:] = [(l2, h2, w) for (l2, h2, w) in ws if not (lo <= l2 and h2 <= hi)]
            ws.append((lo, hi, o))
            rs = self.region_readers.get(name)
            if rs:
                rs[:] = [(l2, h2, r) for (l2, h2, r) in rs if not (lo < h2 and l2 < hi)]
        for (name, lo, hi) in o.reads:
            self.region_readers.setdefault(name, []).append((lo, hi, o))
        return o

    def dma(self, queue, out, in_, tag=""):
        assert queue in _DMA_QUEUES
        rd = [in_]
        wr = [out]
        return self.op(queue if queue != "act" else "act",
                       lambda e, out=out, in_=in_: e.dma_start(out=out, in_=in_),
                       reads=rd, writes=wr, kind="dma", tag=tag)

    def dma_accum(self, out, in_, tag=""):
        return self.op("pool",
                       lambda e, out=out, in_=in_: e.dma_start(out=out, in_=in_,
                                                               accum_op=ALU.add),
                       reads=[in_, out], writes=[out], kind="dma", tag=tag)

    def mm(self, out, lhsT, rhs, start=True, stop=True, tag=""):
        return self.op("pe",
                       lambda e, out=out, l=lhsT, r=rhs, s1=start, s2=stop:
                           e.matmul(out, l, r, start=s1, stop=s2),
                       reads=[lhsT, rhs], writes=[out], tag=tag)

    def transpose(self, out, in_, ident, tag=""):
        return self.op("pe",
                       lambda e, out=out, i=in_, d=ident: e.transpose(out, i, d),
                       reads=[in_, ident], writes=[out], tag=tag)

    def act(self, out, in_, func, bias=0.0, scale=1.0, accum_out=None, tag=""):
        rd = [in_] + ([bias] if isinstance(bias, bass.AP) else []) \
            + ([scale] if isinstance(scale, bass.AP) else [])
        wr = [out] + ([accum_out] if accum_out is not None else [])
        def fn(e, out=out, in_=in_, func=func, bias=bias, scale=scale, ac=accum_out):
            kw = {}
            if ac is not None:
                kw["accum_out"] = ac
            return e.activation(out, in_, func, bias=bias, scale=scale, **kw)
        return self.op("act", fn, reads=rd, writes=wr, tag=tag)

    def act_copy(self, out, in_, mul=1.0, tag=""):
        rd = [in_] + ([mul] if isinstance(mul, bass.AP) else [])
        return self.op("act", lambda e, o=out, i=in_, m=mul: e.mul(o, i, m),
                       reads=rd, writes=[out], tag=tag)

    def dve(self, method, out, *ins, extra_reads=(), tag="", **kwargs):
        rd = [i for i in ins if isinstance(i, bass.AP)] + list(extra_reads)
        def fn(e, method=method, out=out, ins=ins, kwargs=kwargs):
            return getattr(e, method)(out, *ins, **kwargs)
        return self.op("dve", fn, reads=rd, writes=[out], tag=tag)

    def pool(self, method, out, *ins, tag="", **kwargs):
        rd = [i for i in ins if isinstance(i, bass.AP)]
        def fn(e, method=method, out=out, ins=ins, kwargs=kwargs):
            return getattr(e, method)(out, *ins, **kwargs)
        return self.op("pool", fn, reads=rd, writes=[out], tag=tag)

    def memset(self, eng, ap, val, tag=""):
        return self.op(eng, lambda e, a=ap, v=val: e.memset(a, v),
                       reads=(), writes=[ap], tag=tag)

    def raw(self, eng, fn, reads=(), writes=(), kind="raw", tag=""):
        return self.op(eng, fn, reads=reads, writes=writes, kind=kind, tag=tag)

    def mark_output(self, op):
        pass

    def build(self, block):
        eng_counts = {}
        for o in self.ops:
            if not o.needs_inc:
                continue
            pool = ("dma_" + o.eng) if o.kind == "dma" else ("es_" + o.eng)
            amt = 16 if o.kind == "dma" else 1
            n = self._sem_counts.get(pool, 0) + amt
            self._sem_counts[pool] = n
            idx = (n - 1) // _SEM_CAP
            base = idx * _SEM_CAP
            o.sem = self._sem(pool, idx)
            o.val = n - base
            o.amt = amt

        per_eng = {e: [] for e in _ENG2BLOCK}
        for o in self.ops:
            per_eng[o.eng].append(o)

        floors = {e: {} for e in _ENG2BLOCK}
        for o in self.ops:
            waits = {}
            for d in o.deps:
                assert d.needs_inc and d.sem is not None
                key = d.sem
                waits[key] = max(waits.get(key, 0), d.val)
            fl = floors[o.eng]
            o.wait_list = []
            for sem, val in waits.items():
                if fl.get(sem, 0) >= val:
                    continue
                fl[sem] = val
                o.wait_list.append((sem, val))

        prog = self

        def make_body(eng):
            ops = per_eng[eng]
            def body(e):
                for o in ops:
                    for sem, val in o.wait_list:
                        e.wait_ge(sem, val)
                    inst = o.fn(e)
                    if o.needs_inc:
                        assert inst is not None, f"op {o.tag} needs inc but fn returned None"
                        inst.then_inc(o.sem, o.amt)
            return body

        final_waits = []
        for q in _DMA_QUEUES:
            n = 16 * self._dma_count[q]
            i = 0
            while i * _SEM_CAP < n:
                final_waits.append((self._sem("dma_" + q, i), min(n - i * _SEM_CAP, _SEM_CAP)))
                i += 1

        for eng, battr in _ENG2BLOCK.items():
            ops = per_eng[eng]
            if not ops and not (eng == "pool" and final_waits):
                continue
            body = make_body(eng)
            def wrapped(e, body=body, eng=eng):
                body(e)
                if eng == "pool":
                    for sem, val in final_waits:
                        e.wait_ge(sem, val)
            getattr(block, battr)(wrapped)


BATCH = 4
SEQ = 4096
D = 1024
NCORES = 8
HALF = SEQ // 2
NHEAD = 16
HDIM = 64



def _fm(a):
    return np.ascontiguousarray(np.asarray(a).T)


if __name__ == "__main__":
    print("kernel.py framework module")



def emit_ln_fm(p, x_f32, x_bf, W, out_bf, ps, sc, gamma=None, beta=None,
               nchunks=8, tag="ln"):
    nblk = (W + 511) // 512
    dim = float(nchunks * 128)
    for blk in range(nblk):
        w0 = blk * 512
        bw = min(512, W - w0)
        sums = ps["sums"][0:1, 0:bw]
        sqs = ps["sqs"][0:1, 0:bw]
        for c in range(nchunks):
            xb = x_bf[:, c * W + w0: c * W + w0 + bw]
            p.mm(sums, sc["ones128"][:, 0:1], xb, start=(c == 0), stop=(c == nchunks - 1), tag=f"{tag}_sum")
        for c in range(nchunks):
            xb = x_bf[:, c * W + w0: c * W + w0 + bw]
            sq = sc["sq" + str(c % 2)][:, 0:bw]
            p.act(sq, xb, AF.Square, tag=f"{tag}_sq")
            p.mm(sqs, sc["ones128"][:, 0:1], sq, start=(c == 0), stop=(c == nchunks - 1), tag=f"{tag}_sqs")
        m = sc["row"][0:1, 0:bw]
        msq = sc["row"][0:1, 512:512 + bw]
        var = sc["row"][0:1, 1024:1024 + bw]
        rstd = sc["row"][0:1, 1536:1536 + bw]
        p.dve("tensor_scalar_mul", m, sums, 1.0 / dim, tag=f"{tag}_m")
        p.dve("tensor_scalar_mul", msq, sqs, 1.0 / dim, tag=f"{tag}_msq")
        p.dve("tensor_mul", var, m, m, tag=f"{tag}_m2")
        p.dve("tensor_sub", var, msq, var, tag=f"{tag}_var")
        p.act(rstd, var, AF.Ln, bias=sc["eps"][0:1, 0:1], tag=f"{tag}_ln")
        p.act(rstd, rstd, AF.Exp, scale=-0.5, tag=f"{tag}_rstd")
        p.dve("tensor_scalar_mul", m, m, -1.0, tag=f"{tag}_negm")
        bc_m = ps["bc0"][:, 0:bw]
        bc_r = ps["bc1"][:, 0:bw]
        p.mm(bc_m, sc["ones1_f"][0:1, :], m, tag=f"{tag}_bcm")
        p.mm(bc_r, sc["ones1_f"][0:1, :], rstd, tag=f"{tag}_bcr")
        for c in range(nchunks):
            xf = x_f32[:, c * W + w0: c * W + w0 + bw]
            ob = out_bf[:, c * W + w0: c * W + w0 + bw]
            t = sc["nrm"][:, 0:bw]
            p.dve("tensor_add", t, xf, bc_m, tag=f"{tag}_sub")
            if gamma is None:
                p.dve("tensor_mul", ob, t, bc_r, tag=f"{tag}_mul")
            else:
                t2 = sc["nrm2"][:, 0:bw]
                p.dve("tensor_mul", t2, t, bc_r, tag=f"{tag}_mul")
                p.dve("tensor_scalar", ob, t2, gamma[:, c:c + 1], beta[:, c:c + 1],
                      op0=ALU.mult, op1=ALU.add, tag=f"{tag}_affine")


def emit_linear_fm(p, out_writer, wt, x_bf, W, kchunks, ochunks, pspool,
                   x_col0=0, w_col0=0, w_kc_stride=None, tag="lin", blk_cols=512,
                   extra_rhs=None):
    OUT = ochunks * 128
    if w_kc_stride is None:
        w_kc_stride = OUT
    nblk = (W + blk_cols - 1) // blk_cols
    pi = 0
    for oc in range(ochunks):
        for blk in range(nblk):
            w0 = blk * blk_cols
            bw = min(blk_cols, W - w0)
            ps = pspool[pi % len(pspool)]
            pi += 1
            for kc in range(kchunks):
                c0 = w_col0 + kc * w_kc_stride + oc * 128
                l = wt[:, c0:c0 + 128]
                r = x_bf[:, x_col0 + kc * W + w0: x_col0 + kc * W + w0 + bw]
                p.mm(ps[:, 0:bw], l, r, start=(kc == 0), stop=(kc == kchunks - 1 and extra_rhs is None),
                     tag=f"{tag}_mm")
            if extra_rhs is not None:
                for kc in range(kchunks):
                    c0 = w_col0 + kc * w_kc_stride + oc * 128
                    l = wt[:, c0:c0 + 128]
                    r2 = extra_rhs(kc, w0, bw)
                    if r2 is None:
                        continue
                    p.mm(ps[:, 0:bw], l, r2, start=False, stop=(kc == kchunks - 1),
                         tag=f"{tag}_mm2")
            out_writer(oc, w0, bw, ps[:, 0:bw])


def ln_scratch_pool(p, pool, blk=512):
    sc = {
        "sq0": pool.get(blk, BF16),
        "sq1": pool.get(blk, BF16),
        "row": pool.get(2048, F32, rows=1),
        "nrm": pool.get(blk, F32),
        "nrm2": pool.get(blk, F32),
        "ones128": pool.get(1, BF16),
        "ones1_f": pool.get(128, F32, rows=1),
        "eps": pool.get(1, F32, rows=1),
    }
    p.memset("dve", sc["ones128"][:, :], 1.0)
    p.memset("dve", sc["ones1_f"][:, :], 1.0)
    p.memset("dve", sc["eps"][:, :], 1e-5)
    return sc


def ln_scratch(p, pref="", blk=512):
    sc = {
        "sq0": p.sbuf(pref + "sq0", [128, blk], BF16),
        "sq1": p.sbuf(pref + "sq1", [128, blk], BF16),
        "row": p.sbuf(pref + "row", [1, 2048], F32),
        "nrm": p.sbuf(pref + "nrm", [128, blk], F32),
        "nrm2": p.sbuf(pref + "nrm2", [128, blk], F32),
        "ones128": p.sbuf(pref + "ones128", [128, 1], BF16),
        "ones1_f": p.sbuf(pref + "ones1f", [1, 128], F32),
        "eps": p.sbuf(pref + "eps", [1, 1], F32),
    }
    p.memset("dve", sc["ones128"][:, :], 1.0)
    p.memset("dve", sc["ones1_f"][:, :], 1.0)
    p.memset("dve", sc["eps"][:, :], 1e-5)
    return sc



def build_launchA():
    nc = bass.Bass("TRN2", target_bir_lowering=False)
    xin = [nc.dram_tensor(f"x{i}", [D, HALF], F32, kind="ExternalInput").ap() for i in range(2)]
    wqk = nc.dram_tensor("wqk", [D, 1024], F32, kind="ExternalInput").ap()
    wv = nc.dram_tensor("wv", [D, 512], F32, kind="ExternalInput").ap()
    qk_out = nc.dram_tensor("qk", [D, SEQ], BF16, kind="ExternalOutput").ap()
    vtm_out = nc.dram_tensor("vtm", [SEQ, 512], BF16, kind="ExternalOutput").ap()

    es = contextlib.ExitStack()
    with es:
        p = Prog(nc, es)
        W = HALF
        xf = p.sbuf("xf", [128, 8 * W], F32)
        xbf = p.sbuf("xbf", [128, 8 * W], BF16)
        xn = p.sbuf("xn", [128, 8 * W], BF16)
        wqk_t = p.sbuf("wqk_t", [128, 8 * 1024], BF16)
        wv_t = p.sbuf("wv_t", [128, 8 * 512], BF16)
        stage = p.sbuf("stage", [128, 512], BF16)
        stage2 = p.sbuf("stage2", [128, 512], BF16)
        sc = ln_scratch(p)
        ps = {
            "sums": p.psum("ps_sums", [1, 512]),
            "sqs": p.psum("ps_sqs", [1, 512]),
            "bc0": p.psum("ps_bc0", [128, 512]),
            "bc1": p.psum("ps_bc1", [128, 512]),
        }
        pspool = [p.psum(f"ps_mm{i}", [128, 512]) for i in range(3)]

        for kc in range(8):
            p.dma("pool", wqk_t[:, kc * 1024:(kc + 1) * 1024], wqk[kc * 128:(kc + 1) * 128, :])
            p.dma("pool", wv_t[:, kc * 512:(kc + 1) * 512], wv[kc * 128:(kc + 1) * 128, :])

        for half in range(2):
            for kc in range(8):
                p.dma("sp", xf[:, kc * W:(kc + 1) * W], xin[half][kc * 128:(kc + 1) * 128, :])
                p.dma("pool", xbf[:, kc * W:(kc + 1) * W], xin[half][kc * 128:(kc + 1) * 128, :])
            emit_ln_fm(p, xf, xbf, W, xn, ps, sc, tag=f"ln0_{half}")

            def qk_writer(oc, w0, bw, ps_ap, half=half):
                st = stage if (oc + w0) % 2 == 0 else stage2
                p.dve("tensor_copy", st[:, 0:bw], ps_ap, tag="qk_evac")
                p.dma("sp", qk_out[oc * 128:(oc + 1) * 128, half * W + w0: half * W + w0 + bw],
                      st[:, 0:bw], tag="qk_st")
            emit_linear_fm(p, qk_writer, wqk_t, xn, W, 8, 8, pspool, tag="qk")

            for tb in range(W // 128):
                psv = pspool[tb % len(pspool)]
                for kc in range(8):
                    l = xn[:, kc * W + tb * 128: kc * W + tb * 128 + 128]
                    r = wv_t[:, kc * 512:(kc + 1) * 512]
                    p.mm(psv[:, 0:512], l, r, start=(kc == 0), stop=(kc == 7), tag="v_mm")
                st = stage if tb % 2 == 0 else stage2
                p.dve("tensor_copy", st[:, 0:512], psv[:, 0:512], tag="v_evac")
                p.dma("sp", vtm_out[half * W + tb * 128: half * W + (tb + 1) * 128, :],
                      st[:, 0:512], tag="v_st")

        block = es.enter_context(nc.Block())
        p.build(block)
    return nc



def build_launchB():
    nc = bass.Bass("TRN2", target_bir_lowering=False)
    qk = nc.dram_tensor("qk", [D, SEQ], BF16, kind="ExternalInput").ap()
    vpack = nc.dram_tensor("vpack", [8 * 128, 2080], BF16, kind="ExternalInput").ap()
    masks = nc.dram_tensor("masks", [4 * 128, 512], BF16, kind="ExternalInput").ap()
    ao = nc.dram_tensor("ao", [512, SEQ], BF16, kind="ExternalOutput").ap()

    es = contextlib.ExitStack()
    with es:
        p = Prog(nc, es)
        qt = [p.sbuf(f"qt{i}", [64, SEQ], BF16) for i in range(2)]
        ktl = [p.sbuf(f"kt{i}", [64, SEQ], BF16) for i in range(2)]
        vt = [p.sbuf(f"vt{i}", [128, 2080], BF16) for i in range(2)]
        mk = p.sbuf("mk", [128, 4 * 512], BF16)
        p2 = [p.sbuf(f"p2_{i}", [128, 1024], BF16) for i in range(2)]
        sq = p.sbuf("sqa", [64, SEQ], BF16)
        ones64 = p.sbuf("ones64", [64, 1], BF16)
        ones1f = p.sbuf("ones1f_b", [1, 128], F32)
        nmax = p.sbuf("nmax", [1, 512], F32)
        cstat = p.sbuf("cstat", [1, 8], F32)
        negc = p.sbuf("negc", [128, 1], F32)
        rrow = p.sbuf("rrow", [1, 512], F32)
        obuf = [p.sbuf(f"obuf{i}", [64, 512], F32) for i in range(2)]
        ostg = [p.sbuf(f"ostg{i}", [64, 512], BF16) for i in range(2)]
        ps_s = [p.psum(f"ps_s{i}", [128, 1024]) for i in range(2)]
        ps_o = [p.psum(f"ps_o{i}", [65, 512]) for i in range(2)]
        ps_bc = p.psum("ps_bcb", [128, 512])
        ps_n = p.psum("ps_n", [1, 512])

        p.memset("dve", ones1f[:, :], 1.0)
        p.memset("dve", ones64[:, :], 1.0)
        for r in range(4):
            p.dma("pool", mk[:, r * 512:(r + 1) * 512], masks[r * 128:(r + 1) * 128, :])

        for h in range(8):
            bi = h % 2
            p.dma("pool", qt[bi][:, :], qk[64 * h:64 * h + 64, :], tag=f"ldq{h}")
            p.dma("pool", ktl[bi][:, :], qk[512 + 64 * h:512 + 64 * h + 64, :], tag=f"ldk{h}")
            p.dma("pool", vt[bi][:, :], vpack[128 * h:128 * h + 128, :], tag=f"ldv{h}")

            for wi, t in ((0, qt[bi]), (1, ktl[bi])):
                p.act(sq[:, :], t[:, :], AF.Square, tag=f"csq{h}")
                for blk in range(8):
                    p.mm(ps_n[0:1, :], ones64[:, 0:1], sq[:, blk * 512:(blk + 1) * 512],
                         tag=f"cn{h}")
                    if blk == 0:
                        p.dve("tensor_copy", nmax[0:1, :], ps_n[0:1, :], tag=f"cmax0{h}")
                    else:
                        p.dve("tensor_max", nmax[0:1, :], nmax[0:1, :], ps_n[0:1, :],
                              tag=f"cmax{h}")
                p.dve("reduce_max", cstat[0:1, wi:wi + 1], nmax[0:1, :], axis=AX.X,
                      tag=f"cred{h}")
            p.dve("tensor_add", cstat[0:1, 2:3], cstat[0:1, 0:1], cstat[0:1, 1:2], tag=f"cc{h}")
            p.dve("tensor_scalar_mul", cstat[0:1, 2:3], cstat[0:1, 2:3], -1.0 / 16.0, tag=f"cs{h}")
            p.mm(ps_bc[:, 0:1], ones1f[0:1, :], cstat[0:1, 2:3], tag=f"cb{h}")
            p.dve("tensor_copy", negc[:, 0:1], ps_bc[:, 0:1], tag=f"cbc{h}")

            for qb in range(8):
                po = ps_o[qb % 2]
                nkt = 4 * qb + 4
                for kp in range(nkt // 2):
                    pss = ps_s[kp % 2]
                    pp = p2[kp % 2]
                    for half in range(2):
                        ktile = 2 * kp + half
                        p.mm(pss[:, half * 512:(half + 1) * 512],
                             ktl[bi][:, 128 * ktile:128 * ktile + 128],
                             qt[bi][:, 512 * qb:512 * qb + 512], tag=f"smm{h}")
                    p.act(pp[:, :], pss[:, :], AF.Exp, bias=negc[:, 0:1], scale=0.125,
                          tag=f"exp{h}")
                    for half in range(2):
                        ktile = 2 * kp + half
                        rel = ktile - 4 * qb
                        if rel >= 0:
                            p.dve("tensor_mul", pp[:, half * 512:(half + 1) * 512],
                                  pp[:, half * 512:(half + 1) * 512],
                                  mk[:, (rel) * 512:(rel + 1) * 512], tag=f"msk{h}")
                    for half in range(2):
                        ktile = 2 * kp + half
                        p.mm(po[:, :], vt[bi][:, 65 * ktile:65 * ktile + 65],
                             pp[:, half * 512:(half + 1) * 512],
                             start=(ktile == 0), stop=(ktile == nkt - 1), tag=f"pv{h}")
                ob = obuf[qb % 2]
                og = ostg[qb % 2]
                p.act(rrow[0:1, :], po[64:65, :], AF.Ln, tag=f"dln{h}")
                p.act(rrow[0:1, :], rrow[0:1, :], AF.Exp, scale=-1.0, tag=f"dexp{h}")
                p.mm(ps_bc[0:64, :], ones1f[0:1, 0:64], rrow[0:1, :], tag=f"dbc{h}")
                p.dve("tensor_copy", ob[:, :], po[0:64, :], tag=f"oevac{h}")
                p.dve("tensor_mul", og[:, :], ob[:, :], ps_bc[0:64, :], tag=f"odiv{h}")
                p.dma("sp", ao[64 * h:64 * h + 64, 512 * qb:512 * (qb + 1)], og[:, :],
                      tag=f"ost{h}")

        block = es.enter_context(nc.Block())
        p.build(block)
    return nc



def build_launchC():
    nc = bass.Bass("TRN2", target_bir_lowering=False)
    xf32 = nc.dram_tensor("xf32", [D, HALF], F32, kind="ExternalInput").ap()
    aof = nc.dram_tensor("aof", [D, HALF], BF16, kind="ExternalInput").ap()
    w_out4 = nc.dram_tensor("w_out4", [D, D], F32, kind="ExternalInput").ap()
    w12 = nc.dram_tensor("w12", [D, D], F32, kind="ExternalInput").ap()
    sw3t = nc.dram_tensor("sw3t", [512, D], F32, kind="ExternalInput").ap()
    pvec = nc.dram_tensor("pvec", [128, 24], F32, kind="ExternalInput").ap()
    sb3v = nc.dram_tensor("sb3v", [128, 8], F32, kind="ExternalInput").ap()
    x1o = nc.dram_tensor("x1o", [D, HALF], F32, kind="ExternalOutput").ap()
    xn1o = nc.dram_tensor("xn1o", [D, HALF], BF16, kind="ExternalOutput").ap()
    sho = nc.dram_tensor("sho", [D, HALF], F32, kind="ExternalOutput").ap()

    es = contextlib.ExitStack()
    with es:
        p = Prog(nc, es)
        W = 512
        wo_t = p.sbuf("wo_t", [128, 8 * 1024], BF16)
        w12_t = p.sbuf("w12_t", [128, 8 * 1024], BF16)
        sw3_t = p.sbuf("sw3_t", [128, 4 * 1024], BF16)
        pv = p.sbuf("pv", [128, 24], F32)
        sb3 = p.sbuf("sb3", [128, 8], F32)
        xs = p.sbuf("xs", [128, 8 * W], F32)
        aos = p.sbuf("aos", [128, 8 * W], BF16)
        x1s = p.sbuf("x1s", [128, 8 * W], F32)
        x1b = p.sbuf("x1b", [128, 8 * W], BF16)
        xn1 = p.sbuf("xn1", [128, 8 * W], BF16)
        h1s = p.sbuf("h1s", [128, 4 * W], BF16)
        h2s = p.sbuf("h2s", [128, 4 * W], BF16)
        prodf = p.sbuf("prodf", [128, 4 * W], F32)
        prodb = p.sbuf("prodb", [128, 4 * W], BF16)
        shn = p.sbuf("shn", [128, 4 * W], BF16)
        shstg = p.sbuf("shstg", [128, W], F32)
        sc = ln_scratch(p)
        ps = {"sums": p.psum("ps_sums", [1, 512]), "sqs": p.psum("ps_sqs", [1, 512]),
              "bc0": p.psum("ps_bc0", [128, 512]), "bc1": p.psum("ps_bc1", [128, 512])}
        pspool = [p.psum(f"ps_mm{i}", [128, 512]) for i in range(3)]

        for kc in range(8):
            p.dma("pool", wo_t[:, kc * 1024:(kc + 1) * 1024], w_out4[kc * 128:(kc + 1) * 128, :])
            p.dma("pool", w12_t[:, kc * 1024:(kc + 1) * 1024], w12[kc * 128:(kc + 1) * 128, :])
        for kc in range(4):
            p.dma("pool", sw3_t[:, kc * 1024:(kc + 1) * 1024], sw3t[kc * 128:(kc + 1) * 128, :])
        p.dma("sp", pv[:, :], pvec)
        p.dma("sp", sb3[:, :], sb3v)

        for s0 in range(0, HALF, W):
            for kc in range(8):
                p.dma("sp", xs[:, kc * W:(kc + 1) * W], xf32[kc * 128:(kc + 1) * 128, s0:s0 + W])
                p.dma("pool", aos[:, kc * W:(kc + 1) * W], aof[kc * 128:(kc + 1) * 128, s0:s0 + W])

            def x1_writer(oc, w0, bw, ps_ap):
                p.dve("scalar_tensor_tensor", x1s[:, oc * W + w0: oc * W + w0 + bw],
                      ps_ap, pv[:, oc:oc + 1], xs[:, oc * W + w0: oc * W + w0 + bw],
                      op0=ALU.add, op1=ALU.add, tag="x1w")
                p.act_copy(x1b[:, oc * W + w0: oc * W + w0 + bw],
                           x1s[:, oc * W + w0: oc * W + w0 + bw], tag="x1cast")
                p.dma("sp", x1o[oc * 128:(oc + 1) * 128, s0 + w0: s0 + w0 + bw],
                      x1s[:, oc * W + w0: oc * W + w0 + bw], tag="x1st")
            emit_linear_fm(p, x1_writer, wo_t, aos, W, 8, 8, pspool, tag="wo")

            emit_ln_fm(p, x1s, x1b, W, xn1, ps, sc, tag="ln1")
            for oc in range(8):
                p.dma("sp", xn1o[oc * 128:(oc + 1) * 128, s0:s0 + W], xn1[:, oc * W:(oc + 1) * W],
                      tag="xn1st")

            def h_writer(oc, w0, bw, ps_ap):
                if oc < 4:
                    p.act(h1s[:, oc * W + w0: oc * W + w0 + bw], ps_ap, AF.Silu,
                          bias=pv[:, 8 + oc:9 + oc], tag="h1w")
                else:
                    p.dve("tensor_scalar", h2s[:, (oc - 4) * W + w0: (oc - 4) * W + w0 + bw],
                          ps_ap, pv[:, 8 + oc:9 + oc], None, op0=ALU.add, tag="h2w")
            emit_linear_fm(p, h_writer, w12_t, xn1, W, 8, 8, pspool, tag="h12")

            for c in range(4):
                p.dve("tensor_mul", prodf[:, c * W:(c + 1) * W], h1s[:, c * W:(c + 1) * W],
                      h2s[:, c * W:(c + 1) * W], tag="prod")
                p.act_copy(prodb[:, c * W:(c + 1) * W], prodf[:, c * W:(c + 1) * W], tag="prodc")
            emit_ln_fm(p, prodf, prodb, W, shn, ps, sc,
                       gamma=pv[:, 16:20], beta=pv[:, 20:24], nchunks=4, tag="sln")

            def sh_writer(oc, w0, bw, ps_ap):
                p.dve("tensor_scalar", shstg[:, w0:w0 + bw], ps_ap, sb3[:, oc:oc + 1], None,
                      op0=ALU.add, tag="shw")
                p.dma("sp", sho[oc * 128:(oc + 1) * 128, s0 + w0: s0 + w0 + bw],
                      shstg[:, w0:w0 + bw], tag="shst")
            emit_linear_fm(p, sh_writer, sw3_t, shn, W, 4, 8, pspool, tag="sw3")

        block = es.enter_context(nc.Block())
        p.build(block)
    return nc



def build_launchD(M=256):
    nc = bass.Bass("TRN2", target_bir_lowering=False)
    xm_in = nc.dram_tensor("xm", [D, M], BF16, kind="ExternalInput").ap()
    shg_in = nc.dram_tensor("shg", [D, M], F32, kind="ExternalInput").ap()
    egw_in = nc.dram_tensor("egw", [D, 9], F32, kind="ExternalInput").ap()
    gb9_in = nc.dram_tensor("gb9", [9, 1], F32, kind="ExternalInput").ap()
    expb_in = nc.dram_tensor("expb", [1, 8], F32, kind="ExternalInput").ap()
    ew12_in = nc.dram_tensor("ew12", [D, 8192], F32, kind="ExternalInput").ap()
    ew3_in = nc.dram_tensor("ew3t", [512, 8192], F32, kind="ExternalInput").ap()
    eb3_in = nc.dram_tensor("eb3t", [8, D], F32, kind="ExternalInput").ap()
    ebv_in = nc.dram_tensor("ebv", [128, 72], F32, kind="ExternalInput").ap()
    id_in = nc.dram_tensor("ident", [128, 128], F32, kind="ExternalInput").ap()
    selm_in = nc.dram_tensor("selm", [9, 9 * 128], F32, kind="ExternalInput").ap()
    comb_o = nc.dram_tensor("comb", [D, M], F32, kind="ExternalOutput").ap()

    nt = M // 128
    es = contextlib.ExitStack()
    with es:
        p = Prog(nc, es)
        xm = p.sbuf("xm_t", [128, 8 * M], BF16)
        shg = p.sbuf("shg_t", [128, 8 * M], F32)
        egw = p.sbuf("egw_t", [128, 8 * 9], BF16)
        gb9 = p.sbuf("gb9_t", [9, 1], F32)
        expb = p.sbuf("expb_t", [1, 8], F32)
        ew12 = [p.sbuf(f"ew12_t{i}", [128, 8 * 1024], BF16) for i in range(2)]
        ew3 = [p.sbuf(f"ew3_t{i}", [128, 4 * 1024], BF16) for i in range(2)]
        eb3 = p.sbuf("eb3_t", [8, D], BF16)
        ebv = p.sbuf("ebv_t", [128, 72], F32)
        ident = p.sbuf("ident_t", [128, 128], F32)
        ones1f = p.sbuf("ones1f_d", [1, 128], F32)
        sgs = p.sbuf("sgs", [9, M], F32)
        stm = p.sbuf("stm", [128, nt * 16], F32)
        gsc = p.sbuf("gsc", [128, nt * 8], F32)
        gml = p.sbuf("gml", [128, nt * 4], F32)
        gpk = p.sbuf("gpk", [128, nt * 9], F32)
        owf = p.sbuf("owf", [9, M], F32)
        owfb = p.sbuf("owfb", [9, M], BF16)
        ebt = p.sbuf("ebt", [128, 8], F32)
        h1s = p.sbuf("h1s_d", [128, 4 * M], BF16)
        h2s = p.sbuf("h2s_d", [128, 4 * M], BF16)
        prodf = p.sbuf("prodf_d", [128, 4 * M], F32)
        prodb = p.sbuf("prodb_d", [128, 4 * M], BF16)
        zn = p.sbuf("zn_d", [128, 4 * M], BF16)
        zs = p.sbuf("zs_d", [128, 4 * M], BF16)
        zbc = p.sbuf("zbc_d", [128, M], F32)
        stg = p.sbuf("stg_d", [128, M], F32)
        sc = ln_scratch(p, "d_")
        stat = p.psum("ps_stat_d", [128, 512])
        ps = {"sums": stat[0:1, :], "sqs": stat[32:33, :],
              "bc0": p.psum("ps_bc0_d", [128, 512]), "bc1": p.psum("ps_bc1_d", [128, 512])}
        pspool = [p.psum("ps_mmd0", [128, 512])]
        ps_out = [p.psum(f"ps_outd{i}", [128, 512]) for i in range(2)]
        acc = p.sbuf("acc_d", [128, 8 * M], F32)
        ps_g = ps["bc1"]

        for kc in range(8):
            p.dma("pool", xm[:, kc * M:(kc + 1) * M], xm_in[kc * 128:(kc + 1) * 128, :])
            p.dma("sp", shg[:, kc * M:(kc + 1) * M], shg_in[kc * 128:(kc + 1) * 128, :])
            p.dma("pool", egw[:, kc * 9:(kc + 1) * 9], egw_in[kc * 128:(kc + 1) * 128, :])
        p.dma("sp", gb9[:, :], gb9_in)
        p.dma("sp", expb[:, :], expb_in)
        p.dma("pool", eb3[:, :], eb3_in)
        p.dma("sp", ebv[:, :], ebv_in)
        p.dma("sp", ident[:, :], id_in)
        selm = p.sbuf("selm_t", [9, 9 * 128], F32)
        p.dma("sp", selm[:, :], selm_in)
        p.memset("dve", ones1f[:, :], 1.0)
        p.memset("dve", acc[:, :], 0.0)

        psg = pspool[0]
        for kc in range(8):
            p.mm(psg[0:9, 0:M], egw[:, kc * 9:(kc + 1) * 9], xm[:, kc * M:(kc + 1) * M],
                 start=(kc == 0), stop=(kc == 7), tag="gmm")
        p.act(sgs[:, :], psg[0:9, 0:M], AF.Sigmoid, bias=gb9[:, 0:1], tag="gsig")
        p.mm(ps_g[:, 0:8], ones1f[0:1, :], expb[0:1, :], tag="ebb")
        p.dve("tensor_copy", ebt[:, :], ps_g[:, 0:8], tag="ebc")
        for t in range(nt):
            p.transpose(ps_g[0:128, 0:9], sgs[:, t * 128:(t + 1) * 128], ident[0:9, 0:9], tag="gt")
            p.dve("tensor_copy", stm[:, t * 16:t * 16 + 9], ps_g[0:128, 0:9], tag="gtc")
            se = stm[:, t * 16:t * 16 + 8]
            ssg = stm[:, t * 16 + 8:t * 16 + 9]
            tp = gsc[:, t * 8:t * 8 + 8]
            m1 = gml[:, t * 4 + 0:t * 4 + 1]
            top1 = gml[:, t * 4 + 1:t * 4 + 2]
            ew = gml[:, t * 4 + 3:t * 4 + 4]
            pk = gpk[:, t * 9:(t + 1) * 9]
            shw = pk[:, 8:9]
            p.dve("tensor_add", tp, se, ebt[:, :], tag="gtp")
            p.dve("reduce_max", m1, tp, axis=AX.X, tag="gm1")
            p.dve("tensor_scalar", tp, tp, m1, None, op0=ALU.is_equal, tag="goh")
            p.dve("tensor_mul", tp, tp, se, tag="gohs")
            p.dve("reduce_max", top1, tp, axis=AX.X, tag="gtop")
            p.dve("tensor_sub", shw, ssg, top1, tag="gd")
            p.act(shw, shw, AF.Sigmoid, tag="gshw")
            p.dve("tensor_scalar", ew, shw, -1.0, 1.0, op0=ALU.mult, op1=ALU.add, tag="gew")
            p.dve("tensor_add", tp, se, ebt[:, :], tag="gtp2")
            p.dve("tensor_scalar", pk[:, 0:8], tp, m1, ew, op0=ALU.is_equal, op1=ALU.mult,
                  tag="gow")
            p.transpose(ps_g[0:9, 0:128], pk, ident[:, :], tag="gt2")
            p.dve("tensor_copy", owf[:, t * 128:(t + 1) * 128], ps_g[0:9, 0:128], tag="gt2c")
            p.dve("tensor_copy", owfb[:, t * 128:(t + 1) * 128], ps_g[0:9, 0:128], tag="gt2b")

        for e in range(8):
            wb = ew12[e % 2]
            w3b = ew3[e % 2]
            for kc in range(8):
                p.dma("pool", wb[:, kc * 1024:(kc + 1) * 1024],
                      ew12_in[kc * 128:(kc + 1) * 128, e * 1024:(e + 1) * 1024], tag=f"ldw12_{e}")
            for kc in range(4):
                p.dma("pool", w3b[:, kc * 1024:(kc + 1) * 1024],
                      ew3_in[kc * 128:(kc + 1) * 128, e * 1024:(e + 1) * 1024], tag=f"ldw3_{e}")
            def h_writer(oc, w0, bw, ps_ap, e=e):
                if oc < 4:
                    p.act(h1s[:, oc * M + w0: oc * M + w0 + bw], ps_ap, AF.Silu,
                          bias=ebv[:, e * 8 + oc:e * 8 + oc + 1], tag="dh1")
                else:
                    p.dve("tensor_scalar", h2s[:, (oc - 4) * M + w0: (oc - 4) * M + w0 + bw],
                          ps_ap, ebv[:, e * 8 + oc:e * 8 + oc + 1], None, op0=ALU.add, tag="dh2")
            emit_linear_fm(p, h_writer, wb, xm, M, 8, 8, pspool, tag=f"dh{e}",
                           blk_cols=min(512, M))
            for c in range(4):
                p.dve("tensor_mul", prodf[:, c * M:(c + 1) * M], h1s[:, c * M:(c + 1) * M],
                      h2s[:, c * M:(c + 1) * M], tag="dprod")
                p.act_copy(prodb[:, c * M:(c + 1) * M], prodf[:, c * M:(c + 1) * M], tag="dprodc")
            emit_ln_fm(p, prodf, prodb, M, zn, ps, sc, gamma=ebv[:, 64:68],
                       beta=ebv[:, 68:72], nchunks=4, tag=f"dsln{e}")
            p.mm(ps["bc0"][:, 0:M], selm[:, e * 128:(e + 1) * 128], owf[:, :], tag="dowb")
            p.dve("tensor_copy", zbc[:, 0:M], ps["bc0"][:, 0:M], tag="dowc")
            for c in range(4):
                p.dve("tensor_mul", zs[:, c * M:(c + 1) * M], zn[:, c * M:(c + 1) * M],
                      zbc[:, 0:M], tag="dzs")
            for oc in range(8):
                po = ps_out[oc % 2][:, 0:M]
                for kc in range(4):
                    c0 = kc * 1024 + oc * 128
                    p.mm(po, w3b[:, c0:c0 + 128], zs[:, kc * M:(kc + 1) * M],
                         start=(kc == 0), stop=(kc == 3), tag=f"dout{e}")
                p.dve("tensor_add", acc[:, oc * M:(oc + 1) * M], acc[:, oc * M:(oc + 1) * M],
                      po, tag="dacc")
        for oc in range(8):
            po = ps_out[oc % 2][:, 0:M]
            p.mm(po, eb3[0:8, oc * 128:(oc + 1) * 128], owfb[0:8, :],
                 start=True, stop=True, tag="deb3")
            p.dve("tensor_add", acc[:, oc * M:(oc + 1) * M], acc[:, oc * M:(oc + 1) * M],
                  po, tag="daccb")
        p.mm(ps["bc0"][:, 0:M], selm[:, 8 * 128:9 * 128], owf[:, :], tag="dshb")
        p.dve("tensor_copy", zbc[:, 0:M], ps["bc0"][:, 0:M], tag="dshc")
        for oc in range(8):
            p.dve("tensor_mul", stg[:, 0:M], shg[:, oc * M:(oc + 1) * M], zbc[:, 0:M],
                  tag="dcmb1")
            p.dve("tensor_add", stg[:, 0:M], stg[:, 0:M], acc[:, oc * M:(oc + 1) * M],
                  tag="dcmb2")
            p.dma("sp", comb_o[oc * 128:(oc + 1) * 128, :], stg[:, 0:M], tag="dcst")
        block = es.enter_context(nc.Block())
        p.build(block)
    return nc



def build_launchE():
    nc = bass.Bass("TRN2", target_bir_lowering=False)
    x1g = nc.dram_tensor("x1g", [D, HALF + 1], F32, kind="ExternalInput").ap()
    f0g = nc.dram_tensor("f0g", [D, HALF + 1], F32, kind="ExternalInput").ap()
    wk3 = nc.dram_tensor("wk3", [D, 3072], F32, kind="ExternalInput").ap()
    lnv_in = nc.dram_tensor("lnv", [128, 16], F32, kind="ExternalInput").ap()
    tmv_in = nc.dram_tensor("tmv", [128, 48], F32, kind="ExternalInput").ap()
    shm_in = nc.dram_tensor("shmask", [128, 1], F32, kind="ExternalInput").ap()
    ko = nc.dram_tensor("ko", [D, HALF], F32, kind="ExternalOutput").ap()
    vo = nc.dram_tensor("vo", [D, HALF], F32, kind="ExternalOutput").ap()
    sro = nc.dram_tensor("sro", [D, HALF], F32, kind="ExternalOutput").ap()
    x2o = nc.dram_tensor("x2o", [D, HALF], F32, kind="ExternalOutput").ap()

    es = contextlib.ExitStack()
    with es:
        p = Prog(nc, es)
        W = 513
        WO = 512
        wkt = p.sbuf("wkt", [128, 8 * 3072], BF16)
        lnv = p.sbuf("lnv_t", [128, 16], F32)
        tmv = p.sbuf("tmv_t", [128, 48], F32)
        shm = p.sbuf("shm_t", [128, 1], F32)
        x1s = p.sbuf("x1s_e", [128, 8 * W], F32)
        f0s = p.sbuf("f0s_e", [128, 8 * W], F32)
        x2s = p.sbuf("x2s_e", [128, 8 * W], F32)
        x2b = p.sbuf("x2b_e", [128, 8 * W], BF16)
        xn0 = p.sbuf("xn0_e", [128, 8 * W], BF16)
        xnl = p.sbuf("xnl_e", [128, 8 * W], BF16)
        dif = p.sbuf("dif_e", [128, 8 * WO], BF16)
        mixk = p.sbuf("mixk_e", [128, 8 * WO], BF16)
        mixv = p.sbuf("mixv_e", [128, 8 * WO], BF16)
        mixr = p.sbuf("mixr_e", [128, 8 * WO], BF16)
        stg = p.sbuf("stg_e", [128, WO], F32)
        sc = ln_scratch(p, "e_")
        stat = p.psum("ps_stat_e", [128, 512])
        ps = {"sums": stat[0:1, :], "sqs": stat[32:33, :],
              "bc0": p.psum("ps_bc0_e", [128, 512]), "bc1": p.psum("ps_bc1_e", [128, 512])}
        pspool = [p.psum(f"ps_mme{i}", [128, 512]) for i in range(3)]

        for kc in range(8):
            p.dma("pool", wkt[:, kc * 3072:(kc + 1) * 3072], wk3[kc * 128:(kc + 1) * 128, :])
        p.dma("sp", lnv[:, :], lnv_in)
        p.dma("sp", tmv[:, :], tmv_in)
        p.dma("sp", shm[:, :], shm_in)

        for si in range(4):
            s0 = si * 512
            for kc in range(8):
                p.dma("sp", x1s[:, kc * W:kc * W + W], x1g[kc * 128:(kc + 1) * 128, s0:s0 + W])
                p.dma("pool", f0s[:, kc * W:kc * W + W], f0g[kc * 128:(kc + 1) * 128, s0:s0 + W])
            for kc in range(8):
                p.dve("tensor_add", x2s[:, kc * W:(kc + 1) * W], x1s[:, kc * W:(kc + 1) * W],
                      f0s[:, kc * W:(kc + 1) * W], tag="x2")
                p.act_copy(x2b[:, kc * W:(kc + 1) * W], x2s[:, kc * W:(kc + 1) * W], tag="x2c")
                p.dma("sp", x2o[kc * 128:(kc + 1) * 128, s0:s0 + WO],
                      x2s[:, kc * W + 1:(kc + 1) * W], tag="x2st")
            emit_ln_fm(p, x2s, x2b, W, xn0, ps, sc, tag=f"eln{si}")
            xn0f = xn0
            emit_ln_fm(p, xn0, xn0, W, xnl, ps, sc, gamma=lnv[:, 0:8], beta=lnv[:, 8:16],
                       tag=f"eln1_{si}")
            if si == 0:
                for kc in range(8):
                    p.dve("tensor_scalar", xnl[:, kc * W:kc * W + 1], xnl[:, kc * W:kc * W + 1],
                          shm[:, 0:1], None, op0=ALU.mult, tag="shmask")
            for kc in range(8):
                a = xnl[:, kc * W + 1:kc * W + W]
                b = xnl[:, kc * W:kc * W + W - 1]
                p.dve("tensor_sub", dif[:, kc * WO:(kc + 1) * WO], a, b, tag="dif")
            for mi, mix in enumerate((mixk, mixv, mixr)):
                for kc in range(8):
                    p.dve("scalar_tensor_tensor", mix[:, kc * WO:(kc + 1) * WO],
                          dif[:, kc * WO:(kc + 1) * WO], tmv[:, mi * 8 + kc:mi * 8 + kc + 1],
                          xnl[:, kc * W:kc * W + W - 1], op0=ALU.mult, op1=ALU.add, tag="mix")
            for wi, (mix, outb) in enumerate(((mixk, ko), (mixv, vo), (mixr, sro))):
                def wr(oc, w0, bw, ps_ap, wi=wi, outb=outb):
                    if wi == 2:
                        p.act(stg[:, 0:bw], ps_ap, AF.Sigmoid, tag="sr")
                    else:
                        p.dve("tensor_copy", stg[:, 0:bw], ps_ap, tag="kv")
                    p.dma("sp", outb[oc * 128:(oc + 1) * 128, s0:s0 + bw], stg[:, 0:bw],
                          tag="kvst")
                emit_linear_fm(p, wr, wkt, mix, WO, 8, 8, pspool,
                               w_col0=wi * 1024, w_kc_stride=3072, tag=f"kv{wi}")

        block = es.enter_context(nc.Block())
        p.build(block)
    return nc



def build_launchF(R=2048):
    nc = bass.Bass("TRN2", target_bir_lowering=False)
    ks_in = nc.dram_tensor("ks", [512, SEQ], F32, kind="ExternalInput").ap()
    vs_in = nc.dram_tensor("vs", [512, SEQ], F32, kind="ExternalInput").ap()
    tf_in = nc.dram_tensor("tf8", [128, 8], F32, kind="ExternalInput").ap()
    yo = nc.dram_tensor("yo", [512, SEQ], F32, kind="ExternalOutput").ap()

    nblk = SEQ // R
    es = contextlib.ExitStack()
    with es:
        p = Prog(nc, es)
        kt = p.sbuf("kt_f", [128, SEQ], F32)
        vt = p.sbuf("vt_f", [128, SEQ], F32)
        tf8 = p.sbuf("tf8_t", [128, 8], F32)
        cn = p.sbuf("cn_f", [128, 8], F32)
        st = p.sbuf("st_f", [128, 8], F32)
        aAb = p.sbuf("aAb_f", [128, 2], F32)
        EWt = p.sbuf("EWt_f", [128, R], F32)
        onesR = p.sbuf("onesR_f", [128, R], F32)
        Et = p.sbuf("Et_f", [128, R], F32)
        Evt = p.sbuf("Evt_f", [128, R], F32)
        At = p.sbuf("At_f", [128, R], F32)
        Bt = p.sbuf("Bt_f", [128, R], F32)
        Eut = p.sbuf("Eut_f", [128, R], F32)
        Euv = p.sbuf("Euv_f", [128, R], F32)
        num = p.sbuf("num_f", [128, R], F32)
        den = p.sbuf("den_f", [128, R], F32)
        yt = p.sbuf("yt_f", [128, R], F32)

        p.dma("sp", tf8[:, :], tf_in)
        p.memset("dve", onesR[:, :], 1.0)

        for c in range(4):
            p.dma("sp", kt[:, :], ks_in[c * 128:(c + 1) * 128, :], tag=f"ldk{c}")
            p.dma("pool", vt[:, :], vs_in[c * 128:(c + 1) * 128, :], tag=f"ldv{c}")
            td = tf8[:, c:c + 1]
            u = tf8[:, 4 + c:5 + c]
            w_ = cn[:, 0:1]
            EW = cn[:, 1:2]
            eu = cn[:, 2:3]
            um = cn[:, 3:4]
            p.act(w_, td, AF.Exp, tag="fw")
            p.dve("tensor_scalar_mul", w_, w_, -1.0, tag="fwn")
            p.act(EW, w_, AF.Exp, tag="few")
            p.act(eu, u, AF.Exp, tag="feu")
            p.dve("tensor_scalar_max", um, u, 0.0, tag="fum")
            p.dve("tensor_scalar", EWt[:, :], onesR[:, :], EW, None, op0=ALU.mult, tag="fewt")
            aa = st[:, 0:1]
            bb = st[:, 1:2]
            pp = st[:, 2:3]
            p.memset("dve", aa, 0.0)
            p.memset("dve", bb, 0.0)
            p.memset("dve", pp, -1e38)
            for b in range(nblk):
                kb = kt[:, b * R:(b + 1) * R]
                vb = vt[:, b * R:(b + 1) * R]
                cmax = st[:, 3:4]
                M = st[:, 4:5]
                negM = st[:, 5:6]
                t1 = st[:, 6:7]
                t2 = st[:, 7:8]
                p.dve("reduce_max", cmax, kb, axis=AX.X, tag="fcm")
                p.dve("tensor_add", t1, pp, w_, tag="ft1")
                p.dve("tensor_add", t2, cmax, um, tag="ft2")
                p.dve("tensor_max", M, t1, t2, tag="fM")
                p.dve("tensor_scalar_mul", negM, M, -1.0, tag="fnM")
                p.act(Et[:, :], kb, AF.Exp, bias=negM, tag="fE")
                p.dve("tensor_mul", Evt[:, :], Et[:, :], vb, tag="fEv")
                cs = t1
                p.act(cs, pp, AF.Exp, bias=negM, tag="fcs")
                p.dve("tensor_mul", aAb[:, 0:1], aa, cs, tag="faA")
                p.dve("tensor_mul", aAb[:, 1:2], bb, cs, tag="fbB")
                p.dve("tensor_tensor_scan", At[:, :], EWt[:, :], Evt[:, :], aAb[:, 0:1],
                      op0=ALU.mult, op1=ALU.add, tag="fsA")
                p.dve("tensor_tensor_scan", Bt[:, :], EWt[:, :], Et[:, :], aAb[:, 1:2],
                      op0=ALU.mult, op1=ALU.add, tag="fsB")
                p.dve("tensor_scalar", Eut[:, :], Et[:, :], eu, None, op0=ALU.mult, tag="fEu")
                p.dve("tensor_mul", Euv[:, :], Eut[:, :], vb, tag="fEuv")
                p.dve("tensor_add", num[:, 1:R], At[:, 0:R - 1], Euv[:, 1:R], tag="fn1")
                p.dve("tensor_add", num[:, 0:1], aAb[:, 0:1], Euv[:, 0:1], tag="fn0")
                p.dve("tensor_add", den[:, 1:R], Bt[:, 0:R - 1], Eut[:, 1:R], tag="fd1")
                p.dve("tensor_add", den[:, 0:1], aAb[:, 1:2], Eut[:, 0:1], tag="fd0")
                p.act(den[:, :], den[:, :], AF.Ln, tag="fln")
                p.act(den[:, :], den[:, :], AF.Exp, scale=-1.0, tag="fexp")
                p.dve("tensor_mul", yt[:, :], num[:, :], den[:, :], tag="fy")
                p.dma("sp", yo[c * 128:(c + 1) * 128, b * R:(b + 1) * R], yt[:, :], tag="fyst")
                p.dve("tensor_copy", aa, At[:, R - 1:R], tag="fua")
                p.dve("tensor_copy", bb, Bt[:, R - 1:R], tag="fub")
                p.dve("tensor_copy", pp, M, tag="fup")

        block = es.enter_context(nc.Block())
        p.build(block)
    return nc



def build_launchG1():
    nc = bass.Bass("TRN2", target_bir_lowering=False)
    x2g = nc.dram_tensor("x2g", [D, HALF + 1], F32, kind="ExternalInput").ap()
    yg = nc.dram_tensor("yg", [D, HALF + 1], F32, kind="ExternalInput").ap()
    srg = nc.dram_tensor("srg", [D, HALF + 1], F32, kind="ExternalInput").ap()
    wog_in = nc.dram_tensor("wog", [D, D], F32, kind="ExternalInput").ap()
    fwk_in = nc.dram_tensor("fwk", [D, 4096], F32, kind="ExternalInput").ap()
    fwv_in = nc.dram_tensor("fwv", [4096, D], F32, kind="ExternalInput").ap()
    fwr_in = nc.dram_tensor("fwr", [D, D], F32, kind="ExternalInput").ap()
    ln2_in = nc.dram_tensor("ln2v", [128, 16], F32, kind="ExternalInput").ap()
    ftm_in = nc.dram_tensor("ftm", [128, 32], F32, kind="ExternalInput").ap()
    shm_in = nc.dram_tensor("shmask", [128, 1], F32, kind="ExternalInput").ap()
    x3m_o = nc.dram_tensor("x3m", [D, HALF], F32, kind="ExternalOutput").ap()

    es = contextlib.ExitStack()
    with es:
        p = Prog(nc, es)
        WO = 256
        W = WO + 1
        wog = p.sbuf("wog_t", [128, 8 * 1024], BF16)
        fwk = p.sbuf("fwk_t", [128, 8 * 4096], BF16)
        fwv = [p.sbuf(f"fwv_t{i}", [128, 8 * 1024], BF16) for i in range(2)]
        fwr = fwv[0]
        ln2v = p.sbuf("ln2v_t", [128, 16], F32)
        ftm = p.sbuf("ftm_t", [128, 32], F32)
        shm = p.sbuf("shm_g1", [128, 1], F32)
        ybf = p.sbuf("ybf", [128, 8 * W], BF16)
        srb = p.sbuf("srb", [128, 8 * W], BF16)
        x2s = p.sbuf("x2s_g", [128, 8 * W], F32)
        x2bf = p.sbuf("x2bf_g", [128, 8 * W], BF16)
        zs = p.sbuf("zs_g", [128, 8 * W], F32)
        xas = p.sbuf("xas_g", [128, 8 * W], F32)
        xab = p.sbuf("xab_g", [128, 8 * W], BF16)
        xn2 = p.sbuf("xn2_g", [128, 8 * W], BF16)
        dif = p.sbuf("dif_g", [128, 8 * WO], BF16)
        xr2 = p.sbuf("xr2_g", [128, 8 * WO], BF16)
        kk = p.sbuf("kk_g", [128, 32 * WO], BF16)
        rr = p.sbuf("rr_g", [128, 8 * WO], BF16)
        rr2 = p.sbuf("rr2_g", [128, 8 * WO], F32)
        stg = p.sbuf("stg_g", [128, W], F32)
        sc = ln_scratch(p, "g_", blk=257)
        stat = p.psum("ps_stat_g", [128, 512])
        ps = {"sums": stat[0:1, :], "sqs": stat[32:33, :],
              "bc0": p.psum("ps_bc0_g", [128, 512]), "bc1": p.psum("ps_bc1_g", [128, 512])}
        pspool = [p.psum(f"ps_mmg{i}", [128, 512]) for i in range(3)]

        for kc in range(8):
            p.dma("pool", wog[:, kc * 1024:(kc + 1) * 1024], wog_in[kc * 128:(kc + 1) * 128, :])
            p.dma("pool", fwk[:, kc * 4096:(kc + 1) * 4096], fwk_in[kc * 128:(kc + 1) * 128, :])
        p.dma("sp", ln2v[:, :], ln2_in)
        p.dma("sp", ftm[:, :], ftm_in)
        p.dma("sp", shm[:, :], shm_in)

        for si in range(HALF // WO):
            s0 = si * WO
            for kc in range(8):
                p.dma("pool", ybf[:, kc * W:(kc + 1) * W], yg[kc * 128:(kc + 1) * 128, s0:s0 + W])
                p.dma("pool", srb[:, kc * W:(kc + 1) * W], srg[kc * 128:(kc + 1) * 128, s0:s0 + W])
                p.dma("sp", x2s[:, kc * W:(kc + 1) * W], x2g[kc * 128:(kc + 1) * 128, s0:s0 + W])
                p.dma("pool", x2bf[:, kc * W:(kc + 1) * W], x2g[kc * 128:(kc + 1) * 128, s0:s0 + W])
            emit_ln_fm(p, x2s, x2bf, W, zs, ps, sc, tag=f"gz{si}")

            def xa_writer(oc, w0, bw, ps_ap):
                sl = slice(oc * W + w0, oc * W + w0 + bw)
                p.dve("tensor_mul", stg[:, 0:bw], srb[:, sl], ps_ap, tag="gxa1")
                p.dve("tensor_add", xas[:, sl], zs[:, sl], stg[:, 0:bw], tag="gxa2")
                p.act_copy(xab[:, sl], xas[:, sl], tag="gxab")
            emit_linear_fm(p, xa_writer, wog, ybf, W, 8, 8, pspool, blk_cols=512, tag="gwo")

            emit_ln_fm(p, xas, xab, W, xn2, ps, sc, gamma=ln2v[:, 0:8], beta=ln2v[:, 8:16],
                       tag=f"gl2{si}")
            if si == 0:
                for kc in range(8):
                    p.dve("tensor_scalar", xn2[:, kc * W:kc * W + 1], xn2[:, kc * W:kc * W + 1],
                          shm[:, 0:1], None, op0=ALU.mult, tag="gshm")
            for kc in range(8):
                a = xn2[:, kc * W + 1:kc * W + W]
                b = xn2[:, kc * W:kc * W + W - 1]
                p.dve("tensor_sub", dif[:, kc * WO:(kc + 1) * WO], a, b, tag="gdif")
            for kc in range(8):
                p.dve("scalar_tensor_tensor", xr2[:, kc * WO:(kc + 1) * WO],
                      dif[:, kc * WO:(kc + 1) * WO], ftm[:, 8 + kc:9 + kc],
                      xn2[:, kc * W:kc * W + W - 1], op0=ALU.mult, op1=ALU.add, tag="gmr")
            for kc in range(8):
                p.dve("scalar_tensor_tensor", dif[:, kc * WO:(kc + 1) * WO],
                      dif[:, kc * WO:(kc + 1) * WO], ftm[:, kc:kc + 1],
                      xn2[:, kc * W:kc * W + W - 1], op0=ALU.mult, op1=ALU.add, tag="gmk")

            def kk_writer(oc, w0, bw, ps_ap):
                p.dve("tensor_relu", stg[:, 0:bw], ps_ap, tag="gkr")
                p.act(kk[:, oc * WO + w0: oc * WO + w0 + bw], stg[:, 0:bw], AF.Square, tag="gks")
            emit_linear_fm(p, kk_writer, fwk, dif, WO, 8, 32, pspool, blk_cols=256,
                           w_kc_stride=4096, tag="gfk")

            for kc in range(8):
                p.dma("pool", fwr[:, kc * 1024:(kc + 1) * 1024],
                      fwr_in[kc * 128:(kc + 1) * 128, :], tag="gldr")
            def rr_writer(oc, w0, bw, ps_ap):
                p.act(rr[:, oc * WO + w0: oc * WO + w0 + bw], ps_ap, AF.Sigmoid, tag="grr")
            emit_linear_fm(p, rr_writer, fwr, xr2, WO, 8, 8, pspool, blk_cols=256, tag="gfr")

            for qt in range(4):
                fb = fwv[qt % 2]
                for kc in range(8):
                    p.dma("pool", fb[:, kc * 1024:(kc + 1) * 1024],
                          fwv_in[(qt * 8 + kc) * 128:(qt * 8 + kc + 1) * 128, :],
                          tag=f"gldv{qt}")
                for oc in range(8):
                    pv_ = pspool[oc % 3]
                    for kc in range(8):
                        c0 = kc * 1024 + oc * 128
                        p.mm(pv_[:, 0:WO], fb[:, c0:c0 + 128],
                             kk[:, (qt * 8 + kc) * WO:(qt * 8 + kc + 1) * WO],
                             start=(kc == 0), stop=(kc == 7), tag="gfv")
                    if qt == 0:
                        p.dve("tensor_copy", rr2[:, oc * WO:(oc + 1) * WO], pv_[:, 0:WO],
                              tag="gvq0")
                    else:
                        p.dve("tensor_add", rr2[:, oc * WO:(oc + 1) * WO],
                              rr2[:, oc * WO:(oc + 1) * WO], pv_[:, 0:WO], tag="gvq")
            for oc in range(8):
                sl = slice(oc * W + 1, oc * W + W)
                slo = slice(oc * WO, (oc + 1) * WO)
                p.dve("tensor_mul", stg[:, 0:WO], rr[:, slo], rr2[:, slo], tag="gv1")
                p.dve("tensor_add", stg[:, 0:WO], stg[:, 0:WO], xas[:, sl], tag="gv2")
                p.dve("tensor_add", stg[:, 0:WO], stg[:, 0:WO], x2s[:, sl], tag="gv3")
                p.dma("sp", x3m_o[oc * 128:(oc + 1) * 128, s0:s0 + WO], stg[:, 0:WO], tag="gvst")

        block = es.enter_context(nc.Block())
        p.build(block)
    return nc



def build_launchG2():
    nc = bass.Bass("TRN2", target_bir_lowering=False)
    x4g = nc.dram_tensor("x4g", [D, HALF], F32, kind="ExternalInput").ap()
    wrt_in = nc.dram_tensor("wrt", [128, 64], F32, kind="ExternalInput").ap()
    brw_in = nc.dram_tensor("brw", [1, 8], F32, kind="ExternalInput").ap()
    w1_in = nc.dram_tensor("w1s", [8192, 4096], F32, kind="ExternalInput").ap()
    w2_in = nc.dram_tensor("w2s", [32768, 1024], F32, kind="ExternalInput").ap()
    b1_in = nc.dram_tensor("b1v", [128, 256], F32, kind="ExternalInput").ap()
    b2_in = nc.dram_tensor("b2v", [128, 64], F32, kind="ExternalInput").ap()
    sel_in = nc.dram_tensor("selm8", [8, 1024], F32, kind="ExternalInput").ap()
    id_in = nc.dram_tensor("ident", [128, 128], F32, kind="ExternalInput").ap()
    fin = nc.dram_tensor("fin", [D, HALF], F32, kind="ExternalOutput").ap()

    es = contextlib.ExitStack()
    with es:
        p = Prog(nc, es)
        W = 256
        w1t = p.sbuf("w1t", [128, 8 * 4096], BF16)
        w2t = p.sbuf("w2t", [128, 32 * 1024], BF16)
        ht = p.sbuf("ht", [128, 32 * W], BF16)
        x4s = ht[:, 0:8 * W * 2].bitcast(F32)
        x4b = ht[:, 8 * W * 2:8 * W * 3]
        xn3 = p.sbuf("xn3", [128, 8 * HALF], BF16)
        wrt = p.sbuf("wrt_t", [128, 8 * 8], BF16)
        brw = p.sbuf("brw_t", [1, 8], F32)
        b1v = p.sbuf("b1v_t", [128, 256], F32)
        b2v = p.sbuf("b2v_t", [128, 64], F32)
        selm = p.sbuf("selm8_t", [8, 1024], F32)
        ident = p.sbuf("ident_g2", [128, 128], F32)
        ones1f = p.sbuf("ones1f_g2", [1, 128], F32)
        brb = p.sbuf("brb", [128, 8], F32)
        ltm = p.sbuf("ltm", [128, 8], F32)
        gx = p.sbuf("gx_g2", [128, 24], F32)
        el = p.sbuf("el_g2", [128, 8], F32)
        l2 = p.sbuf("l2_g2", [128, 8], F32)
        msk = p.sbuf("msk_g2", [128, 8], F32)
        wtm = p.sbuf("wtm_g2", [128, 16 * 8], F32)
        wfm = p.sbuf("wfm_g2", [8, HALF], F32)
        wbc = p.sbuf("wbc_g2", [128, W], F32)
        stg = [p.sbuf(f"stg_g2_{i}", [128, W], F32) for i in range(4)]
        sc = ln_scratch(p, "g2_", blk=256)
        stat = p.psum("ps_stat_g2", [128, 512])
        ps = {"sums": stat[0:1, :], "sqs": stat[32:33, :],
              "bc0": p.psum("ps_bc0_g2", [128, 512]), "bc1": p.psum("ps_bc1_g2", [128, 512])}
        pspool = [p.psum(f"ps_g2_{i}", [128, 512]) for i in range(3)]

        p.dma("pool", wrt[:, :], wrt_in)
        p.dma("sp", brw[:, :], brw_in)
        p.dma("sp", b1v[:, :], b1_in)
        p.dma("sp", b2v[:, :], b2_in)
        p.dma("sp", selm[:, :], sel_in)
        p.dma("sp", ident[:, :], id_in)
        p.memset("dve", ones1f[:, :], 1.0)
        p.dma("sp", fin, x4g, tag="init_fin")
        p.mm(ps["bc0"][:, 0:8], ones1f[0:1, :], brw[0:1, :], tag="brb")
        p.dve("tensor_copy", brb[:, :], ps["bc0"][:, 0:8], tag="brbc")

        for si in range(HALF // W):
            s0 = si * W
            for kc in range(8):
                p.dma("sp", x4s[:, kc * W:(kc + 1) * W], x4g[kc * 128:(kc + 1) * 128, s0:s0 + W])
                p.dma("pool", x4b[:, kc * W:(kc + 1) * W], x4g[kc * 128:(kc + 1) * 128, s0:s0 + W])
            xn3v = xn3[:, 0:0]
            class _shim:
                def __getitem__(self, idx):
                    return None
            emit_ln_fm_strided(p, x4s, x4b, W, xn3, HALF, s0, ps, sc, tag=f"g2ln{si}")
            for tt in range(W // 128):
                t0 = s0 + tt * 128
                psr = pspool[tt % 2]
                for kc in range(8):
                    p.mm(psr[0:128, 0:8], xn3[:, kc * HALF + t0:kc * HALF + t0 + 128],
                         wrt[:, kc * 8:(kc + 1) * 8], start=(kc == 0), stop=(kc == 7),
                         tag="rmm")
                p.dve("tensor_add", ltm[:, :], psr[0:128, 0:8], brb[:, :], tag="rl")
                m1 = gx[:, 0:1]
                negm1 = gx[:, 1:2]
                m2 = gx[:, 2:3]
                den = gx[:, 3:4]
                rec = gx[:, 4:5]
                p.dve("reduce_max", m1, ltm[:, :], axis=AX.X, tag="rm1")
                p.dve("tensor_scalar_mul", negm1, m1, -1.0, tag="rnm")
                p.act(el[:, :], ltm[:, :], AF.Exp, bias=negm1, tag="rel")
                p.dve("tensor_scalar", msk[:, :], ltm[:, :], m1, None, op0=ALU.is_equal, tag="roh")
                p.dve("scalar_tensor_tensor", l2[:, :], msk[:, :], -1e9, ltm[:, :],
                      op0=ALU.mult, op1=ALU.add, tag="rl2")
                p.dve("reduce_max", m2, l2[:, :], axis=AX.X, tag="rm2")
                p.dve("tensor_scalar", l2[:, :], l2[:, :], m2, None, op0=ALU.is_equal, tag="roh2")
                p.dve("tensor_add", msk[:, :], msk[:, :], l2[:, :], tag="rmsk")
                p.dve("tensor_mul", el[:, :], el[:, :], msk[:, :], tag="rnum")
                p.dve("reduce_sum", den, el[:, :], axis=AX.X, tag="rden")
                p.dve("reciprocal", rec, den, tag="rrec")
                p.dve("tensor_scalar", wtm[:, (si * (W // 128) + tt) * 8:(si * (W // 128) + tt) * 8 + 8],
                      el[:, :], rec, None, op0=ALU.mult, tag="rw")
        for t in range(16):
            p.transpose(ps["bc1"][0:8, 0:128], wtm[:, t * 8:(t + 1) * 8], ident[:, :], tag="wt")
            p.dve("tensor_copy", wfm[:, t * 128:(t + 1) * 128], ps["bc1"][0:8, 0:128], tag="wtc")

        for e in range(8):
            for kc in range(8):
                p.dma("pool", w1t[:, kc * 4096:(kc + 1) * 4096],
                      w1_in[e * 1024 + kc * 128:e * 1024 + (kc + 1) * 128, :], tag=f"ldw1_{e}")
            for kc in range(32):
                p.dma("pool", w2t[:, kc * 1024:(kc + 1) * 1024],
                      w2_in[e * 4096 + kc * 128:e * 4096 + (kc + 1) * 128, :], tag=f"ldw2_{e}")
            for blk in range(HALF // W):
                s0 = blk * W
                for hc in range(32):
                    psh = pspool[hc % 2]
                    for kc in range(8):
                        p.mm(psh[:, 0:W], w1t[:, kc * 4096 + hc * 128:kc * 4096 + hc * 128 + 128],
                             xn3[:, kc * HALF + s0:kc * HALF + s0 + W],
                             start=(kc == 0), stop=(kc == 7), tag=f"hmm{e}")
                    p.act(ht[:, hc * W:(hc + 1) * W], psh[:, 0:W], AF.Relu,
                          bias=b1v[:, e * 32 + hc:e * 32 + hc + 1], tag=f"hrelu{e}")
                p.mm(ps["bc0"][:, 0:W], selm[:, e * 128:(e + 1) * 128], wfm[:, s0:s0 + W],
                     tag="wsel")
                p.dve("tensor_copy", wbc[:, :], ps["bc0"][:, 0:W], tag="wselc")
                for oc in range(8):
                    psy = pspool[2]
                    for kc in range(32):
                        p.mm(psy[:, 0:W], w2t[:, kc * 1024 + oc * 128:kc * 1024 + oc * 128 + 128],
                             ht[:, kc * W:(kc + 1) * W], start=(kc == 0), stop=(kc == 31),
                             tag=f"ymm{e}")
                    sg = stg[oc % 4]
                    p.dve("tensor_scalar", sg[:, :], psy[:, 0:W], b2v[:, e * 8 + oc:e * 8 + oc + 1],
                          None, op0=ALU.add, tag="yb")
                    p.dve("tensor_mul", sg[:, :], sg[:, :], wbc[:, :], tag="yw")
                    p.dma_accum(fin[oc * 128:(oc + 1) * 128, s0:s0 + W], sg[:, :], tag="yacc")

        block = es.enter_context(nc.Block())
        p.build(block)
    return nc


def emit_ln_fm_strided(p, x_f32, x_bf, W, out_bf, out_stride, out_off, ps, sc, tag="ln"):
    nblk = (W + 511) // 512
    for blk in range(nblk):
        w0 = blk * 512
        bw = min(512, W - w0)
        sums = ps["sums"][0:1, 0:bw]
        sqs = ps["sqs"][0:1, 0:bw]
        for c in range(8):
            xb = x_bf[:, c * W + w0: c * W + w0 + bw]
            p.mm(sums, sc["ones128"][:, 0:1], xb, start=(c == 0), stop=(c == 7), tag=f"{tag}_sum")
        for c in range(8):
            xb = x_bf[:, c * W + w0: c * W + w0 + bw]
            sq = sc["sq" + str(c % 2)][:, 0:bw]
            p.act(sq, xb, AF.Square, tag=f"{tag}_sq")
            p.mm(sqs, sc["ones128"][:, 0:1], sq, start=(c == 0), stop=(c == 7), tag=f"{tag}_sqs")
        m = sc["row"][0:1, 0:bw]
        msq = sc["row"][0:1, 512:512 + bw]
        var = sc["row"][0:1, 1024:1024 + bw]
        rstd = sc["row"][0:1, 1536:1536 + bw]
        p.dve("tensor_scalar_mul", m, sums, 1.0 / D, tag=f"{tag}_m")
        p.dve("tensor_scalar_mul", msq, sqs, 1.0 / D, tag=f"{tag}_msq")
        p.dve("tensor_mul", var, m, m, tag=f"{tag}_m2")
        p.dve("tensor_sub", var, msq, var, tag=f"{tag}_var")
        p.act(rstd, var, AF.Ln, bias=sc["eps"][0:1, 0:1], tag=f"{tag}_ln")
        p.act(rstd, rstd, AF.Exp, scale=-0.5, tag=f"{tag}_rstd")
        p.dve("tensor_scalar_mul", m, m, -1.0, tag=f"{tag}_negm")
        bc_m = ps["bc0"][:, 0:bw]
        bc_r = ps["bc1"][:, 0:bw]
        p.mm(bc_m, sc["ones1_f"][0:1, :], m, tag=f"{tag}_bcm")
        p.mm(bc_r, sc["ones1_f"][0:1, :], rstd, tag=f"{tag}_bcr")
        for c in range(8):
            xf = x_f32[:, c * W + w0: c * W + w0 + bw]
            ob = out_bf[:, c * out_stride + out_off + w0: c * out_stride + out_off + w0 + bw]
            t = sc["nrm"][:, 0:bw]
            p.dve("tensor_add", t, xf, bc_m, tag=f"{tag}_sub")
            p.dve("tensor_mul", ob, t, bc_r, tag=f"{tag}_mul")



def _pv(v):
    v = np.asarray(v, np.float32).reshape(-1)
    n = v.size // 128
    return np.ascontiguousarray(v.reshape(n, 128).T)


def _make_masks():
    m = np.zeros((4 * 128, 512), np.float32)
    for rel in range(4):
        i = np.arange(128)[:, None] + 128 * rel
        j = np.arange(512)[None, :]
        m[rel * 128:(rel + 1) * 128] = (i <= j).astype(np.float32)
    import ml_dtypes
    return m.astype(ml_dtypes.bfloat16)


def _pack_v(vtm, h2):
    import ml_dtypes
    out = np.zeros((8 * 128, 2080), ml_dtypes.bfloat16)
    one = np.ones((), ml_dtypes.bfloat16)
    for h in range(8):
        blk = out[128 * h:128 * h + 128]
        for c in range(32):
            blk[:, 65 * c:65 * c + 64] = vtm[128 * c:128 * c + 128, 64 * h:64 * h + 64]
            blk[:, 65 * c + 64] = one
    return out


def kernel(**inputs):
    import ml_dtypes
    BF = ml_dtypes.bfloat16
    F = np.float32
    g = lambda k: np.asarray(inputs[k])
    x = g("L0_attn_x").astype(F)
    xT = [np.ascontiguousarray(x[b].T) for b in range(BATCH)]

    w_qkv = g("L0_attn_w_qkv")
    ident = np.eye(128, dtype=F)
    masks = _make_masks()

    ncA = build_launchA()
    imsA = []
    for c in range(NCORES):
        b, h2 = c // 2, c % 2
        q0 = 512 * h2
        wqk = np.ascontiguousarray(np.concatenate(
            [w_qkv[:, q0:q0 + 512], w_qkv[:, 1024 + q0:1024 + q0 + 512]], axis=1))
        wv = np.ascontiguousarray(w_qkv[:, 2048 + q0:2048 + q0 + 512])
        imsA.append({"x0": np.ascontiguousarray(xT[b][:, :HALF]),
                     "x1": np.ascontiguousarray(xT[b][:, HALF:]),
                     "wqk": wqk, "wv": wv})
    resA = run_bass_kernel_spmd(ncA, imsA, core_ids=list(range(NCORES))).results

    ncB = build_launchB()
    imsB = []
    for c in range(NCORES):
        imsB.append({"qk": resA[c]["qk"], "vpack": _pack_v(resA[c]["vtm"], c % 2),
                     "masks": masks})
    resB = run_bass_kernel_spmd(ncB, imsB, core_ids=list(range(NCORES))).results

    sw1, sw2, sw3 = g("L0_ffn_sw1"), g("L0_ffn_sw2"), g("L0_ffn_sw3")
    pvec = np.zeros((128, 24), F)
    pvec[:, 0:8] = _pv(g("L0_attn_b_out"))
    pvec[:, 8:12] = _pv(g("L0_ffn_sb1"))
    pvec[:, 12:16] = _pv(g("L0_ffn_sb2"))
    pvec[:, 16:20] = _pv(g("L0_ffn_s_ln_g"))
    pvec[:, 20:24] = _pv(g("L0_ffn_s_ln_b"))
    w12 = np.ascontiguousarray(np.concatenate([sw1.T, sw2.T], axis=1), dtype=F)
    sw3t = np.ascontiguousarray(sw3.T, dtype=F)
    sb3v = _pv(g("L0_ffn_sb3"))
    ncC = build_launchC()
    imsC = []
    for c in range(NCORES):
        b, h2 = c // 2, c % 2
        ao_full = np.concatenate([resB[2 * b]["ao"], resB[2 * b + 1]["ao"]], axis=0)
        aof = np.ascontiguousarray(ao_full[:, h2 * HALF:(h2 + 1) * HALF])
        imsC.append({"xf32": np.ascontiguousarray(xT[b][:, h2 * HALF:(h2 + 1) * HALF]),
                     "aof": aof, "w_out4": g("L0_attn_w_out").astype(F),
                     "w12": w12, "sw3t": sw3t, "pvec": pvec, "sb3v": sb3v})
    resC = run_bass_kernel_spmd(ncC, imsC, core_ids=list(range(NCORES))).results

    idx_b = g("L0_ffn_index_b").astype(np.int64)
    idx_t = g("L0_ffn_index_t").astype(np.int64)
    Msel = idx_b.size
    Mpad = ((Msel + NCORES * 128 - 1) // (NCORES * 128)) * (NCORES * 128)
    Mper = Mpad // NCORES
    owner = 2 * idx_b + (idx_t >= HALF)
    lcol = idx_t % HALF
    xm_all = np.zeros((D, Mpad), BF)
    shg_all = np.zeros((D, Mpad), F)
    for m in range(Msel):
        xm_all[:, m] = resC[owner[m]]["xn1o"][:, lcol[m]]
        shg_all[:, m] = resC[owner[m]]["sho"][:, lcol[m]]
    egw = np.ascontiguousarray(
        np.concatenate([g("L0_ffn_eg_w"), g("L0_ffn_sg_w")], axis=0).T, dtype=F)
    gb9 = np.concatenate([g("L0_ffn_eg_b"), g("L0_ffn_sg_b")]).reshape(9, 1).astype(F)
    expb = g("L0_ffn_exp_bias").reshape(1, 8).astype(F)
    ew1, ew2, ew3 = g("L0_ffn_ew1"), g("L0_ffn_ew2"), g("L0_ffn_ew3")
    ew12 = np.zeros((D, 8192), F)
    ew3t = np.zeros((512, 8192), F)
    for e in range(8):
        ew12[:, e * 1024:e * 1024 + 512] = ew1[e].T
        ew12[:, e * 1024 + 512:(e + 1) * 1024] = ew2[e].T
        ew3t[:, e * 1024:(e + 1) * 1024] = ew3[e].T
    ebv = np.zeros((128, 72), F)
    eb1, eb2 = g("L0_ffn_eb1"), g("L0_ffn_eb2")
    for e in range(8):
        ebv[:, e * 8:e * 8 + 4] = _pv(eb1[e])
        ebv[:, e * 8 + 4:e * 8 + 8] = _pv(eb2[e])
    ebv[:, 64:68] = _pv(g("L0_ffn_sel_ln_g"))
    ebv[:, 68:72] = _pv(g("L0_ffn_sel_ln_b"))
    selm = np.zeros((9, 9 * 128), F)
    for e in range(9):
        selm[e, e * 128:(e + 1) * 128] = 1.0
    ncD = build_launchD(Mper)
    imsD = []
    for c in range(NCORES):
        imsD.append({"xm": np.ascontiguousarray(xm_all[:, c * Mper:(c + 1) * Mper]),
                     "shg": np.ascontiguousarray(shg_all[:, c * Mper:(c + 1) * Mper]),
                     "egw": egw, "gb9": gb9, "expb": expb, "ew12": ew12,
                     "ew3t": ew3t, "eb3t": g("L0_ffn_eb3").astype(F), "ebv": ebv,
                     "ident": ident, "selm": selm})
    resD = run_bass_kernel_spmd(ncD, imsD, core_ids=list(range(NCORES))).results
    final0 = [np.array(resC[c]["sho"]) for c in range(NCORES)]
    for m in range(Msel):
        comb = resD[m // Mper]["comb"][:, m % Mper]
        final0[owner[m]][:, lcol[m]] = comb

    tmk = g("L1_attn_tmk").reshape(-1)
    tmv_ = g("L1_attn_tmv").reshape(-1)
    tmr = g("L1_attn_tmr").reshape(-1)
    lnvE = np.concatenate([_pv(g("L1_attn_ln1_g")), _pv(g("L1_attn_ln1_b"))], axis=1)
    tmvE = np.concatenate([_pv(tmk), _pv(tmv_), _pv(tmr),
                           _pv(1 - tmk), _pv(1 - tmv_), _pv(1 - tmr)], axis=1)
    wk3 = np.ascontiguousarray(np.concatenate(
        [g("L1_attn_Wk").T, g("L1_attn_Wv").T, g("L1_attn_Wr").T], axis=1), dtype=F)
    ncE = build_launchE()
    imsE = []
    for c in range(NCORES):
        b, h2 = c // 2, c % 2
        if h2 == 0:
            x1p = np.zeros((D, 1), F)
            f0p = np.zeros((D, 1), F)
            shm = np.zeros((128, 1), F)
        else:
            x1p = resC[2 * b]["x1o"][:, -1:]
            f0p = final0[2 * b][:, -1:]
            shm = np.ones((128, 1), F)
        imsE.append({"x1g": np.ascontiguousarray(np.concatenate([x1p, resC[c]["x1o"]], axis=1)),
                     "f0g": np.ascontiguousarray(np.concatenate([f0p, final0[c]], axis=1)),
                     "wk3": wk3, "lnv": lnvE, "tmv": tmvE, "shmask": shm})
    resE = run_bass_kernel_spmd(ncE, imsE, core_ids=list(range(NCORES))).results

    td = g("L1_attn_time_decay").reshape(-1)
    tf = g("L1_attn_time_first").reshape(-1)
    ncF = build_launchF()
    imsF = []
    for c in range(NCORES):
        b, h2 = c // 2, c % 2
        kfull = np.concatenate([resE[2 * b]["ko"], resE[2 * b + 1]["ko"]], axis=1)
        vfull = np.concatenate([resE[2 * b]["vo"], resE[2 * b + 1]["vo"]], axis=1)
        r0 = 512 * h2
        tf8 = np.zeros((128, 8), F)
        for cc in range(4):
            tf8[:, cc] = td[r0 + cc * 128:r0 + (cc + 1) * 128]
            tf8[:, 4 + cc] = tf[r0 + cc * 128:r0 + (cc + 1) * 128]
        imsF.append({"ks": np.ascontiguousarray(kfull[r0:r0 + 512]),
                     "vs": np.ascontiguousarray(vfull[r0:r0 + 512]), "tf8": tf8})
    resF = run_bass_kernel_spmd(ncF, imsF, core_ids=list(range(NCORES))).results

    ftmk = g("L1_attn_f_tmk").reshape(-1)
    ftmr = g("L1_attn_f_tmr").reshape(-1)
    ln2v = np.concatenate([_pv(g("L1_attn_ln2_g")), _pv(g("L1_attn_ln2_b"))], axis=1)
    ftm = np.concatenate([_pv(ftmk), _pv(ftmr), _pv(1 - ftmk), _pv(1 - ftmr)], axis=1)
    ncG1 = build_launchG1()
    imsG1 = []
    yfull = {}
    for b in range(BATCH):
        yfull[b] = np.concatenate([resF[2 * b]["yo"], resF[2 * b + 1]["yo"]], axis=0)
    for c in range(NCORES):
        b, h2 = c // 2, c % 2
        t0 = h2 * HALF
        def win(arr2049src_full, prev_zero):
            if h2 == 0:
                pc = np.zeros((D, 1), F)
            else:
                pc = arr2049src_full[:, t0 - 1:t0]
            return np.ascontiguousarray(
                np.concatenate([pc, arr2049src_full[:, t0:t0 + HALF]], axis=1))
        x2full = np.concatenate([resE[2 * b]["x2o"], resE[2 * b + 1]["x2o"]], axis=1)
        srfull = np.concatenate([resE[2 * b]["sro"], resE[2 * b + 1]["sro"]], axis=1)
        imsG1.append({"x2g": win(x2full, True), "yg": win(yfull[b], True),
                      "srg": win(srfull, True),
                      "wog": np.ascontiguousarray(g("L1_attn_Wo").T, dtype=F),
                      "fwk": np.ascontiguousarray(g("L1_attn_fWk").T, dtype=F),
                      "fwv": np.ascontiguousarray(g("L1_attn_fWv").T, dtype=F),
                      "fwr": np.ascontiguousarray(g("L1_attn_fWr").T, dtype=F),
                      "ln2v": ln2v, "ftm": ftm,
                      "shmask": np.full((128, 1), 0.0 if h2 == 0 else 1.0, F)})
    resG1 = run_bass_kernel_spmd(ncG1, imsG1, core_ids=list(range(NCORES))).results

    Wrr = g("L1_ffn_W_router").astype(F)
    b1, b2 = g("L1_ffn_b1"), g("L1_ffn_b2")
    b1v = np.zeros((128, 256), F)
    b2v = np.zeros((128, 64), F)
    for e in range(8):
        b1v[:, e * 32:(e + 1) * 32] = _pv(b1[e])
        b2v[:, e * 8:(e + 1) * 8] = _pv(b2[e])
    selm8 = np.zeros((8, 1024), F)
    for e in range(8):
        selm8[e, e * 128:(e + 1) * 128] = 1.0
    ncG2 = build_launchG2()
    imsG2 = []
    for c in range(NCORES):
        imsG2.append({"x4g": resG1[c]["x3m"],
                      "wrt": np.ascontiguousarray(
                          Wrr.reshape(8, 128, 8).transpose(1, 0, 2).reshape(128, 64)),
                      "brw": g("L1_ffn_b_router").reshape(1, 8).astype(F),
                      "w1s": g("L1_ffn_W1").reshape(8 * D, 4096).astype(F),
                      "w2s": g("L1_ffn_W2").reshape(8 * 4096, D).astype(F),
                      "b1v": b1v, "b2v": b2v, "selm8": selm8, "ident": ident})
    resG2 = run_bass_kernel_spmd(ncG2, imsG2, core_ids=list(range(NCORES))).results

    out = np.empty((BATCH, SEQ, D), np.float32)
    for c in range(NCORES):
        b, h2 = c // 2, c % 2
        out[b, h2 * HALF:(h2 + 1) * HALF, :] = np.asarray(resG2[c]["fin"]).T
    return out



class TilePool:

    def __init__(self, p, arena_bytes=2048 * 111):
        self.p = p
        self.arenas = []
        self.live = {}
        self.arena_bytes = arena_bytes

    def _new_arena(self, need):
        size = max(self.arena_bytes, need)
        att = 0
        while size >= need:
            try:
                t = self.p.sbuf(f"arena{len(self.arenas)}_{att}", [128, size // 2], BF16)
                break
            except AssertionError:
                size -= 2048
                att += 1
        else:
            raise AssertionError(f"TilePool: cannot allocate arena of {need}B")
        self.arenas.append([t, [[0, size]]])
        return len(self.arenas) - 1

    def get(self, cols, dtype, rows=128, want=0):
        esz = 2 if dtype == BF16 else (1 if dtype == U8 else 4)
        need = (cols * esz + 63) // 64 * 64
        small = need <= 8448
        best = None
        for ai, (t, frees) in enumerate(self.arenas):
            for fi, (lo, hi) in enumerate(frees):
                if hi - lo < need:
                    continue
                if small:
                    if best is None or hi > best[3]:
                        best = (ai, fi, lo, hi)
                else:
                    if best is None or (hi - lo) < (best[3] - best[2]):
                        best = (ai, fi, lo, hi)
        if best is None:
            ai = self._new_arena(need)
            fi = 0
        else:
            ai, fi = best[0], best[1]
        t, frees = self.arenas[ai]
        iv = frees[fi]
        if small:
            lo = iv[1] - need
            if iv[1] - iv[0] == need:
                frees.pop(fi)
            else:
                iv[1] -= need
        else:
            lo = iv[0]
            if iv[1] - iv[0] == need:
                frees.pop(fi)
            else:
                iv[0] += need
        if dtype == BF16:
            view = t[0:rows, lo // 2: lo // 2 + cols]
        elif dtype == U8:
            view = t[:, lo // 2: lo // 2 + (cols + 1) // 2].bitcast(U8)[0:rows, 0:cols]
        else:
            view = t[:, lo // 2: lo // 2 + 2 * cols].bitcast(dtype)[0:rows, 0:cols]
        self.live[id(view)] = (ai, lo, lo + need)
        return view

    def put(self, *views):
        for v in views:
            ai, lo, hi = self.live.pop(id(v))
            frees = self.arenas[ai][1]
            frees.append([lo, hi])
            frees.sort()
            merged = []
            for iv in frees:
                if merged and merged[-1][1] == iv[0]:
                    merged[-1][1] = iv[1]
                else:
                    merged.append(iv)
            self.arenas[ai][1] = merged


def pair_signal(p, flag_sh, epv, parity_reads, seq, deps):
    def fn(e):
        with e.register(f"sg{seq}") as r:
            e.load(r, parity_reads[0][0:1, 0:1])
            with e.If_eq(r, 0):
                e.dma_start(out=flag_sh.ap()[0:1, 0:1], in_=epv[0:1, seq - 1:seq]
                            ).then_inc(p._par_dma_sem, 16)
            with e.Else():
                e.dma_start(out=flag_sh.ap()[1:2, 0:1], in_=epv[0:1, seq - 1:seq]
                            ).then_inc(p._par_dma_sem, 16)
        p._par_dma_count += 16
        return e.nop()
    return p.raw("pool", fn, reads=list(deps) + [epv, parity_reads[0]],
                 writes=[flag_sh], kind="compute", tag=f"sig{seq}")


def pair_wait(p, flag_sh, flagv, trash, parity_reads, epv, seq, written):
    def fn(e):
        with e.register(f"pw{seq}") as r, e.register(f"pf{seq}") as f, \
                e.register(f"pg{seq}") as go, e.register(f"pt{seq}") as tgt, \
                e.register(f"ph{seq}") as hi, e.register(f"pc{seq}") as c1:
            e.load(r, parity_reads[0][0:1, 0:1])
            e.load(tgt, epv[0:1, seq - 1:seq])
            e.load(hi, epv[0:1, 7:8])
            e.reg_mov(go, 1)
            with e.While(go):
                e.nop(cycle_cnt=1000)
                with e.If_eq(r, 0):
                    e.dma_start(out=flagv[0:1, :], in_=flag_sh.ap()[1:2, :]).then_inc(trash, 16)
                with e.Else():
                    e.dma_start(out=flagv[0:1, :], in_=flag_sh.ap()[0:1, :]).then_inc(trash, 16)
                e.nop(cycle_cnt=2000)
                e.load(f, flagv[0:1, 0:1])
                e.reg_alu(go, f, tgt, ALU.is_lt)
                e.reg_alu(c1, f, hi, ALU.is_lt)
                e.reg_alu(go, go, 1, ALU.add)
                e.reg_alu(go, go, c1, ALU.subtract)
        return e.nop()
    return p.raw("pool", fn, reads=[flag_sh, parity_reads[0], epv],
                 writes=list(written) + [flagv], kind="compute", tag=f"wait{seq}")


def par_dma(p, parity_reads, even_list, odd_list, reads, writes, tag=""):
    def fn(e):
        with e.register("pd" + tag) as r:
            e.load(r, parity_reads[0][0:1, 0:1])
            with e.If_eq(r, 0):
                for out, in_ in even_list:
                    e.dma_start(out=out, in_=in_, allow_slow_non_contiguous=True
                                ).then_inc(p._par_dma_sem, 16)
            with e.Else():
                for out, in_ in odd_list:
                    e.dma_start(out=out, in_=in_, allow_slow_non_contiguous=True
                                ).then_inc(p._par_dma_sem, 16)
        n = max(len(even_list), len(odd_list))
        assert len(even_list) == len(odd_list), "parity branches must issue equally"
        e.wait_ge(p._par_dma_sem, p._par_dma_count + 16 * n)
        p._par_dma_count += 16 * n
        return e.nop()
    return p.raw("pool", fn, reads=list(reads) + [parity_reads[0]], writes=list(writes),
                 kind="compute", tag="pdma" + tag)


def strided_cols(tile, c0, step, n):
    t = tile.tensor if isinstance(tile, bass.AP) else tile
    base_off = tile.offset if isinstance(tile, bass.AP) else 0
    row = t.shape[1]
    return bass.AP(t, base_off + c0, [[row, 128], [step, n]])


def build_fused(gather_info, dbg=None):
    nc = bass.Bass("TRN2", target_bir_lowering=False)
    A = lambda name, shape, dt=F32: nc.dram_tensor(name, list(shape), dt, kind="ExternalInput").ap()
    x0 = A("x0", [D, HALF]); x1 = A("x1", [D, HALF]); xh = A("xh", [D, HALF])
    wqk_i = A("wqk", [D, 1024]); wv_i = A("wv", [D, 512])
    masks_i = A("masks", [4 * 128, 512], BF16)
    wout_i = A("w_out4", [D, D]); w12_i = A("w12", [D, D]); sw3_i = A("sw3t", [512, D])
    pvec_i = A("pvec", [128, 24]); sb3_i = A("sb3v", [128, 8])
    egw_i = A("egw", [D, 9]); gb9_i = A("gb9", [9, 1]); expb_i = A("expb", [1, 8])
    ew12_i = A("ew12", [D, 8192]); ew3_i = A("ew3t", [512, 8192]); eb3_i = A("eb3t", [8, D])
    ebv_i = A("ebv", [128, 72]); id_i = A("ident", [128, 128]); selm_i = A("selm", [9, 9 * 128])
    mmask_i = A("mmask", [128, HALF], U8)
    wk3_i = A("wk3", [D, 3072]); lnvE_i = A("lnv", [128, 16]); tmvE_i = A("tmv", [128, 48])
    shm_i = A("shmask", [128, 1]); tf8_i = A("tf8", [128, 8])
    wog_i = A("wog", [D, D]); fwk_i = A("fwk", [D, 4096]); fwv_i = A("fwv", [4096, D])
    fwr_i = A("fwr", [D, D]); ln2_i = A("ln2v", [128, 16]); ftm_i = A("ftm", [128, 32])
    wrt_i = A("wrt", [128, 64]); brw_i = A("brw", [1, 8])
    w1_i = A("w1s", [8192, 4096]); w2_i = A("w2s", [32768, 1024])
    b1_i = A("b1v", [128, 256]); b2_i = A("b2v", [128, 64]); sel8_i = A("selm8", [8, 1024])
    par_i = A("parity", [1, 1], I32)
    epv_i = A("epv", [1, 8], I32)
    fin = nc.dram_tensor("fin", [D, HALF], F32, kind="ExternalOutput").ap()

    L = lambda name, shape, dt=F32: nc.dram_tensor(name, list(shape), dt)
    qk_d = L("qk_d", [D, SEQ], BF16)
    vpl_d = L("vpl_d", [D, 2048], BF16)
    x1o_d = L("x1o_d", [D, HALF])
    xn1_d = L("xn1_d", [D, HALF], BF16)
    sho_d = L("sho_d", [D, HALF])
    f0_d = L("f0_d", [D, HALF])
    k_d = L("k_d", [D, HALF]); v_d = L("v_d", [D, HALF])
    sr_d = L("sr_d", [D, HALF]); x2_d = L("x2_d", [D, HALF])
    x3m_d = L("x3m_d", [D, HALF])
    S = lambda name, shape, dt=F32: nc.dram_tensor(name, list(shape), dt, addr_space="Shared")
    ao_sh = S("ao_sh", [1024, SEQ], BF16)
    mg_sh = S("mg_sh", [2 * D, 2048], BF16)
    ms_sh = S("ms_sh", [2 * D, 2048])
    comb_sh = S("comb_sh", [2 * D, 512])
    shw_sh = S("shw_sh", [2, 512])
    x1b_sh = S("x1b_sh", [2 * D, 1]); f0b_sh = S("f0b_sh", [2 * D, 1])
    x2b_sh = S("x2b_sh", [2 * D, 1]); srb_sh = S("srb_sh", [2 * D, 1])
    k_sh = S("k_sh", [1024, SEQ]); v_sh = S("v_sh", [1024, SEQ])
    y_sh = S("y_sh", [1024, SEQ + 1])
    dbg_d = L("dbg_d", [D, HALF])
    flags_sh = nc.dram_tensor("flags_sh", [2, 16], I32, addr_space="Shared")

    es = contextlib.ExitStack()
    with es:
        p = Prog(nc, es)
        p._par_dma_sem = p._sem("pardma", 0)
        p._par_dma_count = 0
        pool = TilePool(p)
        part = pool.get(1, I32, rows=1)
        flagv = pool.get(16, I32, rows=1)
        flagc = pool.get(16, I32, rows=1)
        trash = p._sem("trash", 0)
        p.dma("sp", part[0:1, :], par_i)
        epv = p.sbuf("epv_t", [1, 8], I32)
        p.dma("sp", epv[0:1, :], epv_i)
        ident = pool.get(128, F32, rows=128)
        p.dma("sp", ident[:, :], id_i)
        seq = [0]

        def SIG(deps):
            seq[0] += 1
            pair_signal(p, flags_sh, epv, [part], seq[0], deps)
            return seq[0]

        def WAIT(n, written):
            pair_wait(p, flags_sh, flagv, trash, [part], epv, n, written)

        wide0 = p.psum("ps_wide0", [128, 1024])
        wide1 = p.psum("ps_wide1", [128, 1024])
        mm0 = p.psum("ps_m0", [128, 512])
        mm1 = p.psum("ps_m1", [128, 512])
        bc0 = p.psum("ps_b0", [128, 512])
        bc1 = p.psum("ps_b1", [128, 512])
        ln_ps = {"sums": wide0[0:1, 0:512], "sqs": wide0[32:33, 0:512], "bc0": bc0, "bc1": bc1}
        pool3 = [mm0, mm1, wide1[:, 0:512]]

        W = 1024
        xf = pool.get(8 * W, F32, want=65536)
        xbf = pool.get(8 * W, BF16, want=65536)
        xn = pool.get(8 * W, BF16)
        wqk_t = pool.get(8 * 1024, BF16)
        wv_t = pool.get(8 * 512, BF16)
        stage = pool.get(512, BF16)
        stage2 = pool.get(512, BF16)
        sc = ln_scratch_pool(p, pool)
        for kc in range(8):
            p.dma("pool", wqk_t[:, kc * 1024:(kc + 1) * 1024], wqk_i[kc * 128:(kc + 1) * 128, :])
            p.dma("pool", wv_t[:, kc * 512:(kc + 1) * 512], wv_i[kc * 128:(kc + 1) * 128, :])
        for qtr in range(4):
            xin = (x0, x1)[qtr // 2]
            q0 = (qtr % 2) * W
            g0 = qtr * W
            for kc in range(8):
                p.dma("sp", xf[:, kc * W:(kc + 1) * W], xin[kc * 128:(kc + 1) * 128, q0:q0 + W])
                p.dma("pool", xbf[:, kc * W:(kc + 1) * W], xin[kc * 128:(kc + 1) * 128, q0:q0 + W])
            emit_ln_fm(p, xf, xbf, W, xn, ln_ps, sc, tag=f"fa_ln{qtr}")
            def qk_writer(oc, w0, bw, ps_ap, g0=g0):
                st = stage if oc % 2 == 0 else stage2
                p.dve("tensor_copy", st[:, 0:bw], ps_ap, tag="fa_qke")
                p.dma("sp", qk_d.ap()[oc * 128:(oc + 1) * 128, g0 + w0: g0 + w0 + bw],
                      st[:, 0:bw], tag="fa_qks")
            emit_linear_fm(p, qk_writer, wqk_t, xn, W, 8, 8, pool3, tag="fa_qk")
            for tb in range(W // 128):
                psv = pool3[tb % 3][:, 0:512]
                for kc in range(8):
                    p.mm(psv, xn[:, kc * W + tb * 128: kc * W + tb * 128 + 128],
                         wv_t[:, kc * 512:(kc + 1) * 512], start=(kc == 0), stop=(kc == 7),
                         tag="fa_vmm")
                st = stage if tb % 2 == 0 else stage2
                p.dve("tensor_copy", st[:, 0:512], psv, tag="fa_ve")
                gc = qtr * 8 + tb
                for h in range(8):
                    p.dma("sp", vpl_d.ap()[128 * h:128 * h + 128, 64 * gc:64 * gc + 64],
                          st[:, 64 * h:64 * h + 64], tag="fa_vs")
        pool.put(xf, xbf, xn, wqk_t, wv_t, stage, stage2)

        qt = [pool.get(SEQ, BF16, rows=64) for _ in range(2)]
        ktl = [pool.get(SEQ, BF16, rows=64) for _ in range(2)]
        vt = [pool.get(2080, BF16) for _ in range(2)]
        mk = pool.get(4 * 512, BF16)
        p2 = [pool.get(1024, BF16) for _ in range(2)]
        sqB = pool.get(SEQ, BF16, rows=64)
        ones64 = pool.get(1, BF16, rows=64)
        ones1f = pool.get(128, F32, rows=1)
        nmax = pool.get(512, F32, rows=1)
        cstat = pool.get(8, F32, rows=1)
        negc = pool.get(1, F32, rows=128)
        rrow = pool.get(512, F32, rows=1)
        obuf = [pool.get(512, F32, rows=64) for _ in range(2)]
        ostg = [pool.get(512, BF16, rows=64) for _ in range(2)]
        p.memset("dve", ones1f[:, :], 1.0)
        p.memset("dve", ones64[:, :], 1.0)
        for i in range(2):
            p.memset("dve", vt[i][:, :], 1.0)
        for r in range(4):
            p.dma("pool", mk[:, r * 512:(r + 1) * 512], masks_i[r * 128:(r + 1) * 128, :])
        ps_s = [wide0, wide1]
        ps_o = [mm0[0:65, :], mm1[0:65, :]]
        ao_stores = []
        for h in range(8):
            bi = h % 2
            p.dma("pool", qt[bi][:, :], qk_d.ap()[64 * h:64 * h + 64, :], tag=f"fb_q{h}")
            p.dma("pool", ktl[bi][:, :], qk_d.ap()[512 + 64 * h:512 + 64 * h + 64, :], tag=f"fb_k{h}")
            vdst = bass.AP(vt[bi].tensor, vt[bi].offset, [[vt[bi].tensor.shape[1], 128], [65, 32], [1, 64]])
            p.dma("pool", vdst, vpl_d.ap()[128 * h:128 * h + 128, :].rearrange("p (c j) -> p c j", j=64),
                  tag=f"fb_v{h}")
            for wi, t in ((0, qt[bi]), (1, ktl[bi])):
                p.act(sqB[:, :], t[:, :], AF.Square, tag=f"fb_sq{h}")
                for blk in range(8):
                    p.mm(bc1[0:1, :], ones64[:, 0:1], sqB[:, blk * 512:(blk + 1) * 512], tag=f"fb_n{h}")
                    if blk == 0:
                        p.dve("tensor_copy", nmax[0:1, :], bc1[0:1, :], tag=f"fb_c0{h}")
                    else:
                        p.dve("tensor_max", nmax[0:1, :], nmax[0:1, :], bc1[0:1, :], tag=f"fb_c{h}")
                p.dve("reduce_max", cstat[0:1, wi:wi + 1], nmax[0:1, :], axis=AX.X, tag=f"fb_r{h}")
            p.dve("tensor_add", cstat[0:1, 2:3], cstat[0:1, 0:1], cstat[0:1, 1:2], tag=f"fb_a{h}")
            p.dve("tensor_scalar_mul", cstat[0:1, 2:3], cstat[0:1, 2:3], -1.0 / 16.0, tag=f"fb_s{h}")
            p.mm(bc0[:, 0:1], ones1f[0:1, :], cstat[0:1, 2:3], tag=f"fb_b{h}")
            p.dve("tensor_copy", negc[:, 0:1], bc0[:, 0:1], tag=f"fb_bc{h}")
            for qb in range(8):
                po = ps_o[qb % 2]
                nkt = 4 * qb + 4
                for kp in range(nkt // 2):
                    pss = ps_s[kp % 2]
                    pp = p2[kp % 2]
                    for hf in range(2):
                        ktile = 2 * kp + hf
                        p.mm(pss[:, hf * 512:(hf + 1) * 512],
                             ktl[bi][:, 128 * ktile:128 * ktile + 128],
                             qt[bi][:, 512 * qb:512 * qb + 512], tag=f"fb_smm{h}")
                    p.act(pp[:, :], pss[:, :], AF.Exp, bias=negc[:, 0:1], scale=0.125, tag=f"fb_e{h}")
                    for hf in range(2):
                        ktile = 2 * kp + hf
                        rel = ktile - 4 * qb
                        if rel >= 0:
                            p.dve("tensor_mul", pp[:, hf * 512:(hf + 1) * 512],
                                  pp[:, hf * 512:(hf + 1) * 512], mk[:, rel * 512:(rel + 1) * 512],
                                  tag=f"fb_m{h}")
                    for hf in range(2):
                        ktile = 2 * kp + hf
                        p.mm(po, vt[bi][:, 65 * ktile:65 * ktile + 65],
                             pp[:, hf * 512:(hf + 1) * 512],
                             start=(ktile == 0), stop=(ktile == nkt - 1), tag=f"fb_pv{h}")
                ob = obuf[qb % 2]
                og = ostg[qb % 2]
                p.act(rrow[0:1, :], po[64:65, :], AF.Ln, tag=f"fb_l{h}")
                p.act(rrow[0:1, :], rrow[0:1, :], AF.Exp, scale=-1.0, tag=f"fb_x{h}")
                p.mm(bc0[0:64, :], ones1f[0:1, 0:64], rrow[0:1, :], tag=f"fb_d{h}")
                p.dve("tensor_copy", ob[:, :], po[0:64, :], tag=f"fb_oe{h}")
                p.dve("tensor_mul", og[:, :], ob[:, :], bc0[0:64, :], tag=f"fb_od{h}")
                par_dma(p, [part],
                        [(ao_sh.ap()[64 * h:64 * h + 64, 512 * qb:512 * (qb + 1)], og[:, :])],
                        [(ao_sh.ap()[512 + 64 * h:512 + 64 * h + 64, 512 * qb:512 * (qb + 1)], og[:, :])],
                        reads=[og], writes=[ao_sh], tag=f"ao{h}_{qb}")
        s_ao = SIG([ao_sh])
        pool.put(*qt, *ktl, *vt, mk, *p2, sqB, nmax, rrow, *obuf, *ostg)

        WAIT(s_ao, [ao_sh])
        Wc = 512
        wo_t = pool.get(8 * 1024, BF16)
        w12_t = pool.get(8 * 1024, BF16)
        sw3_t = pool.get(4 * 1024, BF16)
        pv = pool.get(24, F32, rows=128)
        sb3 = pool.get(8, F32, rows=128)
        xs = pool.get(8 * Wc, F32)
        aos = pool.get(8 * Wc, BF16)
        x1s = pool.get(8 * Wc, F32)
        x1b = pool.get(8 * Wc, BF16)
        xn1 = pool.get(8 * Wc, BF16)
        h1s = pool.get(4 * Wc, BF16)
        h2s = pool.get(4 * Wc, BF16)
        prodf = pool.get(4 * Wc, F32)
        prodb = pool.get(4 * Wc, BF16)
        shn = pool.get(4 * Wc, BF16)
        shstg = pool.get(Wc, F32)
        for kc in range(8):
            p.dma("pool", wo_t[:, kc * 1024:(kc + 1) * 1024], wout_i[kc * 128:(kc + 1) * 128, :])
            p.dma("pool", w12_t[:, kc * 1024:(kc + 1) * 1024], w12_i[kc * 128:(kc + 1) * 128, :])
        for kc in range(4):
            p.dma("pool", sw3_t[:, kc * 1024:(kc + 1) * 1024], sw3_i[kc * 128:(kc + 1) * 128, :])
        p.dma("sp", pv[:, :], pvec_i)
        p.dma("sp", sb3[:, :], sb3_i)
        for s0 in range(0, HALF, Wc):
            for kc in range(8):
                p.dma("sp", xs[:, kc * Wc:(kc + 1) * Wc], xh[kc * 128:(kc + 1) * 128, s0:s0 + Wc])
            par_dma(p, [part],
                    [(aos[:, kc * Wc:(kc + 1) * Wc],
                      ao_sh.ap()[kc * 128:(kc + 1) * 128, s0:s0 + Wc]) for kc in range(8)],
                    [(aos[:, kc * Wc:(kc + 1) * Wc],
                      ao_sh.ap()[kc * 128:(kc + 1) * 128, HALF + s0:HALF + s0 + Wc]) for kc in range(8)],
                    reads=[ao_sh], writes=[aos], tag=f"aold{s0}")
            def x1_writer(oc, w0, bw, ps_ap, s0=s0):
                p.dve("scalar_tensor_tensor", x1s[:, oc * Wc + w0: oc * Wc + w0 + bw],
                      ps_ap, pv[:, oc:oc + 1], xs[:, oc * Wc + w0: oc * Wc + w0 + bw],
                      op0=ALU.add, op1=ALU.add, tag="fc_x1")
                p.act_copy(x1b[:, oc * Wc + w0: oc * Wc + w0 + bw],
                           x1s[:, oc * Wc + w0: oc * Wc + w0 + bw], tag="fc_x1c")
                p.dma("sp", x1o_d.ap()[oc * 128:(oc + 1) * 128, s0 + w0: s0 + w0 + bw],
                      x1s[:, oc * Wc + w0: oc * Wc + w0 + bw], tag="fc_x1s")
            emit_linear_fm(p, x1_writer, wo_t, aos, Wc, 8, 8, pool3, tag="fc_wo")
            emit_ln_fm(p, x1s, x1b, Wc, xn1, ln_ps, sc, tag=f"fc_ln{s0}")
            for oc in range(8):
                p.dma("sp", xn1_d.ap()[oc * 128:(oc + 1) * 128, s0:s0 + Wc],
                      xn1[:, oc * Wc:(oc + 1) * Wc], tag="fc_xn1s")
            def h_writer(oc, w0, bw, ps_ap):
                if oc < 4:
                    p.act(h1s[:, oc * Wc + w0: oc * Wc + w0 + bw], ps_ap, AF.Silu,
                          bias=pv[:, 8 + oc:9 + oc], tag="fc_h1")
                else:
                    p.dve("tensor_scalar", h2s[:, (oc - 4) * Wc + w0: (oc - 4) * Wc + w0 + bw],
                          ps_ap, pv[:, 8 + oc:9 + oc], None, op0=ALU.add, tag="fc_h2")
            emit_linear_fm(p, h_writer, w12_t, xn1, Wc, 8, 8, pool3, tag="fc_h12")
            for c in range(4):
                p.dve("tensor_mul", prodf[:, c * Wc:(c + 1) * Wc], h1s[:, c * Wc:(c + 1) * Wc],
                      h2s[:, c * Wc:(c + 1) * Wc], tag="fc_pr")
                p.act_copy(prodb[:, c * Wc:(c + 1) * Wc], prodf[:, c * Wc:(c + 1) * Wc], tag="fc_prc")
            emit_ln_fm(p, prodf, prodb, Wc, shn, ln_ps, sc, gamma=pv[:, 16:20], beta=pv[:, 20:24],
                       nchunks=4, tag=f"fc_sln{s0}")
            def sh_writer(oc, w0, bw, ps_ap, s0=s0):
                p.dve("tensor_scalar", shstg[:, w0:w0 + bw], ps_ap, sb3[:, oc:oc + 1], None,
                      op0=ALU.add, tag="fc_sh")
                p.dma("sp", sho_d.ap()[oc * 128:(oc + 1) * 128, s0 + w0: s0 + w0 + bw],
                      shstg[:, w0:w0 + bw], tag="fc_shs")
            emit_linear_fm(p, sh_writer, sw3_t, shn, Wc, 4, 8, pool3, tag="fc_sw3")
        par_dma(p, [part],
                [(mg_sh.ap()[0:D, :], xn1_d.ap()), (ms_sh.ap()[0:D, :], sho_d.ap()),
                 (x1b_sh.ap()[0:D, :], x1o_d.ap()[:, HALF - 1:HALF])],
                [(mg_sh.ap()[D:2 * D, :], xn1_d.ap()), (ms_sh.ap()[D:2 * D, :], sho_d.ap()),
                 (x1b_sh.ap()[D:2 * D, :], x1o_d.ap()[:, HALF - 1:HALF])],
                reads=[xn1_d, sho_d, x1o_d], writes=[mg_sh, ms_sh, x1b_sh], tag="moepush")
        s_moe = SIG([mg_sh, ms_sh, x1b_sh])
        pool.put(wo_t, w12_t, sw3_t, xs, aos, x1s, x1b, xn1, h1s, h2s, prodf, prodb, shn, shstg)

        env0 = locals()
        env0["dbg"] = dbg
        env0["dbg_d"] = dbg_d
        _fused_DEFG(p, pool, env0)

        if dbg and dbg != "g1":
            srcs = {"dbg": (dbg_d, slice(0, D), slice(0, HALF)),
                    "qk": (qk_d, slice(0, D), slice(0, HALF)),
                    "x1o": (x1o_d, slice(0, D), slice(0, HALF)),
                    "xn1": (xn1_d, slice(0, D), slice(0, HALF)),
                    "sho": (sho_d, slice(0, D), slice(0, HALF)),
                    "f0": (f0_d, slice(0, D), slice(0, HALF)),
                    "x2": (x2_d, slice(0, D), slice(0, HALF)),
                    "k": (k_d, slice(0, D), slice(0, HALF)),
                    "v": (v_d, slice(0, D), slice(0, HALF)),
                    "sr": (sr_d, slice(0, D), slice(0, HALF)),
                    "y": (y_sh, slice(0, D), slice(1, HALF + 1))}
            key = "dbg" if (dbg.startswith("D_") or dbg.startswith("G2_")) else dbg
            t, rs, cs = srcs[key]
            p.dma("pool", fin, t.ap()[rs, cs], tag="dbgdump")

        block = es.enter_context(nc.Block())
        p.build(block)
    return nc


def _fused_DEFG(p, pool, env):
    nc = p.nc
    part = env["part"]
    SIG, WAIT = env["SIG"], env["WAIT"]
    ln_ps, pool3, sc = env["ln_ps"], env["pool3"], env["sc"]
    wide0, wide1, mm0, mm1, bc0, bc1 = (env[k] for k in ("wide0", "wide1", "mm0", "mm1", "bc0", "bc1"))
    ident, ones1f = env["ident"], env["ones1f"]
    (mg_sh, ms_sh, comb_sh, x1b_sh, f0b_sh, x2b_sh, srb_sh, k_sh, v_sh, y_sh) = (
        env[k] for k in ("mg_sh", "ms_sh", "comb_sh", "x1b_sh", "f0b_sh", "x2b_sh",
                         "srb_sh", "k_sh", "v_sh", "y_sh"))
    (x1o_d, xn1_d, sho_d, f0_d, k_d, v_d, sr_d, x2_d, x3m_d) = (
        env[k] for k in ("x1o_d", "xn1_d", "sho_d", "f0_d", "k_d", "v_d", "sr_d", "x2_d", "x3m_d"))
    (egw_i, gb9_i, expb_i, ew12_i, ew3_i, eb3_i, ebv_i, selm_i, mmask_i) = (
        env[k] for k in ("egw_i", "gb9_i", "expb_i", "ew12_i", "ew3_i", "eb3_i", "ebv_i",
                         "selm_i", "mmask_i"))
    (wk3_i, lnvE_i, tmvE_i, shm_i, tf8_i) = (env[k] for k in ("wk3_i", "lnvE_i", "tmvE_i", "shm_i", "tf8_i"))
    (wog_i, fwk_i, fwv_i, fwr_i, ln2_i, ftm_i) = (env[k] for k in ("wog_i", "fwk_i", "fwv_i", "fwr_i", "ln2_i", "ftm_i"))
    (wrt_i, brw_i, w1_i, w2_i, b1_i, b2_i, sel8_i) = (env[k] for k in ("wrt_i", "brw_i", "w1_i", "w2_i", "b1_i", "b2_i", "sel8_i"))
    fin = env["fin"]
    s_moe = env["s_moe"]

    M = 512
    WAIT(s_moe, [mg_sh, ms_sh, x1b_sh])
    xg = pool.get(8 * 1024, BF16)
    egw = pool.get(8 * 9, BF16)
    gb9 = pool.get(1, F32, rows=9)
    expb = pool.get(8, F32, rows=1)
    ew12 = [pool.get(8 * 1024, BF16) for _ in range(2)]
    ew3 = [pool.get(4 * 1024, BF16) for _ in range(2)]
    eb3 = pool.get(D, BF16, rows=8)
    ebv = pool.get(72, F32, rows=128)
    selm = pool.get(9 * 128, F32, rows=9)
    env["selm_t"] = selm
    sgs = pool.get(M, F32, rows=9)
    nt = M // 128
    stm = pool.get(nt * 16, F32)
    gsc = pool.get(nt * 8, F32)
    gml = pool.get(nt * 4, F32)
    gpk = pool.get(nt * 9, F32)
    owf = pool.get(M, F32, rows=9)
    owfb = pool.get(M, BF16, rows=9)
    ebt = pool.get(8, F32, rows=128)
    h1s = pool.get(4 * M, BF16)
    h2s = pool.get(4 * M, BF16)
    prodf = pool.get(4 * M, F32)
    prodb = pool.get(4 * M, BF16)
    zn = pool.get(4 * M, BF16)
    zs = pool.get(4 * M, BF16)
    zbc = pool.get(M, F32)
    acc = pool.get(8 * M, F32)
    par_dma(p, [part],
            [(xg[:, kc * 1024:(kc + 1) * 1024], xn1_d.ap()[kc * 128:(kc + 1) * 128, 0:1024])
             for kc in range(8)],
            [(xg[:, kc * 1024:(kc + 1) * 1024], mg_sh.ap()[kc * 128:(kc + 1) * 128, 1024:2048])
             for kc in range(8)],
            reads=[xn1_d, mg_sh], writes=[xg], tag="dgather")
    def xm_ap(kc):
        base_off = xg.offset + kc * 1024
        return bass.AP(xg.tensor, base_off, [[xg.tensor.shape[1], 128], [2, M]])
    for kc in range(8):
        p.dma("pool", egw[:, kc * 9:(kc + 1) * 9], egw_i[kc * 128:(kc + 1) * 128, :])
    p.dma("sp", gb9[:, :], gb9_i)
    p.dma("sp", expb[:, :], expb_i)
    p.dma("pool", eb3[:, :], eb3_i)
    p.dma("sp", ebv[:, :], ebv_i)
    p.dma("sp", selm[:, :], selm_i)
    ps_out = [mm0, mm1]
    psg = wide1[:, 0:512]
    p.memset("dve", acc[:, :], 0.0)
    for kc in range(8):
        p.mm(psg[0:9, 0:M], egw[:, kc * 9:(kc + 1) * 9], xm_ap(kc),
             start=(kc == 0), stop=(kc == 7), tag="fd_gmm")
    p.act(sgs[:, :], psg[0:9, 0:M], AF.Sigmoid, bias=gb9[:, 0:1], tag="fd_gsig")
    p.mm(bc1[:, 0:8], ones1f[0:1, :], expb[0:1, :], tag="fd_ebb")
    p.dve("tensor_copy", ebt[:, :], bc1[:, 0:8], tag="fd_ebc")
    for t in range(nt):
        p.transpose(bc1[0:128, 0:9], sgs[:, t * 128:(t + 1) * 128], ident[0:9, 0:9], tag="fd_gt")
        p.dve("tensor_copy", stm[:, t * 16:t * 16 + 9], bc1[0:128, 0:9], tag="fd_gtc")
        se = stm[:, t * 16:t * 16 + 8]
        ssg = stm[:, t * 16 + 8:t * 16 + 9]
        tp = gsc[:, t * 8:t * 8 + 8]
        m1 = gml[:, t * 4 + 0:t * 4 + 1]
        top1 = gml[:, t * 4 + 1:t * 4 + 2]
        ew = gml[:, t * 4 + 3:t * 4 + 4]
        pk = gpk[:, t * 9:(t + 1) * 9]
        shw = pk[:, 8:9]
        p.dve("tensor_add", tp, se, ebt[:, :], tag="fd_tp")
        p.dve("reduce_max", m1, tp, axis=AX.X, tag="fd_m1")
        p.dve("tensor_scalar", tp, tp, m1, None, op0=ALU.is_equal, tag="fd_oh")
        p.dve("tensor_mul", tp, tp, se, tag="fd_ohs")
        p.dve("reduce_max", top1, tp, axis=AX.X, tag="fd_top")
        p.dve("tensor_sub", shw, ssg, top1, tag="fd_d")
        p.act(shw, shw, AF.Sigmoid, tag="fd_shw")
        p.dve("tensor_scalar", ew, shw, -1.0, 1.0, op0=ALU.mult, op1=ALU.add, tag="fd_ew")
        p.dve("tensor_add", tp, se, ebt[:, :], tag="fd_tp2")
        p.dve("tensor_scalar", pk[:, 0:8], tp, m1, ew, op0=ALU.is_equal, op1=ALU.mult, tag="fd_ow")
        p.transpose(bc1[0:9, 0:128], pk, ident[:, :], tag="fd_gt2")
        p.dve("tensor_copy", owf[:, t * 128:(t + 1) * 128], bc1[0:9, 0:128], tag="fd_g2c")
        p.dve("tensor_copy", owfb[:, t * 128:(t + 1) * 128], bc1[0:9, 0:128], tag="fd_g2b")
    for e in range(8):
        wb = ew12[e % 2]
        w3b = ew3[e % 2]
        for kc in range(8):
            p.dma("pool", wb[:, kc * 1024:(kc + 1) * 1024],
                  ew12_i[kc * 128:(kc + 1) * 128, e * 1024:(e + 1) * 1024], tag=f"fd_w12_{e}")
        for kc in range(4):
            p.dma("pool", w3b[:, kc * 1024:(kc + 1) * 1024],
                  ew3_i[kc * 128:(kc + 1) * 128, e * 1024:(e + 1) * 1024], tag=f"fd_w3_{e}")
        for oc in range(8):
            ps_ = pool3[oc % 2][:, 0:M]
            for kc in range(8):
                l = wb[:, kc * 1024 + oc * 128:kc * 1024 + oc * 128 + 128]
                p.mm(ps_, l, xm_ap(kc), start=(kc == 0), stop=(kc == 7), tag=f"fd_h{e}")
            if oc < 4:
                p.act(h1s[:, oc * M:(oc + 1) * M], ps_, AF.Silu,
                      bias=ebv[:, e * 8 + oc:e * 8 + oc + 1], tag="fd_h1")
            else:
                p.dve("tensor_scalar", h2s[:, (oc - 4) * M:(oc - 3) * M],
                      ps_, ebv[:, e * 8 + oc:e * 8 + oc + 1], None, op0=ALU.add, tag="fd_h2")
        for c in range(4):
            p.dve("tensor_mul", prodf[:, c * M:(c + 1) * M], h1s[:, c * M:(c + 1) * M],
                  h2s[:, c * M:(c + 1) * M], tag="fd_pr")
            p.act_copy(prodb[:, c * M:(c + 1) * M], prodf[:, c * M:(c + 1) * M], tag="fd_prc")
        emit_ln_fm(p, prodf, prodb, M, zn, ln_ps, sc, gamma=ebv[:, 64:68], beta=ebv[:, 68:72],
                   nchunks=4, tag=f"fd_sln{e}")
        p.mm(bc0[:, 0:M], selm[:, e * 128:(e + 1) * 128], owf[:, :], tag="fd_owb")
        p.dve("tensor_copy", zbc[:, 0:M], bc0[:, 0:M], tag="fd_owc")
        for c in range(4):
            p.dve("tensor_mul", zs[:, c * M:(c + 1) * M], zn[:, c * M:(c + 1) * M],
                  zbc[:, 0:M], tag="fd_zs")
        for oc in range(8):
            po = ps_out[oc % 2][:, 0:M]
            for kc in range(4):
                c0 = kc * 1024 + oc * 128
                p.mm(po, w3b[:, c0:c0 + 128], zs[:, kc * M:(kc + 1) * M],
                     start=(kc == 0), stop=(kc == 3), tag=f"fd_o{e}")
            p.dve("tensor_add", acc[:, oc * M:(oc + 1) * M], acc[:, oc * M:(oc + 1) * M],
                  po, tag="fd_acc")
    for oc in range(8):
        po = ps_out[oc % 2][:, 0:M]
        p.mm(po, eb3[0:8, oc * 128:(oc + 1) * 128], owfb[0:8, :], start=True, stop=True,
             tag="fd_eb3")
        p.dve("tensor_add", acc[:, oc * M:(oc + 1) * M], acc[:, oc * M:(oc + 1) * M], po,
              tag="fd_accb")
    dbgk = env.get("dbg")
    dbg_d = env.get("dbg_d")
    if dbgk == "D_xg":
        for kc in range(8):
            p.dma("pool", dbg_d.ap()[kc * 128:(kc + 1) * 128, 0:1024],
                  xg[:, kc * 1024:(kc + 1) * 1024], tag="dxg")
    if dbgk == "D_owf":
        p.dma("sp", dbg_d.ap()[0:9, 0:M], owf[:, :], tag="dowf")
    if dbgk == "D_acc":
        for oc in range(8):
            p.dma("sp", dbg_d.ap()[oc * 128:(oc + 1) * 128, 0:M], acc[:, oc * M:(oc + 1) * M],
                  tag="dacc")
    pool.put(xg, egw, *ew12, *ew3, sgs, stm, gsc, gml, gpk, h1s, h2s,
             prodf, prodb, zn, zs, zbc)
    shw_sh = env["shw_sh"]
    comb_pushes_e = [(shw_sh.ap()[0:1, :], owf[8:9, :])]
    comb_pushes_o = [(shw_sh.ap()[1:2, :], owf[8:9, :])]
    for oc in range(8):
        comb_pushes_e.append((comb_sh.ap()[oc * 128:(oc + 1) * 128, :], acc[:, oc * M:(oc + 1) * M]))
        comb_pushes_o.append((comb_sh.ap()[D + oc * 128:D + (oc + 1) * 128, :], acc[:, oc * M:(oc + 1) * M]))
    par_dma(p, [part], comb_pushes_e, comb_pushes_o, reads=[acc, owf], writes=[comb_sh, shw_sh],
            tag="combpush")
    s_comb = SIG([comb_sh, shw_sh])
    WAIT(s_comb, [comb_sh, shw_sh])
    combp = pool.get(8 * M, F32)
    shwp = pool.get(M, F32, rows=1)
    par_dma(p, [part],
            [(combp[:, oc * M:(oc + 1) * M], comb_sh.ap()[D + oc * 128:D + (oc + 1) * 128, :])
             for oc in range(8)] + [(shwp[0:1, :], shw_sh.ap()[1:2, :])],
            [(combp[:, oc * M:(oc + 1) * M], comb_sh.ap()[oc * 128:(oc + 1) * 128, :])
             for oc in range(8)] + [(shwp[0:1, :], shw_sh.ap()[0:1, :])],
            reads=[comb_sh, shw_sh], writes=[combp, shwp], tag="combpull")
    shs = pool.get(8 * 512, F32)
    scat = pool.get(8 * 512, F32)
    mms = pool.get(512, U8)
    f0stg = pool.get(512, F32)
    f0stg2 = pool.get(512, F32)
    shwrow = pool.get(512, F32, rows=1)
    exshw = pool.get(M, F32, rows=1)
    p.memset("dve", shwrow[0:1, :], 0.0)
    p.mm(bc1[0:1, 0:M], selm[0:9, 1024:1025], owf[0:9, :], tag="fd_exsh")
    p.dve("tensor_copy", exshw[0:1, :], bc1[0:1, 0:M], tag="fd_exshc")
    for si in range(4):
        s0 = si * 512
        src = acc if si < 2 else combp
        shw_src = exshw[0:1, :] if si < 2 else shwp[0:1, :]
        half = si % 2
        p.dma("pool", mms[0:128, :], mmask_i[:, s0:s0 + 512], tag="fd_mm")
        for kc in range(8):
            p.dma("sp", shs[:, kc * 512:(kc + 1) * 512], sho_d.ap()[kc * 128:(kc + 1) * 128, s0:s0 + 512])
        p.dve("tensor_copy",
              bass.AP(shwrow.tensor, shwrow.offset, [[shwrow.tensor.shape[1], 1], [2, 256]]),
              shw_src[0:1, half * 256:half * 256 + 256], tag="fd_shwp")
        p.mm(bc0[:, 0:512], ones1f[0:1, :], shwrow[0:1, :], tag="fd_shwb")
        for kc in range(8):
            p.dve("tensor_copy",
                  bass.AP(scat.tensor, scat.offset + kc * 512, [[scat.tensor.shape[1], 128], [2, 256]]),
                  src[:, kc * M + half * 256: kc * M + half * 256 + 256], tag="fd_sc")
        for kc in range(8):
            p.dve("tensor_mul", f0stg[:, :], shs[:, kc * 512:(kc + 1) * 512], bc0[:, 0:512],
                  tag="fd_sw")
            p.dve("tensor_add", f0stg[:, :], f0stg[:, :], scat[:, kc * 512:(kc + 1) * 512],
                  tag="fd_sa")
            p.dve("select", f0stg2[:, :], mms[0:128, :], f0stg[:, :],
                  shs[:, kc * 512:(kc + 1) * 512], tag="fd_sel")
            p.dma("sp", f0_d.ap()[kc * 128:(kc + 1) * 128, s0:s0 + 512], f0stg2[:, :], tag="fd_f0s")
    par_dma(p, [part],
            [(f0b_sh.ap()[0:D, :], f0_d.ap()[:, HALF - 1:HALF])],
            [(f0b_sh.ap()[D:2 * D, :], f0_d.ap()[:, HALF - 1:HALF])],
            reads=[f0_d], writes=[f0b_sh], tag="f0push")
    s_f0 = SIG([f0b_sh])
    pool.put(owf, owfb, acc, combp, shwp, shs, scat, mms, f0stg, f0stg2, shwrow, exshw)

    WAIT(s_f0, [f0b_sh])
    WE = 513
    WO = 512
    wkt = pool.get(8 * 3072, BF16)
    lnv = pool.get(16, F32, rows=128)
    tmv = pool.get(48, F32, rows=128)
    shm = pool.get(1, F32, rows=128)
    x1s = pool.get(8 * WE, F32)
    f0s = pool.get(8 * WE, F32)
    x2s = pool.get(8 * WE, F32)
    x2b = pool.get(8 * WE, BF16)
    xn0 = pool.get(8 * WE, BF16)
    xnl = pool.get(8 * WE, BF16)
    dif = pool.get(8 * WO, BF16)
    mixk = pool.get(8 * WO, BF16)
    mixv = pool.get(8 * WO, BF16)
    mixr = pool.get(8 * WO, BF16)
    stgE = pool.get(WO, F32)
    for kc in range(8):
        p.dma("pool", wkt[:, kc * 3072:(kc + 1) * 3072], wk3_i[kc * 128:(kc + 1) * 128, :])
    p.dma("sp", lnv[:, :], lnvE_i)
    p.dma("sp", tmv[:, :], tmvE_i)
    p.dma("sp", shm[:, :], shm_i)
    for si in range(4):
        s0 = si * 512
        for kc in range(8):
            if si == 0:
                p.dma("sp", x1s[:, kc * WE:kc * WE + 1], x1b_sh.ap()[kc * 128:(kc + 1) * 128, :])
                p.dma("sp", f0s[:, kc * WE:kc * WE + 1], f0b_sh.ap()[kc * 128:(kc + 1) * 128, :])
                p.dma("sp", x1s[:, kc * WE + 1:kc * WE + WE], x1o_d.ap()[kc * 128:(kc + 1) * 128, 0:WO])
                p.dma("pool", f0s[:, kc * WE + 1:kc * WE + WE], f0_d.ap()[kc * 128:(kc + 1) * 128, 0:WO])
            else:
                p.dma("sp", x1s[:, kc * WE:kc * WE + WE], x1o_d.ap()[kc * 128:(kc + 1) * 128, s0 - 1:s0 + WO])
                p.dma("pool", f0s[:, kc * WE:kc * WE + WE], f0_d.ap()[kc * 128:(kc + 1) * 128, s0 - 1:s0 + WO])
        for kc in range(8):
            p.dve("tensor_add", x2s[:, kc * WE:(kc + 1) * WE], x1s[:, kc * WE:(kc + 1) * WE],
                  f0s[:, kc * WE:(kc + 1) * WE], tag="fe_x2")
            p.act_copy(x2b[:, kc * WE:(kc + 1) * WE], x2s[:, kc * WE:(kc + 1) * WE], tag="fe_x2c")
            p.dma("sp", x2_d.ap()[kc * 128:(kc + 1) * 128, s0:s0 + WO],
                  x2s[:, kc * WE + 1:(kc + 1) * WE], tag="fe_x2s")
        emit_ln_fm(p, x2s, x2b, WE, xn0, ln_ps, sc, tag=f"fe_ln{si}")
        emit_ln_fm(p, xn0, xn0, WE, xnl, ln_ps, sc, gamma=lnv[:, 0:8], beta=lnv[:, 8:16],
                   tag=f"fe_l1{si}")
        if si == 0:
            for kc in range(8):
                p.dve("tensor_scalar", xnl[:, kc * WE:kc * WE + 1], xnl[:, kc * WE:kc * WE + 1],
                      shm[:, 0:1], None, op0=ALU.mult, tag="fe_shm")
        for kc in range(8):
            a = xnl[:, kc * WE + 1:kc * WE + WE]
            b = xnl[:, kc * WE:kc * WE + WE - 1]
            p.dve("tensor_sub", dif[:, kc * WO:(kc + 1) * WO], a, b, tag="fe_dif")
        for mi, mix in enumerate((mixk, mixv, mixr)):
            for kc in range(8):
                p.dve("scalar_tensor_tensor", mix[:, kc * WO:(kc + 1) * WO],
                      dif[:, kc * WO:(kc + 1) * WO], tmv[:, mi * 8 + kc:mi * 8 + kc + 1],
                      xnl[:, kc * WE:kc * WE + WE - 1], op0=ALU.mult, op1=ALU.add, tag="fe_mix")
        for wi, (mix, outb) in enumerate(((mixk, k_d), (mixv, v_d), (mixr, sr_d))):
            def wr(oc, w0, bw, ps_ap, wi=wi, outb=outb, s0=s0):
                if wi == 2:
                    p.act(stgE[:, 0:bw], ps_ap, AF.Sigmoid, tag="fe_sr")
                else:
                    p.dve("tensor_copy", stgE[:, 0:bw], ps_ap, tag="fe_kv")
                p.dma("sp", outb.ap()[oc * 128:(oc + 1) * 128, s0 + w0:s0 + w0 + bw],
                      stgE[:, 0:bw], tag="fe_kvs")
            emit_linear_fm(p, wr, wkt, mix, WO, 8, 8, pool3,
                           w_col0=wi * 1024, w_kc_stride=3072, tag=f"fe_m{wi}")
    par_dma(p, [part],
            [(k_sh.ap()[:, 0:HALF], k_d.ap()), (v_sh.ap()[:, 0:HALF], v_d.ap()),
             (x2b_sh.ap()[0:D, :], x2_d.ap()[:, HALF - 1:HALF]),
             (srb_sh.ap()[0:D, :], sr_d.ap()[:, HALF - 1:HALF])],
            [(k_sh.ap()[:, HALF:SEQ], k_d.ap()), (v_sh.ap()[:, HALF:SEQ], v_d.ap()),
             (x2b_sh.ap()[D:2 * D, :], x2_d.ap()[:, HALF - 1:HALF]),
             (srb_sh.ap()[D:2 * D, :], sr_d.ap()[:, HALF - 1:HALF])],
            reads=[k_d, v_d, x2_d, sr_d], writes=[k_sh, v_sh, x2b_sh, srb_sh], tag="kvpush")
    s_kv = SIG([k_sh, v_sh, x2b_sh, srb_sh])
    pool.put(wkt, x1s, f0s, x2s, x2b, xn0, xnl, dif, mixk, mixv, mixr, stgE)

    WAIT(s_kv, [k_sh, v_sh])
    R = 2048
    kt = pool.get(SEQ, F32)
    vt = pool.get(SEQ, F32)
    tf8 = pool.get(8, F32, rows=128)
    cn = pool.get(8, F32, rows=128)
    st = pool.get(8, F32, rows=128)
    aAb = pool.get(2, F32, rows=128)
    EWt = pool.get(R, F32)
    onesR = pool.get(R, F32)
    Et = pool.get(R, F32)
    Evt = pool.get(R, F32)
    At = pool.get(R, F32)
    Bt = pool.get(R, F32)
    Eut = pool.get(R, F32)
    Euv = pool.get(R, F32)
    num = pool.get(R, F32)
    den = pool.get(R, F32)
    yt = pool.get(R, F32)
    p.dma("sp", tf8[:, :], tf8_i)
    p.memset("dve", onesR[:, :], 1.0)
    for c in range(4):
        par_dma(p, [part],
                [(kt[:, :], k_sh.ap()[c * 128:(c + 1) * 128, :]),
                 (vt[:, :], v_sh.ap()[c * 128:(c + 1) * 128, :])],
                [(kt[:, :], k_sh.ap()[512 + c * 128:512 + (c + 1) * 128, :]),
                 (vt[:, :], v_sh.ap()[512 + c * 128:512 + (c + 1) * 128, :])],
                reads=[k_sh, v_sh], writes=[kt, vt], tag=f"fkv{c}")
        td = tf8[:, c:c + 1]
        u = tf8[:, 4 + c:5 + c]
        w_ = cn[:, 0:1]; EW = cn[:, 1:2]; eu = cn[:, 2:3]; um = cn[:, 3:4]
        p.act(w_, td, AF.Exp, tag="ff_w")
        p.dve("tensor_scalar_mul", w_, w_, -1.0, tag="ff_wn")
        p.act(EW, w_, AF.Exp, tag="ff_ew")
        p.act(eu, u, AF.Exp, tag="ff_eu")
        p.dve("tensor_scalar_max", um, u, 0.0, tag="ff_um")
        p.dve("tensor_scalar", EWt[:, :], onesR[:, :], EW, None, op0=ALU.mult, tag="ff_ewt")
        aa = st[:, 0:1]; bb = st[:, 1:2]; pp = st[:, 2:3]
        p.memset("dve", aa, 0.0)
        p.memset("dve", bb, 0.0)
        p.memset("dve", pp, -1e38)
        for b in range(SEQ // R):
            kb = kt[:, b * R:(b + 1) * R]
            vb = vt[:, b * R:(b + 1) * R]
            cmax = st[:, 3:4]; Mx = st[:, 4:5]; negM = st[:, 5:6]
            t1 = st[:, 6:7]; t2 = st[:, 7:8]
            p.dve("reduce_max", cmax, kb, axis=AX.X, tag="ff_cm")
            p.dve("tensor_add", t1, pp, w_, tag="ff_t1")
            p.dve("tensor_add", t2, cmax, um, tag="ff_t2")
            p.dve("tensor_max", Mx, t1, t2, tag="ff_M")
            p.dve("tensor_scalar_mul", negM, Mx, -1.0, tag="ff_nM")
            p.act(Et[:, :], kb, AF.Exp, bias=negM, tag="ff_E")
            p.dve("tensor_mul", Evt[:, :], Et[:, :], vb, tag="ff_Ev")
            cs = t1
            p.act(cs, pp, AF.Exp, bias=negM, tag="ff_cs")
            p.dve("tensor_mul", aAb[:, 0:1], aa, cs, tag="ff_aA")
            p.dve("tensor_mul", aAb[:, 1:2], bb, cs, tag="ff_bB")
            p.dve("tensor_tensor_scan", At[:, :], EWt[:, :], Evt[:, :], aAb[:, 0:1],
                  op0=ALU.mult, op1=ALU.add, tag="ff_sA")
            p.dve("tensor_tensor_scan", Bt[:, :], EWt[:, :], Et[:, :], aAb[:, 1:2],
                  op0=ALU.mult, op1=ALU.add, tag="ff_sB")
            p.dve("tensor_scalar", Eut[:, :], Et[:, :], eu, None, op0=ALU.mult, tag="ff_Eu")
            p.dve("tensor_mul", Euv[:, :], Eut[:, :], vb, tag="ff_Euv")
            p.dve("tensor_add", num[:, 1:R], At[:, 0:R - 1], Euv[:, 1:R], tag="ff_n1")
            p.dve("tensor_add", num[:, 0:1], aAb[:, 0:1], Euv[:, 0:1], tag="ff_n0")
            p.dve("tensor_add", den[:, 1:R], Bt[:, 0:R - 1], Eut[:, 1:R], tag="ff_d1")
            p.dve("tensor_add", den[:, 0:1], aAb[:, 1:2], Eut[:, 0:1], tag="ff_d0")
            p.act(den[:, :], den[:, :], AF.Ln, tag="ff_ln")
            p.act(den[:, :], den[:, :], AF.Exp, scale=-1.0, tag="ff_exp")
            p.dve("tensor_mul", yt[:, :], num[:, :], den[:, :], tag="ff_y")
            ylist_e = [(y_sh.ap()[c * 128:(c + 1) * 128, 1 + b * R:1 + (b + 1) * R], yt[:, :])]
            ylist_o = [(y_sh.ap()[512 + c * 128:512 + (c + 1) * 128, 1 + b * R:1 + (b + 1) * R], yt[:, :])]
            if b == 0:
                ylist_e.append((y_sh.ap()[c * 128:(c + 1) * 128, 0:1], yt[:, 0:1]))
                ylist_o.append((y_sh.ap()[512 + c * 128:512 + (c + 1) * 128, 0:1], yt[:, 0:1]))
            par_dma(p, [part], ylist_e, ylist_o, reads=[yt], writes=[y_sh], tag=f"fy{c}_{b}")
            p.dve("tensor_copy", aa, At[:, R - 1:R], tag="ff_ua")
            p.dve("tensor_copy", bb, Bt[:, R - 1:R], tag="ff_ub")
            p.dve("tensor_copy", pp, Mx, tag="ff_up")
    s_y = SIG([y_sh])
    pool.put(kt, vt, EWt, onesR, Et, Evt, At, Bt, Eut, Euv, num, den, yt)

    env["shm"] = shm
    _fused_G(p, pool, env, s_y)


def _fused_G(p, pool, env, s_y):
    nc = p.nc
    part = env["part"]
    SIG, WAIT = env["SIG"], env["WAIT"]
    ln_ps, pool3, sc = env["ln_ps"], env["pool3"], env["sc"]
    wide0, wide1, mm0, mm1, bc0, bc1 = (env[k] for k in ("wide0", "wide1", "mm0", "mm1", "bc0", "bc1"))
    ident, ones1f = env["ident"], env["ones1f"]
    (x2b_sh, srb_sh, y_sh) = (env[k] for k in ("x2b_sh", "srb_sh", "y_sh"))
    (sr_d, x2_d) = (env[k] for k in ("sr_d", "x2_d"))
    (wog_i, fwk_i, fwv_i, fwr_i, ln2_i, ftm_i, shm_i) = (
        env[k] for k in ("wog_i", "fwk_i", "fwv_i", "fwr_i", "ln2_i", "ftm_i", "shm_i"))
    (wrt_i, brw_i, w1_i, w2_i, b1_i, b2_i, sel8_i) = (
        env[k] for k in ("wrt_i", "brw_i", "w1_i", "w2_i", "b1_i", "b2_i", "sel8_i"))
    fin = env["fin"]

    WAIT(s_y, [y_sh])
    WO = 256
    W = WO + 1
    wog = pool.get(8 * 1024, BF16)
    fwk = pool.get(8 * 4096, BF16)
    fwv = [pool.get(8 * 1024, BF16) for _ in range(2)]
    fwr = fwv[0]
    ln2v = pool.get(16, F32, rows=128)
    ftm = pool.get(32, F32, rows=128)
    shmG = env["shm"]
    for kc in range(8):
        p.dma("pool", wog[:, kc * 1024:(kc + 1) * 1024], wog_i[kc * 128:(kc + 1) * 128, :])
        p.dma("pool", fwk[:, kc * 4096:(kc + 1) * 4096], fwk_i[kc * 128:(kc + 1) * 128, :])
    p.dma("sp", ln2v[:, :], ln2_i)
    p.dma("sp", ftm[:, :], ftm_i)
    for si in range(HALF // WO):
        s0 = si * WO
        ybf = pool.get(8 * W, BF16)
        srb = pool.get(8 * W, BF16)
        x2s = pool.get(8 * W, F32)
        x2bf = pool.get(8 * W, BF16)
        zs = pool.get(8 * W, F32)
        xas = pool.get(8 * W, F32)
        xab = pool.get(8 * W, BF16)
        stg = pool.get(W, F32)
        par_dma(p, [part],
                [(ybf[:, kc * W:(kc + 1) * W], y_sh.ap()[kc * 128:(kc + 1) * 128, s0:s0 + W])
                 for kc in range(8)],
                [(ybf[:, kc * W:(kc + 1) * W], y_sh.ap()[kc * 128:(kc + 1) * 128, HALF + s0:HALF + s0 + W])
                 for kc in range(8)],
                reads=[y_sh], writes=[ybf], tag=f"fgy{si}")
        for kc in range(8):
            if si == 0:
                p.dma("sp", x2s[:, kc * W:kc * W + 1], x2b_sh.ap()[kc * 128:(kc + 1) * 128, :])
                p.dma("pool", x2bf[:, kc * W:kc * W + 1], x2b_sh.ap()[kc * 128:(kc + 1) * 128, :])
                p.dma("pool", srb[:, kc * W:kc * W + 1], srb_sh.ap()[kc * 128:(kc + 1) * 128, :])
                p.dma("sp", x2s[:, kc * W + 1:(kc + 1) * W], x2_d.ap()[kc * 128:(kc + 1) * 128, 0:WO])
                p.dma("pool", x2bf[:, kc * W + 1:(kc + 1) * W], x2_d.ap()[kc * 128:(kc + 1) * 128, 0:WO])
                p.dma("pool", srb[:, kc * W + 1:(kc + 1) * W], sr_d.ap()[kc * 128:(kc + 1) * 128, 0:WO])
            else:
                p.dma("sp", x2s[:, kc * W:(kc + 1) * W], x2_d.ap()[kc * 128:(kc + 1) * 128, s0 - 1:s0 + WO])
                p.dma("pool", x2bf[:, kc * W:(kc + 1) * W], x2_d.ap()[kc * 128:(kc + 1) * 128, s0 - 1:s0 + WO])
                p.dma("pool", srb[:, kc * W:(kc + 1) * W], sr_d.ap()[kc * 128:(kc + 1) * 128, s0 - 1:s0 + WO])
        emit_ln_fm(p, x2s, x2bf, W, zs, ln_ps, sc, tag=f"fg_z{si}")
        def xa_writer(oc, w0, bw, ps_ap):
            sl = slice(oc * W + w0, oc * W + w0 + bw)
            p.dve("tensor_mul", stg[:, 0:bw], srb[:, sl], ps_ap, tag="fg_xa1")
            p.dve("tensor_add", xas[:, sl], zs[:, sl], stg[:, 0:bw], tag="fg_xa2")
            p.act_copy(xab[:, sl], xas[:, sl], tag="fg_xab")
        emit_linear_fm(p, xa_writer, wog, ybf, W, 8, 8, pool3, blk_cols=512, tag="fg_wo")
        pool.put(ybf, srb, zs, x2bf)
        xn2 = pool.get(8 * W, BF16)
        emit_ln_fm(p, xas, xab, W, xn2, ln_ps, sc, gamma=ln2v[:, 0:8], beta=ln2v[:, 8:16],
                   tag=f"fg_l2{si}")
        pool.put(xab)
        dif = pool.get(8 * WO, BF16)
        xr2 = pool.get(8 * WO, BF16)
        if si == 0:
            for kc in range(8):
                p.dve("tensor_scalar", xn2[:, kc * W:kc * W + 1], xn2[:, kc * W:kc * W + 1],
                      shmG[:, 0:1], None, op0=ALU.mult, tag="fg_shm")
        for kc in range(8):
            a = xn2[:, kc * W + 1:kc * W + W]
            b = xn2[:, kc * W:kc * W + W - 1]
            p.dve("tensor_sub", dif[:, kc * WO:(kc + 1) * WO], a, b, tag="fg_dif")
        for kc in range(8):
            p.dve("scalar_tensor_tensor", xr2[:, kc * WO:(kc + 1) * WO],
                  dif[:, kc * WO:(kc + 1) * WO], ftm[:, 8 + kc:9 + kc],
                  xn2[:, kc * W:kc * W + W - 1], op0=ALU.mult, op1=ALU.add, tag="fg_mr")
        for kc in range(8):
            p.dve("scalar_tensor_tensor", dif[:, kc * WO:(kc + 1) * WO],
                  dif[:, kc * WO:(kc + 1) * WO], ftm[:, kc:kc + 1],
                  xn2[:, kc * W:kc * W + W - 1], op0=ALU.mult, op1=ALU.add, tag="fg_mk")
        pool.put(xn2)
        kk = pool.get(32 * WO, BF16)
        def kk_writer(oc, w0, bw, ps_ap):
            p.dve("tensor_relu", stg[:, 0:bw], ps_ap, tag="fg_kr")
            p.act(kk[:, oc * WO + w0: oc * WO + w0 + bw], stg[:, 0:bw], AF.Square, tag="fg_ks")
        emit_linear_fm(p, kk_writer, fwk, dif, WO, 8, 32, pool3, blk_cols=256,
                       w_kc_stride=4096, tag="fg_fk")
        pool.put(dif)
        rr = pool.get(8 * WO, BF16)
        for kc in range(8):
            p.dma("pool", fwr[:, kc * 1024:(kc + 1) * 1024], fwr_i[kc * 128:(kc + 1) * 128, :],
                  tag="fg_ldr")
        def rr_writer(oc, w0, bw, ps_ap):
            p.act(rr[:, oc * WO + w0: oc * WO + w0 + bw], ps_ap, AF.Sigmoid, tag="fg_rr")
        emit_linear_fm(p, rr_writer, fwr, xr2, WO, 8, 8, pool3, blk_cols=256, tag="fg_fr")
        pool.put(xr2)
        rr2 = pool.get(8 * WO, F32)
        for qt in range(4):
            fb = fwv[qt % 2]
            for kc in range(8):
                p.dma("pool", fb[:, kc * 1024:(kc + 1) * 1024],
                      fwv_i[(qt * 8 + kc) * 128:(qt * 8 + kc + 1) * 128, :], tag=f"fg_ldv{qt}")
            for oc in range(8):
                pv_ = pool3[oc % 3][:, 0:WO]
                for kc in range(8):
                    c0 = kc * 1024 + oc * 128
                    p.mm(pv_, fb[:, c0:c0 + 128],
                         kk[:, (qt * 8 + kc) * WO:(qt * 8 + kc + 1) * WO],
                         start=(kc == 0), stop=(kc == 7), tag="fg_fv")
                if qt == 0:
                    p.dve("tensor_copy", rr2[:, oc * WO:(oc + 1) * WO], pv_, tag="fg_vq0")
                else:
                    p.dve("tensor_add", rr2[:, oc * WO:(oc + 1) * WO],
                          rr2[:, oc * WO:(oc + 1) * WO], pv_, tag="fg_vq")
        for oc in range(8):
            sl = slice(oc * W + 1, oc * W + W)
            slo = slice(oc * WO, (oc + 1) * WO)
            p.dve("tensor_mul", stg[:, 0:WO], rr[:, slo], rr2[:, slo], tag="fg_v1")
            p.dve("tensor_add", stg[:, 0:WO], stg[:, 0:WO], xas[:, sl], tag="fg_v2")
            p.dve("tensor_add", stg[:, 0:WO], stg[:, 0:WO], x2s[:, sl], tag="fg_v3")
            p.dma("sp", fin[oc * 128:(oc + 1) * 128, s0:s0 + WO], stg[:, 0:WO], tag="fg_st")
        pool.put(x2s, xas, kk, rr, rr2, stg)
    pool.put(wog, fwk, *fwv)

    if env.get("dbg") == "g1":
        return
    WG = 512
    xn3 = pool.get(8 * HALF, BF16)
    w1q = [pool.get(8 * 1024, BF16) for _ in range(2)]
    w2t = pool.get(32 * 1024, BF16)
    ht = pool.get(32 * WG, BF16)
    x4s = ht[0:128, 0:2 * 8 * WG].bitcast(F32)
    x4b = ht[0:128, 2 * 8 * WG:3 * 8 * WG]
    wrt = pool.get(64, BF16)
    brw = pool.get(8, F32, rows=1)
    b1v = pool.get(256, F32, rows=128)
    b2v = pool.get(64, F32, rows=128)
    selm8 = env["selm_t"][0:9, 0:1152]
    brb = pool.get(8, F32, rows=128)
    ltm = pool.get(8, F32, rows=128)
    gx = pool.get(24, F32, rows=128)
    el = pool.get(8, F32, rows=128)
    l2t = pool.get(8, F32, rows=128)
    msk = pool.get(8, F32, rows=128)
    wtm = pool.get(16 * 8, F32, rows=128)
    wfm = pool.get(HALF, F32, rows=8)
    wbc = pool.get(WG, F32)
    stg2 = [pool.get(WG, F32) for _ in range(2)]
    p.dma("pool", wrt[:, :], wrt_i)
    p.dma("sp", brw[:, :], brw_i)
    p.dma("sp", b1v[:, :], b1_i)
    p.dma("sp", b2v[:, :], b2_i)

    p.mm(bc0[:, 0:8], ones1f[0:1, :], brw[0:1, :], tag="fg2_brb")
    p.dve("tensor_copy", brb[:, :], bc0[:, 0:8], tag="fg2_brbc")
    for si in range(HALF // WG):
        s0 = si * WG
        for kc in range(8):
            p.dma("sp", x4s[:, kc * WG:(kc + 1) * WG], fin[kc * 128:(kc + 1) * 128, s0:s0 + WG])
            p.dma("pool", x4b[:, kc * WG:(kc + 1) * WG], fin[kc * 128:(kc + 1) * 128, s0:s0 + WG])
        emit_ln_fm_strided(p, x4s, x4b, WG, xn3, HALF, s0, ln_ps, sc, tag=f"fg2_ln{si}")
        for tt in range(WG // 128):
            t0 = s0 + tt * 128
            psr = pool3[tt % 2]
            for kc in range(8):
                p.mm(psr[0:128, 0:8], xn3[:, kc * HALF + t0:kc * HALF + t0 + 128],
                     wrt[:, kc * 8:(kc + 1) * 8], start=(kc == 0), stop=(kc == 7), tag="fg2_rmm")
            p.dve("tensor_add", ltm[:, :], psr[0:128, 0:8], brb[:, :], tag="fg2_rl")
            m1 = gx[:, 0:1]; negm1 = gx[:, 1:2]; m2 = gx[:, 2:3]
            den = gx[:, 3:4]; rec = gx[:, 4:5]
            p.dve("reduce_max", m1, ltm[:, :], axis=AX.X, tag="fg2_m1")
            p.dve("tensor_scalar_mul", negm1, m1, -1.0, tag="fg2_nm")
            p.act(el[:, :], ltm[:, :], AF.Exp, bias=negm1, tag="fg2_el")
            p.dve("tensor_scalar", msk[:, :], ltm[:, :], m1, None, op0=ALU.is_equal, tag="fg2_oh")
            p.dve("scalar_tensor_tensor", l2t[:, :], msk[:, :], -1e9, ltm[:, :],
                  op0=ALU.mult, op1=ALU.add, tag="fg2_l2")
            p.dve("reduce_max", m2, l2t[:, :], axis=AX.X, tag="fg2_m2")
            p.dve("tensor_scalar", l2t[:, :], l2t[:, :], m2, None, op0=ALU.is_equal, tag="fg2_oh2")
            p.dve("tensor_add", msk[:, :], msk[:, :], l2t[:, :], tag="fg2_msk")
            p.dve("tensor_mul", el[:, :], el[:, :], msk[:, :], tag="fg2_num")
            p.dve("reduce_sum", den, el[:, :], axis=AX.X, tag="fg2_den")
            p.dve("reciprocal", rec, den, tag="fg2_rec")
            p.dve("tensor_scalar", wtm[:, (si * (WG // 128) + tt) * 8:(si * (WG // 128) + tt) * 8 + 8],
                  el[:, :], rec, None, op0=ALU.mult, tag="fg2_w")
    for t in range(16):
        p.transpose(bc1[0:8, 0:128], wtm[:, t * 8:(t + 1) * 8], ident[:, :], tag="fg2_wt")
        p.dve("tensor_copy", wfm[:, t * 128:(t + 1) * 128], bc1[0:8, 0:128], tag="fg2_wtc")
    dbgk = env.get("dbg")
    dbg_d = env.get("dbg_d")
    if dbgk == "G2_xn3":
        for kc in range(8):
            p.dma("pool", dbg_d.ap()[kc * 128:(kc + 1) * 128, :], xn3[:, kc * HALF:(kc + 1) * HALF],
                  tag="dxn3")
    if dbgk == "G2_wfm":
        p.dma("sp", dbg_d.ap()[0:8, 0:HALF], wfm[:, :], tag="dwfm")
    for e in range(8):
        for kc in range(32):
            p.dma("pool", w2t[:, kc * 1024:(kc + 1) * 1024],
                  w2_i[e * 4096 + kc * 128:e * 4096 + (kc + 1) * 128, :], tag=f"fg2_w2_{e}")
        for blk in range(HALF // WG):
            s0 = blk * WG
            for q in range(4):
                wq = w1q[q % 2]
                for kc in range(8):
                    p.dma("pool", wq[:, kc * 1024:(kc + 1) * 1024],
                          w1_i[e * 1024 + kc * 128:e * 1024 + (kc + 1) * 128,
                               q * 1024:(q + 1) * 1024], tag=f"fg2_w1_{e}_{q}")
                for hq in range(8):
                    hc = q * 8 + hq
                    psh = pool3[hc % 2]
                    for kc in range(8):
                        p.mm(psh[:, 0:WG], wq[:, kc * 1024 + hq * 128:kc * 1024 + hq * 128 + 128],
                             xn3[:, kc * HALF + s0:kc * HALF + s0 + WG],
                             start=(kc == 0), stop=(kc == 7), tag=f"fg2_h{e}")
                    p.act(ht[:, hc * WG:(hc + 1) * WG], psh[:, 0:WG], AF.Relu,
                          bias=b1v[:, e * 32 + hc:e * 32 + hc + 1], tag=f"fg2_hr{e}")
            p.mm(bc0[:, 0:WG], selm8[0:8, e * 128:(e + 1) * 128], wfm[:, s0:s0 + WG], tag="fg2_ws")
            p.dve("tensor_copy", wbc[:, :], bc0[:, 0:WG], tag="fg2_wsc")
            for oc in range(8):
                psy = wide0[:, (oc % 2) * 512:(oc % 2) * 512 + WG] if oc % 2 == 0 \
                    else wide0[:, 512:512 + WG]
                psy = wide0[:, 0:WG] if oc % 2 == 0 else wide1[:, 0:WG]
                for kc in range(32):
                    p.mm(psy, w2t[:, kc * 1024 + oc * 128:kc * 1024 + oc * 128 + 128],
                         ht[:, kc * WG:(kc + 1) * WG], start=(kc == 0), stop=(kc == 31),
                         tag=f"fg2_y{e}")
                sg = stg2[oc % 2]
                p.dve("tensor_scalar", sg[:, :], psy, b2v[:, e * 8 + oc:e * 8 + oc + 1],
                      None, op0=ALU.add, tag="fg2_yb")
                p.dve("tensor_mul", sg[:, :], sg[:, :], wbc[:, :], tag="fg2_yw")
                p.dma_accum(fin[oc * 128:(oc + 1) * 128, s0:s0 + WG], sg[:, :], tag="fg2_acc")


def _check_uniform_moe(idx_b, idx_t):
    try:
        per = {b: [] for b in range(BATCH)}
        for b, t in zip(idx_b.tolist(), idx_t.tolist()):
            per[b].append(t)
        sel_batches = [b for b in range(BATCH) if per[b]]
        for b in sel_batches:
            if per[b] != list(range(0, HALF, 2)):
                return False
        return True
    except Exception:
        return False


def kernel_fused(inputs, dbg=None):
    import ml_dtypes
    BF = ml_dtypes.bfloat16
    F = np.float32
    g = lambda k: np.asarray(inputs[k])
    x = g("L0_attn_x").astype(F)
    xT = [np.ascontiguousarray(x[b].T) for b in range(BATCH)]
    w_qkv = g("L0_attn_w_qkv")
    ident = np.eye(128, dtype=F)
    masks = _make_masks()

    idx_b = g("L0_ffn_index_b").astype(np.int64)
    idx_t = g("L0_ffn_index_t").astype(np.int64)
    sw1, sw2, sw3 = g("L0_ffn_sw1"), g("L0_ffn_sw2"), g("L0_ffn_sw3")
    pvec = np.zeros((128, 24), F)
    pvec[:, 0:8] = _pv(g("L0_attn_b_out"))
    pvec[:, 8:12] = _pv(g("L0_ffn_sb1"))
    pvec[:, 12:16] = _pv(g("L0_ffn_sb2"))
    pvec[:, 16:20] = _pv(g("L0_ffn_s_ln_g"))
    pvec[:, 20:24] = _pv(g("L0_ffn_s_ln_b"))
    w12 = np.ascontiguousarray(np.concatenate([sw1.T, sw2.T], axis=1), dtype=F)
    sw3t = np.ascontiguousarray(sw3.T, dtype=F)
    sb3v = _pv(g("L0_ffn_sb3"))
    egw = np.ascontiguousarray(
        np.concatenate([g("L0_ffn_eg_w"), g("L0_ffn_sg_w")], axis=0).T, dtype=F)
    gb9 = np.concatenate([g("L0_ffn_eg_b"), g("L0_ffn_sg_b")]).reshape(9, 1).astype(F)
    expb = g("L0_ffn_exp_bias").reshape(1, 8).astype(F)
    ew1, ew2, ew3 = g("L0_ffn_ew1"), g("L0_ffn_ew2"), g("L0_ffn_ew3")
    ew12 = np.zeros((D, 8192), F)
    ew3t = np.zeros((512, 8192), F)
    for e in range(8):
        ew12[:, e * 1024:e * 1024 + 512] = ew1[e].T
        ew12[:, e * 1024 + 512:(e + 1) * 1024] = ew2[e].T
        ew3t[:, e * 1024:(e + 1) * 1024] = ew3[e].T
    ebv = np.zeros((128, 72), F)
    eb1, eb2 = g("L0_ffn_eb1"), g("L0_ffn_eb2")
    for e in range(8):
        ebv[:, e * 8:e * 8 + 4] = _pv(eb1[e])
        ebv[:, e * 8 + 4:e * 8 + 8] = _pv(eb2[e])
    ebv[:, 64:68] = _pv(g("L0_ffn_sel_ln_g"))
    ebv[:, 68:72] = _pv(g("L0_ffn_sel_ln_b"))
    selm = np.zeros((9, 9 * 128), F)
    for e in range(9):
        selm[e, e * 128:(e + 1) * 128] = 1.0
    selm8 = np.zeros((8, 1024), F)
    for e in range(8):
        selm8[e, e * 128:(e + 1) * 128] = 1.0
    tmk = g("L1_attn_tmk").reshape(-1)
    tmv_ = g("L1_attn_tmv").reshape(-1)
    tmr = g("L1_attn_tmr").reshape(-1)
    lnvE = np.concatenate([_pv(g("L1_attn_ln1_g")), _pv(g("L1_attn_ln1_b"))], axis=1)
    tmvE = np.concatenate([_pv(tmk), _pv(tmv_), _pv(tmr),
                           _pv(1 - tmk), _pv(1 - tmv_), _pv(1 - tmr)], axis=1)
    wk3 = np.ascontiguousarray(np.concatenate(
        [g("L1_attn_Wk").T, g("L1_attn_Wv").T, g("L1_attn_Wr").T], axis=1), dtype=F)
    td = g("L1_attn_time_decay").reshape(-1)
    tf = g("L1_attn_time_first").reshape(-1)
    ftmk = g("L1_attn_f_tmk").reshape(-1)
    ftmr = g("L1_attn_f_tmr").reshape(-1)
    ln2v = np.concatenate([_pv(g("L1_attn_ln2_g")), _pv(g("L1_attn_ln2_b"))], axis=1)
    ftm = np.concatenate([_pv(ftmk), _pv(ftmr), _pv(1 - ftmk), _pv(1 - ftmr)], axis=1)
    Wrr = g("L1_ffn_W_router").astype(F)
    b1, b2 = g("L1_ffn_b1"), g("L1_ffn_b2")
    b1v = np.zeros((128, 256), F)
    b2v = np.zeros((128, 64), F)
    for e in range(8):
        b1v[:, e * 32:(e + 1) * 32] = _pv(b1[e])
        b2v[:, e * 8:(e + 1) * 8] = _pv(b2[e])

    sel_set = {(int(b), int(t)) for b, t in zip(idx_b, idx_t)}
    import time as _time
    ep = int(_time.time_ns() // 1_000_000) % 1_800_000_000
    epv_arr = np.array([[ep + 1, ep + 2, ep + 3, ep + 4, ep + 5, ep + 6, 0, ep + 1000]],
                       np.int32)
    nc = build_fused({}, dbg=dbg)
    ims = []
    for c in range(NCORES):
        b, h2 = c // 2, c % 2
        q0 = 512 * h2
        wqk = np.ascontiguousarray(np.concatenate(
            [w_qkv[:, q0:q0 + 512], w_qkv[:, 1024 + q0:1024 + q0 + 512]], axis=1))
        wv = np.ascontiguousarray(w_qkv[:, 2048 + q0:2048 + q0 + 512])
        mmask = np.zeros((128, HALF), F)
        for t in range(HALF):
            if (b, h2 * HALF + t) in sel_set:
                mmask[:, t] = 1.0
        r0 = 512 * h2
        tf8 = np.zeros((128, 8), F)
        for cc in range(4):
            tf8[:, cc] = td[r0 + cc * 128:r0 + (cc + 1) * 128]
            tf8[:, 4 + cc] = tf[r0 + cc * 128:r0 + (cc + 1) * 128]
        ims.append({
            "x0": np.ascontiguousarray(xT[b][:, :HALF]),
            "x1": np.ascontiguousarray(xT[b][:, HALF:]),
            "xh": np.ascontiguousarray(xT[b][:, h2 * HALF:(h2 + 1) * HALF]),
            "wqk": wqk, "wv": wv, "masks": masks,
            "w_out4": g("L0_attn_w_out").astype(F), "w12": w12, "sw3t": sw3t,
            "pvec": pvec, "sb3v": sb3v,
            "egw": egw, "gb9": gb9, "expb": expb, "ew12": ew12, "ew3t": ew3t,
            "eb3t": g("L0_ffn_eb3").astype(F), "ebv": ebv, "ident": ident,
            "selm": selm, "mmask": mmask.astype(np.uint8),
            "wk3": wk3, "lnv": lnvE, "tmv": tmvE,
            "shmask": np.full((128, 1), 0.0 if h2 == 0 else 1.0, F),
            "tf8": tf8,
            "wog": np.ascontiguousarray(g("L1_attn_Wo").T, dtype=F),
            "fwk": np.ascontiguousarray(g("L1_attn_fWk").T, dtype=F),
            "fwv": np.ascontiguousarray(g("L1_attn_fWv").T, dtype=F),
            "fwr": np.ascontiguousarray(g("L1_attn_fWr").T, dtype=F),
            "ln2v": ln2v, "ftm": ftm,
            "wrt": np.ascontiguousarray(
                Wrr.reshape(8, 128, 8).transpose(1, 0, 2).reshape(128, 64)),
            "brw": g("L1_ffn_b_router").reshape(1, 8).astype(F),
            "w1s": g("L1_ffn_W1").reshape(8 * D, 4096).astype(F),
            "w2s": g("L1_ffn_W2").reshape(8 * 4096, D).astype(F),
            "b1v": b1v, "b2v": b2v, "selm8": selm8,
            "parity": np.array([[h2]], np.int32),
            "epv": epv_arr,
        })
    res = run_bass_kernel_spmd(nc, ims, core_ids=list(range(NCORES))).results
    out = np.empty((BATCH, SEQ, D), np.float32)
    for c in range(NCORES):
        b, h2 = c // 2, c % 2
        out[b, h2 * HALF:(h2 + 1) * HALF, :] = np.asarray(res[c]["fin"]).T
    return out


_kernel_multilaunch = kernel


def kernel(**inputs):
    idx_b = np.asarray(inputs["L0_ffn_index_b"]).astype(np.int64)
    idx_t = np.asarray(inputs["L0_ffn_index_t"]).astype(np.int64)
    if _check_uniform_moe(idx_b, idx_t):
        return kernel_fused(inputs)
    return _kernel_multilaunch(**inputs)
